# version 02
# baseline (speedup 1.0000x reference)
import numpy as np
import ml_dtypes
from contextlib import ExitStack

import concourse.bass as bass
import concourse.mybir as mybir
from concourse.bass_utils import run_bass_kernel_spmd

F32 = mybir.dt.float32
BF16 = mybir.dt.bfloat16
U32 = mybir.dt.uint32
I32 = mybir.dt.int32
AF = mybir.ActivationFunctionType
ALU = mybir.AluOpType
AX = mybir.AxisListType

NCORE = 8
B, N, D = 2, 8192, 1024
H = 16
HALF = 4096
NT = HALF // 128
OUTT = 2048
NOT = OUTT // 128
LOCW = 64
EXPB = 20.0



def _chunks_T(w):
    dout, din = w.shape
    return np.ascontiguousarray(
        w.T.reshape(din // 128, 128, dout).transpose(1, 0, 2), dtype=np.float32)


def _chunks_N(w):
    din, dout = w.shape
    return np.ascontiguousarray(
        w.reshape(din // 128, 128, dout).transpose(1, 0, 2), dtype=np.float32)


def _host_masks(first_prev_real):
    j = np.arange(128)
    tri = (j[:, None] <= j[None, :]).astype(np.float32)
    i = np.arange(128)[None, :]
    jj = np.arange(128)[:, None]
    prev = ((i - jj + 128 >= 0) & (i - jj + 128 <= LOCW - 1)).astype(np.float32)
    cur = ((i - jj >= 0) & (i - jj <= LOCW - 1)).astype(np.float32)
    ident = np.eye(128, dtype=np.float32)
    lpf = prev if first_prev_real else np.zeros_like(prev)
    return np.ascontiguousarray(np.stack([tri, ident, lpf, cur, prev, cur], 1))


def _host_at():
    out = np.zeros((128, 4, 128), np.float32)
    t = np.arange(128)
    for lv in range(4):
        blk = 2 << lv
        A = ((t[:, None] // blk == t[None, :] // blk)
             & (t[None, :] <= t[:, None])).astype(np.float32)
        A /= (t % blk + 1)[:, None]
        out[:, lv, :] = A.T
    return out


def _host_bd(haar_wk, haar_wv):
    out = np.zeros((128, 8, 128), np.float32)
    for lv in range(4):
        out[:64, lv, :64] = haar_wk[lv].T
        out[64:, lv, 64:] = haar_wk[lv].T
        out[:64, 4 + lv, :64] = haar_wv[lv].T
        out[64:, 4 + lv, 64:] = haar_wv[lv].T
    return out



ENGS = ("sync", "tensor", "vector", "scalar", "gpsimd")


class Prog:
    def __init__(self, nc):
        self.nc = nc
        self.ops = {k: [] for k in ENGS}

    def run_block(self):
        ops, self.ops = self.ops, {k: [] for k in ENGS}
        with self.nc.Block() as blk:
            for name in ENGS:
                if ops[name]:
                    getattr(blk, name)(
                        (lambda _o: (lambda e: [f(e) for f in _o]))(ops[name]))


class Sem:
    def __init__(self, ctx, nc, name):
        self.h = ctx.enter_context(nc.semaphore(name))
        self.n = 0


def _apname(a):
    try:
        return a.tensor.name
    except AttributeError:
        pass
    try:
        return a.name
    except AttributeError:
        return str(a)


class Emit:

    TRACKED = ("vector", "scalar", "gpsimd")

    def __init__(self, P):
        self.P = P
        self.written = {k: set() for k in self.TRACKED}
        self.readed = {k: set() for k in self.TRACKED}

    def op(self, eng, fn, sem=None, k=1, r=(), w=()):
        if eng in self.written:
            rn = {_apname(a) for a in r}
            wn = {_apname(a) for a in w}
            if (rn & self.written[eng]) or (wn & (self.written[eng] | self.readed[eng])):
                self.P.ops[eng].append(lambda e: e.drain())
                self.written[eng].clear()
                self.readed[eng].clear()
            self.written[eng] |= wn
            self.readed[eng] |= rn
        semh = None
        if sem is not None:
            sem.n += k
            semh = sem.h
        def f(e, fn=fn, semh=semh, k=k):
            ins = fn(e)
            if semh is not None:
                ins.then_inc(semh, k)
        self.P.ops[eng].append(f)

    def barrier_reset(self):
        for s in self.written.values():
            s.clear()
        for s in self.readed.values():
            s.clear()

    def wait(self, eng, sem, val=None):
        v = sem.n if val is None else val
        if v <= 0:
            return
        self.P.ops[eng].append(lambda e, s=sem.h, v=v: e.wait_ge(s, v))

    def mm(self, out, lhsT, rhs, start, stop, sem=None, k=1, tp=None):
        self.op("tensor",
                lambda e, o=out, l=lhsT, r=rhs, a=start, b=stop, tp=tp:
                e.matmul(o, l, r, start=a, stop=b, tile_position=tp),
                sem, k)

    def trans(self, out, in_, ident, sem=None, k=1):
        self.op("tensor",
                lambda e, o=out, i=in_, d=ident: e.transpose(o, i, d), sem, k)


def hs(t, h):
    j, half = divmod(h, 2)
    return t[64 * half:64 * (half + 1), 128 * j:128 * (j + 1)]



def build_program(debug_taps=(), nt=NT, max_phase=99, solo=False):
    nc = bass.Bass("TRN2", target_bir_lowering=False)

    def ext(name, shape, dtype=F32):
        return nc.dram_tensor(name, list(shape), dtype, kind="ExternalInput")

    x_own = ext("x_own", [HALF, D])
    x_pre = ext("x_pre", [HALF, D])
    pre_gate = ext("pre_gate", [128, 1])
    w5T = ext("w5T", [128, 41, D])
    wgT = ext("wgT", [128, 16, D])
    woT = ext("woT", [128, 8, D])
    bd_mats = ext("bd_mats", [128, 8, 128])
    at_mats = ext("at_mats", [128, 4, 128])
    masks = ext("masks", [128, 10, 128])
    rt8 = ext("rt8", [128, 8, 8])
    ew = ext("ew", [8, 128, 12288])
    shw = ext("shw", [128, 24, 1024])
    w3T = ext("w3T", [128, 26, 1024])
    d_rope = ext("d_rope", [HALF, 1024], BF16)
    slot = ext("slot", [1, 1], U32)
    wr4 = ext("wr4", [128, 9, 8])
    ew4 = ext("ew4", [8, 128, 17, 1024])

    y = nc.dram_tensor("y", [HALF, D], F32, kind="ExternalOutput")
    y_x1 = None
    if "x1" in debug_taps:
        y_x1 = nc.dram_tensor("y_x1", [HALF, D], F32, kind="ExternalOutput")
    y_x2 = None
    if "x2" in debug_taps:
        y_x2 = nc.dram_tensor("y_x2", [HALF, D], F32, kind="ExternalOutput")
    y_x3 = None
    if "x3" in debug_taps:
        y_x3 = nc.dram_tensor("y_x3", [HALF, D], F32, kind="ExternalOutput")

    d_x1 = nc.dram_tensor("d_x1", [HALF, D], F32, kind="Internal")
    d_ln1 = nc.dram_tensor("d_ln1", [HALF, D], BF16, kind="Internal")
    d_x2 = nc.dram_tensor("d_x2", [HALF, D], F32, kind="Internal")
    d_ln2 = nc.dram_tensor("d_ln2", [HALF, D], BF16, kind="Internal")
    d_x3 = nc.dram_tensor("d_x3", [HALF, D], F32, kind="Internal")
    d_ln3 = nc.dram_tensor("d_ln3", [HALF, D], BF16, kind="Internal")
    d_ex = nc.dram_tensor("d_ex", [2, 128, 520], F32, kind="Internal",
                          addr_space="Shared")
    d_exf = nc.dram_tensor("d_exf", [2, 16], U32, kind="Internal",
                           addr_space="Shared")
    d_ex2 = nc.dram_tensor("d_ex2", [2, 1, 8], F32, kind="Internal",
                           addr_space="Shared")

    P = Prog(nc)
    E = Emit(P)

    with ExitStack() as pc:
        sx = Sem(pc, nc, "sx")
        sxp = [Sem(pc, nc, "sxp0"), Sem(pc, nc, "sxp1")]
        sw = Sem(pc, nc, "sw")
        st = Sem(pc, nc, "st")
        sv = Sem(pc, nc, "sv")
        ss = Sem(pc, nc, "ss")
        sg = Sem(pc, nc, "sg")
        sd = Sem(pc, nc, "sd")
        sdg = Sem(pc, nc, "sdg")
        sxc = Sem(pc, nc, "sxc")
        sep = [Sem(pc, nc, "sep0"), Sem(pc, nc, "sep1")]
        sec = Sem(pc, nc, "sec")
        sxq = [Sem(pc, nc, "sxq0"), Sem(pc, nc, "sxq1")]
        spl = Sem(pc, nc, "spl")
        sgw = Sem(pc, nc, "sgw")
        stf = Sem(pc, nc, "stf")

        with ExitStack() as c1:
            en = c1.enter_context
            W5 = en(nc.sbuf_tensor("W5", [128, 41, D], BF16))
            WG = en(nc.sbuf_tensor("WG", [128, 16, D], BF16))
            WO = en(nc.sbuf_tensor("WO", [128, 8, D], BF16))
            BD = en(nc.sbuf_tensor("BD", [128, 8, 128], BF16))
            AT = en(nc.sbuf_tensor("AT", [128, 4, 128], BF16))
            MK = en(nc.sbuf_tensor("MK", [128, 10, 128], BF16))
            BRh = en(nc.sbuf_tensor("BRh", [1, 64], BF16))
            PG = en(nc.sbuf_tensor("PG", [128, 1], BF16))
            ONES = en(nc.sbuf_tensor("ONES", [128, 1], BF16))
            ONR = en(nc.sbuf_tensor("ONR", [128, 128], BF16))
            WGOb = en(nc.sbuf_tensor("WGOb", [128, D], BF16))
            SWb = en(nc.sbuf_tensor("SWb", [128, 4], F32))
            BGOb = en(nc.sbuf_tensor("BGOb", [128, 1], F32))
            SKV = en(nc.sbuf_tensor("SKV", [128, 5, 512], F32))
            SKc = en(nc.sbuf_tensor("SKc", [128, 5, 8], F32))
            SKVh = en(nc.sbuf_tensor("SKVh", [128, 512], BF16))
            SKch = en(nc.sbuf_tensor("SKch", [128, 16], BF16))
            XI = en(nc.sbuf_tensor("XI", [128, 2, D], F32))
            SM = en(nc.sbuf_tensor("SM", [128, 64], F32))
            LNt = en(nc.sbuf_tensor("LNt", [128, D], BF16))
            LNf = en(nc.sbuf_tensor("LNf", [128, D], BF16))
            Qt = en(nc.sbuf_tensor("Qt", [128, D], BF16))
            Qf = en(nc.sbuf_tensor("Qf", [128, D], BF16))
            PQf = en(nc.sbuf_tensor("PQf", [128, D], BF16))
            Kt = en(nc.sbuf_tensor("Kt", [128, D], BF16))
            PKt = en(nc.sbuf_tensor("PKt", [128, D], BF16))
            PKf = en(nc.sbuf_tensor("PKf", [128, D], BF16))
            Vt = en(nc.sbuf_tensor("Vt", [128, D], BF16))
            KLt = en(nc.sbuf_tensor("KLt", [128, D], BF16))
            KLfb = [en(nc.sbuf_tensor(f"KLf{i}", [128, D], BF16)) for i in range(2)]
            VLtb = [en(nc.sbuf_tensor(f"VLt{i}", [128, D], BF16)) for i in range(2)]
            KM = en(nc.sbuf_tensor("KM", [128, D], BF16))
            PTS = en(nc.sbuf_tensor("PTS", [128, 8, 128], BF16))
            GLB = en(nc.sbuf_tensor("GLB", [128, D], F32))
            DIFW = en(nc.sbuf_tensor("DIFW", [128, D], F32))
            DIFf = en(nc.sbuf_tensor("DIFf", [128, D], BF16))
            MX = en(nc.sbuf_tensor("MXD", [128, D], BF16))
            TM0 = en(nc.sbuf_tensor("TM0", [128, D], F32))
            PF = en(nc.psum_tensor("PF", [128, 7, 512], F32))
            PH = en(nc.psum_tensor("PH", [128, 1024], BF16))

            ident = MK[:, 1, :]
            DIF = Qt
            LN1 = LNt
            PS1 = LNt
            GH = LNt
            MXf = DIFf
            PS3 = MX
            SQS = DIFW
            VLVt = KM
            X1 = GLB
            KLVf = PKf
            KLVt = PKt

            def wload(dst, srcp, nch):
                step = 4
                for c0 in range(0, nch, step):
                    c1 = min(nch, c0 + step)
                    E.op("gpsimd", lambda e, d=dst[:, c0:c1, :], s=srcp[:, c0:c1, :]:
                         e.dma_start(d, s), sw, 16)
            wload(W5, w5T, 41)
            wload(WG, wgT, 16)
            wload(WO, woT, 8)
            for dst, srcp in ((BD[:], bd_mats[:]), (AT[:], at_mats[:]),
                              (MK[:], masks[:]), (PG[:], pre_gate[:])):
                E.op("gpsimd", lambda e, d=dst, s=srcp: e.dma_start(d, s), sw, 16)
            E.op("gpsimd", lambda e: e.memset(ONES[:], 1.0))
            E.op("gpsimd", lambda e: e.memset(ONR[:], 1.0))
            E.op("gpsimd", lambda e: e.memset(SKV[:], 0.0))
            E.op("gpsimd", lambda e: e.memset(SKc[:], 0.0))
            E.op("gpsimd", lambda e: e.memset(KLfb[1][:], 0.0))
            E.op("gpsimd", lambda e: e.memset(VLtb[1][:], 0.0))
            E.op("gpsimd", lambda e: e.memset(SM[:, 49:50], -EXPB))
            E.op("gpsimd", lambda e: e.memset(SM[:, 52:53], 1e-6))
            E.op("gpsimd", lambda e: e.memset(SM[:, 50:51], 0.125), sg, 1)

            E.wait("tensor", sw)
            E.wait("tensor", sg)
            E.mm(PF[:, 4, :], ONR[64:65, :], W5[64:65, 40, 0:512], True, True,
                 None, 1, tp=(64, 0))
            E.mm(PF[:, 5, :], ONR[64:65, :], W5[64:65, 40, 512:1024], True, True,
                 st, 1, tp=(64, 0))
            E.wait("vector", st)
            E.wait("vector", sw)
            E.op("vector", lambda e: e.tensor_copy(WGOb[:, 0:512], PF[:, 4, :]), w=(WGOb,))
            E.op("vector", lambda e: e.tensor_copy(WGOb[:, 512:1024], PF[:, 5, :]), w=(WGOb,))
            E.op("vector", lambda e: e.tensor_scalar(
                W5[:, 24:32, :], W5[:, 24:32, :], 0.125, None, ALU.mult), w=(W5,))
            E.op("vector", lambda e: e.tensor_copy(SM[0:1, 16:20], MK[0:1, 6, 0:4]), r=(MK,), w=(SM,))
            E.op("vector", lambda e: e.scalar_tensor_tensor(
                SM[0:1, 20:21], SM[0:1, 16:17], -1.0, SM[0:1, 17:18],
                ALU.mult, ALU.add), r=(SM,), w=(SM,))
            E.op("vector", lambda e: e.scalar_tensor_tensor(
                SM[0:1, 21:22], SM[0:1, 18:19], -1.0, SM[0:1, 19:20],
                ALU.mult, ALU.add), sv, 1, r=(SM,), w=(SM,))
            E.wait("scalar", sv)
            E.op("scalar", lambda e: e.activation(SM[0:1, 20:22], SM[0:1, 20:22], AF.Relu),
                 ss, 1, r=(SM,), w=(SM,))
            E.wait("vector", ss)
            E.op("vector", lambda e: e.tensor_add(SM[0:1, 22:23], SM[0:1, 16:17], SM[0:1, 20:21]), r=(SM,), w=(SM,))
            E.op("vector", lambda e: e.tensor_add(SM[0:1, 23:24], SM[0:1, 18:19], SM[0:1, 21:22]), r=(SM,), w=(SM,))
            E.op("vector", lambda e: e.scalar_tensor_tensor(
                SM[0:1, 24:25], SM[0:1, 22:23], -1.0, SM[0:1, 23:24],
                ALU.mult, ALU.add), sv, 1, r=(SM,), w=(SM,))
            E.wait("scalar", sv)
            E.op("scalar", lambda e: e.activation(SM[0:1, 24:25], SM[0:1, 24:25], AF.Relu),
                 ss, 1, r=(SM,), w=(SM,))
            E.wait("vector", ss)
            E.op("vector", lambda e: e.tensor_add(SM[0:1, 0:1], SM[0:1, 22:23], SM[0:1, 24:25]), r=(SM,), w=(SM,))
            E.op("vector", lambda e: e.tensor_scalar(
                SM[0:1, 25:26], SM[0:1, 0:1], -1.0, None, ALU.mult), r=(SM,), w=(SM,))
            E.op("vector", lambda e: e.tensor_scalar(
                SM[0:1, 1:5], SM[0:1, 16:20], SM[0:1, 25:26], None, ALU.add), sv, 1, r=(SM,), w=(SM,))
            E.wait("scalar", sv)
            E.op("scalar", lambda e: e.activation(
                SM[0:1, 5:9], SM[0:1, 1:5], AF.Exp, accum_out=SM[0:1, 9:10]), ss, 1, r=(SM,), w=(SM,))
            E.wait("vector", ss)
            E.op("vector", lambda e: e.reciprocal(SM[0:1, 10:11], SM[0:1, 9:10]), r=(SM,), w=(SM,))
            E.op("vector", lambda e: e.tensor_scalar(
                SM[0:1, 11:15], SM[0:1, 5:9], SM[0:1, 10:11], None, ALU.mult), r=(SM,), w=(SM,))
            E.op("vector", lambda e: e.tensor_copy(BRh[0:1, 0:4], SM[0:1, 11:15]), r=(SM,), w=(BRh,))
            E.op("vector", lambda e: e.tensor_copy(BRh[0:1, 4:5], MK[0:1, 6, 4:5]), sv, 1, r=(MK,), w=(BRh,))
            E.wait("tensor", sv)
            E.mm(PF[:, 0, 0:5], ONR[0:1, :], BRh[0:1, 0:5], True, True, st, 1)
            E.wait("vector", st)
            E.op("vector", lambda e: e.tensor_copy(SWb[:], PF[:, 0, 0:4]), w=(SWb,))
            E.op("vector", lambda e: e.tensor_copy(BGOb[:], PF[:, 0, 4:5]), sv, 1, w=(BGOb,))

            with nc.named_scope("s1pre"):
                P.run_block()
            E.barrier_reset()

            n_prior_sx = [sx.n // 16]

            def emit_ln(xsrc, out_bf, eps=1e-5):
                E.op("vector", lambda e, x=xsrc: e.reduce_sum(SM[:, 0:1], x, AX.X), sv, 1,
                     r=(xsrc,), w=(SM,))
                E.op("scalar", lambda e, x=xsrc: e.activation(
                    SQS[:], x, AF.Square, accum_out=SM[:, 1:2]), ss, 1,
                    r=(xsrc,), w=(SQS, SM))
                E.wait("vector", ss)
                E.op("vector", lambda e: e.tensor_scalar(
                    SM[:, 2:3], SM[:, 0:1], 1.0 / D, None, ALU.mult), r=(SM,), w=(SM,))
                E.op("vector", lambda e: e.tensor_mul(SM[:, 3:4], SM[:, 2:3], SM[:, 2:3]), r=(SM,), w=(SM,))
                E.op("vector", lambda e: e.scalar_tensor_tensor(
                    SM[:, 4:5], SM[:, 1:2], 1.0 / D, SM[:, 3:4], ALU.mult, ALU.subtract), r=(SM,), w=(SM,))
                E.op("vector", lambda e, p=eps: e.tensor_scalar(
                    SM[:, 4:5], SM[:, 4:5], p, None, ALU.add), sv, 1, r=(SM,), w=(SM,))
                E.wait("scalar", sv)
                E.op("scalar", lambda e: e.sqrt(SM[:, 5:6], SM[:, 4:5]), ss, 1, r=(SM,), w=(SM,))
                E.wait("vector", ss)
                E.op("vector", lambda e: e.reciprocal(SM[:, 6:7], SM[:, 5:6]), r=(SM,), w=(SM,))
                E.op("vector", lambda e: e.tensor_mul(SM[:, 7:8], SM[:, 2:3], SM[:, 6:7]), r=(SM,), w=(SM,))
                E.op("vector", lambda e: e.tensor_scalar(
                    SM[:, 7:8], SM[:, 7:8], -1.0, None, ALU.mult), sv, 1, r=(SM,), w=(SM,))
                E.wait("scalar", sv)
                E.op("scalar", lambda e, o=out_bf, x=xsrc: e.activation(
                    o, x, AF.Identity, bias=SM[:, 7:8], scale=SM[:, 6:7]), ss, 1,
                    r=(xsrc, SM), w=(out_bf,))

            def emit_phi(xin_bf, out_bf):
                E.wait("scalar", sv)
                E.op("scalar", lambda e, x=xin_bf, o=out_bf: e.activation(o, x, AF.Relu),
                     ss, 1, r=(xin_bf,), w=(out_bf,))
                E.wait("vector", ss)
                E.op("vector", lambda e, x=xin_bf, o=out_bf: e.scalar_tensor_tensor(
                    PS1[:], o, -1.0, x, ALU.mult, ALU.add), sv, 1,
                    r=(xin_bf, out_bf), w=(PS1,))
                E.wait("scalar", sv)
                E.op("scalar", lambda e: e.activation(PS1[:], PS1[:], AF.Exp), ss, 1,
                     r=(PS1,), w=(PS1,))
                E.wait("vector", ss)
                E.op("vector", lambda e, o=out_bf: e.tensor_add(o, o, PS1[:]), sv, 1,
                     r=(out_bf, PS1), w=(out_bf,))

            def trans8(src_bf, dst_bf, bank):
                E.wait("tensor", sv)
                E.wait("tensor", ss)
                for j in range(8):
                    E.trans(bank[:, 128 * j:128 * (j + 1)],
                            src_bf[:, 128 * j:128 * (j + 1)], ident,
                            st if j == 7 else None, 1)
                E.wait("vector", st)
                E.op("vector", lambda e, d=dst_bf, b=bank: e.tensor_copy(d, b), sv, 1,
                     w=(dst_bf,))

            def hmm8(dst_bf, src_tok, rhs_at, bd_slice):
                E.wait("tensor", sv)
                for j in range(8):
                    bank, col = j // 4, (j % 4) * 128
                    if bd_slice is None:
                        E.mm(PF[:, bank, col:col + 128],
                             src_tok[:, 128 * j:128 * (j + 1)], rhs_at,
                             True, True, st if j == 7 else None, 1)
                    else:
                        E.mm(PF[:, bank, col:col + 128], bd_slice,
                             src_tok[:, 128 * j:128 * (j + 1)],
                             True, True, st if j == 7 else None, 1)
                E.wait("vector", st)
                E.op("vector", lambda e, d=dst_bf: e.tensor_copy(
                    d[:, 0:512], PF[:, 0, :]), w=(dst_bf,))
                E.op("vector", lambda e, d=dst_bf: e.tensor_copy(
                    d[:, 512:1024], PF[:, 1, :]), sv, 1, w=(dst_bf,))

            def haar_prep(lv):
                hmm8(KM[:], Kt[:], AT[:, lv, :], None)
                hmm8(PS3[:], KM[:], None, BD[:, lv, :])
                emit_phi(PS3[:], KLVf[:])
                trans8(KLVf[:], KLVt[:], PH[:])
                hmm8(KM[:], Vt[:], AT[:, lv, :], None)
                hmm8(PS3[:], KM[:], None, BD[:, 4 + lv, :])
                trans8(PS3[:], VLVt[:], PH[:])

            def tile_body(ti, own, xsrc_dram, sk_rhs, base_consumed):
                xi = XI[:, ti % 2, :]
                par = (ti % 2) if own else ((ti + nt) % 2)
                kl_cur, kl_prv = KLfb[par], KLfb[1 - par]
                vl_cur, vl_prv = VLtb[par], VLtb[1 - par]

                if ti + 1 < nt:
                    E.wait("sync", sxc, base_consumed + ti)
                    E.op("sync", lambda e, t=ti + 1, s=xsrc_dram: e.dma_start(
                        XI[:, t % 2, :], s[128 * t:128 * (t + 1), :]),
                        sxp[(ti + 1) % 2], 16)

                E.wait("vector", sxp[ti % 2])
                E.wait("scalar", sxp[ti % 2])
                E.wait("scalar", sd)
                E.wait("vector", sd)

                def _cut0(ph):
                    if max_phase < ph:
                        if own:
                            E.op("vector", lambda e: e.memzero(SM[:, 51:52]),
                                 sxc, 1, w=(SM,))
                        else:
                            E.op("scalar", lambda e: e.memzero(SM[:, 48:49]),
                                 sxc, 1, w=(SM,))
                        return True
                    return False

                emit_ln(xi, LNt[:])
                if _cut0(0.3):
                    return
                trans8(LNt[:], LNf[:], PH[:])
                if _cut0(0.6):
                    return

                E.wait("tensor", sv)
                grpA = [(0, 0), (1, 2), (2, 4)] if own else [(1, 0), (2, 2)]
                for j in range(8):
                    lhsT = LNf[:, 128 * j:128 * (j + 1)]
                    for gi, (pi, bank) in enumerate(grpA):
                        for hcol in range(2):
                            last = (j == 7 and gi == len(grpA) - 1 and hcol == 1)
                            E.mm(PF[:, bank + hcol, :], lhsT,
                                 W5[:, 8 * pi + j, 512 * hcol:512 * (hcol + 1)],
                                 j == 0, j == 7, st if last else None, 1)
                E.wait("scalar", st)
                E.wait("vector", st)
                kb = 2 if own else 0
                vb = 4 if own else 2
                if own:
                    E.op("scalar", lambda e: e.copy(Qt[:, 0:512], PF[:, 0, :]), w=(Qt,))
                    E.op("scalar", lambda e: e.copy(Qt[:, 512:1024], PF[:, 1, :]), ss, 1, w=(Qt,))
                E.op("vector", lambda e, b=kb: e.tensor_copy(Kt[:, 0:512], PF[:, b, :]), w=(Kt,))
                E.op("vector", lambda e, b=kb: e.tensor_copy(Kt[:, 512:1024], PF[:, b + 1, :]), w=(Kt,))
                E.op("vector", lambda e, b=vb: e.tensor_copy(Vt[:, 0:512], PF[:, b, :]), w=(Vt,))
                E.op("vector", lambda e, b=vb: e.tensor_copy(Vt[:, 512:1024], PF[:, b + 1, :]), sv, 1, w=(Vt,))

                _cut = _cut0

                if _cut(1):
                    return
                tail = (not own) and ti == nt - 1
                if own or tail:
                    E.wait("tensor", sv)
                    E.wait("tensor", ss)
                    for j in range(8):
                        lhsT = LNf[:, 128 * j:128 * (j + 1)]
                        for gi, (pi, bank) in enumerate([(3, 0), (4, 2)]):
                            for hcol in range(2):
                                last = (j == 7 and gi == 1 and hcol == 1)
                                E.mm(PF[:, bank + hcol, :], lhsT,
                                     W5[:, 8 * pi + j, 512 * hcol:512 * (hcol + 1)],
                                     j == 0, j == 7, st if last else None, 1)
                    E.wait("vector", st)
                    E.op("vector", lambda e: e.tensor_copy(KLt[:, 0:512], PF[:, 0, :]), w=(KLt,))
                    E.op("vector", lambda e: e.tensor_copy(KLt[:, 512:1024], PF[:, 1, :]), w=(KLt,))
                    E.op("vector", lambda e, v=vl_cur: e.tensor_copy(v[:, 0:512], PF[:, 2, :]), w=(vl_cur,))
                    E.op("vector", lambda e, v=vl_cur: e.tensor_copy(v[:, 512:1024], PF[:, 3, :]), sv, 1, w=(vl_cur,))

                if _cut(2):
                    return
                emit_phi(Kt[:], PKt[:])
                if _cut(2.2):
                    return
                if own:
                    trans8(PKt[:], PKf[:], PH[:])
                if _cut(2.4):
                    return
                if own:
                    trans8(Qt[:], Qf[:], PH[:])
                if _cut(2.6):
                    return
                if own or tail:
                    trans8(KLt[:], kl_cur[:], PH[:])
                if _cut(2.8):
                    return
                if own:
                    emit_phi(Qf[:], PQf[:])

                if _cut(3):
                    return
                ncalls = min(5, int(max_phase) - 3) if max_phase >= 4 \
                    else (1 if max_phase > 3.05 else 0)
                for cc in range(ncalls):
                    if cc > 0:
                        haar_prep(cc - 1)
                    kf = PKf[:] if cc == 0 else KLVf[:]
                    ktok = PKt[:] if cc == 0 else KLVt[:]
                    vtok = Vt[:] if cc == 0 else VLVt[:]
                    den_bank = 4

                    if cc == 0 and _cut(3.07):
                        return
                    E.wait("tensor", sv)
                    if cc == 0 and _cut(3.1):
                        return
                    if own:
                        E.op("vector", lambda e, cc=cc: e.tensor_copy(SKVh[:], SKV[:, cc, :]),
                             r=(SKV,), w=(SKVh,))
                        E.op("vector", lambda e, cc=cc: e.tensor_copy(
                            SKch[:].rearrange("p (a b) -> p a b", a=8),
                            SKc[:, cc, :].unsqueeze(2).broadcast_to([128, 8, 2])), sv, 1,
                            r=(SKc,), w=(SKch,))
                        E.wait("tensor", sv)
                        for hg in range(4):
                            p = hg % 2
                            pslot = PTS[:, 4 * p:4 * p + 4, :]
                            if hg >= 2:
                                E.wait("tensor", sv)
                            for half in range(2):
                                for jj in range(2):
                                    h = 4 * hg + 2 * jj + half
                                    E.mm(PF[:, half, 256 * p + 128 * jj:256 * p + 128 * jj + 128],
                                         hs(kf, h), hs(PQf, h), True, True,
                                         st if jj == 1 else None, 1,
                                         tp=(64 * half, 0))
                            E.wait("vector", st)
                            E.op("vector", lambda e, p=p, ps=pslot: e.tensor_tensor(
                                ps.rearrange("p (a b) c -> p a b c", a=2),
                                PF[:, 0:2, 256 * p:256 * p + 256].rearrange(
                                    "p a (b c) -> p a b c", b=2),
                                MK[:, 0:1, :].unsqueeze(1).broadcast_to([128, 2, 2, 128]),
                                ALU.mult), sv, 1, r=(MK,), w=(PTS,))
                            E.wait("tensor", sv)
                            for idx in range(4):
                                half, jj = divmod(idx, 2)
                                h = 4 * hg + 2 * jj + half
                                slot = PTS[:, 4 * p + idx, :]
                                ob = PF[:, 2 + h // 8, 64 * (h % 8):64 * (h % 8) + 64]
                                E.mm(ob, slot, vtok[:, 64 * h:64 * h + 64], True, True)
                                E.mm(PF[:, den_bank, h:h + 1], slot, ONES[:],
                                     True, True)
                        E.wait("tensor", sv)
                        for h in range(0, 16, 2):
                            E.mm(PF[:, 0, 64 * (h // 2):64 * (h // 2) + 64],
                                 hs(PQf, h), SKVh[0:64, 64 * (h // 2):64 * (h // 2) + 64],
                                 True, True, None, 1, tp=(0, 0))
                        for h in range(0, 16, 2):
                            j = h // 2
                            E.mm(PF[:, den_bank, 16 + 2 * j:18 + 2 * j],
                                 hs(PQf, h), SKch[0:64, 2 * j:2 * j + 2],
                                 True, True, None, 1, tp=(0, 0))
                        for h in range(1, 16, 2):
                            E.mm(PF[:, 1, 64 * (h // 2):64 * (h // 2) + 64],
                                 hs(PQf, h), SKVh[64:128, 64 * (h // 2):64 * (h // 2) + 64],
                                 True, True, None, 1, tp=(64, 0))
                    if cc == 0 and _cut(3.3):
                        return
                    for j in range(8):
                        E.mm(PF[:, 5 + j // 4, (j % 4) * 128:(j % 4) * 128 + 128],
                             ktok[:, 128 * j:128 * (j + 1)],
                             vtok[:, 128 * j:128 * (j + 1)],
                             True, True, None, 1)
                        E.mm(PF[:, den_bank, 32 + j:33 + j],
                             ktok[:, 128 * j:128 * (j + 1)], sk_rhs,
                             True, True,
                             (st if j == 7 and not own else None), 1)
                    if cc == 0 and _cut(3.4):
                        return
                    if own:
                        for h in range(1, 16, 2):
                            j = h // 2
                            E.mm(PF[:, den_bank, 40 + 2 * j:42 + 2 * j],
                                 hs(PQf, h), SKch[64:128, 2 * j:2 * j + 2],
                                 True, True, st if h == 15 else None, 1,
                                 tp=(64, 0))
                    oden_st = st.n
                    if cc == 0 and _cut(3.5):
                        return

                    E.wait("vector", st, oden_st)
                    if own:
                        E.op("vector", lambda e, b=den_bank: e.tensor_copy(
                            SM[:, 32:48].rearrange("p (a b) -> p a b", a=8)[:, :, 0:1],
                            PF[:, b, 16:32].rearrange("p (a b) -> p a b", a=8)[:, :, 0:1]),
                            w=(SM,))
                        E.op("vector", lambda e, b=den_bank: e.tensor_copy(
                            SM[:, 32:48].rearrange("p (a b) -> p a b", a=8)[:, :, 1:2],
                            PF[:, b, 40:56].rearrange("p (a b) -> p a b", a=8)[:, :, 0:1]),
                            w=(SM,))
                        E.op("vector", lambda e, b=den_bank: e.tensor_add(
                            SM[:, 16:32], PF[:, b, 0:16], SM[:, 32:48]),
                            r=(SM,), w=(SM,))
                        E.op("vector", lambda e: e.scalar_tensor_tensor(
                            SM[:, 32:48], SM[:, 16:32], -1.0,
                            SM[:, 52:53].broadcast_to([128, 16]),
                            ALU.mult, ALU.add), sv, 1, r=(SM,), w=(SM,))
                        E.wait("scalar", sv)
                        E.op("scalar", lambda e: e.activation(
                            SM[:, 32:48], SM[:, 32:48], AF.Relu), ss, 1,
                            r=(SM,), w=(SM,))
                        E.wait("vector", ss)
                        E.op("vector", lambda e: e.tensor_add(
                            SM[:, 16:32], SM[:, 16:32], SM[:, 32:48]),
                            r=(SM,), w=(SM,))
                        E.op("vector", lambda e: e.reciprocal(SM[:, 16:32], SM[:, 16:32]),
                             r=(SM,), w=(SM,))
                        if cc > 0:
                            E.op("vector", lambda e, cc=cc: e.tensor_scalar(
                                SM[:, 16:32], SM[:, 16:32], SWb[:, cc - 1:cc], None,
                                ALU.mult), r=(SM, SWb), w=(SM,))
                        for hb in range(2):
                            oa = PF[:, 2 + hb, :].rearrange(
                                "p (a b c) -> p a b c", a=4, b=2)
                            obp = PF[:, 0:2, 256 * hb:256 * hb + 256].rearrange(
                                "p a (b c) -> p b a c", b=4)
                            coef_b = SM[:, 16 + 8 * hb:24 + 8 * hb].rearrange(
                                "p (a b) -> p a b", a=4) \
                                .unsqueeze(3).broadcast_to([128, 4, 2, 64])
                            tm4 = TM0[:, 0:512].rearrange(
                                "p (a b c) -> p a b c", a=4, b=2)
                            E.op("vector", lambda e, ob=obp, t4=tm4: e.tensor_copy(
                                t4, ob), w=(TM0,))
                            E.op("vector", lambda e, oa=oa, t4=tm4: e.scalar_tensor_tensor(
                                t4, oa, 1.0, t4, ALU.mult, ALU.add), r=(TM0,), w=(TM0,))
                            E.op("vector", lambda e, cb=coef_b, t4=tm4: e.tensor_tensor(
                                t4, t4, cb, ALU.mult), r=(SM, TM0), w=(TM0,))
                            if cc == 0:
                                E.op("vector", lambda e, hb=hb: e.tensor_copy(
                                    GLB[:, 512 * hb:512 * (hb + 1)], TM0[:, 0:512]),
                                    r=(TM0,), w=(GLB,))
                            else:
                                E.op("vector", lambda e, hb=hb: e.tensor_add(
                                    GLB[:, 512 * hb:512 * (hb + 1)],
                                    GLB[:, 512 * hb:512 * (hb + 1)], TM0[:, 0:512]),
                                    r=(GLB, TM0), w=(GLB,))
                    if cc == 0 and _cut(3.7):
                        return
                    for half in range(2):
                        pr = slice(64 * half, 64 * (half + 1))
                        co = slice(64 * half, 64 * half + 64)
                        E.op("vector", lambda e, cc=cc, pr=pr, co=co: e.tensor_add(
                            SKV[pr, cc, :].rearrange("p (a b) -> p a b", a=8),
                            SKV[pr, cc, :].rearrange("p (a b) -> p a b", a=8),
                            PF[pr, 5:7, :].rearrange("p a (b c) -> p (a b) c", b=4)[:, :, co]),
                            r=(SKV,), w=(SKV,))
                    E.op("vector", lambda e, cc=cc, b=den_bank: e.tensor_add(
                        SKc[:, cc, :], SKc[:, cc, :], PF[:, b, 32:40]), sv, 1,
                        r=(SKc,), w=(SKc,))

                if _cut(9):
                    return
                if own:
                    E.wait("tensor", sv)
                    for hp in range(8):
                        p = hp % 2
                        plslot = PTS[:, 4 * p:4 * p + 4, :]
                        if hp >= 2:
                            E.wait("tensor", sg)
                        for half in range(2):
                            h = 2 * hp + half
                            for w, klw in ((0, kl_prv), (1, kl_cur)):
                                E.mm(PF[:, half, 256 * p + 128 * w:256 * p + 128 * w + 128],
                                     hs(klw, h), hs(Qf, h), True, True,
                                     st if w == 1 else None, 1,
                                     tp=(64 * half, 0))
                        E.wait("scalar", st)
                        E.op("scalar", lambda e, p=p, ps=plslot: e.activation(
                            ps.rearrange("p (a b) c -> p a b c", a=2),
                            PF[:, 0:2, 256 * p:256 * p + 256].rearrange(
                                "p a (b c) -> p a b c", b=2),
                            AF.Exp, bias=SM[:, 49:50]), ss, 1,
                            r=(SM,), w=(PTS,))
                        E.wait("vector", ss)
                        mk0 = 2 if ti == 0 else 4
                        E.op("vector", lambda e, ps=plslot, m0=mk0: e.tensor_tensor(
                            ps.rearrange("p (a b) c -> p a b c", a=2),
                            ps.rearrange("p (a b) c -> p a b c", a=2),
                            MK[:, m0:m0 + 2, :].unsqueeze(1).broadcast_to([128, 2, 2, 128]),
                            ALU.mult), sg, 1, r=(PTS, MK), w=(PTS,))
                        E.wait("tensor", sg)
                        for half in range(2):
                            h = 2 * hp + half
                            ob = PF[:, 2 + h // 8, 64 * (h % 8):64 * (h % 8) + 64]
                            dn = PF[:, 4, h:h + 1]
                            sprv = PTS[:, 4 * p + 2 * half, :]
                            scur = PTS[:, 4 * p + 2 * half + 1, :]
                            E.mm(ob, sprv, vl_prv[:, 64 * h:64 * h + 64],
                                 True, False)
                            E.mm(ob, scur, vl_cur[:, 64 * h:64 * h + 64],
                                 False, True)
                            E.mm(dn, sprv, ONES[:], True, False)
                            E.mm(dn, scur, ONES[:], False, True,
                                 st if h == H - 1 else None, 1)
                    loc_st = st.n
                    E.wait("vector", st, loc_st)
                    E.op("vector", lambda e: e.reciprocal(SM[:, 32:48], PF[:, 4, 0:16]), w=(SM,))
                    for hb in range(2):
                        coef_b = SM[:, 32 + 8 * hb:40 + 8 * hb] \
                            .unsqueeze(2).broadcast_to([128, 8, 64])
                        E.op("vector", lambda e, hb=hb, cb=coef_b: e.tensor_tensor(
                            TM0[:, 512 * hb:512 * (hb + 1)].rearrange(
                                "p (a b) -> p a b", a=8),
                            PF[:, 2 + hb, :].rearrange("p (a b) -> p a b", a=8),
                            cb, ALU.mult), r=(SM,), w=(TM0,))
                    E.op("vector", lambda e: e.scalar_tensor_tensor(
                        DIFW[:], GLB[:], -1.0, TM0[:], ALU.mult, ALU.add),
                        r=(TM0, GLB), w=(DIFW,))
                    E.op("vector", lambda e: e.tensor_copy(DIF[:], DIFW[:]), sv, 1,
                         r=(DIFW,), w=(DIF,))

                if _cut(10):
                    return
                if own:
                    pass
                if own:
                    trans8(DIF[:], DIFf[:], PH[:])
                    E.wait("tensor", sv)
                    for j in range(16):
                        lhsT = LNf[:, 128 * j:128 * (j + 1)] if j < 8 \
                            else DIFf[:, 128 * (j - 8):128 * (j - 7)]
                        for hcol in range(2):
                            E.mm(PF[:, hcol, :], lhsT,
                                 WG[:, j, 512 * hcol:512 * (hcol + 1)],
                                 j == 0, False)
                    for hcol in range(2):
                        E.mm(PF[:, hcol, :], MK[:, 7, :],
                             W5[:, 40, 512 * hcol:512 * (hcol + 1)], False, True,
                             st if hcol == 1 else None, 1)
                    E.wait("scalar", st)
                    E.op("scalar", lambda e: e.activation(GH[:, 0:512], PF[:, 0, :], AF.Silu), w=(GH,))
                    E.op("scalar", lambda e: e.activation(GH[:, 512:1024], PF[:, 1, :], AF.Silu), ss, 1, w=(GH,))
                    E.wait("vector", ss)
                    E.op("vector", lambda e: e.tensor_mul(KM[:], GH[:], WGOb[:]),
                         r=(GH, WGOb), w=(KM,))
                    E.op("vector", lambda e: e.reduce_sum(SM[:, 8:9], KM[:], AX.X), sv, 1,
                         r=(KM,), w=(SM,))
                    E.wait("scalar", sv)
                    E.op("scalar", lambda e: e.activation(
                        SM[:, 9:10], SM[:, 8:9], AF.Sigmoid, bias=BGOb[:]), ss, 1,
                        r=(SM, BGOb), w=(SM,))
                    E.wait("vector", ss)
                    E.op("vector", lambda e: e.scalar_tensor_tensor(
                        TM0[:], DIFW[:], SM[:, 9:10], GLB[:], ALU.mult, ALU.add),
                        r=(DIFW, SM, GLB), w=(TM0,))
                    E.op("vector", lambda e: e.tensor_copy(MX[:], TM0[:]), sv, 1,
                         r=(TM0,), w=(MX,))

                    trans8(MX[:], MXf[:], PH[:])
                    E.wait("tensor", sv)
                    for j in range(8):
                        for hcol in range(2):
                            E.mm(PF[:, 2 + hcol, :], MXf[:, 128 * j:128 * (j + 1)],
                                 WO[:, j, 512 * hcol:512 * (hcol + 1)],
                                 j == 0, False)
                    for hcol in range(2):
                        E.mm(PF[:, 2 + hcol, :], MK[:, 8, :],
                             W5[:, 40, 512 * hcol:512 * (hcol + 1)], False, True,
                             st if hcol == 1 else None, 1)
                    E.wait("vector", st)
                    for hb in range(2):
                        E.op("vector", lambda e, hb=hb, x=xi: e.tensor_add(
                            X1[:, 512 * hb:512 * (hb + 1)], PF[:, 2 + hb, :],
                            x[:, 512 * hb:512 * (hb + 1)]),
                            sxc if hb == 1 else None, 1, r=(XI,), w=(X1,))
                    E.wait("scalar", sxc)
                    emit_ln(X1[:], LN1[:])
                    ln1_ss = ss.n
                    E.wait("sync", sxc)
                    E.op("sync", lambda e, t=ti: e.dma_start(
                        d_x1[128 * t:128 * (t + 1), :], X1[:]), sd, 16)
                    if y_x1 is not None:
                        E.op("sync", lambda e, t=ti: e.dma_start(
                            y_x1[128 * t:128 * (t + 1), :], X1[:]), sd, 16)
                    E.wait("sync", ss, ln1_ss)
                    E.op("sync", lambda e, t=ti: e.dma_start(
                        d_ln1[128 * t:128 * (t + 1), :], LN1[:]), sd, 16)
                else:
                    E.op("scalar", lambda e: e.memzero(SM[:, 48:49]),
                         sxc, 1, w=(SM,))

            if max_phase >= 0.1:
                E.op("sync", lambda e: e.dma_start(XI[:, 0, :], x_pre[0:128, :]), sxp[0], 16)
                for ti in range(nt):
                    tile_body(ti, False, x_pre, PG[:], 0)

                n_prior_sx[0] += nt
                E.wait("sync", sxc, nt)
                E.op("sync", lambda e: e.dma_start(XI[:, 0, :], x_own[0:128, :]), sxp[0], 16)
                for ti in range(nt):
                    tile_body(ti, True, x_own, ONES[:], nt)

            with nc.named_scope("s1"):
                P.run_block()
            E.barrier_reset()

        if max_phase >= 11:
          with ExitStack() as c2:
            en = c2.enter_context
            SH = en(nc.sbuf_tensor("SH", [128, 24, 1024], BF16))
            EWb = [en(nc.sbuf_tensor(f"EWs{i}", [128, 12288], BF16)) for i in range(2)]
            RT = en(nc.sbuf_tensor("RT", [128, 8, 8], BF16))
            ACC = en(nc.sbuf_tensor("ACC", [128, 8, 1024], F32))
            LN1F = en(nc.sbuf_tensor("LN1F", [128, 8, 1024], BF16))
            LTK = en(nc.sbuf_tensor("LTK", [128, 2, 1024], BF16))
            XT1 = en(nc.sbuf_tensor("XT1", [128, 2, 1024], F32))
            HS2 = en(nc.sbuf_tensor("HS2", [128, 1024], BF16))
            HST = en(nc.sbuf_tensor("HST", [128, 1024], BF16))
            SGb = en(nc.sbuf_tensor("SGb", [128, 1024], BF16))
            HE = en(nc.sbuf_tensor("HE2", [128, 512], BF16))
            HET = en(nc.sbuf_tensor("HET", [128, 512], BF16))
            CFA = en(nc.sbuf_tensor("CFA", [128, 8, 8], F32))
            SM2 = en(nc.sbuf_tensor("SM2", [128, 128], F32))
            SQ2 = en(nc.sbuf_tensor("SQ2", [128, 1024], F32))
            X2T = en(nc.sbuf_tensor("X2T", [128, 1024], F32))
            LN2T = en(nc.sbuf_tensor("LN2T", [128, 1024], BF16))
            ID2 = en(nc.sbuf_tensor("ID2", [128, 128], BF16))
            PB = en(nc.psum_tensor("PB", [128, 7, 512], F32))
            PH2 = en(nc.psum_tensor("PH2", [128, 1024], BF16))

            for c0 in range(0, 24, 4):
                E.op("gpsimd", lambda e, c0=c0: e.dma_start(
                    SH[:, c0:c0 + 4, :], shw[:, c0:c0 + 4, :]), sw, 16)
            E.op("gpsimd", lambda e: e.dma_start(RT[:], rt8[:]), sw, 16)
            E.op("gpsimd", lambda e: e.dma_start(ID2[:], masks[:, 1, :]), sw, 16)
            E.op("gpsimd", lambda e: e.memset(SM2[:, 96:97], 1.0))
            E.op("gpsimd", lambda e: e.memset(SM2[:, 97:98], -5.0), sg, 1)

            def eload(eidx, pi):
                for c0 in range(0, 12288, 3072):
                    E.op("gpsimd", lambda e, eidx=eidx, pi=pi, c0=c0: e.dma_start(
                        EWb[pi][:, c0:c0 + 3072], ew[eidx, :, c0:c0 + 3072]),
                        sep[pi], 16)
            eload(0, 0)

            def emit_ln2(xsrc, out_bf, eps=1e-5):
                E.op("vector", lambda e, x=xsrc: e.reduce_sum(SM2[:, 0:1], x, AX.X), sv, 1,
                     r=(xsrc,), w=(SM2,))
                E.op("scalar", lambda e, x=xsrc: e.activation(
                    SQ2[:], x, AF.Square, accum_out=SM2[:, 1:2]), ss, 1,
                    r=(xsrc,), w=(SQ2, SM2))
                E.wait("vector", ss)
                E.op("vector", lambda e: e.tensor_scalar(
                    SM2[:, 2:3], SM2[:, 0:1], 1.0 / D, None, ALU.mult), r=(SM2,), w=(SM2,))
                E.op("vector", lambda e: e.tensor_mul(SM2[:, 3:4], SM2[:, 2:3], SM2[:, 2:3]), r=(SM2,), w=(SM2,))
                E.op("vector", lambda e: e.scalar_tensor_tensor(
                    SM2[:, 4:5], SM2[:, 1:2], 1.0 / D, SM2[:, 3:4], ALU.mult, ALU.subtract), r=(SM2,), w=(SM2,))
                E.op("vector", lambda e, p=eps: e.tensor_scalar(
                    SM2[:, 4:5], SM2[:, 4:5], p, None, ALU.add), sv, 1, r=(SM2,), w=(SM2,))
                E.wait("scalar", sv)
                E.op("scalar", lambda e: e.sqrt(SM2[:, 5:6], SM2[:, 4:5]), ss, 1, r=(SM2,), w=(SM2,))
                E.wait("vector", ss)
                E.op("vector", lambda e: e.reciprocal(SM2[:, 6:7], SM2[:, 5:6]), r=(SM2,), w=(SM2,))
                E.op("vector", lambda e: e.tensor_mul(SM2[:, 7:8], SM2[:, 2:3], SM2[:, 6:7]), r=(SM2,), w=(SM2,))
                E.op("vector", lambda e: e.tensor_scalar(
                    SM2[:, 7:8], SM2[:, 7:8], -1.0, None, ALU.mult), sv, 1, r=(SM2,), w=(SM2,))
                E.wait("scalar", sv)
                E.op("scalar", lambda e, o=out_bf, x=xsrc: e.activation(
                    o, x, AF.Identity, bias=SM2[:, 7:8], scale=SM2[:, 6:7]), ss, 1,
                    r=(xsrc, SM2), w=(out_bf,))

            def trans2(src_bf, dst_ap, nchunk, dst_name):
                E.wait("tensor", sv)
                E.wait("tensor", ss)
                for j in range(nchunk):
                    E.trans(PH2[:, 128 * j:128 * (j + 1)],
                            src_bf[:, 128 * j:128 * (j + 1)], ID2[:],
                            st if j == nchunk - 1 else None, 1)
                E.wait("vector", st)
                E.op("vector", lambda e, d=dst_ap, n=nchunk: e.tensor_copy(
                    d, PH2[:, 0:128 * n]), sv, 1, w=(dst_name,))

            E.wait("sync", sd)
            E.wait("tensor", sw)
            E.wait("tensor", sg)

            nsc = (nt + 7) // 8
            for sc in range(nsc):
                tl = list(range(sc * 8, min(nt, sc * 8 + 8)))
                for t in tl:
                    ts_ = t - sc * 8
                    par = t % 2
                    E.wait("sync", st)
                    E.op("sync", lambda e, t=t, par=par: e.dma_start(
                        LTK[:, par, :], d_ln1[128 * t:128 * (t + 1), :]), sxp[par], 16)
                    E.wait("tensor", sxp[par])
                    trans2(LTK[:, par, :], LN1F[:, ts_, :], 8, LN1F)
                    E.wait("tensor", sv)
                    E.wait("tensor", ss)
                    for j in range(8):
                        lhsT = LN1F[:, ts_, 128 * j:128 * (j + 1)]
                        E.mm(PB[:, 6, 0:8], lhsT, RT[:, j, :], j == 0, j == 7)
                        for hb in range(2):
                            E.mm(PB[:, hb, :], lhsT, SH[:, j, 512 * hb:512 * (hb + 1)],
                                 j == 0, j == 7)
                        for hb in range(2):
                            E.mm(PB[:, 2 + hb, :], lhsT, SH[:, 8 + j, 512 * hb:512 * (hb + 1)],
                                 j == 0, j == 7, st if j == 7 and hb == 1 else None, 1)
                    E.wait("scalar", st)
                    E.op("scalar", lambda e: e.activation(SGb[:, 0:512], PB[:, 0, :], AF.Silu), w=(SGb,))
                    E.op("scalar", lambda e: e.activation(SGb[:, 512:1024], PB[:, 1, :], AF.Silu), w=(SGb,))
                    E.op("scalar", lambda e: e.activation(
                        SM2[:, 16:24], PB[:, 6, 0:8], AF.Exp,
                        accum_out=SM2[:, 24:25]), ss, 1, w=(SM2,))
                    E.wait("vector", ss)
                    E.op("vector", lambda e: e.tensor_tensor(
                        HS2[:, 0:512], PB[:, 2, :], SGb[:, 0:512], ALU.mult),
                        r=(SGb,), w=(HS2,))
                    E.op("vector", lambda e: e.tensor_tensor(
                        HS2[:, 512:1024], PB[:, 3, :], SGb[:, 512:1024], ALU.mult), sv, 1,
                        r=(SGb,), w=(HS2,))
                    E.op("vector", lambda e: e.reciprocal(SM2[:, 25:26], SM2[:, 24:25]), r=(SM2,), w=(SM2,))
                    E.op("vector", lambda e: e.tensor_scalar(
                        SM2[:, 16:24], SM2[:, 16:24], SM2[:, 25:26], None, ALU.mult), r=(SM2,), w=(SM2,))
                    T88 = SM2[:, 32:96].rearrange("p (a b) -> p a b", a=8)
                    p8 = SM2[:, 16:24]
                    E.op("vector", lambda e, T=T88: e.scalar_tensor_tensor(
                        T, p8.unsqueeze(2).broadcast_to([128, 8, 8]), -1.0,
                        p8.unsqueeze(1).broadcast_to([128, 8, 8]),
                        ALU.mult, ALU.add), r=(SM2,), w=(SM2,))
                    E.op("vector", lambda e, T=T88: e.scalar_tensor_tensor(
                        T, T, 1e8, SM2[:, 96:97].unsqueeze(1).broadcast_to([128, 8, 8]),
                        ALU.mult, ALU.add), sv, 1, r=(SM2,), w=(SM2,))
                    E.wait("scalar", sv)
                    E.op("scalar", lambda e: e.activation(SM2[:, 32:96], SM2[:, 32:96], AF.Relu),
                         ss, 1, r=(SM2,), w=(SM2,))
                    E.wait("vector", ss)
                    E.op("vector", lambda e, T=T88: e.scalar_tensor_tensor(
                        T, T, -1.0, SM2[:, 96:97].unsqueeze(1).broadcast_to([128, 8, 8]),
                        ALU.mult, ALU.add), sv, 1, r=(SM2,), w=(SM2,))
                    E.wait("scalar", sv)
                    E.op("scalar", lambda e: e.activation(SM2[:, 32:96], SM2[:, 32:96], AF.Relu),
                         ss, 1, r=(SM2,), w=(SM2,))
                    E.wait("vector", ss)
                    E.op("vector", lambda e, T=T88: e.reduce_sum(SM2[:, 104:112], T, AX.X),
                         r=(SM2,), w=(SM2,))
                    E.op("vector", lambda e: e.tensor_scalar(
                        SM2[:, 104:112], SM2[:, 104:112], SM2[:, 97:98], None, ALU.add), sv, 1,
                        r=(SM2,), w=(SM2,))
                    E.wait("scalar", sv)
                    E.op("scalar", lambda e: e.activation(SM2[:, 104:112], SM2[:, 104:112], AF.Relu),
                         ss, 1, r=(SM2,), w=(SM2,))
                    E.wait("vector", ss)
                    E.op("vector", lambda e: e.scalar_tensor_tensor(
                        SM2[:, 104:112], SM2[:, 104:112], -1.0,
                        SM2[:, 96:97].broadcast_to([128, 8]), ALU.mult, ALU.add), sv, 1,
                        r=(SM2,), w=(SM2,))
                    E.wait("scalar", sv)
                    E.op("scalar", lambda e: e.activation(SM2[:, 104:112], SM2[:, 104:112], AF.Relu),
                         ss, 1, r=(SM2,), w=(SM2,))
                    E.wait("vector", ss)
                    E.op("vector", lambda e: e.scalar_tensor_tensor(
                        SM2[:, 104:112], SM2[:, 104:112], -1.0,
                        SM2[:, 96:97].broadcast_to([128, 8]), ALU.mult, ALU.add),
                        r=(SM2,), w=(SM2,))
                    E.op("vector", lambda e, t=ts_: e.tensor_tensor(
                        CFA[:, t, :], SM2[:, 16:24], SM2[:, 104:112], ALU.mult), sv, 1,
                        r=(SM2,), w=(CFA,))
                    trans2(HS2[:], HST[:], 8, HST)
                    E.wait("tensor", sv)
                    for j in range(8):
                        for hb in range(2):
                            E.mm(PB[:, 4 + hb, :], HST[:, 128 * j:128 * (j + 1)],
                                 SH[:, 16 + j, 512 * hb:512 * (hb + 1)],
                                 j == 0, j == 7, st if j == 7 and hb == 1 else None, 1)
                    E.wait("vector", st)
                    E.op("vector", lambda e, t=ts_: e.tensor_copy(ACC[:, t, 0:512], PB[:, 4, :]), w=(ACC,))
                    E.op("vector", lambda e, t=ts_: e.tensor_copy(ACC[:, t, 512:1024], PB[:, 5, :]), sv, 1, w=(ACC,))

                for ei in range(8):
                    par = ei % 2
                    if ei < 7 or sc + 1 < nsc:
                        E.wait("gpsimd", sec)
                        eload((ei + 1) % 8, 1 - par)
                    E.wait("tensor", sep[par])
                    for t in tl:
                        ts_ = t - sc * 8
                        E.wait("tensor", ss)
                        E.wait("tensor", sv)
                        for j in range(8):
                            lhsT = LN1F[:, ts_, 128 * j:128 * (j + 1)]
                            E.mm(PB[:, 0, :], lhsT, EWb[par][:, 512 * j:512 * (j + 1)],
                                 j == 0, j == 7)
                            E.mm(PB[:, 1, :], lhsT, EWb[par][:, 4096 + 512 * j:4096 + 512 * (j + 1)],
                                 j == 0, j == 7, st if j == 7 else None, 1)
                        E.wait("scalar", st)
                        E.op("scalar", lambda e: e.activation(SGb[:, 0:512], PB[:, 0, :], AF.Silu),
                             ss, 1, w=(SGb,))
                        E.wait("vector", ss)
                        E.op("vector", lambda e: e.tensor_tensor(
                            HE[:], PB[:, 1, :], SGb[:, 0:512], ALU.mult), r=(SGb,), w=(HE,))
                        E.op("vector", lambda e, t=ts_, ei=ei: e.tensor_scalar(
                            HE[:], HE[:], CFA[:, t, ei:ei + 1], None, ALU.mult), sv, 1,
                            r=(HE, CFA), w=(HE,))
                        trans2(HE[:], HET[:], 4, HET)
                        E.wait("tensor", sv)
                        for jd in range(4):
                            for hb in range(2):
                                E.mm(PB[:, 2 + hb, :], HET[:, 128 * jd:128 * (jd + 1)],
                                     EWb[par][:, 8192 + 1024 * jd + 512 * hb:8192 + 1024 * jd + 512 * (hb + 1)],
                                     jd == 0, jd == 3, st if jd == 3 and hb == 1 else None, 1)
                        E.wait("vector", st)
                        E.op("vector", lambda e, t=ts_: e.scalar_tensor_tensor(
                            ACC[:, t, 0:512], PB[:, 2, :], 1.0, ACC[:, t, 0:512],
                            ALU.mult, ALU.add), r=(ACC,), w=(ACC,))
                        E.op("vector", lambda e, t=ts_: e.scalar_tensor_tensor(
                            ACC[:, t, 512:1024], PB[:, 3, :], 1.0, ACC[:, t, 512:1024],
                            ALU.mult, ALU.add), sv, 1, r=(ACC,), w=(ACC,))
                        if t == tl[-1]:
                            E.op("vector", lambda e: e.memzero(SM2[:, 98:99]),
                                 sec, 1, w=(SM2,))

                for t in tl:
                    ts_ = t - sc * 8
                    par = t % 2
                    E.wait("sync", sv)
                    E.op("sync", lambda e, t=t, par=par: e.dma_start(
                        XT1[:, par, :], d_x1[128 * t:128 * (t + 1), :]), sxq[par], 16)
                    E.wait("vector", sxq[par])
                    E.wait("vector", sd)
                    E.op("vector", lambda e, t=ts_, par=par: e.tensor_add(
                        X2T[:], ACC[:, t, :], XT1[:, par, :]), sv, 1,
                        r=(ACC,), w=(X2T,))
                    E.wait("scalar", sv)
                    emit_ln2(X2T[:], LN2T[:])
                    ln2_ss = ss.n
                    E.wait("sync", sv)
                    E.op("sync", lambda e, t=t: e.dma_start(
                        d_x2[128 * t:128 * (t + 1), :], X2T[:]), sd, 16)
                    if y_x2 is not None:
                        E.op("sync", lambda e, t=t: e.dma_start(
                            y_x2[128 * t:128 * (t + 1), :], X2T[:]), sd, 16)
                    E.wait("sync", ss, ln2_ss)
                    E.op("sync", lambda e, t=t: e.dma_start(
                        d_ln2[128 * t:128 * (t + 1), :], LN2T[:]), sd, 16)

            with nc.named_scope("s2"):
                P.run_block()
            E.barrier_reset()

        if max_phase >= 12:
          with ExitStack() as c3:
            en = c3.enter_context
            W3 = en(nc.sbuf_tensor("W3", [128, 26, 1024], BF16))
            MK3 = en(nc.sbuf_tensor("MK3", [128, 4, 128], BF16))
            GMB = en(nc.sbuf_tensor("GMB", [128, 1024], F32))
            BTB = en(nc.sbuf_tensor("BTB", [128, 1024], F32))
            RPT = en(nc.sbuf_tensor("RPT", [128, 2, 1024], BF16))
            LTK3 = en(nc.sbuf_tensor("LTK3", [128, 2, 1024], BF16))
            XTK3 = en(nc.sbuf_tensor("XTK3", [128, 2, 1024], F32))
            LN2F = en(nc.sbuf_tensor("LN2F", [128, 1024], BF16))
            QT3 = en(nc.sbuf_tensor("QT3", [128, 1024], BF16))
            KT3 = en(nc.sbuf_tensor("KT3", [128, 1024], BF16))
            VT3 = en(nc.sbuf_tensor("VT3", [128, 1024], BF16))
            QR3 = en(nc.sbuf_tensor("QR3", [128, 1024], BF16))
            KR3 = en(nc.sbuf_tensor("KR3", [128, 1024], BF16))
            RS3 = en(nc.sbuf_tensor("RS3", [128, 1024], F32))
            QF3b = en(nc.sbuf_tensor("QF3b", [128, 1024], BF16))
            QRF3 = en(nc.sbuf_tensor("QRF3", [128, 1024], BF16))
            KVP = en(nc.sbuf_tensor("KVP", [128, 520], F32))
            KVQ = en(nc.sbuf_tensor("KVQ", [128, 520], F32))
            KVSh3 = en(nc.sbuf_tensor("KVSh3", [128, 512], BF16))
            KMH = en(nc.sbuf_tensor("KMH", [128, 8, 16], BF16))
            ONES3 = en(nc.sbuf_tensor("ONES3", [128, 1], BF16))
            SL3 = en(nc.sbuf_tensor("SL3", [1, 1], U32))
            FL3 = en(nc.sbuf_tensor("FL3", [1, 16], U32))
            FLo = en(nc.sbuf_tensor("FLo", [1, 16], U32))
            RES3 = en(nc.sbuf_tensor("RES3", [128, 1024], F32))
            LNO3 = en(nc.sbuf_tensor("LNO3", [128, 1024], F32))
            X3T = en(nc.sbuf_tensor("X3T", [128, 1024], F32))
            LN3T = en(nc.sbuf_tensor("LN3T", [128, 1024], BF16))
            SM3 = en(nc.sbuf_tensor("SM3", [128, 128], F32))
            SQ3 = en(nc.sbuf_tensor("SQ3", [128, 1024], F32))
            PC = en(nc.psum_tensor("PC", [128, 7, 512], F32))
            PH3 = en(nc.psum_tensor("PH3", [128, 1024], BF16))

            for c0 in range(0, 26, 4):
                c1_ = min(26, c0 + 4)
                E.op("gpsimd", lambda e, c0=c0, c1_=c1_: e.dma_start(
                    W3[:, c0:c1_, :], w3T[:, c0:c1_, :]), sw, 16)
            for i, sl_ in enumerate((1, 7, 8, 9)):
                E.op("gpsimd", lambda e, i=i, sl_=sl_: e.dma_start(
                    MK3[:, i, :], masks[:, sl_, :]), sw, 16)
            E.op("gpsimd", lambda e: e.dma_start(SL3[:], slot[:]), sw, 16)
            E.op("gpsimd", lambda e: e.memset(FLo[:], 7))
            E.op("gpsimd", lambda e: e.memset(ONES3[:], 1.0))
            E.op("gpsimd", lambda e: e.memset(KVP[:], 0.0))
            E.op("gpsimd", lambda e: e.memset(KVQ[:], 0.0))
            E.op("gpsimd", lambda e: e.memset(KMH[:], 0.0), sg, 1)
            E.wait("tensor", sw)
            E.wait("tensor", sg)
            for hb in range(2):
                E.mm(PC[:, hb, :], MK3[:, 1, :], W3[:, 25, 512 * hb:512 * (hb + 1)],
                     True, True)
                E.mm(PC[:, 2 + hb, :], MK3[:, 2, :], W3[:, 25, 512 * hb:512 * (hb + 1)],
                     True, True, st if hb == 1 else None, 1)
            E.wait("vector", st)
            for hb in range(2):
                E.op("vector", lambda e, hb=hb: e.tensor_copy(
                    GMB[:, 512 * hb:512 * (hb + 1)], PC[:, hb, :]), w=(GMB,))
                E.op("vector", lambda e, hb=hb: e.tensor_copy(
                    BTB[:, 512 * hb:512 * (hb + 1)], PC[:, 2 + hb, :]), sv, 1, w=(BTB,))
            E.wait("sync", sd)

            def trans3(src_bf, dst_ap, nchunk, dst_name):
                E.wait("tensor", sv)
                E.wait("tensor", ss)
                for j in range(nchunk):
                    E.trans(PH3[:, 128 * j:128 * (j + 1)],
                            src_bf[:, 128 * j:128 * (j + 1)], MK3[:, 0, :],
                            st if j == nchunk - 1 else None, 1)
                E.wait("vector", st)
                E.op("vector", lambda e, d=dst_ap, n=nchunk: e.tensor_copy(
                    d, PH3[:, 0:128 * n]), sv, 1, w=(dst_name,))

            def emit_ln3(xsrc, out_t, eps):
                E.op("vector", lambda e, x=xsrc: e.reduce_sum(SM3[:, 0:1], x, AX.X), sv, 1,
                     r=(xsrc,), w=(SM3,))
                E.op("scalar", lambda e, x=xsrc: e.activation(
                    SQ3[:], x, AF.Square, accum_out=SM3[:, 1:2]), ss, 1,
                    r=(xsrc,), w=(SQ3, SM3))
                E.wait("vector", ss)
                E.op("vector", lambda e: e.tensor_scalar(
                    SM3[:, 2:3], SM3[:, 0:1], 1.0 / D, None, ALU.mult), r=(SM3,), w=(SM3,))
                E.op("vector", lambda e: e.tensor_mul(SM3[:, 3:4], SM3[:, 2:3], SM3[:, 2:3]), r=(SM3,), w=(SM3,))
                E.op("vector", lambda e: e.scalar_tensor_tensor(
                    SM3[:, 4:5], SM3[:, 1:2], 1.0 / D, SM3[:, 3:4], ALU.mult, ALU.subtract), r=(SM3,), w=(SM3,))
                E.op("vector", lambda e, p=eps: e.tensor_scalar(
                    SM3[:, 4:5], SM3[:, 4:5], p, None, ALU.add), sv, 1, r=(SM3,), w=(SM3,))
                E.wait("scalar", sv)
                E.op("scalar", lambda e: e.sqrt(SM3[:, 5:6], SM3[:, 4:5]), ss, 1, r=(SM3,), w=(SM3,))
                E.wait("vector", ss)
                E.op("vector", lambda e: e.reciprocal(SM3[:, 6:7], SM3[:, 5:6]), r=(SM3,), w=(SM3,))
                E.op("vector", lambda e: e.tensor_mul(SM3[:, 7:8], SM3[:, 2:3], SM3[:, 6:7]), r=(SM3,), w=(SM3,))
                E.op("vector", lambda e: e.tensor_scalar(
                    SM3[:, 7:8], SM3[:, 7:8], -1.0, None, ALU.mult), sv, 1, r=(SM3,), w=(SM3,))
                E.wait("scalar", sv)
                E.op("scalar", lambda e, o=out_t, x=xsrc: e.activation(
                    o, x, AF.Identity, bias=SM3[:, 7:8], scale=SM3[:, 6:7]), ss, 1,
                    r=(xsrc, SM3), w=(out_t,))

            def emit_rope(src_bf, dst_bf, par):
                xr = src_bf.rearrange("p (a b) -> p a b", a=512)
                dr = dst_bf.rearrange("p (a b) -> p a b", a=512)
                cos3 = RPT[:, par, 0:512].unsqueeze(2)
                sin3 = RPT[:, par, 512:1024].unsqueeze(2)
                t1 = RS3[:, 0:512].rearrange("p (a b) -> p a b", a=512)
                t2 = RS3[:, 512:1024].rearrange("p (a b) -> p a b", a=512)
                E.op("vector", lambda e: e.tensor_tensor(
                    t1, xr[:, :, 0:1], cos3, ALU.mult), r=(src_bf, RPT), w=(RS3,))
                E.op("vector", lambda e: e.tensor_tensor(
                    t2, xr[:, :, 1:2], sin3, ALU.mult), r=(src_bf, RPT), w=(RS3,))
                E.op("vector", lambda e: e.scalar_tensor_tensor(
                    dr[:, :, 0:1], t2, -1.0, t1, ALU.mult, ALU.add),
                    r=(RS3,), w=(dst_bf,))
                E.op("vector", lambda e: e.tensor_tensor(
                    t1, xr[:, :, 1:2], cos3, ALU.mult), r=(src_bf, RPT), w=(RS3,))
                E.op("vector", lambda e: e.tensor_tensor(
                    t2, xr[:, :, 0:1], sin3, ALU.mult), r=(src_bf, RPT), w=(RS3,))
                E.op("vector", lambda e: e.tensor_add(
                    dr[:, :, 1:2], t1, t2), sv, 1, r=(RS3,), w=(dst_bf,))

            for t in range(nt):
                par = t % 2
                E.wait("sync", st)
                E.wait("sync", sv)
                E.op("sync", lambda e, t=t, par=par: e.dma_start(
                    LTK3[:, par, :], d_ln2[128 * t:128 * (t + 1), :]), sxp[par], 16)
                E.op("sync", lambda e, t=t, par=par: e.dma_start(
                    RPT[:, par, :], d_rope[128 * t:128 * (t + 1), :]), sxp[par], 16)
                E.wait("tensor", sxp[par])
                trans3(LTK3[:, par, :], LN2F[:], 8, LN2F)
                E.wait("tensor", sv)
                E.wait("tensor", ss)
                for j in range(8):
                    lhsT = LN2F[:, 128 * j:128 * (j + 1)]
                    for hb in range(2):
                        E.mm(PC[:, hb, :], lhsT, W3[:, 8 + j, 512 * hb:512 * (hb + 1)],
                             j == 0, False)
                    for hb in range(2):
                        E.mm(PC[:, 2 + hb, :], lhsT, W3[:, 16 + j, 512 * hb:512 * (hb + 1)],
                             j == 0, False)
                for hb in range(2):
                    E.mm(PC[:, hb, :], MK3[:, 2, :], W3[:, 24, 512 * hb:512 * (hb + 1)],
                         False, True)
                    E.mm(PC[:, 2 + hb, :], MK3[:, 3, :], W3[:, 24, 512 * hb:512 * (hb + 1)],
                         False, True, st if hb == 1 else None, 1)
                E.wait("scalar", st)
                E.op("scalar", lambda e: e.activation(KT3[:, 0:512], PC[:, 0, :], AF.Gelu), w=(KT3,))
                E.op("scalar", lambda e: e.activation(KT3[:, 512:1024], PC[:, 1, :], AF.Gelu), ss, 1, w=(KT3,))
                E.wait("vector", ss)
                E.op("vector", lambda e: e.tensor_scalar(
                    KT3[:], KT3[:], 0.21, None, ALU.add), r=(KT3,), w=(KT3,))
                E.op("vector", lambda e: e.tensor_copy(VT3[:, 0:512], PC[:, 2, :]), w=(VT3,))
                E.op("vector", lambda e: e.tensor_copy(VT3[:, 512:1024], PC[:, 3, :]), sv, 1, w=(VT3,))
                emit_rope(KT3[:], KR3[:], par)
                E.wait("tensor", sv)
                for j in range(8):
                    E.mm(PC[:, 4 + j // 4, (j % 4) * 128:(j % 4) * 128 + 128],
                         KR3[:, 128 * j:128 * (j + 1)], VT3[:, 128 * j:128 * (j + 1)],
                         j % 4 == 0, j % 4 == 3, None, 1)
                    E.mm(PC[:, 6, j:j + 1], KT3[:, 128 * j:128 * (j + 1)], ONES3[:],
                         True, True, st if j == 7 else None, 1)
                E.wait("vector", st)
                for half in range(2):
                    pr = slice(64 * half, 64 * (half + 1))
                    co = slice(64 * half, 64 * half + 64)
                    E.op("vector", lambda e, pr=pr, co=co: e.tensor_add(
                        KVP[pr, 0:512].rearrange("p (a b) -> p a b", a=8),
                        KVP[pr, 0:512].rearrange("p (a b) -> p a b", a=8),
                        PC[pr, 4:6, :].rearrange("p a (b c) -> p (a b) c", b=4)[:, :, co]),
                        r=(KVP,), w=(KVP,))
                E.op("vector", lambda e: e.tensor_add(
                    KVP[:, 512:520], KVP[:, 512:520], PC[:, 6, 0:8]), sv, 1,
                    r=(KVP,), w=(KVP,))

            if not solo:
                E.wait("gpsimd", sv)

                def exch(g, base=sg.n, spl=spl):
                    with g.register("sl3") as rsl:
                        g.reg_load(rsl, SL3[0:1, 0:1].bitcast(I32))
                        for me in (0, 1):
                            sib = 1 - me
                            with g.If_eq(rsl, me):
                                g.dma_start(d_ex[me, :, :], KVP[:]).then_inc(sg.h, 16)
                                g.wait_ge(sg.h, base + 16)
                                g.dma_start(d_exf[me:me + 1, :], FLo[:]).then_inc(sg.h, 16)
                                g.wait_ge(sg.h, base + 32)
                                with g.register(f"fl3{me}") as rfl:
                                    g.reg_mov(rfl, 0)
                                    with g.Fori(0, 4000) as _i:
                                        with g.If_ne(rfl, 7):
                                            g.dma_start(FL3[:], d_exf[sib:sib + 1, :]).then_inc(spl.h, 16)
                                            g.drain()
                                            g.reg_load(rfl, FL3[0:1, 0:1].bitcast(I32))
                                g.dma_start(KVQ[:], d_ex[sib, :, :]).then_inc(sg.h, 16)
                                g.wait_ge(sg.h, base + 48)
                            g.end_ifs()
                E.op("gpsimd", exch, None, 0, r=(KVP,), w=(KVQ,))
                sg.n += 48
            E.wait("vector", sg)
            E.op("vector", lambda e: e.tensor_add(RS3[:, 0:512], KVP[:, 0:512], KVQ[:, 0:512]),
                 r=(KVP, KVQ), w=(RS3,))
            E.op("vector", lambda e: e.tensor_scalar(
                RS3[:, 0:512], RS3[:, 0:512], 1.0 / 8192.0, None, ALU.mult), r=(RS3,), w=(RS3,))
            E.op("vector", lambda e: e.tensor_copy(KVSh3[:], RS3[:, 0:512]), r=(RS3,), w=(KVSh3,))
            E.op("vector", lambda e: e.tensor_add(SM3[:, 8:16], KVP[:, 512:520], KVQ[:, 512:520]),
                 r=(KVP, KVQ), w=(SM3,))
            E.op("vector", lambda e: e.tensor_scalar(
                SM3[:, 8:16], SM3[:, 8:16], 1.0 / 8192.0, None, ALU.mult), r=(SM3,), w=(SM3,))
            for j in range(8):
                E.op("vector", lambda e, j=j: e.tensor_copy(
                    KMH[0:64, j, 2 * j:2 * j + 1], SM3[0:64, 8 + j:9 + j]), r=(SM3,), w=(KMH,))
                E.op("vector", lambda e, j=j: e.tensor_copy(
                    KMH[64:128, j, 2 * j + 1:2 * j + 2], SM3[64:128, 8 + j:9 + j]),
                    sv if j == 7 else None, 1, r=(SM3,), w=(KMH,))

            for t in range(nt):
                par = t % 2
                E.wait("sync", st)
                E.wait("sync", sv)
                E.op("sync", lambda e, t=t, par=par: e.dma_start(
                    LTK3[:, par, :], d_ln2[128 * t:128 * (t + 1), :]), sxp[par], 16)
                E.op("sync", lambda e, t=t, par=par: e.dma_start(
                    RPT[:, par, :], d_rope[128 * t:128 * (t + 1), :]), sxp[par], 16)
                E.wait("sync", sv)
                E.op("sync", lambda e, t=t, par=par: e.dma_start(
                    XTK3[:, par, :], d_x2[128 * t:128 * (t + 1), :]), sxq[par], 16)
                E.wait("tensor", sxp[par])
                trans3(LTK3[:, par, :], LN2F[:], 8, LN2F)
                E.wait("tensor", sv)
                E.wait("tensor", ss)
                for j in range(8):
                    lhsT = LN2F[:, 128 * j:128 * (j + 1)]
                    for hb in range(2):
                        E.mm(PC[:, hb, :], lhsT, W3[:, j, 512 * hb:512 * (hb + 1)],
                             j == 0, False)
                for hb in range(2):
                    E.mm(PC[:, hb, :], MK3[:, 1, :], W3[:, 24, 512 * hb:512 * (hb + 1)],
                         False, True, st if hb == 1 else None, 1)
                E.wait("scalar", st)
                E.op("scalar", lambda e: e.activation(QT3[:, 0:512], PC[:, 0, :], AF.Gelu), w=(QT3,))
                E.op("scalar", lambda e: e.activation(QT3[:, 512:1024], PC[:, 1, :], AF.Gelu), ss, 1, w=(QT3,))
                E.wait("vector", ss)
                E.op("vector", lambda e: e.tensor_scalar(
                    QT3[:], QT3[:], 0.21, None, ALU.add), sv, 1, r=(QT3,), w=(QT3,))
                emit_rope(QT3[:], QR3[:], par)
                trans3(QT3[:], QF3b[:], 8, QF3b)
                trans3(QR3[:], QRF3[:], 8, QRF3)
                E.wait("tensor", sv)
                for j in range(8):
                    E.mm(PC[:, 6, 0:16], QF3b[:, 128 * j:128 * (j + 1)], KMH[:, j, :],
                         j == 0, j == 7)
                for h in range(0, 16, 2):
                    E.mm(PC[:, 2, 64 * (h // 2):64 * (h // 2) + 64],
                         hs(QRF3, h), KVSh3[0:64, 64 * (h // 2):64 * (h // 2) + 64],
                         True, True, st if h == 14 else None, 1, tp=(0, 0))
                for h in range(1, 16, 2):
                    E.mm(PC[:, 3, 64 * (h // 2):64 * (h // 2) + 64],
                         hs(QRF3, h), KVSh3[64:128, 64 * (h // 2):64 * (h // 2) + 64],
                         True, True, st if h == 15 else None, 1, tp=(64, 0))
                E.wait("vector", st)
                E.op("vector", lambda e: e.tensor_scalar(
                    SM3[:, 16:32], PC[:, 6, 0:16], 1e-6, None, ALU.add), w=(SM3,))
                E.op("vector", lambda e: e.reciprocal(SM3[:, 16:32], SM3[:, 16:32]),
                     r=(SM3,), w=(SM3,))
                zb = SM3[:, 16:32].rearrange("p (a b) -> p a b", a=8) \
                    .unsqueeze(3).broadcast_to([128, 8, 2, 64])
                E.op("vector", lambda e, zb=zb: e.tensor_tensor(
                    RES3[:].rearrange("p (a b c) -> p a b c", a=8, b=2),
                    PC[:, 2:4, :].rearrange("p a (b c) -> p b a c", b=8),
                    zb, ALU.mult), r=(SM3,), w=(RES3,))
                E.op("vector", lambda e, par=par: e.tensor_add(
                    RES3[:], RES3[:], LTK3[:, par, :]), sv, 1, r=(RES3,), w=(RES3,))
                E.wait("scalar", sv)
                emit_ln3(RES3[:], LNO3[:], 1e-12)
                E.wait("vector", ss)
                E.op("vector", lambda e: e.tensor_tensor(RS3[:], LNO3[:], GMB[:], ALU.mult),
                     r=(LNO3, GMB), w=(RS3,))
                E.op("vector", lambda e: e.tensor_add(RS3[:], RS3[:], BTB[:]),
                     r=(RS3, BTB), w=(RS3,))
                E.wait("vector", sxq[par])
                E.wait("vector", sd)
                E.op("vector", lambda e, par=par: e.tensor_add(
                    X3T[:], RS3[:], XTK3[:, par, :]), sv, 1, r=(RS3,), w=(X3T,))
                E.wait("scalar", sv)
                emit_ln3(X3T[:], LN3T[:], 1e-5)
                ln3_ss = ss.n
                E.wait("sync", sv)
                E.op("sync", lambda e, t=t: e.dma_start(
                    d_x3[128 * t:128 * (t + 1), :], X3T[:]), sd, 16)
                if y_x3 is not None:
                    E.op("sync", lambda e, t=t: e.dma_start(
                        y_x3[128 * t:128 * (t + 1), :], X3T[:]), sd, 16)
                E.wait("sync", ss, ln3_ss)
                E.op("sync", lambda e, t=t: e.dma_start(
                    d_ln3[128 * t:128 * (t + 1), :], LN3T[:]), sd, 16)

            with nc.named_scope("s3"):
                P.run_block()
            E.barrier_reset()

        if max_phase >= 13:
          with ExitStack() as c4:
            en = c4.enter_context
            EA = en(nc.sbuf_tensor("EA4", [128, 17, 1024], BF16))
            EB = en(nc.sbuf_tensor("EB4", [128, 17, 1024], BF16))
            WR4s = en(nc.sbuf_tensor("WR4s", [128, 9, 8], BF16))
            MK4 = en(nc.sbuf_tensor("MK4", [128, 3, 128], BF16))
            LTK4 = en(nc.sbuf_tensor("LTK4", [128, 2, 1024], BF16))
            XTK4 = en(nc.sbuf_tensor("XTK4", [128, 2, 1024], F32))
            LN3F = en(nc.sbuf_tensor("LN3F", [128, 1024], BF16))
            HA4 = en(nc.sbuf_tensor("HA4", [128, 1024], BF16))
            HB4 = en(nc.sbuf_tensor("HB4", [128, 1024], BF16))
            HAF = en(nc.sbuf_tensor("HAF", [128, 1024], BF16))
            HBF = en(nc.sbuf_tensor("HBF", [128, 1024], BF16))
            X4T = en(nc.sbuf_tensor("X4T", [128, 1024], F32))
            SM4 = en(nc.sbuf_tensor("SM4", [128, 128], F32))
            SL4 = en(nc.sbuf_tensor("SL4", [1, 1], U32))
            FL4 = en(nc.sbuf_tensor("FL4", [1, 1], U32))
            FLo4 = en(nc.sbuf_tensor("FLo4", [1, 1], U32))
            PD = en(nc.psum_tensor("PD", [128, 7, 512], F32))
            PH4 = en(nc.psum_tensor("PH4", [128, 1024], BF16))

            E.op("gpsimd", lambda e: e.dma_start(WR4s[:], wr4[:]), sw, 16)
            E.op("gpsimd", lambda e: e.dma_start(MK4[:, 0, :], masks[:, 1, :]), sw, 16)
            E.op("gpsimd", lambda e: e.dma_start(MK4[:, 1, :], masks[:, 7, :]), sw, 16)
            E.op("gpsimd", lambda e: e.dma_start(MK4[:, 2, :], masks[:, 8, :]), sw, 16)
            E.op("gpsimd", lambda e: e.dma_start(SL4[:], slot[:]), sw, 16)
            E.op("gpsimd", lambda e: e.memset(FLo4[:], 9))
            E.op("gpsimd", lambda e: e.memset(SM4[:, 32:33], 1.0), sg, 1)
            E.wait("sync", sd)
            E.wait("tensor", sw)
            E.wait("tensor", sg)

            def trans4(src_bf, dst_ap, nchunk, dst_name):
                E.wait("tensor", stf)
                E.wait("tensor", ss)
                for j in range(nchunk):
                    E.trans(PH4[:, 128 * j:128 * (j + 1)],
                            src_bf[:, 128 * j:128 * (j + 1)], MK4[:, 0, :],
                            st if j == nchunk - 1 else None, 1)
                E.wait("vector", st)
                E.op("vector", lambda e, d=dst_ap, n=nchunk: e.tensor_copy(
                    d, PH4[:, 0:128 * n]), stf, 1, w=(dst_name,))

            E.op("sync", lambda e: e.dma_start(
                LTK4[:, 0, :], d_ln3[0:128, :]), sxp[0], 16)
            E.wait("tensor", sxp[0])
            trans4(LTK4[:, 0, :], LN3F[:], 8, LN3F)
            E.wait("tensor", stf)
            for j in range(8):
                E.mm(PD[:, 6, 0:8], LN3F[:, 128 * j:128 * (j + 1)], WR4s[:, j, :],
                     j == 0, False)
            E.mm(PD[:, 6, 0:8], MK4[:, 1, :], WR4s[:, 8, :], False, True, st, 1)
            E.wait("vector", st)
            E.op("vector", lambda e: e.tensor_copy(SM4[:, 0:8], PD[:, 6, 0:8]), sv, 1,
                 w=(SM4,))
            if not solo:
                E.wait("gpsimd", sv)

                def exch4(g, base=sg.n, spl=spl):
                    with g.register("sl4") as rsl:
                        g.reg_load(rsl, SL4[0:1, 0:1].bitcast(I32))
                        for me in (0, 1):
                            with g.If_eq(rsl, me):
                                g.dma_start(d_ex2[me, :, :], SM4[0:1, 0:8]).then_inc(sg.h, 16)
                                g.wait_ge(sg.h, base + 16)
                                g.dma_start(d_exf[me:me + 1, 4:5], FLo4[:]).then_inc(sg.h, 16)
                                g.wait_ge(sg.h, base + 32)
                            g.end_ifs()
                    with g.register("fl4") as rfl:
                        g.reg_mov(rfl, 0)
                        with g.Fori(0, 4000) as _i:
                            with g.If_ne(rfl, 9):
                                g.dma_start(FL4[:], d_exf[0:1, 4:5]).then_inc(spl.h, 16)
                                g.drain()
                                g.reg_load(rfl, FL4[0:1, 0:1].bitcast(I32))
                    g.dma_start(SM4[0:1, 0:8], d_ex2[0, :, :]).then_inc(sg.h, 16)
                    g.wait_ge(sg.h, base + 48)
                E.op("gpsimd", exch4, None, 0, r=(SM4,), w=(SM4,))
                sg.n += 48
                E.wait("vector", sg)
            p8b = SM4[:, 0:8]
            T884 = SM4[:, 40:104].rearrange("p (a b) -> p a b", a=8)
            E.op("vector", lambda e, T=T884: e.scalar_tensor_tensor(
                T, p8b.unsqueeze(2).broadcast_to([128, 8, 8]), -1.0,
                p8b.unsqueeze(1).broadcast_to([128, 8, 8]),
                ALU.mult, ALU.add), r=(SM4,), w=(SM4,))
            E.op("vector", lambda e, T=T884: e.scalar_tensor_tensor(
                T, T, 1e8, SM4[:, 32:33].unsqueeze(1).broadcast_to([128, 8, 8]),
                ALU.mult, ALU.add), sv, 1, r=(SM4,), w=(SM4,))
            E.wait("scalar", sv)
            E.op("scalar", lambda e: e.activation(SM4[:, 40:104], SM4[:, 40:104], AF.Relu),
                 ss, 1, r=(SM4,), w=(SM4,))
            E.wait("vector", ss)
            E.op("vector", lambda e, T=T884: e.scalar_tensor_tensor(
                T, T, -1.0, SM4[:, 32:33].unsqueeze(1).broadcast_to([128, 8, 8]),
                ALU.mult, ALU.add), sv, 1, r=(SM4,), w=(SM4,))
            E.wait("scalar", sv)
            E.op("scalar", lambda e: e.activation(SM4[:, 40:104], SM4[:, 40:104], AF.Relu),
                 ss, 1, r=(SM4,), w=(SM4,))
            E.wait("vector", ss)
            E.op("vector", lambda e, T=T884: e.reduce_sum(SM4[:, 8:16], T, AX.X),
                 r=(SM4,), w=(SM4,))
            E.op("vector", lambda e: e.tensor_scalar(
                SM4[:, 16:24], SM4[:, 8:16], -5.0, None, ALU.add), r=(SM4,), w=(SM4,))
            E.op("vector", lambda e: e.tensor_scalar(
                SM4[:, 24:32], SM4[:, 8:16], -6.0, None, ALU.add), sv, 1, r=(SM4,), w=(SM4,))
            E.wait("scalar", sv)
            E.op("scalar", lambda e: e.activation(SM4[:, 16:32], SM4[:, 16:32], AF.Relu),
                 ss, 1, r=(SM4,), w=(SM4,))
            E.wait("vector", ss)
            E.op("vector", lambda e: e.scalar_tensor_tensor(
                SM4[:, 16:24], SM4[:, 16:24], -1.0,
                SM4[:, 32:33].broadcast_to([128, 8]), ALU.mult, ALU.add), sv, 1,
                r=(SM4,), w=(SM4,))
            E.wait("scalar", sv)
            E.op("scalar", lambda e: e.activation(SM4[:, 16:24], SM4[:, 16:24], AF.Relu),
                 ss, 1, r=(SM4,), w=(SM4,))
            E.wait("vector", ss)
            E.op("vector", lambda e: e.scalar_tensor_tensor(
                SM4[:, 16:24], SM4[:, 16:24], -1.0,
                SM4[:, 32:33].broadcast_to([128, 8]), ALU.mult, ALU.add),
                r=(SM4,), w=(SM4,))
            E.op("vector", lambda e: e.scalar_tensor_tensor(
                SM4[:, 24:32], SM4[:, 24:32], -1.0, SM4[:, 16:24],
                ALU.mult, ALU.add), sv, 1, r=(SM4,), w=(SM4,))
            E.op("vector", lambda e: e.scalar_tensor_tensor(
                SM4[:, 8:16], SM4[:, 24:32], -1.0, SM4[:, 16:24],
                ALU.mult, ALU.add), sv, 1, r=(SM4,), w=(SM4,))
            E.wait("gpsimd", sv)

            def gather_emit(g, base=sgw.n):
                for cols, buf in ((8, EA), (24, EB)):
                    with g.register(f"rk{cols}") as rk:
                        for e in range(8):
                            g.reg_load(rk, SM4[0:1, cols + e:cols + e + 1].bitcast(I32))
                            with g.If_eq(rk, 0x3F800000):
                                for c0 in range(0, 17, 3):
                                    c1_ = min(17, c0 + 3)
                                    g.dma_start(buf[:, c0:c1_, :],
                                                ew4[e, :, c0:c1_, :]).then_inc(sgw.h, 16)
                            g.end_ifs()
                g.wait_ge(sgw.h, base + 2 * 6 * 16)
            E.op("gpsimd", gather_emit, None, 0, r=(SM4,), w=(EA, EB))
            sgw.n += 2 * 6 * 16
            E.wait("tensor", sgw)

            for t in range(nt):
                par = t % 2
                E.wait("sync", st)
                E.wait("sync", sv)
                E.op("sync", lambda e, t=t, par=par: e.dma_start(
                    LTK4[:, par, :], d_ln3[128 * t:128 * (t + 1), :]), sxp[par], 16)
                E.op("sync", lambda e, t=t, par=par: e.dma_start(
                    XTK4[:, par, :], d_x3[128 * t:128 * (t + 1), :]), sxq[par], 16)
                E.wait("tensor", sxp[par])
                trans4(LTK4[:, par, :], LN3F[:], 8, LN3F)
                E.wait("tensor", stf)
                E.wait("tensor", ss)
                for j in range(8):
                    lhsT = LN3F[:, 128 * j:128 * (j + 1)]
                    for hb in range(2):
                        E.mm(PD[:, hb, :], lhsT, EA[:, j, 512 * hb:512 * (hb + 1)],
                             j == 0, False)
                        E.mm(PD[:, 2 + hb, :], lhsT, EB[:, j, 512 * hb:512 * (hb + 1)],
                             j == 0, False)
                for hb in range(2):
                    E.mm(PD[:, hb, :], MK4[:, 1, :], EA[:, 16, 512 * hb:512 * (hb + 1)],
                         False, True)
                    E.mm(PD[:, 2 + hb, :], MK4[:, 1, :], EB[:, 16, 512 * hb:512 * (hb + 1)],
                         False, True, st if hb == 1 else None, 1)
                E.wait("scalar", st)
                E.op("scalar", lambda e: e.activation(HA4[:, 0:512], PD[:, 0, :], AF.Gelu), w=(HA4,))
                E.op("scalar", lambda e: e.activation(HA4[:, 512:1024], PD[:, 1, :], AF.Gelu), w=(HA4,))
                E.op("scalar", lambda e: e.activation(HB4[:, 0:512], PD[:, 2, :], AF.Gelu), w=(HB4,))
                E.op("scalar", lambda e: e.activation(HB4[:, 512:1024], PD[:, 3, :], AF.Gelu), ss, 1, w=(HB4,))
                trans4(HA4[:], HAF[:], 8, HAF)
                trans4(HB4[:], HBF[:], 8, HBF)
                E.wait("tensor", stf)
                E.wait("tensor", sv)
                for j in range(8):
                    for hb in range(2):
                        E.mm(PD[:, 4 + hb, :], HAF[:, 128 * j:128 * (j + 1)],
                             EA[:, 8 + j, 512 * hb:512 * (hb + 1)], j == 0, False)
                for j in range(8):
                    for hb in range(2):
                        E.mm(PD[:, 4 + hb, :], HBF[:, 128 * j:128 * (j + 1)],
                             EB[:, 8 + j, 512 * hb:512 * (hb + 1)], False, False)
                for hb in range(2):
                    E.mm(PD[:, 4 + hb, :], MK4[:, 2, :],
                         EA[:, 16, 512 * hb:512 * (hb + 1)], False, False)
                for hb in range(2):
                    E.mm(PD[:, 4 + hb, :], MK4[:, 2, :],
                         EB[:, 16, 512 * hb:512 * (hb + 1)], False, True,
                         st if hb == 1 else None, 1)
                E.wait("vector", st)
                E.wait("vector", sxq[par])
                E.wait("vector", sd)
                for hb in range(2):
                    E.op("vector", lambda e, hb=hb, par=par: e.scalar_tensor_tensor(
                        X4T[:, 512 * hb:512 * (hb + 1)], PD[:, 4 + hb, :], 0.5,
                        XTK4[:, par, 512 * hb:512 * (hb + 1)], ALU.mult, ALU.add),
                        sv if hb == 1 else None, 1, w=(X4T,))
                E.wait("sync", sv)
                E.op("sync", lambda e, t=t: e.dma_start(
                    y[128 * t:128 * (t + 1), :], X4T[:]), sd, 16)

            with nc.named_scope("s4"):
                P.run_block()
            E.barrier_reset()

        E.wait("sync", sd)
        if max_phase < 13:
            d_last = d_x3 if max_phase >= 12 else (d_x2 if max_phase >= 11 else d_x1)
            rows = min(HALF, nt * 128) if max_phase >= 10 else 0
            if rows:
                E.op("sync", lambda e, r=rows, s=d_last: e.dma_start(y[0:r, :], s[0:r, :]), sd, 16)
            if rows < HALF:
                E.op("sync", lambda e, r=rows: e.dma_start(
                    y[r:HALF, :], x_own[r:HALF, :]), sd, 16)
        E.wait("sync", sd)
        E.wait("vector", sw)
        E.wait("vector", sep[0])
        E.wait("vector", sep[1])
        E.wait("sync", sxp[0])
        E.wait("sync", sxp[1])
        E.wait("sync", sxq[0])
        E.wait("sync", sxq[1])
        with nc.named_scope("fin"):
            P.run_block()

    return nc



def _prep_core_inputs(c, I):
    b, hf = c // 4, c % 2
    x = np.asarray(I["L0_attn_x"])
    x_own = np.ascontiguousarray(x[b, hf * HALF:(hf + 1) * HALF, :], np.float32)
    x_pre = (np.ascontiguousarray(x[b, 0:HALF, :], np.float32) if hf == 1
             else np.zeros((HALF, D), np.float32))
    pre_gate = np.full((128, 1), 1.0 if hf == 1 else 0.0, np.float32)

    bias_chunk = np.zeros((128, 1, D), np.float32)
    bias_chunk[0, 0] = np.asarray(I["L0_attn_bg"])
    bias_chunk[32, 0] = np.asarray(I["L0_attn_bo"])
    bias_chunk[64, 0] = np.asarray(I["L0_attn_Wgo"]).reshape(-1)
    w5 = np.concatenate([_chunks_T(np.asarray(I[f"L0_attn_{n}"]))
                         for n in ("Wq", "Wk", "Wv", "Wkl", "Wvl")]
                        + [bias_chunk], axis=1)
    wg = _chunks_T(np.asarray(I["L0_attn_Wg"]))
    wo = _chunks_T(np.asarray(I["L0_attn_Wo"]))
    mk = _host_masks(hf == 1)
    misc = np.zeros((128, 1, 128), np.float32)
    misc[0, 0, 0:4] = np.asarray(I["L0_attn_haar_scale"])
    misc[0, 0, 4] = float(np.asarray(I["L0_attn_bgo"]).reshape(-1)[0])
    sel = np.zeros((128, 3, 128), np.float32)
    sel[0, 0, :] = 1.0
    sel[32, 1, :] = 1.0
    sel[64, 2, :] = 1.0
    mk = np.concatenate([mk, misc, sel], axis=1)
    bd = _host_bd(np.asarray(I["L0_attn_haar_Wk"]), np.asarray(I["L0_attn_haar_Wv"]))

    def _pack_expert(g, u, d):
        return np.concatenate([_chunks_N(g).reshape(128, -1),
                               _chunks_N(u).reshape(128, -1),
                               _chunks_N(d).reshape(128, -1)], 1)
    eg = np.asarray(I["L0_ffn_experts_gate"])
    eu = np.asarray(I["L0_ffn_experts_up"])
    ed = np.asarray(I["L0_ffn_experts_down"])
    ew = np.ascontiguousarray(np.stack(
        [_pack_expert(eg[e], eu[e], ed[e]) for e in range(8)]))
    shw = np.ascontiguousarray(np.concatenate(
        [_chunks_N(np.asarray(I["L0_ffn_shared_gate"])),
         _chunks_N(np.asarray(I["L0_ffn_shared_up"])),
         _chunks_N(np.asarray(I["L0_ffn_shared_down"]))], 1))
    rt8 = _chunks_N(np.asarray(I["L0_ffn_router"]))

    b3 = np.zeros((128, 1, D), np.float32)
    b3[0, 0] = np.asarray(I["L1_attn_bq"])
    b3[32, 0] = np.asarray(I["L1_attn_bk"])
    b3[64, 0] = np.asarray(I["L1_attn_bv"])
    gb = np.zeros((128, 1, D), np.float32)
    gb[0, 0] = np.asarray(I["L1_attn_gamma"])
    gb[32, 0] = np.asarray(I["L1_attn_beta"])
    w3 = np.concatenate(
        [_chunks_T(np.asarray(I[f"L1_attn_{n}"])) for n in ("Wq", "Wk", "Wv")]
        + [b3, gb], axis=1)
    g = hf * HALF + np.arange(HALF, dtype=np.float64)
    inv = 1.0 / (10000.0 ** (np.arange(512, dtype=np.float64) / 512.0))
    ang = g[:, None] * inv[None, :]
    rope = np.concatenate([np.cos(ang), np.sin(ang)], 1).astype(
        ml_dtypes.bfloat16)
    wr4 = np.concatenate([_chunks_N(np.asarray(I["L1_ffn_Wr"])),
                          np.zeros((128, 1, 8), np.float32)], 1)
    wr4[0, 8, :] = np.asarray(I["L1_ffn_br"])
    w1 = np.asarray(I["L1_ffn_W1"])
    w2 = np.asarray(I["L1_ffn_W2"])
    b1 = np.asarray(I["L1_ffn_b1"])
    b2 = np.asarray(I["L1_ffn_b2"])

    def _pack4(e):
        bc = np.zeros((128, 1, D), np.float32)
        bc[0, 0] = b1[e]
        bc[32, 0] = b2[e]
        return np.concatenate([_chunks_N(w1[e]), _chunks_N(w2[e]), bc], 1)
    ew4 = np.ascontiguousarray(np.stack([_pack4(e) for e in range(8)]))
    return {
        "x_own": x_own, "x_pre": x_pre, "pre_gate": pre_gate,
        "w5T": w5, "wgT": wg, "woT": wo,
        "bd_mats": bd, "at_mats": _host_at(), "masks": np.ascontiguousarray(mk),
        "rt8": rt8, "ew": ew, "shw": shw,
        "w3T": w3, "d_rope": np.ascontiguousarray(rope),
        "slot": np.array([[hf]], dtype=np.uint32),
        "wr4": np.ascontiguousarray(wr4), "ew4": ew4,
    }


def kernel(**inputs):
    nc = build_program()
    in_maps = [_prep_core_inputs(c, inputs) for c in range(NCORE)]
    res = None
    err = None
    for _att in range(3):
        try:
            res = run_bass_kernel_spmd(nc, in_maps, core_ids=list(range(NCORE)))
            break
        except Exception as e:
            err = e
    if res is None:
        raise err
    out = np.zeros((B, N, D), np.float32)
    for b in range(B):
        out[b, 0:HALF, :] = np.asarray(res.results[4 * b]["y"])
        out[b, HALF:N, :] = np.asarray(res.results[4 * b + 1]["y"])
    return out


# version 03
# speedup vs baseline: 1.0187x; 1.0187x over previous
import numpy as np
import ml_dtypes
from contextlib import ExitStack

import concourse.bass as bass
import concourse.mybir as mybir
from concourse.bass_utils import run_bass_kernel_spmd

F32 = mybir.dt.float32
BF16 = mybir.dt.bfloat16
U32 = mybir.dt.uint32
I32 = mybir.dt.int32
AF = mybir.ActivationFunctionType
ALU = mybir.AluOpType
AX = mybir.AxisListType

NCORE = 8
B, N, D = 2, 8192, 1024
H = 16
HALF = 4096
NT = HALF // 128
OUTT = 2048
NOT = OUTT // 128
LOCW = 64
EXPB = 20.0



def _chunks_T(w):
    dout, din = w.shape
    return np.ascontiguousarray(
        w.T.reshape(din // 128, 128, dout).transpose(1, 0, 2), dtype=np.float32)


def _chunks_N(w):
    din, dout = w.shape
    return np.ascontiguousarray(
        w.reshape(din // 128, 128, dout).transpose(1, 0, 2), dtype=np.float32)


def _host_masks(first_prev_real):
    j = np.arange(128)
    tri = (j[:, None] <= j[None, :]).astype(np.float32)
    i = np.arange(128)[None, :]
    jj = np.arange(128)[:, None]
    prev = ((i - jj + 128 >= 0) & (i - jj + 128 <= LOCW - 1)).astype(np.float32)
    cur = ((i - jj >= 0) & (i - jj <= LOCW - 1)).astype(np.float32)
    ident = np.eye(128, dtype=np.float32)
    lpf = prev if first_prev_real else np.zeros_like(prev)
    return np.ascontiguousarray(np.stack([tri, ident, lpf, cur, prev, cur], 1))


def _host_at():
    out = np.zeros((128, 4, 128), np.float32)
    t = np.arange(128)
    for lv in range(4):
        blk = 2 << lv
        A = ((t[:, None] // blk == t[None, :] // blk)
             & (t[None, :] <= t[:, None])).astype(np.float32)
        A /= (t % blk + 1)[:, None]
        out[:, lv, :] = A.T
    return out


def _host_bd(haar_wk, haar_wv):
    out = np.zeros((128, 8, 128), np.float32)
    for lv in range(4):
        out[:64, lv, :64] = haar_wk[lv].T
        out[64:, lv, 64:] = haar_wk[lv].T
        out[:64, 4 + lv, :64] = haar_wv[lv].T
        out[64:, 4 + lv, 64:] = haar_wv[lv].T
    return out



ENGS = ("sync", "tensor", "vector", "scalar", "gpsimd")


class Prog:
    def __init__(self, nc):
        self.nc = nc
        self.ops = {k: [] for k in ENGS}

    def run_block(self):
        ops, self.ops = self.ops, {k: [] for k in ENGS}
        with self.nc.Block() as blk:
            for name in ENGS:
                if ops[name]:
                    getattr(blk, name)(
                        (lambda _o: (lambda e: [f(e) for f in _o]))(ops[name]))


class Sem:
    def __init__(self, ctx, nc, name):
        self.h = ctx.enter_context(nc.semaphore(name))
        self.n = 0


def _apname(a):
    try:
        return a.tensor.name
    except AttributeError:
        pass
    try:
        return a.name
    except AttributeError:
        return str(a)


class Emit:

    TRACKED = ("vector", "scalar", "gpsimd")

    def __init__(self, P):
        self.P = P
        self.written = {k: set() for k in self.TRACKED}
        self.readed = {k: set() for k in self.TRACKED}

    def op(self, eng, fn, sem=None, k=1, r=(), w=()):
        if eng in self.written:
            rn = {_apname(a) for a in r}
            wn = {_apname(a) for a in w}
            if (rn & self.written[eng]) or (wn & (self.written[eng] | self.readed[eng])):
                self.P.ops[eng].append(lambda e: e.drain())
                self.written[eng].clear()
                self.readed[eng].clear()
            self.written[eng] |= wn
            self.readed[eng] |= rn
        semh = None
        if sem is not None:
            sem.n += k
            semh = sem.h
        def f(e, fn=fn, semh=semh, k=k):
            ins = fn(e)
            if semh is not None:
                ins.then_inc(semh, k)
        self.P.ops[eng].append(f)

    def barrier_reset(self):
        for s in self.written.values():
            s.clear()
        for s in self.readed.values():
            s.clear()

    def wait(self, eng, sem, val=None):
        v = sem.n if val is None else val
        if v <= 0:
            return
        self.P.ops[eng].append(lambda e, s=sem.h, v=v: e.wait_ge(s, v))

    def mm(self, out, lhsT, rhs, start, stop, sem=None, k=1, tp=None):
        self.op("tensor",
                lambda e, o=out, l=lhsT, r=rhs, a=start, b=stop, tp=tp:
                e.matmul(o, l, r, start=a, stop=b, tile_position=tp),
                sem, k)

    def trans(self, out, in_, ident, sem=None, k=1):
        self.op("tensor",
                lambda e, o=out, i=in_, d=ident: e.transpose(o, i, d), sem, k)


def hs(t, h):
    j, half = divmod(h, 2)
    return t[64 * half:64 * (half + 1), 128 * j:128 * (j + 1)]



def build_program(debug_taps=(), nt=NT, max_phase=99, solo=False):
    nc = bass.Bass("TRN2", target_bir_lowering=False)

    def ext(name, shape, dtype=F32):
        return nc.dram_tensor(name, list(shape), dtype, kind="ExternalInput")

    x_own = ext("x_own", [HALF, D])
    x_pre = ext("x_pre", [HALF, D])
    pre_gate = ext("pre_gate", [128, 1])
    w5T = ext("w5T", [128, 41, D])
    wgT = ext("wgT", [128, 16, D])
    woT = ext("woT", [128, 8, D])
    bd_mats = ext("bd_mats", [128, 8, 128])
    at_mats = ext("at_mats", [128, 4, 128])
    masks = ext("masks", [128, 10, 128])
    rt8 = ext("rt8", [128, 8, 8])
    ew = ext("ew", [8, 128, 12288])
    shw = ext("shw", [128, 24, 1024])
    w3T = ext("w3T", [128, 26, 1024])
    d_rope = ext("d_rope", [HALF, 1024], BF16)
    slot = ext("slot", [1, 1], U32)
    wr4 = ext("wr4", [128, 9, 8])
    ew4 = ext("ew4", [8, 128, 17, 1024])

    y = nc.dram_tensor("y", [HALF, D], F32, kind="ExternalOutput")
    y_x1 = None
    if "x1" in debug_taps:
        y_x1 = nc.dram_tensor("y_x1", [HALF, D], F32, kind="ExternalOutput")
    y_x2 = None
    if "x2" in debug_taps:
        y_x2 = nc.dram_tensor("y_x2", [HALF, D], F32, kind="ExternalOutput")
    y_x3 = None
    if "x3" in debug_taps:
        y_x3 = nc.dram_tensor("y_x3", [HALF, D], F32, kind="ExternalOutput")

    d_x1 = nc.dram_tensor("d_x1", [HALF, D], F32, kind="Internal")
    d_ln1 = nc.dram_tensor("d_ln1", [HALF, D], BF16, kind="Internal")
    d_x2 = nc.dram_tensor("d_x2", [HALF, D], F32, kind="Internal")
    d_ln2 = nc.dram_tensor("d_ln2", [HALF, D], BF16, kind="Internal")
    d_x3 = nc.dram_tensor("d_x3", [HALF, D], F32, kind="Internal")
    d_ln3 = nc.dram_tensor("d_ln3", [HALF, D], BF16, kind="Internal")
    d_ex = nc.dram_tensor("d_ex", [2, 128, 520], F32, kind="Internal",
                          addr_space="Shared")
    d_exf = nc.dram_tensor("d_exf", [2, 16], U32, kind="Internal",
                           addr_space="Shared")
    d_ex2 = nc.dram_tensor("d_ex2", [2, 1, 8], F32, kind="Internal",
                           addr_space="Shared")

    P = Prog(nc)
    E = Emit(P)

    with ExitStack() as pc:
        sx = Sem(pc, nc, "sx")
        sxp = [Sem(pc, nc, "sxp0"), Sem(pc, nc, "sxp1")]
        sw = Sem(pc, nc, "sw")
        st = Sem(pc, nc, "st")
        sv = Sem(pc, nc, "sv")
        ss = Sem(pc, nc, "ss")
        sg = Sem(pc, nc, "sg")
        sd = Sem(pc, nc, "sd")
        sdg = Sem(pc, nc, "sdg")
        sxc = Sem(pc, nc, "sxc")
        sep = [Sem(pc, nc, "sep0"), Sem(pc, nc, "sep1")]
        sec = Sem(pc, nc, "sec")
        sxq = [Sem(pc, nc, "sxq0"), Sem(pc, nc, "sxq1")]
        spl = Sem(pc, nc, "spl")
        sgw = Sem(pc, nc, "sgw")
        stf = Sem(pc, nc, "stf")

        with ExitStack() as c1:
            en = c1.enter_context
            W5 = en(nc.sbuf_tensor("W5", [128, 41, D], BF16))
            WG = en(nc.sbuf_tensor("WG", [128, 16, D], BF16))
            WO = en(nc.sbuf_tensor("WO", [128, 8, D], BF16))
            BD = en(nc.sbuf_tensor("BD", [128, 8, 128], BF16))
            AT = en(nc.sbuf_tensor("AT", [128, 4, 128], BF16))
            MK = en(nc.sbuf_tensor("MK", [128, 10, 128], BF16))
            BRh = en(nc.sbuf_tensor("BRh", [1, 64], BF16))
            PG = en(nc.sbuf_tensor("PG", [128, 1], BF16))
            ONES = en(nc.sbuf_tensor("ONES", [128, 1], BF16))
            ONR = en(nc.sbuf_tensor("ONR", [128, 128], BF16))
            WGOb = en(nc.sbuf_tensor("WGOb", [128, D], BF16))
            SWb = en(nc.sbuf_tensor("SWb", [128, 4], F32))
            BGOb = en(nc.sbuf_tensor("BGOb", [128, 1], F32))
            SKV = en(nc.sbuf_tensor("SKV", [128, 5, 512], F32))
            SKc = en(nc.sbuf_tensor("SKc", [128, 5, 8], F32))
            SKVh = en(nc.sbuf_tensor("SKVh", [128, 512], BF16))
            SKch = en(nc.sbuf_tensor("SKch", [128, 16], BF16))
            XI = en(nc.sbuf_tensor("XI", [128, 2, D], F32))
            SM = en(nc.sbuf_tensor("SM", [128, 64], F32))
            LNt = en(nc.sbuf_tensor("LNt", [128, D], BF16))
            LNf = en(nc.sbuf_tensor("LNf", [128, D], BF16))
            Qt = en(nc.sbuf_tensor("Qt", [128, D], BF16))
            Qf = en(nc.sbuf_tensor("Qf", [128, D], BF16))
            PQf = en(nc.sbuf_tensor("PQf", [128, D], BF16))
            Kt = en(nc.sbuf_tensor("Kt", [128, D], BF16))
            PKt = en(nc.sbuf_tensor("PKt", [128, D], BF16))
            PKf = en(nc.sbuf_tensor("PKf", [128, D], BF16))
            Vt = en(nc.sbuf_tensor("Vt", [128, D], BF16))
            KLt = en(nc.sbuf_tensor("KLt", [128, D], BF16))
            KLfb = [en(nc.sbuf_tensor(f"KLf{i}", [128, D], BF16)) for i in range(2)]
            VLtb = [en(nc.sbuf_tensor(f"VLt{i}", [128, D], BF16)) for i in range(2)]
            KM = en(nc.sbuf_tensor("KM", [128, D], BF16))
            PTS = en(nc.sbuf_tensor("PTS", [128, 8, 128], BF16))
            GLB = en(nc.sbuf_tensor("GLB", [128, D], F32))
            DIFW = en(nc.sbuf_tensor("DIFW", [128, D], F32))
            DIFf = en(nc.sbuf_tensor("DIFf", [128, D], BF16))
            MX = en(nc.sbuf_tensor("MXD", [128, D], BF16))
            TM0 = en(nc.sbuf_tensor("TM0", [128, D], F32))
            PF = en(nc.psum_tensor("PF", [128, 7, 512], F32))
            PH = en(nc.psum_tensor("PH", [128, 1024], BF16))

            ident = MK[:, 1, :]
            DIF = Qt
            LN1 = LNt
            PS1 = LNt
            GH = LNt
            MXf = DIFf
            PS3 = MX
            SQS = DIFW
            VLVt = KM
            X1 = GLB
            KLVf = PKf
            KLVt = PKt

            def wload(dst, srcp, nch):
                step = 4
                for c0 in range(0, nch, step):
                    c1 = min(nch, c0 + step)
                    E.op("gpsimd", lambda e, d=dst[:, c0:c1, :], s=srcp[:, c0:c1, :]:
                         e.dma_start(d, s), sw, 16)
            wload(W5, w5T, 41)
            wload(WG, wgT, 16)
            wload(WO, woT, 8)
            for dst, srcp in ((BD[:], bd_mats[:]), (AT[:], at_mats[:]),
                              (MK[:], masks[:]), (PG[:], pre_gate[:])):
                E.op("gpsimd", lambda e, d=dst, s=srcp: e.dma_start(d, s), sw, 16)
            E.op("gpsimd", lambda e: e.memset(ONES[:], 1.0))
            E.op("gpsimd", lambda e: e.memset(ONR[:], 1.0))
            E.op("gpsimd", lambda e: e.memset(SKV[:], 0.0))
            E.op("gpsimd", lambda e: e.memset(SKc[:], 0.0))
            E.op("gpsimd", lambda e: e.memset(KLfb[1][:], 0.0))
            E.op("gpsimd", lambda e: e.memset(VLtb[1][:], 0.0))
            E.op("gpsimd", lambda e: e.memset(SM[:, 49:50], -EXPB))
            E.op("gpsimd", lambda e: e.memset(SM[:, 52:53], 1e-6))
            E.op("gpsimd", lambda e: e.memset(SM[:, 50:51], 0.125), sg, 1)

            E.wait("tensor", sw)
            E.wait("tensor", sg)
            E.mm(PF[:, 4, :], ONR[64:65, :], W5[64:65, 40, 0:512], True, True,
                 None, 1, tp=(64, 0))
            E.mm(PF[:, 5, :], ONR[64:65, :], W5[64:65, 40, 512:1024], True, True,
                 st, 1, tp=(64, 0))
            E.wait("vector", st)
            E.wait("vector", sw)
            E.op("vector", lambda e: e.tensor_copy(WGOb[:, 0:512], PF[:, 4, :]), w=(WGOb,))
            E.op("vector", lambda e: e.tensor_copy(WGOb[:, 512:1024], PF[:, 5, :]), w=(WGOb,))
            E.op("vector", lambda e: e.tensor_scalar(
                W5[:, 24:32, :], W5[:, 24:32, :], 0.125, None, ALU.mult), w=(W5,))
            E.op("vector", lambda e: e.tensor_copy(SM[0:1, 16:20], MK[0:1, 6, 0:4]), r=(MK,), w=(SM,))
            E.op("vector", lambda e: e.scalar_tensor_tensor(
                SM[0:1, 20:21], SM[0:1, 16:17], -1.0, SM[0:1, 17:18],
                ALU.mult, ALU.add), r=(SM,), w=(SM,))
            E.op("vector", lambda e: e.scalar_tensor_tensor(
                SM[0:1, 21:22], SM[0:1, 18:19], -1.0, SM[0:1, 19:20],
                ALU.mult, ALU.add), sv, 1, r=(SM,), w=(SM,))
            E.wait("scalar", sv)
            E.op("scalar", lambda e: e.activation(SM[0:1, 20:22], SM[0:1, 20:22], AF.Relu),
                 ss, 1, r=(SM,), w=(SM,))
            E.wait("vector", ss)
            E.op("vector", lambda e: e.tensor_add(SM[0:1, 22:23], SM[0:1, 16:17], SM[0:1, 20:21]), r=(SM,), w=(SM,))
            E.op("vector", lambda e: e.tensor_add(SM[0:1, 23:24], SM[0:1, 18:19], SM[0:1, 21:22]), r=(SM,), w=(SM,))
            E.op("vector", lambda e: e.scalar_tensor_tensor(
                SM[0:1, 24:25], SM[0:1, 22:23], -1.0, SM[0:1, 23:24],
                ALU.mult, ALU.add), sv, 1, r=(SM,), w=(SM,))
            E.wait("scalar", sv)
            E.op("scalar", lambda e: e.activation(SM[0:1, 24:25], SM[0:1, 24:25], AF.Relu),
                 ss, 1, r=(SM,), w=(SM,))
            E.wait("vector", ss)
            E.op("vector", lambda e: e.tensor_add(SM[0:1, 0:1], SM[0:1, 22:23], SM[0:1, 24:25]), r=(SM,), w=(SM,))
            E.op("vector", lambda e: e.tensor_scalar(
                SM[0:1, 25:26], SM[0:1, 0:1], -1.0, None, ALU.mult), r=(SM,), w=(SM,))
            E.op("vector", lambda e: e.tensor_scalar(
                SM[0:1, 1:5], SM[0:1, 16:20], SM[0:1, 25:26], None, ALU.add), sv, 1, r=(SM,), w=(SM,))
            E.wait("scalar", sv)
            E.op("scalar", lambda e: e.activation(
                SM[0:1, 5:9], SM[0:1, 1:5], AF.Exp, accum_out=SM[0:1, 9:10]), ss, 1, r=(SM,), w=(SM,))
            E.wait("vector", ss)
            E.op("vector", lambda e: e.reciprocal(SM[0:1, 10:11], SM[0:1, 9:10]), r=(SM,), w=(SM,))
            E.op("vector", lambda e: e.tensor_scalar(
                SM[0:1, 11:15], SM[0:1, 5:9], SM[0:1, 10:11], None, ALU.mult), r=(SM,), w=(SM,))
            E.op("vector", lambda e: e.tensor_copy(BRh[0:1, 0:4], SM[0:1, 11:15]), r=(SM,), w=(BRh,))
            E.op("vector", lambda e: e.tensor_copy(BRh[0:1, 4:5], MK[0:1, 6, 4:5]), sv, 1, r=(MK,), w=(BRh,))
            E.wait("tensor", sv)
            E.mm(PF[:, 0, 0:5], ONR[0:1, :], BRh[0:1, 0:5], True, True, st, 1)
            E.wait("vector", st)
            E.op("vector", lambda e: e.tensor_copy(SWb[:], PF[:, 0, 0:4]), w=(SWb,))
            E.op("vector", lambda e: e.tensor_copy(BGOb[:], PF[:, 0, 4:5]), sv, 1, w=(BGOb,))

            with nc.named_scope("s1pre"):
                P.run_block()
            E.barrier_reset()

            n_prior_sx = [sx.n // 16]

            def emit_ln(xsrc, out_bf, eps=1e-5):
                E.op("vector", lambda e, x=xsrc: e.reduce_sum(SM[:, 0:1], x, AX.X), sv, 1,
                     r=(xsrc,), w=(SM,))
                E.op("scalar", lambda e, x=xsrc: e.activation(
                    SQS[:], x, AF.Square, accum_out=SM[:, 1:2]), ss, 1,
                    r=(xsrc,), w=(SQS, SM))
                E.wait("vector", ss)
                E.op("vector", lambda e: e.tensor_scalar(
                    SM[:, 2:3], SM[:, 0:1], 1.0 / D, None, ALU.mult), r=(SM,), w=(SM,))
                E.op("vector", lambda e: e.tensor_mul(SM[:, 3:4], SM[:, 2:3], SM[:, 2:3]), r=(SM,), w=(SM,))
                E.op("vector", lambda e: e.scalar_tensor_tensor(
                    SM[:, 4:5], SM[:, 1:2], 1.0 / D, SM[:, 3:4], ALU.mult, ALU.subtract), r=(SM,), w=(SM,))
                E.op("vector", lambda e, p=eps: e.tensor_scalar(
                    SM[:, 4:5], SM[:, 4:5], p, None, ALU.add), sv, 1, r=(SM,), w=(SM,))
                E.wait("scalar", sv)
                E.op("scalar", lambda e: e.sqrt(SM[:, 5:6], SM[:, 4:5]), ss, 1, r=(SM,), w=(SM,))
                E.wait("vector", ss)
                E.op("vector", lambda e: e.reciprocal(SM[:, 6:7], SM[:, 5:6]), r=(SM,), w=(SM,))
                E.op("vector", lambda e: e.tensor_mul(SM[:, 7:8], SM[:, 2:3], SM[:, 6:7]), r=(SM,), w=(SM,))
                E.op("vector", lambda e: e.tensor_scalar(
                    SM[:, 7:8], SM[:, 7:8], -1.0, None, ALU.mult), sv, 1, r=(SM,), w=(SM,))
                E.wait("scalar", sv)
                E.op("scalar", lambda e, o=out_bf, x=xsrc: e.activation(
                    o, x, AF.Identity, bias=SM[:, 7:8], scale=SM[:, 6:7]), ss, 1,
                    r=(xsrc, SM), w=(out_bf,))

            def emit_phi(xin_bf, out_bf):
                E.wait("scalar", sv)
                E.op("scalar", lambda e, x=xin_bf, o=out_bf: e.activation(o, x, AF.Relu),
                     ss, 1, r=(xin_bf,), w=(out_bf,))
                E.wait("vector", ss)
                E.op("vector", lambda e, x=xin_bf, o=out_bf: e.scalar_tensor_tensor(
                    PS1[:], o, -1.0, x, ALU.mult, ALU.add), sv, 1,
                    r=(xin_bf, out_bf), w=(PS1,))
                E.wait("scalar", sv)
                E.op("scalar", lambda e: e.activation(PS1[:], PS1[:], AF.Exp), ss, 1,
                     r=(PS1,), w=(PS1,))
                E.wait("vector", ss)
                E.op("vector", lambda e, o=out_bf: e.tensor_add(o, o, PS1[:]), sv, 1,
                     r=(out_bf, PS1), w=(out_bf,))

            def trans8(src_bf, dst_bf, bank):
                E.wait("tensor", sv)
                E.wait("tensor", ss)
                for j in range(8):
                    E.trans(bank[:, 128 * j:128 * (j + 1)],
                            src_bf[:, 128 * j:128 * (j + 1)], ident,
                            st if j == 7 else None, 1)
                E.wait("vector", st)
                E.op("vector", lambda e, d=dst_bf, b=bank: e.tensor_copy(d, b), sv, 1,
                     w=(dst_bf,))

            def hmm8(dst_bf, src_tok, rhs_at, bd_slice):
                E.wait("tensor", sv)
                for j in range(8):
                    bank, col = j // 4, (j % 4) * 128
                    if bd_slice is None:
                        E.mm(PF[:, bank, col:col + 128],
                             src_tok[:, 128 * j:128 * (j + 1)], rhs_at,
                             True, True, st if j == 7 else None, 1)
                    else:
                        E.mm(PF[:, bank, col:col + 128], bd_slice,
                             src_tok[:, 128 * j:128 * (j + 1)],
                             True, True, st if j == 7 else None, 1)
                E.wait("vector", st)
                E.op("vector", lambda e, d=dst_bf: e.tensor_copy(
                    d[:, 0:512], PF[:, 0, :]), w=(dst_bf,))
                E.op("vector", lambda e, d=dst_bf: e.tensor_copy(
                    d[:, 512:1024], PF[:, 1, :]), sv, 1, w=(dst_bf,))

            def haar_prep(lv):
                hmm8(KM[:], Kt[:], AT[:, lv, :], None)
                hmm8(PS3[:], KM[:], None, BD[:, lv, :])
                emit_phi(PS3[:], KLVf[:])
                trans8(KLVf[:], KLVt[:], PH[:])
                hmm8(KM[:], Vt[:], AT[:, lv, :], None)
                hmm8(PS3[:], KM[:], None, BD[:, 4 + lv, :])
                trans8(PS3[:], VLVt[:], PH[:])

            def tile_body(ti, own, xsrc_dram, sk_rhs, base_consumed):
                xi = XI[:, ti % 2, :]
                par = (ti % 2) if own else ((ti + nt) % 2)
                kl_cur, kl_prv = KLfb[par], KLfb[1 - par]
                vl_cur, vl_prv = VLtb[par], VLtb[1 - par]

                if ti + 1 < nt:
                    E.wait("sync", sxc, base_consumed + ti)
                    E.op("sync", lambda e, t=ti + 1, s=xsrc_dram: e.dma_start(
                        XI[:, t % 2, :], s[128 * t:128 * (t + 1), :]),
                        sxp[(ti + 1) % 2], 16)

                E.wait("vector", sxp[ti % 2])
                E.wait("scalar", sxp[ti % 2])
                E.wait("scalar", sd)
                E.wait("vector", sd)

                def _cut0(ph):
                    if max_phase < ph:
                        if own:
                            E.op("vector", lambda e: e.memzero(SM[:, 51:52]),
                                 sxc, 1, w=(SM,))
                        else:
                            E.op("scalar", lambda e: e.memzero(SM[:, 48:49]),
                                 sxc, 1, w=(SM,))
                        return True
                    return False

                emit_ln(xi, LNt[:])
                if _cut0(0.3):
                    return
                trans8(LNt[:], LNf[:], PH[:])
                if _cut0(0.6):
                    return

                E.wait("tensor", sv)
                grpA = [(0, 0), (1, 2), (2, 4)] if own else [(1, 0), (2, 2)]
                for j in range(8):
                    lhsT = LNf[:, 128 * j:128 * (j + 1)]
                    for gi, (pi, bank) in enumerate(grpA):
                        for hcol in range(2):
                            last = (j == 7 and gi == len(grpA) - 1 and hcol == 1)
                            E.mm(PF[:, bank + hcol, :], lhsT,
                                 W5[:, 8 * pi + j, 512 * hcol:512 * (hcol + 1)],
                                 j == 0, j == 7, st if last else None, 1)
                E.wait("scalar", st)
                E.wait("vector", st)
                kb = 2 if own else 0
                vb = 4 if own else 2
                if own:
                    E.op("scalar", lambda e: e.copy(Qt[:, 0:512], PF[:, 0, :]), w=(Qt,))
                    E.op("scalar", lambda e: e.copy(Qt[:, 512:1024], PF[:, 1, :]), ss, 1, w=(Qt,))
                E.op("vector", lambda e, b=kb: e.tensor_copy(Kt[:, 0:512], PF[:, b, :]), w=(Kt,))
                E.op("vector", lambda e, b=kb: e.tensor_copy(Kt[:, 512:1024], PF[:, b + 1, :]), w=(Kt,))
                E.op("vector", lambda e, b=vb: e.tensor_copy(Vt[:, 0:512], PF[:, b, :]), w=(Vt,))
                E.op("vector", lambda e, b=vb: e.tensor_copy(Vt[:, 512:1024], PF[:, b + 1, :]), sv, 1, w=(Vt,))

                _cut = _cut0

                if _cut(1):
                    return
                tail = (not own) and ti == nt - 1
                if own or tail:
                    E.wait("tensor", sv)
                    E.wait("tensor", ss)
                    for j in range(8):
                        lhsT = LNf[:, 128 * j:128 * (j + 1)]
                        for gi, (pi, bank) in enumerate([(3, 0), (4, 2)]):
                            for hcol in range(2):
                                last = (j == 7 and gi == 1 and hcol == 1)
                                E.mm(PF[:, bank + hcol, :], lhsT,
                                     W5[:, 8 * pi + j, 512 * hcol:512 * (hcol + 1)],
                                     j == 0, j == 7, st if last else None, 1)
                    E.wait("vector", st)
                    E.op("vector", lambda e: e.tensor_copy(KLt[:, 0:512], PF[:, 0, :]), w=(KLt,))
                    E.op("vector", lambda e: e.tensor_copy(KLt[:, 512:1024], PF[:, 1, :]), w=(KLt,))
                    E.op("vector", lambda e, v=vl_cur: e.tensor_copy(v[:, 0:512], PF[:, 2, :]), w=(vl_cur,))
                    E.op("vector", lambda e, v=vl_cur: e.tensor_copy(v[:, 512:1024], PF[:, 3, :]), sv, 1, w=(vl_cur,))

                if _cut(2):
                    return
                emit_phi(Kt[:], PKt[:])
                if _cut(2.2):
                    return
                if own:
                    trans8(PKt[:], PKf[:], PH[:])
                if _cut(2.4):
                    return
                if own:
                    trans8(Qt[:], Qf[:], PH[:])
                if _cut(2.6):
                    return
                if own or tail:
                    trans8(KLt[:], kl_cur[:], PH[:])
                if _cut(2.8):
                    return
                if own:
                    emit_phi(Qf[:], PQf[:])

                if _cut(3):
                    return
                ncalls = min(5, int(max_phase) - 3) if max_phase >= 4 \
                    else (1 if max_phase > 3.05 else 0)
                for cc in range(ncalls):
                    if cc > 0:
                        haar_prep(cc - 1)
                    kf = PKf[:] if cc == 0 else KLVf[:]
                    ktok = PKt[:] if cc == 0 else KLVt[:]
                    vtok = Vt[:] if cc == 0 else VLVt[:]
                    den_bank = 4

                    if cc == 0 and _cut(3.07):
                        return
                    E.wait("tensor", sv)
                    if cc == 0 and _cut(3.1):
                        return
                    if own:
                        E.op("vector", lambda e, cc=cc: e.tensor_copy(SKVh[:], SKV[:, cc, :]),
                             r=(SKV,), w=(SKVh,))
                        E.op("vector", lambda e, cc=cc: e.tensor_copy(
                            SKch[:].rearrange("p (a b) -> p a b", a=8),
                            SKc[:, cc, :].unsqueeze(2).broadcast_to([128, 8, 2])), sv, 1,
                            r=(SKc,), w=(SKch,))
                        E.wait("tensor", sv)
                        for hg in range(4):
                            p = hg % 2
                            pslot = PTS[:, 4 * p:4 * p + 4, :]
                            if hg >= 2:
                                E.wait("tensor", sv)
                            for half in range(2):
                                for jj in range(2):
                                    h = 4 * hg + 2 * jj + half
                                    E.mm(PF[:, half, 256 * p + 128 * jj:256 * p + 128 * jj + 128],
                                         hs(kf, h), hs(PQf, h), True, True,
                                         st if jj == 1 else None, 1,
                                         tp=(64 * half, 0))
                            E.wait("vector", st)
                            E.op("vector", lambda e, p=p, ps=pslot: e.tensor_tensor(
                                ps.rearrange("p (a b) c -> p a b c", a=2),
                                PF[:, 0:2, 256 * p:256 * p + 256].rearrange(
                                    "p a (b c) -> p a b c", b=2),
                                MK[:, 0:1, :].unsqueeze(1).broadcast_to([128, 2, 2, 128]),
                                ALU.mult), sv, 1, r=(MK,), w=(PTS,))
                            E.wait("tensor", sv)
                            for idx in range(4):
                                half, jj = divmod(idx, 2)
                                h = 4 * hg + 2 * jj + half
                                slot = PTS[:, 4 * p + idx, :]
                                ob = PF[:, 2 + h // 8, 64 * (h % 8):64 * (h % 8) + 64]
                                E.mm(ob, slot, vtok[:, 64 * h:64 * h + 64], True, True)
                                E.mm(PF[:, den_bank, h:h + 1], slot, ONES[:],
                                     True, True)
                        E.wait("tensor", sv)
                        for h in range(0, 16, 2):
                            E.mm(PF[:, 0, 64 * (h // 2):64 * (h // 2) + 64],
                                 hs(PQf, h), SKVh[0:64, 64 * (h // 2):64 * (h // 2) + 64],
                                 True, True, None, 1, tp=(0, 0))
                        for h in range(0, 16, 2):
                            j = h // 2
                            E.mm(PF[:, den_bank, 16 + 2 * j:18 + 2 * j],
                                 hs(PQf, h), SKch[0:64, 2 * j:2 * j + 2],
                                 True, True, None, 1, tp=(0, 0))
                        for h in range(1, 16, 2):
                            E.mm(PF[:, 1, 64 * (h // 2):64 * (h // 2) + 64],
                                 hs(PQf, h), SKVh[64:128, 64 * (h // 2):64 * (h // 2) + 64],
                                 True, True, None, 1, tp=(64, 0))
                    if cc == 0 and _cut(3.3):
                        return
                    for j in range(8):
                        E.mm(PF[:, 5 + j // 4, (j % 4) * 128:(j % 4) * 128 + 128],
                             ktok[:, 128 * j:128 * (j + 1)],
                             vtok[:, 128 * j:128 * (j + 1)],
                             True, True, None, 1)
                        E.mm(PF[:, den_bank, 32 + j:33 + j],
                             ktok[:, 128 * j:128 * (j + 1)], sk_rhs,
                             True, True,
                             (st if j == 7 and not own else None), 1)
                    if cc == 0 and _cut(3.4):
                        return
                    if own:
                        for h in range(1, 16, 2):
                            j = h // 2
                            E.mm(PF[:, den_bank, 40 + 2 * j:42 + 2 * j],
                                 hs(PQf, h), SKch[64:128, 2 * j:2 * j + 2],
                                 True, True, st if h == 15 else None, 1,
                                 tp=(64, 0))
                    oden_st = st.n
                    if cc == 0 and _cut(3.5):
                        return

                    E.wait("vector", st, oden_st)
                    if own:
                        E.op("vector", lambda e, b=den_bank: e.tensor_copy(
                            SM[:, 32:48].rearrange("p (a b) -> p a b", a=8)[:, :, 0:1],
                            PF[:, b, 16:32].rearrange("p (a b) -> p a b", a=8)[:, :, 0:1]),
                            w=(SM,))
                        E.op("vector", lambda e, b=den_bank: e.tensor_copy(
                            SM[:, 32:48].rearrange("p (a b) -> p a b", a=8)[:, :, 1:2],
                            PF[:, b, 40:56].rearrange("p (a b) -> p a b", a=8)[:, :, 0:1]),
                            w=(SM,))
                        E.op("vector", lambda e, b=den_bank: e.tensor_add(
                            SM[:, 16:32], PF[:, b, 0:16], SM[:, 32:48]),
                            r=(SM,), w=(SM,))
                        E.op("vector", lambda e: e.scalar_tensor_tensor(
                            SM[:, 32:48], SM[:, 16:32], -1.0,
                            SM[:, 52:53].broadcast_to([128, 16]),
                            ALU.mult, ALU.add), sv, 1, r=(SM,), w=(SM,))
                        E.wait("scalar", sv)
                        E.op("scalar", lambda e: e.activation(
                            SM[:, 32:48], SM[:, 32:48], AF.Relu), ss, 1,
                            r=(SM,), w=(SM,))
                        E.wait("vector", ss)
                        E.op("vector", lambda e: e.tensor_add(
                            SM[:, 16:32], SM[:, 16:32], SM[:, 32:48]),
                            r=(SM,), w=(SM,))
                        E.op("vector", lambda e: e.reciprocal(SM[:, 16:32], SM[:, 16:32]),
                             r=(SM,), w=(SM,))
                        if cc > 0:
                            E.op("vector", lambda e, cc=cc: e.tensor_scalar(
                                SM[:, 16:32], SM[:, 16:32], SWb[:, cc - 1:cc], None,
                                ALU.mult), r=(SM, SWb), w=(SM,))
                        for hb in range(2):
                            oa = PF[:, 2 + hb, :].rearrange(
                                "p (a b c) -> p a b c", a=4, b=2)
                            obp = PF[:, 0:2, 256 * hb:256 * hb + 256].rearrange(
                                "p a (b c) -> p b a c", b=4)
                            coef_b = SM[:, 16 + 8 * hb:24 + 8 * hb].rearrange(
                                "p (a b) -> p a b", a=4) \
                                .unsqueeze(3).broadcast_to([128, 4, 2, 64])
                            tm4 = TM0[:, 0:512].rearrange(
                                "p (a b c) -> p a b c", a=4, b=2)
                            E.op("vector", lambda e, ob=obp, t4=tm4: e.tensor_copy(
                                t4, ob), w=(TM0,))
                            E.op("vector", lambda e, oa=oa, t4=tm4: e.scalar_tensor_tensor(
                                t4, oa, 1.0, t4, ALU.mult, ALU.add), r=(TM0,), w=(TM0,))
                            E.op("vector", lambda e, cb=coef_b, t4=tm4: e.tensor_tensor(
                                t4, t4, cb, ALU.mult), r=(SM, TM0), w=(TM0,))
                            if cc == 0:
                                E.op("vector", lambda e, hb=hb: e.tensor_copy(
                                    GLB[:, 512 * hb:512 * (hb + 1)], TM0[:, 0:512]),
                                    r=(TM0,), w=(GLB,))
                            else:
                                E.op("vector", lambda e, hb=hb: e.tensor_add(
                                    GLB[:, 512 * hb:512 * (hb + 1)],
                                    GLB[:, 512 * hb:512 * (hb + 1)], TM0[:, 0:512]),
                                    r=(GLB, TM0), w=(GLB,))
                    if cc == 0 and _cut(3.7):
                        return
                    for half in range(2):
                        pr = slice(64 * half, 64 * (half + 1))
                        co = slice(64 * half, 64 * half + 64)
                        E.op("vector", lambda e, cc=cc, pr=pr, co=co: e.tensor_add(
                            SKV[pr, cc, :].rearrange("p (a b) -> p a b", a=8),
                            SKV[pr, cc, :].rearrange("p (a b) -> p a b", a=8),
                            PF[pr, 5:7, :].rearrange("p a (b c) -> p (a b) c", b=4)[:, :, co]),
                            r=(SKV,), w=(SKV,))
                    E.op("vector", lambda e, cc=cc, b=den_bank: e.tensor_add(
                        SKc[:, cc, :], SKc[:, cc, :], PF[:, b, 32:40]), sv, 1,
                        r=(SKc,), w=(SKc,))

                if _cut(9):
                    return
                if own:
                    E.wait("tensor", sv)
                    for hp in range(8):
                        p = hp % 2
                        plslot = PTS[:, 4 * p:4 * p + 4, :]
                        if hp >= 2:
                            E.wait("tensor", sg)
                        for half in range(2):
                            h = 2 * hp + half
                            for w, klw in ((0, kl_prv), (1, kl_cur)):
                                E.mm(PF[:, half, 256 * p + 128 * w:256 * p + 128 * w + 128],
                                     hs(klw, h), hs(Qf, h), True, True,
                                     st if w == 1 else None, 1,
                                     tp=(64 * half, 0))
                        E.wait("scalar", st)
                        E.op("scalar", lambda e, p=p, ps=plslot: e.activation(
                            ps.rearrange("p (a b) c -> p a b c", a=2),
                            PF[:, 0:2, 256 * p:256 * p + 256].rearrange(
                                "p a (b c) -> p a b c", b=2),
                            AF.Exp, bias=SM[:, 49:50]), ss, 1,
                            r=(SM,), w=(PTS,))
                        E.wait("vector", ss)
                        mk0 = 2 if ti == 0 else 4
                        E.op("vector", lambda e, ps=plslot, m0=mk0: e.tensor_tensor(
                            ps.rearrange("p (a b) c -> p a b c", a=2),
                            ps.rearrange("p (a b) c -> p a b c", a=2),
                            MK[:, m0:m0 + 2, :].unsqueeze(1).broadcast_to([128, 2, 2, 128]),
                            ALU.mult), sg, 1, r=(PTS, MK), w=(PTS,))
                        E.wait("tensor", sg)
                        for half in range(2):
                            h = 2 * hp + half
                            ob = PF[:, 2 + h // 8, 64 * (h % 8):64 * (h % 8) + 64]
                            dn = PF[:, 4, h:h + 1]
                            sprv = PTS[:, 4 * p + 2 * half, :]
                            scur = PTS[:, 4 * p + 2 * half + 1, :]
                            E.mm(ob, sprv, vl_prv[:, 64 * h:64 * h + 64],
                                 True, False)
                            E.mm(ob, scur, vl_cur[:, 64 * h:64 * h + 64],
                                 False, True)
                            E.mm(dn, sprv, ONES[:], True, False)
                            E.mm(dn, scur, ONES[:], False, True,
                                 st if h == H - 1 else None, 1)
                    loc_st = st.n
                    E.wait("vector", st, loc_st)
                    E.op("vector", lambda e: e.reciprocal(SM[:, 32:48], PF[:, 4, 0:16]), w=(SM,))
                    for hb in range(2):
                        coef_b = SM[:, 32 + 8 * hb:40 + 8 * hb] \
                            .unsqueeze(2).broadcast_to([128, 8, 64])
                        E.op("vector", lambda e, hb=hb, cb=coef_b: e.tensor_tensor(
                            TM0[:, 512 * hb:512 * (hb + 1)].rearrange(
                                "p (a b) -> p a b", a=8),
                            PF[:, 2 + hb, :].rearrange("p (a b) -> p a b", a=8),
                            cb, ALU.mult), r=(SM,), w=(TM0,))
                    E.op("vector", lambda e: e.scalar_tensor_tensor(
                        DIFW[:], GLB[:], -1.0, TM0[:], ALU.mult, ALU.add),
                        r=(TM0, GLB), w=(DIFW,))
                    E.op("vector", lambda e: e.tensor_copy(DIF[:], DIFW[:]), sv, 1,
                         r=(DIFW,), w=(DIF,))

                if _cut(10):
                    return
                if own:
                    pass
                if own:
                    trans8(DIF[:], DIFf[:], PH[:])
                    E.wait("tensor", sv)
                    for j in range(16):
                        lhsT = LNf[:, 128 * j:128 * (j + 1)] if j < 8 \
                            else DIFf[:, 128 * (j - 8):128 * (j - 7)]
                        for hcol in range(2):
                            E.mm(PF[:, hcol, :], lhsT,
                                 WG[:, j, 512 * hcol:512 * (hcol + 1)],
                                 j == 0, False)
                    for hcol in range(2):
                        E.mm(PF[:, hcol, :], MK[:, 7, :],
                             W5[:, 40, 512 * hcol:512 * (hcol + 1)], False, True,
                             st if hcol == 1 else None, 1)
                    E.wait("scalar", st)
                    E.op("scalar", lambda e: e.activation(GH[:, 0:512], PF[:, 0, :], AF.Silu), w=(GH,))
                    E.op("scalar", lambda e: e.activation(GH[:, 512:1024], PF[:, 1, :], AF.Silu), ss, 1, w=(GH,))
                    E.wait("vector", ss)
                    E.op("vector", lambda e: e.tensor_mul(KM[:], GH[:], WGOb[:]),
                         r=(GH, WGOb), w=(KM,))
                    E.op("vector", lambda e: e.reduce_sum(SM[:, 8:9], KM[:], AX.X), sv, 1,
                         r=(KM,), w=(SM,))
                    E.wait("scalar", sv)
                    E.op("scalar", lambda e: e.activation(
                        SM[:, 9:10], SM[:, 8:9], AF.Sigmoid, bias=BGOb[:]), ss, 1,
                        r=(SM, BGOb), w=(SM,))
                    E.wait("vector", ss)
                    E.op("vector", lambda e: e.scalar_tensor_tensor(
                        TM0[:], DIFW[:], SM[:, 9:10], GLB[:], ALU.mult, ALU.add),
                        r=(DIFW, SM, GLB), w=(TM0,))
                    E.op("vector", lambda e: e.tensor_copy(MX[:], TM0[:]), sv, 1,
                         r=(TM0,), w=(MX,))

                    trans8(MX[:], MXf[:], PH[:])
                    E.wait("tensor", sv)
                    for j in range(8):
                        for hcol in range(2):
                            E.mm(PF[:, 2 + hcol, :], MXf[:, 128 * j:128 * (j + 1)],
                                 WO[:, j, 512 * hcol:512 * (hcol + 1)],
                                 j == 0, False)
                    for hcol in range(2):
                        E.mm(PF[:, 2 + hcol, :], MK[:, 8, :],
                             W5[:, 40, 512 * hcol:512 * (hcol + 1)], False, True,
                             st if hcol == 1 else None, 1)
                    E.wait("vector", st)
                    for hb in range(2):
                        E.op("vector", lambda e, hb=hb, x=xi: e.tensor_add(
                            X1[:, 512 * hb:512 * (hb + 1)], PF[:, 2 + hb, :],
                            x[:, 512 * hb:512 * (hb + 1)]),
                            sxc if hb == 1 else None, 1, r=(XI,), w=(X1,))
                    E.wait("scalar", sxc)
                    emit_ln(X1[:], LN1[:])
                    ln1_ss = ss.n
                    E.wait("sync", sxc)
                    E.op("sync", lambda e, t=ti: e.dma_start(
                        d_x1[128 * t:128 * (t + 1), :], X1[:]), sd, 16)
                    if y_x1 is not None:
                        E.op("sync", lambda e, t=ti: e.dma_start(
                            y_x1[128 * t:128 * (t + 1), :], X1[:]), sd, 16)
                    E.wait("sync", ss, ln1_ss)
                    E.op("sync", lambda e, t=ti: e.dma_start(
                        d_ln1[128 * t:128 * (t + 1), :], LN1[:]), sd, 16)
                else:
                    E.op("scalar", lambda e: e.memzero(SM[:, 48:49]),
                         sxc, 1, w=(SM,))

            if max_phase >= 0.1:
                E.op("sync", lambda e: e.dma_start(XI[:, 0, :], x_pre[0:128, :]), sxp[0], 16)
                for ti in range(nt):
                    tile_body(ti, False, x_pre, PG[:], 0)

                n_prior_sx[0] += nt
                E.wait("sync", sxc, nt)
                E.op("sync", lambda e: e.dma_start(XI[:, 0, :], x_own[0:128, :]), sxp[0], 16)
                for ti in range(nt):
                    tile_body(ti, True, x_own, ONES[:], nt)

            with nc.named_scope("s1"):
                P.run_block()
            E.barrier_reset()

        if max_phase >= 11:
          with ExitStack() as c2:
            en = c2.enter_context
            SH = en(nc.sbuf_tensor("SH", [128, 24, 1024], BF16))
            EWb = [en(nc.sbuf_tensor(f"EWs{i}", [128, 12288], BF16)) for i in range(2)]
            RT = en(nc.sbuf_tensor("RT", [128, 8, 8], BF16))
            ACC = en(nc.sbuf_tensor("ACC", [128, 8, 1024], F32))
            LN1F = en(nc.sbuf_tensor("LN1F", [128, 8, 1024], BF16))
            LTK = en(nc.sbuf_tensor("LTK", [128, 2, 1024], BF16))
            XT1 = en(nc.sbuf_tensor("XT1", [128, 2, 1024], F32))
            HS2 = en(nc.sbuf_tensor("HS2", [128, 1024], BF16))
            HST = en(nc.sbuf_tensor("HST", [128, 1024], BF16))
            SGb = en(nc.sbuf_tensor("SGb", [128, 1024], BF16))
            HE = en(nc.sbuf_tensor("HE2", [128, 512], BF16))
            HET = en(nc.sbuf_tensor("HET", [128, 512], BF16))
            CFA = en(nc.sbuf_tensor("CFA", [128, 8, 8], F32))
            SM2 = en(nc.sbuf_tensor("SM2", [128, 128], F32))
            SQ2 = en(nc.sbuf_tensor("SQ2", [128, 1024], F32))
            X2T = en(nc.sbuf_tensor("X2T", [128, 1024], F32))
            LN2T = en(nc.sbuf_tensor("LN2T", [128, 1024], BF16))
            ID2 = en(nc.sbuf_tensor("ID2", [128, 128], BF16))
            PB = en(nc.psum_tensor("PB", [128, 7, 512], F32))
            PH2 = en(nc.psum_tensor("PH2", [128, 1024], BF16))

            for c0 in range(0, 24, 4):
                E.op("gpsimd", lambda e, c0=c0: e.dma_start(
                    SH[:, c0:c0 + 4, :], shw[:, c0:c0 + 4, :]), sw, 16)
            E.op("gpsimd", lambda e: e.dma_start(RT[:], rt8[:]), sw, 16)
            E.op("gpsimd", lambda e: e.dma_start(ID2[:], masks[:, 1, :]), sw, 16)
            E.op("gpsimd", lambda e: e.memset(SM2[:, 96:97], 1.0))
            E.op("gpsimd", lambda e: e.memset(SM2[:, 97:98], -5.0), sg, 1)

            def eload(eidx, pi):
                for c0 in range(0, 12288, 3072):
                    E.op("gpsimd", lambda e, eidx=eidx, pi=pi, c0=c0: e.dma_start(
                        EWb[pi][:, c0:c0 + 3072], ew[eidx, :, c0:c0 + 3072]),
                        sep[pi], 16)
            eload(0, 0)

            def emit_ln2(xsrc, out_bf, eps=1e-5):
                E.op("vector", lambda e, x=xsrc: e.reduce_sum(SM2[:, 0:1], x, AX.X), sv, 1,
                     r=(xsrc,), w=(SM2,))
                E.op("scalar", lambda e, x=xsrc: e.activation(
                    SQ2[:], x, AF.Square, accum_out=SM2[:, 1:2]), ss, 1,
                    r=(xsrc,), w=(SQ2, SM2))
                E.wait("vector", ss)
                E.op("vector", lambda e: e.tensor_scalar(
                    SM2[:, 2:3], SM2[:, 0:1], 1.0 / D, None, ALU.mult), r=(SM2,), w=(SM2,))
                E.op("vector", lambda e: e.tensor_mul(SM2[:, 3:4], SM2[:, 2:3], SM2[:, 2:3]), r=(SM2,), w=(SM2,))
                E.op("vector", lambda e: e.scalar_tensor_tensor(
                    SM2[:, 4:5], SM2[:, 1:2], 1.0 / D, SM2[:, 3:4], ALU.mult, ALU.subtract), r=(SM2,), w=(SM2,))
                E.op("vector", lambda e, p=eps: e.tensor_scalar(
                    SM2[:, 4:5], SM2[:, 4:5], p, None, ALU.add), sv, 1, r=(SM2,), w=(SM2,))
                E.wait("scalar", sv)
                E.op("scalar", lambda e: e.sqrt(SM2[:, 5:6], SM2[:, 4:5]), ss, 1, r=(SM2,), w=(SM2,))
                E.wait("vector", ss)
                E.op("vector", lambda e: e.reciprocal(SM2[:, 6:7], SM2[:, 5:6]), r=(SM2,), w=(SM2,))
                E.op("vector", lambda e: e.tensor_mul(SM2[:, 7:8], SM2[:, 2:3], SM2[:, 6:7]), r=(SM2,), w=(SM2,))
                E.op("vector", lambda e: e.tensor_scalar(
                    SM2[:, 7:8], SM2[:, 7:8], -1.0, None, ALU.mult), sv, 1, r=(SM2,), w=(SM2,))
                E.wait("scalar", sv)
                E.op("scalar", lambda e, o=out_bf, x=xsrc: e.activation(
                    o, x, AF.Identity, bias=SM2[:, 7:8], scale=SM2[:, 6:7]), ss, 1,
                    r=(xsrc, SM2), w=(out_bf,))

            def trans2(src_bf, dst_ap, nchunk, dst_name):
                E.wait("tensor", sv)
                E.wait("tensor", ss)
                for j in range(nchunk):
                    E.trans(PH2[:, 128 * j:128 * (j + 1)],
                            src_bf[:, 128 * j:128 * (j + 1)], ID2[:],
                            st if j == nchunk - 1 else None, 1)
                E.wait("vector", st)
                E.op("vector", lambda e, d=dst_ap, n=nchunk: e.tensor_copy(
                    d, PH2[:, 0:128 * n]), sv, 1, w=(dst_name,))

            E.wait("sync", sd)
            E.wait("tensor", sw)
            E.wait("tensor", sg)

            nsc = (nt + 7) // 8
            for sc in range(nsc):
                tl = list(range(sc * 8, min(nt, sc * 8 + 8)))
                for t in tl:
                    ts_ = t - sc * 8
                    par = t % 2
                    E.wait("sync", st)
                    E.op("sync", lambda e, t=t, par=par: e.dma_start(
                        LTK[:, par, :], d_ln1[128 * t:128 * (t + 1), :]), sxp[par], 16)
                    E.wait("tensor", sxp[par])
                    trans2(LTK[:, par, :], LN1F[:, ts_, :], 8, LN1F)
                    E.wait("tensor", sv)
                    E.wait("tensor", ss)
                    for j in range(8):
                        lhsT = LN1F[:, ts_, 128 * j:128 * (j + 1)]
                        E.mm(PB[:, 6, 0:8], lhsT, RT[:, j, :], j == 0, j == 7)
                        for hb in range(2):
                            E.mm(PB[:, hb, :], lhsT, SH[:, j, 512 * hb:512 * (hb + 1)],
                                 j == 0, j == 7)
                        for hb in range(2):
                            E.mm(PB[:, 2 + hb, :], lhsT, SH[:, 8 + j, 512 * hb:512 * (hb + 1)],
                                 j == 0, j == 7, st if j == 7 and hb == 1 else None, 1)
                    E.wait("scalar", st)
                    E.op("scalar", lambda e: e.activation(SGb[:, 0:512], PB[:, 0, :], AF.Silu), w=(SGb,))
                    E.op("scalar", lambda e: e.activation(SGb[:, 512:1024], PB[:, 1, :], AF.Silu), w=(SGb,))
                    E.op("scalar", lambda e: e.activation(
                        SM2[:, 16:24], PB[:, 6, 0:8], AF.Exp,
                        accum_out=SM2[:, 24:25]), ss, 1, w=(SM2,))
                    E.wait("vector", ss)
                    E.op("vector", lambda e: e.tensor_tensor(
                        HS2[:, 0:512], PB[:, 2, :], SGb[:, 0:512], ALU.mult),
                        r=(SGb,), w=(HS2,))
                    E.op("vector", lambda e: e.tensor_tensor(
                        HS2[:, 512:1024], PB[:, 3, :], SGb[:, 512:1024], ALU.mult), sv, 1,
                        r=(SGb,), w=(HS2,))
                    E.op("vector", lambda e: e.reciprocal(SM2[:, 25:26], SM2[:, 24:25]), r=(SM2,), w=(SM2,))
                    E.op("vector", lambda e: e.tensor_scalar(
                        SM2[:, 16:24], SM2[:, 16:24], SM2[:, 25:26], None, ALU.mult), r=(SM2,), w=(SM2,))
                    T88 = SM2[:, 32:96].rearrange("p (a b) -> p a b", a=8)
                    p8 = SM2[:, 16:24]
                    E.op("vector", lambda e, T=T88: e.scalar_tensor_tensor(
                        T, p8.unsqueeze(2).broadcast_to([128, 8, 8]), -1.0,
                        p8.unsqueeze(1).broadcast_to([128, 8, 8]),
                        ALU.mult, ALU.add), r=(SM2,), w=(SM2,))
                    E.op("vector", lambda e, T=T88: e.scalar_tensor_tensor(
                        T, T, 1e8, SM2[:, 96:97].unsqueeze(1).broadcast_to([128, 8, 8]),
                        ALU.mult, ALU.add), sv, 1, r=(SM2,), w=(SM2,))
                    E.wait("scalar", sv)
                    E.op("scalar", lambda e: e.activation(SM2[:, 32:96], SM2[:, 32:96], AF.Relu),
                         ss, 1, r=(SM2,), w=(SM2,))
                    E.wait("vector", ss)
                    E.op("vector", lambda e, T=T88: e.scalar_tensor_tensor(
                        T, T, -1.0, SM2[:, 96:97].unsqueeze(1).broadcast_to([128, 8, 8]),
                        ALU.mult, ALU.add), sv, 1, r=(SM2,), w=(SM2,))
                    E.wait("scalar", sv)
                    E.op("scalar", lambda e: e.activation(SM2[:, 32:96], SM2[:, 32:96], AF.Relu),
                         ss, 1, r=(SM2,), w=(SM2,))
                    E.wait("vector", ss)
                    E.op("vector", lambda e, T=T88: e.reduce_sum(SM2[:, 104:112], T, AX.X),
                         r=(SM2,), w=(SM2,))
                    E.op("vector", lambda e: e.tensor_scalar(
                        SM2[:, 104:112], SM2[:, 104:112], SM2[:, 97:98], None, ALU.add), sv, 1,
                        r=(SM2,), w=(SM2,))
                    E.wait("scalar", sv)
                    E.op("scalar", lambda e: e.activation(SM2[:, 104:112], SM2[:, 104:112], AF.Relu),
                         ss, 1, r=(SM2,), w=(SM2,))
                    E.wait("vector", ss)
                    E.op("vector", lambda e: e.scalar_tensor_tensor(
                        SM2[:, 104:112], SM2[:, 104:112], -1.0,
                        SM2[:, 96:97].broadcast_to([128, 8]), ALU.mult, ALU.add), sv, 1,
                        r=(SM2,), w=(SM2,))
                    E.wait("scalar", sv)
                    E.op("scalar", lambda e: e.activation(SM2[:, 104:112], SM2[:, 104:112], AF.Relu),
                         ss, 1, r=(SM2,), w=(SM2,))
                    E.wait("vector", ss)
                    E.op("vector", lambda e: e.scalar_tensor_tensor(
                        SM2[:, 104:112], SM2[:, 104:112], -1.0,
                        SM2[:, 96:97].broadcast_to([128, 8]), ALU.mult, ALU.add),
                        r=(SM2,), w=(SM2,))
                    E.op("vector", lambda e, t=ts_: e.tensor_tensor(
                        CFA[:, t, :], SM2[:, 16:24], SM2[:, 104:112], ALU.mult), sv, 1,
                        r=(SM2,), w=(CFA,))
                    trans2(HS2[:], HST[:], 8, HST)
                    E.wait("tensor", sv)
                    for j in range(8):
                        for hb in range(2):
                            E.mm(PB[:, 4 + hb, :], HST[:, 128 * j:128 * (j + 1)],
                                 SH[:, 16 + j, 512 * hb:512 * (hb + 1)],
                                 j == 0, j == 7, st if j == 7 and hb == 1 else None, 1)
                    E.wait("vector", st)
                    E.op("vector", lambda e, t=ts_: e.tensor_copy(ACC[:, t, 0:512], PB[:, 4, :]), w=(ACC,))
                    E.op("vector", lambda e, t=ts_: e.tensor_copy(ACC[:, t, 512:1024], PB[:, 5, :]), sv, 1, w=(ACC,))

                sv_hcons = None
                for ei in range(8):
                    par = ei % 2
                    if ei < 7 or sc + 1 < nsc:
                        E.wait("gpsimd", sec)
                        eload((ei + 1) % 8, 1 - par)
                    E.wait("tensor", sep[par])
                    for t in tl:
                        ts_ = t - sc * 8
                        E.wait("tensor", ss)
                        E.wait("tensor", sv, sv_hcons)
                        for j in range(8):
                            lhsT = LN1F[:, ts_, 128 * j:128 * (j + 1)]
                            E.mm(PB[:, 0, :], lhsT, EWb[par][:, 512 * j:512 * (j + 1)],
                                 j == 0, j == 7)
                            E.mm(PB[:, 1, :], lhsT, EWb[par][:, 4096 + 512 * j:4096 + 512 * (j + 1)],
                                 j == 0, j == 7, st if j == 7 else None, 1)
                        E.wait("scalar", st)
                        E.op("scalar", lambda e: e.activation(SGb[:, 0:512], PB[:, 0, :], AF.Silu),
                             ss, 1, w=(SGb,))
                        E.wait("vector", ss)
                        E.op("vector", lambda e: e.tensor_tensor(
                            HE[:], PB[:, 1, :], SGb[:, 0:512], ALU.mult), r=(SGb,), w=(HE,))
                        E.op("vector", lambda e, t=ts_, ei=ei: e.tensor_scalar(
                            HE[:], HE[:], CFA[:, t, ei:ei + 1], None, ALU.mult), sv, 1,
                            r=(HE, CFA), w=(HE,))
                        sv_hcons = sv.n
                        trans2(HE[:], HET[:], 4, HET)
                        E.wait("tensor", sv)
                        for jd in range(4):
                            for hb in range(2):
                                E.mm(PB[:, 2 + hb, :], HET[:, 128 * jd:128 * (jd + 1)],
                                     EWb[par][:, 8192 + 1024 * jd + 512 * hb:8192 + 1024 * jd + 512 * (hb + 1)],
                                     jd == 0, jd == 3, st if jd == 3 and hb == 1 else None, 1)
                        E.wait("vector", st)
                        E.op("vector", lambda e, t=ts_: e.scalar_tensor_tensor(
                            ACC[:, t, 0:512], PB[:, 2, :], 1.0, ACC[:, t, 0:512],
                            ALU.mult, ALU.add), r=(ACC,), w=(ACC,))
                        E.op("vector", lambda e, t=ts_: e.scalar_tensor_tensor(
                            ACC[:, t, 512:1024], PB[:, 3, :], 1.0, ACC[:, t, 512:1024],
                            ALU.mult, ALU.add), sv, 1, r=(ACC,), w=(ACC,))
                        if t == tl[-1]:
                            E.op("vector", lambda e: e.memzero(SM2[:, 98:99]),
                                 sec, 1, w=(SM2,))

                for t in tl:
                    ts_ = t - sc * 8
                    par = t % 2
                    E.wait("sync", sv)
                    E.op("sync", lambda e, t=t, par=par: e.dma_start(
                        XT1[:, par, :], d_x1[128 * t:128 * (t + 1), :]), sxq[par], 16)
                    E.wait("vector", sxq[par])
                    E.wait("vector", sd)
                    E.op("vector", lambda e, t=ts_, par=par: e.tensor_add(
                        X2T[:], ACC[:, t, :], XT1[:, par, :]), sv, 1,
                        r=(ACC,), w=(X2T,))
                    E.wait("scalar", sv)
                    emit_ln2(X2T[:], LN2T[:])
                    ln2_ss = ss.n
                    E.wait("sync", sv)
                    E.op("sync", lambda e, t=t: e.dma_start(
                        d_x2[128 * t:128 * (t + 1), :], X2T[:]), sd, 16)
                    if y_x2 is not None:
                        E.op("sync", lambda e, t=t: e.dma_start(
                            y_x2[128 * t:128 * (t + 1), :], X2T[:]), sd, 16)
                    E.wait("sync", ss, ln2_ss)
                    E.op("sync", lambda e, t=t: e.dma_start(
                        d_ln2[128 * t:128 * (t + 1), :], LN2T[:]), sd, 16)

            with nc.named_scope("s2"):
                P.run_block()
            E.barrier_reset()

        if max_phase >= 12:
          with ExitStack() as c3:
            en = c3.enter_context
            W3 = en(nc.sbuf_tensor("W3", [128, 26, 1024], BF16))
            MK3 = en(nc.sbuf_tensor("MK3", [128, 4, 128], BF16))
            GMB = en(nc.sbuf_tensor("GMB", [128, 1024], F32))
            BTB = en(nc.sbuf_tensor("BTB", [128, 1024], F32))
            RPT = en(nc.sbuf_tensor("RPT", [128, 2, 1024], BF16))
            LTK3 = en(nc.sbuf_tensor("LTK3", [128, 2, 1024], BF16))
            XTK3 = en(nc.sbuf_tensor("XTK3", [128, 2, 1024], F32))
            LN2F = en(nc.sbuf_tensor("LN2F", [128, 1024], BF16))
            QT3 = en(nc.sbuf_tensor("QT3", [128, 1024], BF16))
            KT3 = en(nc.sbuf_tensor("KT3", [128, 1024], BF16))
            VT3 = en(nc.sbuf_tensor("VT3", [128, 1024], BF16))
            QR3 = en(nc.sbuf_tensor("QR3", [128, 1024], BF16))
            KR3 = en(nc.sbuf_tensor("KR3", [128, 1024], BF16))
            RS3 = en(nc.sbuf_tensor("RS3", [128, 1024], F32))
            QF3b = en(nc.sbuf_tensor("QF3b", [128, 1024], BF16))
            QRF3 = en(nc.sbuf_tensor("QRF3", [128, 1024], BF16))
            KVP = en(nc.sbuf_tensor("KVP", [128, 520], F32))
            KVQ = en(nc.sbuf_tensor("KVQ", [128, 520], F32))
            KVSh3 = en(nc.sbuf_tensor("KVSh3", [128, 512], BF16))
            KMH = en(nc.sbuf_tensor("KMH", [128, 8, 16], BF16))
            ONES3 = en(nc.sbuf_tensor("ONES3", [128, 1], BF16))
            SL3 = en(nc.sbuf_tensor("SL3", [1, 1], U32))
            FL3 = en(nc.sbuf_tensor("FL3", [1, 16], U32))
            FLo = en(nc.sbuf_tensor("FLo", [1, 16], U32))
            RES3 = en(nc.sbuf_tensor("RES3", [128, 1024], F32))
            LNO3 = en(nc.sbuf_tensor("LNO3", [128, 1024], F32))
            X3T = en(nc.sbuf_tensor("X3T", [128, 1024], F32))
            LN3T = en(nc.sbuf_tensor("LN3T", [128, 1024], BF16))
            SM3 = en(nc.sbuf_tensor("SM3", [128, 128], F32))
            SQ3 = en(nc.sbuf_tensor("SQ3", [128, 1024], F32))
            PC = en(nc.psum_tensor("PC", [128, 7, 512], F32))
            PH3 = en(nc.psum_tensor("PH3", [128, 1024], BF16))

            for c0 in range(0, 26, 4):
                c1_ = min(26, c0 + 4)
                E.op("gpsimd", lambda e, c0=c0, c1_=c1_: e.dma_start(
                    W3[:, c0:c1_, :], w3T[:, c0:c1_, :]), sw, 16)
            for i, sl_ in enumerate((1, 7, 8, 9)):
                E.op("gpsimd", lambda e, i=i, sl_=sl_: e.dma_start(
                    MK3[:, i, :], masks[:, sl_, :]), sw, 16)
            E.op("gpsimd", lambda e: e.dma_start(SL3[:], slot[:]), sw, 16)
            E.op("gpsimd", lambda e: e.memset(FLo[:], 7))
            E.op("gpsimd", lambda e: e.memset(ONES3[:], 1.0))
            E.op("gpsimd", lambda e: e.memset(KVP[:], 0.0))
            E.op("gpsimd", lambda e: e.memset(KVQ[:], 0.0))
            E.op("gpsimd", lambda e: e.memset(KMH[:], 0.0), sg, 1)
            E.wait("tensor", sw)
            E.wait("tensor", sg)
            for hb in range(2):
                E.mm(PC[:, hb, :], MK3[:, 1, :], W3[:, 25, 512 * hb:512 * (hb + 1)],
                     True, True)
                E.mm(PC[:, 2 + hb, :], MK3[:, 2, :], W3[:, 25, 512 * hb:512 * (hb + 1)],
                     True, True, st if hb == 1 else None, 1)
            E.wait("vector", st)
            for hb in range(2):
                E.op("vector", lambda e, hb=hb: e.tensor_copy(
                    GMB[:, 512 * hb:512 * (hb + 1)], PC[:, hb, :]), w=(GMB,))
                E.op("vector", lambda e, hb=hb: e.tensor_copy(
                    BTB[:, 512 * hb:512 * (hb + 1)], PC[:, 2 + hb, :]), sv, 1, w=(BTB,))
            E.wait("sync", sd)

            def trans3(src_bf, dst_ap, nchunk, dst_name, wait_sv=False):
                E.wait("tensor", stf)
                if wait_sv:
                    E.wait("tensor", sv)
                E.wait("tensor", ss)
                for j in range(nchunk):
                    E.trans(PH3[:, 128 * j:128 * (j + 1)],
                            src_bf[:, 128 * j:128 * (j + 1)], MK3[:, 0, :],
                            st if j == nchunk - 1 else None, 1)
                E.wait("vector", st)
                E.op("vector", lambda e, d=dst_ap, n=nchunk: e.tensor_copy(
                    d, PH3[:, 0:128 * n]), stf, 1, w=(dst_name,))

            def emit_ln3(xsrc, out_t, eps):
                E.op("vector", lambda e, x=xsrc: e.reduce_sum(SM3[:, 0:1], x, AX.X), sv, 1,
                     r=(xsrc,), w=(SM3,))
                E.op("scalar", lambda e, x=xsrc: e.activation(
                    SQ3[:], x, AF.Square, accum_out=SM3[:, 1:2]), ss, 1,
                    r=(xsrc,), w=(SQ3, SM3))
                E.wait("vector", ss)
                E.op("vector", lambda e: e.tensor_scalar(
                    SM3[:, 2:3], SM3[:, 0:1], 1.0 / D, None, ALU.mult), r=(SM3,), w=(SM3,))
                E.op("vector", lambda e: e.tensor_mul(SM3[:, 3:4], SM3[:, 2:3], SM3[:, 2:3]), r=(SM3,), w=(SM3,))
                E.op("vector", lambda e: e.scalar_tensor_tensor(
                    SM3[:, 4:5], SM3[:, 1:2], 1.0 / D, SM3[:, 3:4], ALU.mult, ALU.subtract), r=(SM3,), w=(SM3,))
                E.op("vector", lambda e, p=eps: e.tensor_scalar(
                    SM3[:, 4:5], SM3[:, 4:5], p, None, ALU.add), sv, 1, r=(SM3,), w=(SM3,))
                E.wait("scalar", sv)
                E.op("scalar", lambda e: e.sqrt(SM3[:, 5:6], SM3[:, 4:5]), ss, 1, r=(SM3,), w=(SM3,))
                E.wait("vector", ss)
                E.op("vector", lambda e: e.reciprocal(SM3[:, 6:7], SM3[:, 5:6]), r=(SM3,), w=(SM3,))
                E.op("vector", lambda e: e.tensor_mul(SM3[:, 7:8], SM3[:, 2:3], SM3[:, 6:7]), r=(SM3,), w=(SM3,))
                E.op("vector", lambda e: e.tensor_scalar(
                    SM3[:, 7:8], SM3[:, 7:8], -1.0, None, ALU.mult), sv, 1, r=(SM3,), w=(SM3,))
                E.wait("scalar", sv)
                E.op("scalar", lambda e, o=out_t, x=xsrc: e.activation(
                    o, x, AF.Identity, bias=SM3[:, 7:8], scale=SM3[:, 6:7]), ss, 1,
                    r=(xsrc, SM3), w=(out_t,))

            def emit_rope(src_bf, dst_bf, par):
                xr = src_bf.rearrange("p (a b) -> p a b", a=512)
                dr = dst_bf.rearrange("p (a b) -> p a b", a=512)
                cos3 = RPT[:, par, 0:512].unsqueeze(2)
                sin3 = RPT[:, par, 512:1024].unsqueeze(2)
                t1 = RS3[:, 0:512].rearrange("p (a b) -> p a b", a=512)
                t2 = RS3[:, 512:1024].rearrange("p (a b) -> p a b", a=512)
                E.op("vector", lambda e: e.tensor_tensor(
                    t1, xr[:, :, 0:1], cos3, ALU.mult), r=(src_bf, RPT), w=(RS3,))
                E.op("vector", lambda e: e.tensor_tensor(
                    t2, xr[:, :, 1:2], sin3, ALU.mult), r=(src_bf, RPT), w=(RS3,))
                E.op("vector", lambda e: e.scalar_tensor_tensor(
                    dr[:, :, 0:1], t2, -1.0, t1, ALU.mult, ALU.add),
                    r=(RS3,), w=(dst_bf,))
                E.op("vector", lambda e: e.tensor_tensor(
                    t1, xr[:, :, 1:2], cos3, ALU.mult), r=(src_bf, RPT), w=(RS3,))
                E.op("vector", lambda e: e.tensor_tensor(
                    t2, xr[:, :, 0:1], sin3, ALU.mult), r=(src_bf, RPT), w=(RS3,))
                E.op("vector", lambda e: e.tensor_add(
                    dr[:, :, 1:2], t1, t2), sv, 1, r=(RS3,), w=(dst_bf,))

            sv_kcons = None
            for t in range(nt):
                par = t % 2
                E.wait("sync", st)
                E.wait("sync", sv)
                E.op("sync", lambda e, t=t, par=par: e.dma_start(
                    LTK3[:, par, :], d_ln2[128 * t:128 * (t + 1), :]), sxp[par], 16)
                E.op("sync", lambda e, t=t, par=par: e.dma_start(
                    RPT[:, par, :], d_rope[128 * t:128 * (t + 1), :]), sxp[par], 16)
                E.wait("tensor", sxp[par])
                trans3(LTK3[:, par, :], LN2F[:], 8, LN2F)
                E.wait("tensor", stf)
                E.wait("tensor", sv, sv_kcons)
                E.wait("tensor", ss)
                for j in range(8):
                    lhsT = LN2F[:, 128 * j:128 * (j + 1)]
                    for hb in range(2):
                        E.mm(PC[:, hb, :], lhsT, W3[:, 8 + j, 512 * hb:512 * (hb + 1)],
                             j == 0, False)
                    for hb in range(2):
                        E.mm(PC[:, 2 + hb, :], lhsT, W3[:, 16 + j, 512 * hb:512 * (hb + 1)],
                             j == 0, False)
                for hb in range(2):
                    E.mm(PC[:, hb, :], MK3[:, 2, :], W3[:, 24, 512 * hb:512 * (hb + 1)],
                         False, True)
                    E.mm(PC[:, 2 + hb, :], MK3[:, 3, :], W3[:, 24, 512 * hb:512 * (hb + 1)],
                         False, True, st if hb == 1 else None, 1)
                E.wait("scalar", st)
                E.op("scalar", lambda e: e.activation(KT3[:, 0:512], PC[:, 0, :], AF.Gelu), w=(KT3,))
                E.op("scalar", lambda e: e.activation(KT3[:, 512:1024], PC[:, 1, :], AF.Gelu), ss, 1, w=(KT3,))
                E.wait("vector", ss)
                E.op("vector", lambda e: e.tensor_scalar(
                    KT3[:], KT3[:], 0.21, None, ALU.add), r=(KT3,), w=(KT3,))
                E.op("vector", lambda e: e.tensor_copy(VT3[:, 0:512], PC[:, 2, :]), w=(VT3,))
                E.op("vector", lambda e: e.tensor_copy(VT3[:, 512:1024], PC[:, 3, :]), sv, 1, w=(VT3,))
                sv_kcons = sv.n
                emit_rope(KT3[:], KR3[:], par)
                E.wait("tensor", sv)
                for j in range(8):
                    E.mm(PC[:, 4 + j // 4, (j % 4) * 128:(j % 4) * 128 + 128],
                         KR3[:, 128 * j:128 * (j + 1)], VT3[:, 128 * j:128 * (j + 1)],
                         j % 4 == 0, j % 4 == 3, None, 1)
                    E.mm(PC[:, 6, j:j + 1], KT3[:, 128 * j:128 * (j + 1)], ONES3[:],
                         True, True, st if j == 7 else None, 1)
                E.wait("vector", st)
                for half in range(2):
                    pr = slice(64 * half, 64 * (half + 1))
                    co = slice(64 * half, 64 * half + 64)
                    E.op("vector", lambda e, pr=pr, co=co: e.tensor_add(
                        KVP[pr, 0:512].rearrange("p (a b) -> p a b", a=8),
                        KVP[pr, 0:512].rearrange("p (a b) -> p a b", a=8),
                        PC[pr, 4:6, :].rearrange("p a (b c) -> p (a b) c", b=4)[:, :, co]),
                        r=(KVP,), w=(KVP,))
                E.op("vector", lambda e: e.tensor_add(
                    KVP[:, 512:520], KVP[:, 512:520], PC[:, 6, 0:8]), sv, 1,
                    r=(KVP,), w=(KVP,))

            if not solo:
                E.wait("gpsimd", sv)

                def exch(g, base=sg.n, spl=spl):
                    with g.register("sl3") as rsl:
                        g.reg_load(rsl, SL3[0:1, 0:1].bitcast(I32))
                        for me in (0, 1):
                            sib = 1 - me
                            with g.If_eq(rsl, me):
                                g.dma_start(d_ex[me, :, :], KVP[:]).then_inc(sg.h, 16)
                                g.wait_ge(sg.h, base + 16)
                                g.dma_start(d_exf[me:me + 1, :], FLo[:]).then_inc(sg.h, 16)
                                g.wait_ge(sg.h, base + 32)
                                with g.register(f"fl3{me}") as rfl:
                                    g.reg_mov(rfl, 0)
                                    with g.Fori(0, 4000) as _i:
                                        with g.If_ne(rfl, 7):
                                            g.dma_start(FL3[:], d_exf[sib:sib + 1, :]).then_inc(spl.h, 16)
                                            g.drain()
                                            g.reg_load(rfl, FL3[0:1, 0:1].bitcast(I32))
                                g.dma_start(KVQ[:], d_ex[sib, :, :]).then_inc(sg.h, 16)
                                g.wait_ge(sg.h, base + 48)
                            g.end_ifs()
                E.op("gpsimd", exch, None, 0, r=(KVP,), w=(KVQ,))
                sg.n += 48
            E.wait("vector", sg)
            E.op("vector", lambda e: e.tensor_add(RS3[:, 0:512], KVP[:, 0:512], KVQ[:, 0:512]),
                 r=(KVP, KVQ), w=(RS3,))
            E.op("vector", lambda e: e.tensor_scalar(
                RS3[:, 0:512], RS3[:, 0:512], 1.0 / 8192.0, None, ALU.mult), r=(RS3,), w=(RS3,))
            E.op("vector", lambda e: e.tensor_copy(KVSh3[:], RS3[:, 0:512]), r=(RS3,), w=(KVSh3,))
            E.op("vector", lambda e: e.tensor_add(SM3[:, 8:16], KVP[:, 512:520], KVQ[:, 512:520]),
                 r=(KVP, KVQ), w=(SM3,))
            E.op("vector", lambda e: e.tensor_scalar(
                SM3[:, 8:16], SM3[:, 8:16], 1.0 / 8192.0, None, ALU.mult), r=(SM3,), w=(SM3,))
            for j in range(8):
                E.op("vector", lambda e, j=j: e.tensor_copy(
                    KMH[0:64, j, 2 * j:2 * j + 1], SM3[0:64, 8 + j:9 + j]), r=(SM3,), w=(KMH,))
                E.op("vector", lambda e, j=j: e.tensor_copy(
                    KMH[64:128, j, 2 * j + 1:2 * j + 2], SM3[64:128, 8 + j:9 + j]),
                    sv if j == 7 else None, 1, r=(SM3,), w=(KMH,))

            for t in range(nt):
                par = t % 2
                E.wait("sync", st)
                E.wait("sync", sv)
                E.op("sync", lambda e, t=t, par=par: e.dma_start(
                    LTK3[:, par, :], d_ln2[128 * t:128 * (t + 1), :]), sxp[par], 16)
                E.op("sync", lambda e, t=t, par=par: e.dma_start(
                    RPT[:, par, :], d_rope[128 * t:128 * (t + 1), :]), sxp[par], 16)
                E.wait("sync", sv)
                E.op("sync", lambda e, t=t, par=par: e.dma_start(
                    XTK3[:, par, :], d_x2[128 * t:128 * (t + 1), :]), sxq[par], 16)
                E.wait("tensor", sxp[par])
                trans3(LTK3[:, par, :], LN2F[:], 8, LN2F)
                E.wait("tensor", stf)
                E.wait("tensor", ss)
                for j in range(8):
                    lhsT = LN2F[:, 128 * j:128 * (j + 1)]
                    for hb in range(2):
                        E.mm(PC[:, hb, :], lhsT, W3[:, j, 512 * hb:512 * (hb + 1)],
                             j == 0, False)
                for hb in range(2):
                    E.mm(PC[:, hb, :], MK3[:, 1, :], W3[:, 24, 512 * hb:512 * (hb + 1)],
                         False, True, st if hb == 1 else None, 1)
                E.wait("scalar", st)
                E.op("scalar", lambda e: e.activation(QT3[:, 0:512], PC[:, 0, :], AF.Gelu), w=(QT3,))
                E.op("scalar", lambda e: e.activation(QT3[:, 512:1024], PC[:, 1, :], AF.Gelu), ss, 1, w=(QT3,))
                E.wait("vector", ss)
                E.op("vector", lambda e: e.tensor_scalar(
                    QT3[:], QT3[:], 0.21, None, ALU.add), sv, 1, r=(QT3,), w=(QT3,))
                emit_rope(QT3[:], QR3[:], par)
                trans3(QT3[:], QF3b[:], 8, QF3b, wait_sv=True)
                trans3(QR3[:], QRF3[:], 8, QRF3, wait_sv=True)
                E.wait("tensor", stf)
                E.wait("tensor", sv)
                for j in range(8):
                    E.mm(PC[:, 6, 0:16], QF3b[:, 128 * j:128 * (j + 1)], KMH[:, j, :],
                         j == 0, j == 7)
                for h in range(0, 16, 2):
                    E.mm(PC[:, 2, 64 * (h // 2):64 * (h // 2) + 64],
                         hs(QRF3, h), KVSh3[0:64, 64 * (h // 2):64 * (h // 2) + 64],
                         True, True, st if h == 14 else None, 1, tp=(0, 0))
                for h in range(1, 16, 2):
                    E.mm(PC[:, 3, 64 * (h // 2):64 * (h // 2) + 64],
                         hs(QRF3, h), KVSh3[64:128, 64 * (h // 2):64 * (h // 2) + 64],
                         True, True, st if h == 15 else None, 1, tp=(64, 0))
                E.wait("vector", st)
                E.op("vector", lambda e: e.tensor_scalar(
                    SM3[:, 16:32], PC[:, 6, 0:16], 1e-6, None, ALU.add), w=(SM3,))
                E.op("vector", lambda e: e.reciprocal(SM3[:, 16:32], SM3[:, 16:32]),
                     r=(SM3,), w=(SM3,))
                zb = SM3[:, 16:32].rearrange("p (a b) -> p a b", a=8) \
                    .unsqueeze(3).broadcast_to([128, 8, 2, 64])
                E.op("vector", lambda e, zb=zb: e.tensor_tensor(
                    RES3[:].rearrange("p (a b c) -> p a b c", a=8, b=2),
                    PC[:, 2:4, :].rearrange("p a (b c) -> p b a c", b=8),
                    zb, ALU.mult), r=(SM3,), w=(RES3,))
                E.op("vector", lambda e, par=par: e.tensor_add(
                    RES3[:], RES3[:], LTK3[:, par, :]), sv, 1, r=(RES3,), w=(RES3,))
                E.wait("scalar", sv)
                emit_ln3(RES3[:], LNO3[:], 1e-12)
                E.wait("vector", ss)
                E.op("vector", lambda e: e.tensor_tensor(RS3[:], LNO3[:], GMB[:], ALU.mult),
                     r=(LNO3, GMB), w=(RS3,))
                E.op("vector", lambda e: e.tensor_add(RS3[:], RS3[:], BTB[:]),
                     r=(RS3, BTB), w=(RS3,))
                E.wait("vector", sxq[par])
                E.wait("vector", sd)
                E.op("vector", lambda e, par=par: e.tensor_add(
                    X3T[:], RS3[:], XTK3[:, par, :]), sv, 1, r=(RS3,), w=(X3T,))
                E.wait("scalar", sv)
                emit_ln3(X3T[:], LN3T[:], 1e-5)
                ln3_ss = ss.n
                E.wait("sync", sv)
                E.op("sync", lambda e, t=t: e.dma_start(
                    d_x3[128 * t:128 * (t + 1), :], X3T[:]), sd, 16)
                if y_x3 is not None:
                    E.op("sync", lambda e, t=t: e.dma_start(
                        y_x3[128 * t:128 * (t + 1), :], X3T[:]), sd, 16)
                E.wait("sync", ss, ln3_ss)
                E.op("sync", lambda e, t=t: e.dma_start(
                    d_ln3[128 * t:128 * (t + 1), :], LN3T[:]), sd, 16)

            with nc.named_scope("s3"):
                P.run_block()
            E.barrier_reset()

        if max_phase >= 13:
          with ExitStack() as c4:
            en = c4.enter_context
            EA = en(nc.sbuf_tensor("EA4", [128, 17, 1024], BF16))
            EB = en(nc.sbuf_tensor("EB4", [128, 17, 1024], BF16))
            WR4s = en(nc.sbuf_tensor("WR4s", [128, 9, 8], BF16))
            MK4 = en(nc.sbuf_tensor("MK4", [128, 3, 128], BF16))
            LTK4 = en(nc.sbuf_tensor("LTK4", [128, 2, 1024], BF16))
            XTK4 = en(nc.sbuf_tensor("XTK4", [128, 2, 1024], F32))
            LN3F = en(nc.sbuf_tensor("LN3F", [128, 1024], BF16))
            HA4 = en(nc.sbuf_tensor("HA4", [128, 1024], BF16))
            HB4 = en(nc.sbuf_tensor("HB4", [128, 1024], BF16))
            HAF = en(nc.sbuf_tensor("HAF", [128, 1024], BF16))
            HBF = en(nc.sbuf_tensor("HBF", [128, 1024], BF16))
            X4T = en(nc.sbuf_tensor("X4T", [128, 1024], F32))
            SM4 = en(nc.sbuf_tensor("SM4", [128, 128], F32))
            SL4 = en(nc.sbuf_tensor("SL4", [1, 1], U32))
            FL4 = en(nc.sbuf_tensor("FL4", [1, 1], U32))
            FLo4 = en(nc.sbuf_tensor("FLo4", [1, 1], U32))
            PD = en(nc.psum_tensor("PD", [128, 7, 512], F32))
            PH4 = en(nc.psum_tensor("PH4", [128, 1024], BF16))

            E.op("gpsimd", lambda e: e.dma_start(WR4s[:], wr4[:]), sw, 16)
            E.op("gpsimd", lambda e: e.dma_start(MK4[:, 0, :], masks[:, 1, :]), sw, 16)
            E.op("gpsimd", lambda e: e.dma_start(MK4[:, 1, :], masks[:, 7, :]), sw, 16)
            E.op("gpsimd", lambda e: e.dma_start(MK4[:, 2, :], masks[:, 8, :]), sw, 16)
            E.op("gpsimd", lambda e: e.dma_start(SL4[:], slot[:]), sw, 16)
            E.op("gpsimd", lambda e: e.memset(FLo4[:], 9))
            E.op("gpsimd", lambda e: e.memset(SM4[:, 32:33], 1.0), sg, 1)
            E.wait("sync", sd)
            E.wait("tensor", sw)
            E.wait("tensor", sg)

            def trans4(src_bf, dst_ap, nchunk, dst_name):
                E.wait("tensor", stf)
                E.wait("tensor", ss)
                for j in range(nchunk):
                    E.trans(PH4[:, 128 * j:128 * (j + 1)],
                            src_bf[:, 128 * j:128 * (j + 1)], MK4[:, 0, :],
                            st if j == nchunk - 1 else None, 1)
                E.wait("vector", st)
                E.op("vector", lambda e, d=dst_ap, n=nchunk: e.tensor_copy(
                    d, PH4[:, 0:128 * n]), stf, 1, w=(dst_name,))

            E.op("sync", lambda e: e.dma_start(
                LTK4[:, 0, :], d_ln3[0:128, :]), sxp[0], 16)
            E.wait("tensor", sxp[0])
            trans4(LTK4[:, 0, :], LN3F[:], 8, LN3F)
            E.wait("tensor", stf)
            for j in range(8):
                E.mm(PD[:, 6, 0:8], LN3F[:, 128 * j:128 * (j + 1)], WR4s[:, j, :],
                     j == 0, False)
            E.mm(PD[:, 6, 0:8], MK4[:, 1, :], WR4s[:, 8, :], False, True, st, 1)
            E.wait("vector", st)
            E.op("vector", lambda e: e.tensor_copy(SM4[:, 0:8], PD[:, 6, 0:8]), sv, 1,
                 w=(SM4,))
            if not solo:
                E.wait("gpsimd", sv)

                def exch4(g, base=sg.n, spl=spl):
                    with g.register("sl4") as rsl:
                        g.reg_load(rsl, SL4[0:1, 0:1].bitcast(I32))
                        for me in (0, 1):
                            with g.If_eq(rsl, me):
                                g.dma_start(d_ex2[me, :, :], SM4[0:1, 0:8]).then_inc(sg.h, 16)
                                g.wait_ge(sg.h, base + 16)
                                g.dma_start(d_exf[me:me + 1, 4:5], FLo4[:]).then_inc(sg.h, 16)
                                g.wait_ge(sg.h, base + 32)
                            g.end_ifs()
                    with g.register("fl4") as rfl:
                        g.reg_mov(rfl, 0)
                        with g.Fori(0, 4000) as _i:
                            with g.If_ne(rfl, 9):
                                g.dma_start(FL4[:], d_exf[0:1, 4:5]).then_inc(spl.h, 16)
                                g.drain()
                                g.reg_load(rfl, FL4[0:1, 0:1].bitcast(I32))
                    g.dma_start(SM4[0:1, 0:8], d_ex2[0, :, :]).then_inc(sg.h, 16)
                    g.wait_ge(sg.h, base + 48)
                E.op("gpsimd", exch4, None, 0, r=(SM4,), w=(SM4,))
                sg.n += 48
                E.wait("vector", sg)
            p8b = SM4[:, 0:8]
            T884 = SM4[:, 40:104].rearrange("p (a b) -> p a b", a=8)
            E.op("vector", lambda e, T=T884: e.scalar_tensor_tensor(
                T, p8b.unsqueeze(2).broadcast_to([128, 8, 8]), -1.0,
                p8b.unsqueeze(1).broadcast_to([128, 8, 8]),
                ALU.mult, ALU.add), r=(SM4,), w=(SM4,))
            E.op("vector", lambda e, T=T884: e.scalar_tensor_tensor(
                T, T, 1e8, SM4[:, 32:33].unsqueeze(1).broadcast_to([128, 8, 8]),
                ALU.mult, ALU.add), sv, 1, r=(SM4,), w=(SM4,))
            E.wait("scalar", sv)
            E.op("scalar", lambda e: e.activation(SM4[:, 40:104], SM4[:, 40:104], AF.Relu),
                 ss, 1, r=(SM4,), w=(SM4,))
            E.wait("vector", ss)
            E.op("vector", lambda e, T=T884: e.scalar_tensor_tensor(
                T, T, -1.0, SM4[:, 32:33].unsqueeze(1).broadcast_to([128, 8, 8]),
                ALU.mult, ALU.add), sv, 1, r=(SM4,), w=(SM4,))
            E.wait("scalar", sv)
            E.op("scalar", lambda e: e.activation(SM4[:, 40:104], SM4[:, 40:104], AF.Relu),
                 ss, 1, r=(SM4,), w=(SM4,))
            E.wait("vector", ss)
            E.op("vector", lambda e, T=T884: e.reduce_sum(SM4[:, 8:16], T, AX.X),
                 r=(SM4,), w=(SM4,))
            E.op("vector", lambda e: e.tensor_scalar(
                SM4[:, 16:24], SM4[:, 8:16], -5.0, None, ALU.add), r=(SM4,), w=(SM4,))
            E.op("vector", lambda e: e.tensor_scalar(
                SM4[:, 24:32], SM4[:, 8:16], -6.0, None, ALU.add), sv, 1, r=(SM4,), w=(SM4,))
            E.wait("scalar", sv)
            E.op("scalar", lambda e: e.activation(SM4[:, 16:32], SM4[:, 16:32], AF.Relu),
                 ss, 1, r=(SM4,), w=(SM4,))
            E.wait("vector", ss)
            E.op("vector", lambda e: e.scalar_tensor_tensor(
                SM4[:, 16:24], SM4[:, 16:24], -1.0,
                SM4[:, 32:33].broadcast_to([128, 8]), ALU.mult, ALU.add), sv, 1,
                r=(SM4,), w=(SM4,))
            E.wait("scalar", sv)
            E.op("scalar", lambda e: e.activation(SM4[:, 16:24], SM4[:, 16:24], AF.Relu),
                 ss, 1, r=(SM4,), w=(SM4,))
            E.wait("vector", ss)
            E.op("vector", lambda e: e.scalar_tensor_tensor(
                SM4[:, 16:24], SM4[:, 16:24], -1.0,
                SM4[:, 32:33].broadcast_to([128, 8]), ALU.mult, ALU.add),
                r=(SM4,), w=(SM4,))
            E.op("vector", lambda e: e.scalar_tensor_tensor(
                SM4[:, 24:32], SM4[:, 24:32], -1.0, SM4[:, 16:24],
                ALU.mult, ALU.add), sv, 1, r=(SM4,), w=(SM4,))
            E.op("vector", lambda e: e.scalar_tensor_tensor(
                SM4[:, 8:16], SM4[:, 24:32], -1.0, SM4[:, 16:24],
                ALU.mult, ALU.add), sv, 1, r=(SM4,), w=(SM4,))
            E.wait("gpsimd", sv)

            def gather_emit(g, base=sgw.n):
                for cols, buf in ((8, EA), (24, EB)):
                    with g.register(f"rk{cols}") as rk:
                        for e in range(8):
                            g.reg_load(rk, SM4[0:1, cols + e:cols + e + 1].bitcast(I32))
                            with g.If_eq(rk, 0x3F800000):
                                for c0 in range(0, 17, 3):
                                    c1_ = min(17, c0 + 3)
                                    g.dma_start(buf[:, c0:c1_, :],
                                                ew4[e, :, c0:c1_, :]).then_inc(sgw.h, 16)
                            g.end_ifs()
                g.wait_ge(sgw.h, base + 2 * 6 * 16)
            E.op("gpsimd", gather_emit, None, 0, r=(SM4,), w=(EA, EB))
            sgw.n += 2 * 6 * 16
            E.wait("tensor", sgw)

            for t in range(nt):
                par = t % 2
                E.wait("sync", st)
                E.wait("sync", sv)
                E.op("sync", lambda e, t=t, par=par: e.dma_start(
                    LTK4[:, par, :], d_ln3[128 * t:128 * (t + 1), :]), sxp[par], 16)
                E.op("sync", lambda e, t=t, par=par: e.dma_start(
                    XTK4[:, par, :], d_x3[128 * t:128 * (t + 1), :]), sxq[par], 16)
                E.wait("tensor", sxp[par])
                trans4(LTK4[:, par, :], LN3F[:], 8, LN3F)
                E.wait("tensor", stf)
                E.wait("tensor", ss)
                for j in range(8):
                    lhsT = LN3F[:, 128 * j:128 * (j + 1)]
                    for hb in range(2):
                        E.mm(PD[:, hb, :], lhsT, EA[:, j, 512 * hb:512 * (hb + 1)],
                             j == 0, False)
                        E.mm(PD[:, 2 + hb, :], lhsT, EB[:, j, 512 * hb:512 * (hb + 1)],
                             j == 0, False)
                for hb in range(2):
                    E.mm(PD[:, hb, :], MK4[:, 1, :], EA[:, 16, 512 * hb:512 * (hb + 1)],
                         False, True)
                    E.mm(PD[:, 2 + hb, :], MK4[:, 1, :], EB[:, 16, 512 * hb:512 * (hb + 1)],
                         False, True, st if hb == 1 else None, 1)
                E.wait("scalar", st)
                E.op("scalar", lambda e: e.activation(HA4[:, 0:512], PD[:, 0, :], AF.Gelu), w=(HA4,))
                E.op("scalar", lambda e: e.activation(HA4[:, 512:1024], PD[:, 1, :], AF.Gelu), w=(HA4,))
                E.op("scalar", lambda e: e.activation(HB4[:, 0:512], PD[:, 2, :], AF.Gelu), w=(HB4,))
                E.op("scalar", lambda e: e.activation(HB4[:, 512:1024], PD[:, 3, :], AF.Gelu), ss, 1, w=(HB4,))
                trans4(HA4[:], HAF[:], 8, HAF)
                trans4(HB4[:], HBF[:], 8, HBF)
                E.wait("tensor", stf)
                E.wait("tensor", sv)
                for j in range(8):
                    for hb in range(2):
                        E.mm(PD[:, 4 + hb, :], HAF[:, 128 * j:128 * (j + 1)],
                             EA[:, 8 + j, 512 * hb:512 * (hb + 1)], j == 0, False)
                for j in range(8):
                    for hb in range(2):
                        E.mm(PD[:, 4 + hb, :], HBF[:, 128 * j:128 * (j + 1)],
                             EB[:, 8 + j, 512 * hb:512 * (hb + 1)], False, False)
                for hb in range(2):
                    E.mm(PD[:, 4 + hb, :], MK4[:, 2, :],
                         EA[:, 16, 512 * hb:512 * (hb + 1)], False, False)
                for hb in range(2):
                    E.mm(PD[:, 4 + hb, :], MK4[:, 2, :],
                         EB[:, 16, 512 * hb:512 * (hb + 1)], False, True,
                         st if hb == 1 else None, 1)
                E.wait("vector", st)
                E.wait("vector", sxq[par])
                E.wait("vector", sd)
                for hb in range(2):
                    E.op("vector", lambda e, hb=hb, par=par: e.scalar_tensor_tensor(
                        X4T[:, 512 * hb:512 * (hb + 1)], PD[:, 4 + hb, :], 0.5,
                        XTK4[:, par, 512 * hb:512 * (hb + 1)], ALU.mult, ALU.add),
                        sv if hb == 1 else None, 1, w=(X4T,))
                E.wait("sync", sv)
                E.op("sync", lambda e, t=t: e.dma_start(
                    y[128 * t:128 * (t + 1), :], X4T[:]), sd, 16)

            with nc.named_scope("s4"):
                P.run_block()
            E.barrier_reset()

        E.wait("sync", sd)
        if max_phase < 13:
            d_last = d_x3 if max_phase >= 12 else (d_x2 if max_phase >= 11 else d_x1)
            rows = min(HALF, nt * 128) if max_phase >= 10 else 0
            if rows:
                E.op("sync", lambda e, r=rows, s=d_last: e.dma_start(y[0:r, :], s[0:r, :]), sd, 16)
            if rows < HALF:
                E.op("sync", lambda e, r=rows: e.dma_start(
                    y[r:HALF, :], x_own[r:HALF, :]), sd, 16)
        E.wait("sync", sd)
        E.wait("vector", sw)
        E.wait("vector", sep[0])
        E.wait("vector", sep[1])
        E.wait("sync", sxp[0])
        E.wait("sync", sxp[1])
        E.wait("sync", sxq[0])
        E.wait("sync", sxq[1])
        with nc.named_scope("fin"):
            P.run_block()

    return nc



def _prep_core_inputs(c, I):
    b, hf = c // 4, c % 2
    x = np.asarray(I["L0_attn_x"])
    x_own = np.ascontiguousarray(x[b, hf * HALF:(hf + 1) * HALF, :], np.float32)
    x_pre = (np.ascontiguousarray(x[b, 0:HALF, :], np.float32) if hf == 1
             else np.zeros((HALF, D), np.float32))
    pre_gate = np.full((128, 1), 1.0 if hf == 1 else 0.0, np.float32)

    bias_chunk = np.zeros((128, 1, D), np.float32)
    bias_chunk[0, 0] = np.asarray(I["L0_attn_bg"])
    bias_chunk[32, 0] = np.asarray(I["L0_attn_bo"])
    bias_chunk[64, 0] = np.asarray(I["L0_attn_Wgo"]).reshape(-1)
    w5 = np.concatenate([_chunks_T(np.asarray(I[f"L0_attn_{n}"]))
                         for n in ("Wq", "Wk", "Wv", "Wkl", "Wvl")]
                        + [bias_chunk], axis=1)
    wg = _chunks_T(np.asarray(I["L0_attn_Wg"]))
    wo = _chunks_T(np.asarray(I["L0_attn_Wo"]))
    mk = _host_masks(hf == 1)
    misc = np.zeros((128, 1, 128), np.float32)
    misc[0, 0, 0:4] = np.asarray(I["L0_attn_haar_scale"])
    misc[0, 0, 4] = float(np.asarray(I["L0_attn_bgo"]).reshape(-1)[0])
    sel = np.zeros((128, 3, 128), np.float32)
    sel[0, 0, :] = 1.0
    sel[32, 1, :] = 1.0
    sel[64, 2, :] = 1.0
    mk = np.concatenate([mk, misc, sel], axis=1)
    bd = _host_bd(np.asarray(I["L0_attn_haar_Wk"]), np.asarray(I["L0_attn_haar_Wv"]))

    def _pack_expert(g, u, d):
        return np.concatenate([_chunks_N(g).reshape(128, -1),
                               _chunks_N(u).reshape(128, -1),
                               _chunks_N(d).reshape(128, -1)], 1)
    eg = np.asarray(I["L0_ffn_experts_gate"])
    eu = np.asarray(I["L0_ffn_experts_up"])
    ed = np.asarray(I["L0_ffn_experts_down"])
    ew = np.ascontiguousarray(np.stack(
        [_pack_expert(eg[e], eu[e], ed[e]) for e in range(8)]))
    shw = np.ascontiguousarray(np.concatenate(
        [_chunks_N(np.asarray(I["L0_ffn_shared_gate"])),
         _chunks_N(np.asarray(I["L0_ffn_shared_up"])),
         _chunks_N(np.asarray(I["L0_ffn_shared_down"]))], 1))
    rt8 = _chunks_N(np.asarray(I["L0_ffn_router"]))

    b3 = np.zeros((128, 1, D), np.float32)
    b3[0, 0] = np.asarray(I["L1_attn_bq"])
    b3[32, 0] = np.asarray(I["L1_attn_bk"])
    b3[64, 0] = np.asarray(I["L1_attn_bv"])
    gb = np.zeros((128, 1, D), np.float32)
    gb[0, 0] = np.asarray(I["L1_attn_gamma"])
    gb[32, 0] = np.asarray(I["L1_attn_beta"])
    w3 = np.concatenate(
        [_chunks_T(np.asarray(I[f"L1_attn_{n}"])) for n in ("Wq", "Wk", "Wv")]
        + [b3, gb], axis=1)
    g = hf * HALF + np.arange(HALF, dtype=np.float64)
    inv = 1.0 / (10000.0 ** (np.arange(512, dtype=np.float64) / 512.0))
    ang = g[:, None] * inv[None, :]
    rope = np.concatenate([np.cos(ang), np.sin(ang)], 1).astype(
        ml_dtypes.bfloat16)
    wr4 = np.concatenate([_chunks_N(np.asarray(I["L1_ffn_Wr"])),
                          np.zeros((128, 1, 8), np.float32)], 1)
    wr4[0, 8, :] = np.asarray(I["L1_ffn_br"])
    w1 = np.asarray(I["L1_ffn_W1"])
    w2 = np.asarray(I["L1_ffn_W2"])
    b1 = np.asarray(I["L1_ffn_b1"])
    b2 = np.asarray(I["L1_ffn_b2"])

    def _pack4(e):
        bc = np.zeros((128, 1, D), np.float32)
        bc[0, 0] = b1[e]
        bc[32, 0] = b2[e]
        return np.concatenate([_chunks_N(w1[e]), _chunks_N(w2[e]), bc], 1)
    ew4 = np.ascontiguousarray(np.stack([_pack4(e) for e in range(8)]))
    return {
        "x_own": x_own, "x_pre": x_pre, "pre_gate": pre_gate,
        "w5T": w5, "wgT": wg, "woT": wo,
        "bd_mats": bd, "at_mats": _host_at(), "masks": np.ascontiguousarray(mk),
        "rt8": rt8, "ew": ew, "shw": shw,
        "w3T": w3, "d_rope": np.ascontiguousarray(rope),
        "slot": np.array([[hf]], dtype=np.uint32),
        "wr4": np.ascontiguousarray(wr4), "ew4": ew4,
    }


def kernel(**inputs):
    nc = build_program()
    in_maps = [_prep_core_inputs(c, inputs) for c in range(NCORE)]
    res = None
    err = None
    for _att in range(3):
        try:
            res = run_bass_kernel_spmd(nc, in_maps, core_ids=list(range(NCORE)))
            break
        except Exception as e:
            err = e
    if res is None:
        raise err
    out = np.zeros((B, N, D), np.float32)
    for b in range(B):
        out[b, 0:HALF, :] = np.asarray(res.results[4 * b]["y"])
        out[b, HALF:N, :] = np.asarray(res.results[4 * b + 1]["y"])
    return out


# version 04
# speedup vs baseline: 1.0267x; 1.0078x over previous
import numpy as np
import ml_dtypes
from contextlib import ExitStack

import concourse.bass as bass
import concourse.mybir as mybir
from concourse.bass_utils import run_bass_kernel_spmd

F32 = mybir.dt.float32
BF16 = mybir.dt.bfloat16
U32 = mybir.dt.uint32
I32 = mybir.dt.int32
AF = mybir.ActivationFunctionType
ALU = mybir.AluOpType
AX = mybir.AxisListType

NCORE = 8
B, N, D = 2, 8192, 1024
H = 16
HALF = 4096
NT = HALF // 128
OUTT = 2048
NOT = OUTT // 128
LOCW = 64
EXPB = 20.0



def _chunks_T(w):
    dout, din = w.shape
    return np.ascontiguousarray(
        w.T.reshape(din // 128, 128, dout).transpose(1, 0, 2), dtype=np.float32)


def _chunks_N(w):
    din, dout = w.shape
    return np.ascontiguousarray(
        w.reshape(din // 128, 128, dout).transpose(1, 0, 2), dtype=np.float32)


def _host_masks(first_prev_real):
    j = np.arange(128)
    tri = (j[:, None] <= j[None, :]).astype(np.float32)
    i = np.arange(128)[None, :]
    jj = np.arange(128)[:, None]
    prev = ((i - jj + 128 >= 0) & (i - jj + 128 <= LOCW - 1)).astype(np.float32)
    cur = ((i - jj >= 0) & (i - jj <= LOCW - 1)).astype(np.float32)
    ident = np.eye(128, dtype=np.float32)
    lpf = prev if first_prev_real else np.zeros_like(prev)
    return np.ascontiguousarray(np.stack([tri, ident, lpf, cur, prev, cur], 1))


def _host_at():
    out = np.zeros((128, 4, 128), np.float32)
    t = np.arange(128)
    for lv in range(4):
        blk = 2 << lv
        A = ((t[:, None] // blk == t[None, :] // blk)
             & (t[None, :] <= t[:, None])).astype(np.float32)
        A /= (t % blk + 1)[:, None]
        out[:, lv, :] = A.T
    return out


def _host_bd(haar_wk, haar_wv):
    out = np.zeros((128, 8, 128), np.float32)
    for lv in range(4):
        out[:64, lv, :64] = haar_wk[lv].T
        out[64:, lv, 64:] = haar_wk[lv].T
        out[:64, 4 + lv, :64] = haar_wv[lv].T
        out[64:, 4 + lv, 64:] = haar_wv[lv].T
    return out



ENGS = ("sync", "tensor", "vector", "scalar", "gpsimd")


class Prog:
    def __init__(self, nc):
        self.nc = nc
        self.ops = {k: [] for k in ENGS}

    def run_block(self):
        ops, self.ops = self.ops, {k: [] for k in ENGS}
        with self.nc.Block() as blk:
            for name in ENGS:
                if ops[name]:
                    getattr(blk, name)(
                        (lambda _o: (lambda e: [f(e) for f in _o]))(ops[name]))


class Sem:
    def __init__(self, ctx, nc, name):
        self.h = ctx.enter_context(nc.semaphore(name))
        self.n = 0


def _apname(a):
    try:
        return a.tensor.name
    except AttributeError:
        pass
    try:
        return a.name
    except AttributeError:
        return str(a)


class Emit:

    TRACKED = ("vector", "scalar", "gpsimd")

    def __init__(self, P):
        self.P = P
        self.written = {k: set() for k in self.TRACKED}
        self.readed = {k: set() for k in self.TRACKED}

    def op(self, eng, fn, sem=None, k=1, r=(), w=()):
        if eng in self.written:
            rn = {_apname(a) for a in r}
            wn = {_apname(a) for a in w}
            if (rn & self.written[eng]) or (wn & (self.written[eng] | self.readed[eng])):
                self.P.ops[eng].append(lambda e: e.drain())
                self.written[eng].clear()
                self.readed[eng].clear()
            self.written[eng] |= wn
            self.readed[eng] |= rn
        semh = None
        if sem is not None:
            sem.n += k
            semh = sem.h
        def f(e, fn=fn, semh=semh, k=k):
            ins = fn(e)
            if semh is not None:
                ins.then_inc(semh, k)
        self.P.ops[eng].append(f)

    def barrier_reset(self):
        for s in self.written.values():
            s.clear()
        for s in self.readed.values():
            s.clear()

    def wait(self, eng, sem, val=None):
        v = sem.n if val is None else val
        if v <= 0:
            return
        self.P.ops[eng].append(lambda e, s=sem.h, v=v: e.wait_ge(s, v))

    def mm(self, out, lhsT, rhs, start, stop, sem=None, k=1, tp=None):
        self.op("tensor",
                lambda e, o=out, l=lhsT, r=rhs, a=start, b=stop, tp=tp:
                e.matmul(o, l, r, start=a, stop=b, tile_position=tp),
                sem, k)

    def trans(self, out, in_, ident, sem=None, k=1):
        self.op("tensor",
                lambda e, o=out, i=in_, d=ident: e.transpose(o, i, d), sem, k)


def hs(t, h):
    j, half = divmod(h, 2)
    return t[64 * half:64 * (half + 1), 128 * j:128 * (j + 1)]



def build_program(debug_taps=(), nt=NT, max_phase=99, solo=False):
    nc = bass.Bass("TRN2", target_bir_lowering=False)

    def ext(name, shape, dtype=F32):
        return nc.dram_tensor(name, list(shape), dtype, kind="ExternalInput")

    x_own = ext("x_own", [HALF, D])
    x_pre = ext("x_pre", [HALF, D])
    pre_gate = ext("pre_gate", [128, 1])
    w5T = ext("w5T", [128, 41, D])
    wgT = ext("wgT", [128, 16, D])
    woT = ext("woT", [128, 8, D])
    bd_mats = ext("bd_mats", [128, 8, 128])
    at_mats = ext("at_mats", [128, 4, 128])
    masks = ext("masks", [128, 10, 128])
    rt8 = ext("rt8", [128, 8, 8])
    ew = ext("ew", [8, 128, 12288])
    shw = ext("shw", [128, 24, 1024])
    w3T = ext("w3T", [128, 26, 1024])
    d_rope = ext("d_rope", [HALF, 1024], BF16)
    slot = ext("slot", [1, 1], U32)
    wr4 = ext("wr4", [128, 9, 8])
    ew4 = ext("ew4", [8, 128, 17, 1024])

    y = nc.dram_tensor("y", [HALF, D], F32, kind="ExternalOutput")
    y_x1 = None
    if "x1" in debug_taps:
        y_x1 = nc.dram_tensor("y_x1", [HALF, D], F32, kind="ExternalOutput")
    y_x2 = None
    if "x2" in debug_taps:
        y_x2 = nc.dram_tensor("y_x2", [HALF, D], F32, kind="ExternalOutput")
    y_x3 = None
    if "x3" in debug_taps:
        y_x3 = nc.dram_tensor("y_x3", [HALF, D], F32, kind="ExternalOutput")

    d_x1 = nc.dram_tensor("d_x1", [HALF, D], F32, kind="Internal")
    d_ln1 = nc.dram_tensor("d_ln1", [HALF, D], BF16, kind="Internal")
    d_x2 = nc.dram_tensor("d_x2", [HALF, D], F32, kind="Internal")
    d_ln2 = nc.dram_tensor("d_ln2", [HALF, D], BF16, kind="Internal")
    d_x3 = nc.dram_tensor("d_x3", [HALF, D], F32, kind="Internal")
    d_ln3 = nc.dram_tensor("d_ln3", [HALF, D], BF16, kind="Internal")
    d_ex = nc.dram_tensor("d_ex", [2, 128, 520], F32, kind="Internal",
                          addr_space="Shared")
    d_exf = nc.dram_tensor("d_exf", [2, 16], U32, kind="Internal",
                           addr_space="Shared")
    d_ex2 = nc.dram_tensor("d_ex2", [2, 1, 8], F32, kind="Internal",
                           addr_space="Shared")

    P = Prog(nc)
    E = Emit(P)

    with ExitStack() as pc:
        sx = Sem(pc, nc, "sx")
        sxp = [Sem(pc, nc, "sxp0"), Sem(pc, nc, "sxp1")]
        sw = Sem(pc, nc, "sw")
        st = Sem(pc, nc, "st")
        sv = Sem(pc, nc, "sv")
        ss = Sem(pc, nc, "ss")
        sg = Sem(pc, nc, "sg")
        sd = Sem(pc, nc, "sd")
        sdg = Sem(pc, nc, "sdg")
        sxc = Sem(pc, nc, "sxc")
        sep = [Sem(pc, nc, "sep0"), Sem(pc, nc, "sep1")]
        sec = Sem(pc, nc, "sec")
        sxq = [Sem(pc, nc, "sxq0"), Sem(pc, nc, "sxq1")]
        spl = Sem(pc, nc, "spl")
        sgw = Sem(pc, nc, "sgw")
        stf = Sem(pc, nc, "stf")

        with ExitStack() as c1:
            en = c1.enter_context
            W5 = en(nc.sbuf_tensor("W5", [128, 41, D], BF16))
            WG = en(nc.sbuf_tensor("WG", [128, 16, D], BF16))
            WO = en(nc.sbuf_tensor("WO", [128, 8, D], BF16))
            BD = en(nc.sbuf_tensor("BD", [128, 8, 128], BF16))
            AT = en(nc.sbuf_tensor("AT", [128, 4, 128], BF16))
            MK = en(nc.sbuf_tensor("MK", [128, 10, 128], BF16))
            BRh = en(nc.sbuf_tensor("BRh", [1, 64], BF16))
            PG = en(nc.sbuf_tensor("PG", [128, 1], BF16))
            ONES = en(nc.sbuf_tensor("ONES", [128, 1], BF16))
            ONR = en(nc.sbuf_tensor("ONR", [128, 128], BF16))
            WGOb = en(nc.sbuf_tensor("WGOb", [128, D], BF16))
            SWb = en(nc.sbuf_tensor("SWb", [128, 4], F32))
            BGOb = en(nc.sbuf_tensor("BGOb", [128, 1], F32))
            SKV = en(nc.sbuf_tensor("SKV", [128, 5, 512], F32))
            SKc = en(nc.sbuf_tensor("SKc", [128, 5, 8], F32))
            SKVh = en(nc.sbuf_tensor("SKVh", [128, 512], BF16))
            SKch = en(nc.sbuf_tensor("SKch", [128, 16], BF16))
            XI = en(nc.sbuf_tensor("XI", [128, 2, D], F32))
            SM = en(nc.sbuf_tensor("SM", [128, 64], F32))
            LNt = en(nc.sbuf_tensor("LNt", [128, D], BF16))
            LNf = en(nc.sbuf_tensor("LNf", [128, D], BF16))
            Qt = en(nc.sbuf_tensor("Qt", [128, D], BF16))
            Qf = en(nc.sbuf_tensor("Qf", [128, D], BF16))
            PQf = en(nc.sbuf_tensor("PQf", [128, D], BF16))
            Kt = en(nc.sbuf_tensor("Kt", [128, D], BF16))
            PKt = en(nc.sbuf_tensor("PKt", [128, D], BF16))
            PKf = en(nc.sbuf_tensor("PKf", [128, D], BF16))
            Vt = en(nc.sbuf_tensor("Vt", [128, D], BF16))
            KLt = en(nc.sbuf_tensor("KLt", [128, D], BF16))
            KLfb = [en(nc.sbuf_tensor(f"KLf{i}", [128, D], BF16)) for i in range(2)]
            VLtb = [en(nc.sbuf_tensor(f"VLt{i}", [128, D], BF16)) for i in range(2)]
            KM = en(nc.sbuf_tensor("KM", [128, D], BF16))
            PTS = en(nc.sbuf_tensor("PTS", [128, 8, 128], BF16))
            GLB = en(nc.sbuf_tensor("GLB", [128, D], F32))
            DIFW = en(nc.sbuf_tensor("DIFW", [128, D], F32))
            DIFf = en(nc.sbuf_tensor("DIFf", [128, D], BF16))
            MX = en(nc.sbuf_tensor("MXD", [128, D], BF16))
            TM0 = en(nc.sbuf_tensor("TM0", [128, D], F32))
            PF = en(nc.psum_tensor("PF", [128, 7, 512], F32))
            PH = en(nc.psum_tensor("PH", [128, 1024], BF16))

            ident = MK[:, 1, :]
            DIF = Qt
            LN1 = LNt
            PS1 = LNt
            GH = LNt
            MXf = DIFf
            PS3 = MX
            SQS = DIFW
            VLVt = KM
            X1 = GLB
            KLVf = PKf
            KLVt = PKt

            def wload(dst, srcp, nch):
                step = 4
                for c0 in range(0, nch, step):
                    c1 = min(nch, c0 + step)
                    E.op("gpsimd", lambda e, d=dst[:, c0:c1, :], s=srcp[:, c0:c1, :]:
                         e.dma_start(d, s), sw, 16)
            wload(W5, w5T, 41)
            wload(WG, wgT, 16)
            wload(WO, woT, 8)
            for dst, srcp in ((BD[:], bd_mats[:]), (AT[:], at_mats[:]),
                              (MK[:], masks[:]), (PG[:], pre_gate[:])):
                E.op("gpsimd", lambda e, d=dst, s=srcp: e.dma_start(d, s), sw, 16)
            E.op("gpsimd", lambda e: e.memset(ONES[:], 1.0))
            E.op("gpsimd", lambda e: e.memset(ONR[:], 1.0))
            E.op("gpsimd", lambda e: e.memset(SKV[:], 0.0))
            E.op("gpsimd", lambda e: e.memset(SKc[:], 0.0))
            E.op("gpsimd", lambda e: e.memset(KLfb[1][:], 0.0))
            E.op("gpsimd", lambda e: e.memset(VLtb[1][:], 0.0))
            E.op("gpsimd", lambda e: e.memset(SM[:, 49:50], -EXPB))
            E.op("gpsimd", lambda e: e.memset(SM[:, 52:53], 1e-6))
            E.op("gpsimd", lambda e: e.memset(SM[:, 50:51], 0.125), sg, 1)

            E.wait("tensor", sw)
            E.wait("tensor", sg)
            E.mm(PF[:, 4, :], ONR[64:65, :], W5[64:65, 40, 0:512], True, True,
                 None, 1, tp=(64, 0))
            E.mm(PF[:, 5, :], ONR[64:65, :], W5[64:65, 40, 512:1024], True, True,
                 st, 1, tp=(64, 0))
            E.wait("vector", st)
            E.wait("vector", sw)
            E.op("vector", lambda e: e.tensor_copy(WGOb[:, 0:512], PF[:, 4, :]), w=(WGOb,))
            E.op("vector", lambda e: e.tensor_copy(WGOb[:, 512:1024], PF[:, 5, :]), w=(WGOb,))
            E.op("vector", lambda e: e.tensor_scalar(
                W5[:, 24:32, :], W5[:, 24:32, :], 0.125, None, ALU.mult), w=(W5,))
            E.op("vector", lambda e: e.tensor_copy(SM[0:1, 16:20], MK[0:1, 6, 0:4]), r=(MK,), w=(SM,))
            E.op("vector", lambda e: e.scalar_tensor_tensor(
                SM[0:1, 20:21], SM[0:1, 16:17], -1.0, SM[0:1, 17:18],
                ALU.mult, ALU.add), r=(SM,), w=(SM,))
            E.op("vector", lambda e: e.scalar_tensor_tensor(
                SM[0:1, 21:22], SM[0:1, 18:19], -1.0, SM[0:1, 19:20],
                ALU.mult, ALU.add), sv, 1, r=(SM,), w=(SM,))
            E.wait("scalar", sv)
            E.op("scalar", lambda e: e.activation(SM[0:1, 20:22], SM[0:1, 20:22], AF.Relu),
                 ss, 1, r=(SM,), w=(SM,))
            E.wait("vector", ss)
            E.op("vector", lambda e: e.tensor_add(SM[0:1, 22:23], SM[0:1, 16:17], SM[0:1, 20:21]), r=(SM,), w=(SM,))
            E.op("vector", lambda e: e.tensor_add(SM[0:1, 23:24], SM[0:1, 18:19], SM[0:1, 21:22]), r=(SM,), w=(SM,))
            E.op("vector", lambda e: e.scalar_tensor_tensor(
                SM[0:1, 24:25], SM[0:1, 22:23], -1.0, SM[0:1, 23:24],
                ALU.mult, ALU.add), sv, 1, r=(SM,), w=(SM,))
            E.wait("scalar", sv)
            E.op("scalar", lambda e: e.activation(SM[0:1, 24:25], SM[0:1, 24:25], AF.Relu),
                 ss, 1, r=(SM,), w=(SM,))
            E.wait("vector", ss)
            E.op("vector", lambda e: e.tensor_add(SM[0:1, 0:1], SM[0:1, 22:23], SM[0:1, 24:25]), r=(SM,), w=(SM,))
            E.op("vector", lambda e: e.tensor_scalar(
                SM[0:1, 25:26], SM[0:1, 0:1], -1.0, None, ALU.mult), r=(SM,), w=(SM,))
            E.op("vector", lambda e: e.tensor_scalar(
                SM[0:1, 1:5], SM[0:1, 16:20], SM[0:1, 25:26], None, ALU.add), sv, 1, r=(SM,), w=(SM,))
            E.wait("scalar", sv)
            E.op("scalar", lambda e: e.activation(
                SM[0:1, 5:9], SM[0:1, 1:5], AF.Exp, accum_out=SM[0:1, 9:10]), ss, 1, r=(SM,), w=(SM,))
            E.wait("vector", ss)
            E.op("vector", lambda e: e.reciprocal(SM[0:1, 10:11], SM[0:1, 9:10]), r=(SM,), w=(SM,))
            E.op("vector", lambda e: e.tensor_scalar(
                SM[0:1, 11:15], SM[0:1, 5:9], SM[0:1, 10:11], None, ALU.mult), r=(SM,), w=(SM,))
            E.op("vector", lambda e: e.tensor_copy(BRh[0:1, 0:4], SM[0:1, 11:15]), r=(SM,), w=(BRh,))
            E.op("vector", lambda e: e.tensor_copy(BRh[0:1, 4:5], MK[0:1, 6, 4:5]), sv, 1, r=(MK,), w=(BRh,))
            E.wait("tensor", sv)
            E.mm(PF[:, 0, 0:5], ONR[0:1, :], BRh[0:1, 0:5], True, True, st, 1)
            E.wait("vector", st)
            E.op("vector", lambda e: e.tensor_copy(SWb[:], PF[:, 0, 0:4]), w=(SWb,))
            E.op("vector", lambda e: e.tensor_copy(BGOb[:], PF[:, 0, 4:5]), sv, 1, w=(BGOb,))

            with nc.named_scope("s1pre"):
                P.run_block()
            E.barrier_reset()

            n_prior_sx = [sx.n // 16]

            def emit_ln(xsrc, out_bf, eps=1e-5):
                E.op("vector", lambda e, x=xsrc: e.reduce_sum(SM[:, 0:1], x, AX.X), sv, 1,
                     r=(xsrc,), w=(SM,))
                E.op("scalar", lambda e, x=xsrc: e.activation(
                    SQS[:], x, AF.Square, accum_out=SM[:, 1:2]), ss, 1,
                    r=(xsrc,), w=(SQS, SM))
                E.wait("vector", ss)
                E.op("vector", lambda e: e.tensor_scalar(
                    SM[:, 2:3], SM[:, 0:1], 1.0 / D, None, ALU.mult), r=(SM,), w=(SM,))
                E.op("vector", lambda e: e.tensor_mul(SM[:, 3:4], SM[:, 2:3], SM[:, 2:3]), r=(SM,), w=(SM,))
                E.op("vector", lambda e: e.scalar_tensor_tensor(
                    SM[:, 4:5], SM[:, 1:2], 1.0 / D, SM[:, 3:4], ALU.mult, ALU.subtract), r=(SM,), w=(SM,))
                E.op("vector", lambda e, p=eps: e.tensor_scalar(
                    SM[:, 4:5], SM[:, 4:5], p, None, ALU.add), sv, 1, r=(SM,), w=(SM,))
                E.wait("scalar", sv)
                E.op("scalar", lambda e: e.sqrt(SM[:, 5:6], SM[:, 4:5]), ss, 1, r=(SM,), w=(SM,))
                E.wait("vector", ss)
                E.op("vector", lambda e: e.reciprocal(SM[:, 6:7], SM[:, 5:6]), r=(SM,), w=(SM,))
                E.op("vector", lambda e: e.tensor_mul(SM[:, 7:8], SM[:, 2:3], SM[:, 6:7]), r=(SM,), w=(SM,))
                E.op("vector", lambda e: e.tensor_scalar(
                    SM[:, 7:8], SM[:, 7:8], -1.0, None, ALU.mult), sv, 1, r=(SM,), w=(SM,))
                E.wait("scalar", sv)
                E.op("scalar", lambda e, o=out_bf, x=xsrc: e.activation(
                    o, x, AF.Identity, bias=SM[:, 7:8], scale=SM[:, 6:7]), ss, 1,
                    r=(xsrc, SM), w=(out_bf,))

            def emit_phi(xin_bf, out_bf):
                E.wait("scalar", sv)
                E.op("scalar", lambda e, x=xin_bf, o=out_bf: e.activation(o, x, AF.Relu),
                     ss, 1, r=(xin_bf,), w=(out_bf,))
                E.wait("vector", ss)
                E.op("vector", lambda e, x=xin_bf, o=out_bf: e.scalar_tensor_tensor(
                    PS1[:], o, -1.0, x, ALU.mult, ALU.add), sv, 1,
                    r=(xin_bf, out_bf), w=(PS1,))
                E.wait("scalar", sv)
                E.op("scalar", lambda e: e.activation(PS1[:], PS1[:], AF.Exp), ss, 1,
                     r=(PS1,), w=(PS1,))
                E.wait("vector", ss)
                E.op("vector", lambda e, o=out_bf: e.tensor_add(o, o, PS1[:]), sv, 1,
                     r=(out_bf, PS1), w=(out_bf,))

            def trans8(src_bf, dst_bf, bank):
                E.wait("tensor", sv)
                E.wait("tensor", ss)
                for j in range(8):
                    E.trans(bank[:, 128 * j:128 * (j + 1)],
                            src_bf[:, 128 * j:128 * (j + 1)], ident,
                            st if j == 7 else None, 1)
                E.wait("vector", st)
                E.op("vector", lambda e, d=dst_bf, b=bank: e.tensor_copy(d, b), sv, 1,
                     w=(dst_bf,))

            def hmm8(dst_bf, src_tok, rhs_at, bd_slice):
                E.wait("tensor", sv)
                for j in range(8):
                    bank, col = j // 4, (j % 4) * 128
                    if bd_slice is None:
                        E.mm(PF[:, bank, col:col + 128],
                             src_tok[:, 128 * j:128 * (j + 1)], rhs_at,
                             True, True, st if j == 7 else None, 1)
                    else:
                        E.mm(PF[:, bank, col:col + 128], bd_slice,
                             src_tok[:, 128 * j:128 * (j + 1)],
                             True, True, st if j == 7 else None, 1)
                E.wait("vector", st)
                E.op("vector", lambda e, d=dst_bf: e.tensor_copy(
                    d[:, 0:512], PF[:, 0, :]), w=(dst_bf,))
                E.op("vector", lambda e, d=dst_bf: e.tensor_copy(
                    d[:, 512:1024], PF[:, 1, :]), sv, 1, w=(dst_bf,))

            def haar_prep(lv):
                hmm8(KM[:], Kt[:], AT[:, lv, :], None)
                hmm8(PS3[:], KM[:], None, BD[:, lv, :])
                emit_phi(PS3[:], KLVf[:])
                trans8(KLVf[:], KLVt[:], PH[:])
                hmm8(KM[:], Vt[:], AT[:, lv, :], None)
                hmm8(PS3[:], KM[:], None, BD[:, 4 + lv, :])
                trans8(PS3[:], VLVt[:], PH[:])

            def tile_body(ti, own, xsrc_dram, sk_rhs, base_consumed):
                xi = XI[:, ti % 2, :]
                par = (ti % 2) if own else ((ti + nt) % 2)
                kl_cur, kl_prv = KLfb[par], KLfb[1 - par]
                vl_cur, vl_prv = VLtb[par], VLtb[1 - par]

                if ti + 1 < nt:
                    E.wait("sync", sxc, base_consumed + ti)
                    E.op("sync", lambda e, t=ti + 1, s=xsrc_dram: e.dma_start(
                        XI[:, t % 2, :], s[128 * t:128 * (t + 1), :]),
                        sxp[(ti + 1) % 2], 16)

                E.wait("vector", sxp[ti % 2])
                E.wait("scalar", sxp[ti % 2])
                E.wait("scalar", sd)
                E.wait("vector", sd)

                def _cut0(ph):
                    if max_phase < ph:
                        if own:
                            E.op("vector", lambda e: e.memzero(SM[:, 51:52]),
                                 sxc, 1, w=(SM,))
                        else:
                            E.op("scalar", lambda e: e.memzero(SM[:, 48:49]),
                                 sxc, 1, w=(SM,))
                        return True
                    return False

                emit_ln(xi, LNt[:])
                if _cut0(0.3):
                    return
                trans8(LNt[:], LNf[:], PH[:])
                if _cut0(0.6):
                    return

                E.wait("tensor", sv)
                grpA = [(0, 0), (1, 2), (2, 4)] if own else [(1, 0), (2, 2)]
                for j in range(8):
                    lhsT = LNf[:, 128 * j:128 * (j + 1)]
                    for gi, (pi, bank) in enumerate(grpA):
                        for hcol in range(2):
                            last = (j == 7 and gi == len(grpA) - 1 and hcol == 1)
                            E.mm(PF[:, bank + hcol, :], lhsT,
                                 W5[:, 8 * pi + j, 512 * hcol:512 * (hcol + 1)],
                                 j == 0, j == 7, st if last else None, 1)
                E.wait("scalar", st)
                E.wait("vector", st)
                kb = 2 if own else 0
                vb = 4 if own else 2
                if own:
                    E.op("scalar", lambda e: e.copy(Qt[:, 0:512], PF[:, 0, :]), w=(Qt,))
                    E.op("scalar", lambda e: e.copy(Qt[:, 512:1024], PF[:, 1, :]), ss, 1, w=(Qt,))
                E.op("vector", lambda e, b=kb: e.tensor_copy(Kt[:, 0:512], PF[:, b, :]), w=(Kt,))
                E.op("vector", lambda e, b=kb: e.tensor_copy(Kt[:, 512:1024], PF[:, b + 1, :]), w=(Kt,))
                E.op("vector", lambda e, b=vb: e.tensor_copy(Vt[:, 0:512], PF[:, b, :]), w=(Vt,))
                E.op("vector", lambda e, b=vb: e.tensor_copy(Vt[:, 512:1024], PF[:, b + 1, :]), sv, 1, w=(Vt,))

                _cut = _cut0

                if _cut(1):
                    return
                tail = (not own) and ti == nt - 1
                if own or tail:
                    E.wait("tensor", sv)
                    E.wait("tensor", ss)
                    for j in range(8):
                        lhsT = LNf[:, 128 * j:128 * (j + 1)]
                        for gi, (pi, bank) in enumerate([(3, 0), (4, 2)]):
                            for hcol in range(2):
                                last = (j == 7 and gi == 1 and hcol == 1)
                                E.mm(PF[:, bank + hcol, :], lhsT,
                                     W5[:, 8 * pi + j, 512 * hcol:512 * (hcol + 1)],
                                     j == 0, j == 7, st if last else None, 1)
                    E.wait("vector", st)
                    E.op("vector", lambda e: e.tensor_copy(KLt[:, 0:512], PF[:, 0, :]), w=(KLt,))
                    E.op("vector", lambda e: e.tensor_copy(KLt[:, 512:1024], PF[:, 1, :]), w=(KLt,))
                    E.op("vector", lambda e, v=vl_cur: e.tensor_copy(v[:, 0:512], PF[:, 2, :]), w=(vl_cur,))
                    E.op("vector", lambda e, v=vl_cur: e.tensor_copy(v[:, 512:1024], PF[:, 3, :]), sv, 1, w=(vl_cur,))

                if _cut(2):
                    return
                emit_phi(Kt[:], PKt[:])
                if _cut(2.2):
                    return
                if own:
                    trans8(PKt[:], PKf[:], PH[:])
                if _cut(2.4):
                    return
                if own:
                    trans8(Qt[:], Qf[:], PH[:])
                if _cut(2.6):
                    return
                if own or tail:
                    trans8(KLt[:], kl_cur[:], PH[:])
                if _cut(2.8):
                    return
                if own:
                    emit_phi(Qf[:], PQf[:])

                if _cut(3):
                    return
                ncalls = min(5, int(max_phase) - 3) if max_phase >= 4 \
                    else (1 if max_phase > 3.05 else 0)
                for cc in range(ncalls):
                    if cc > 0:
                        haar_prep(cc - 1)
                    kf = PKf[:] if cc == 0 else KLVf[:]
                    ktok = PKt[:] if cc == 0 else KLVt[:]
                    vtok = Vt[:] if cc == 0 else VLVt[:]
                    den_bank = 4

                    if cc == 0 and _cut(3.07):
                        return
                    E.wait("tensor", sv)
                    if cc == 0 and _cut(3.1):
                        return
                    if own:
                        E.op("vector", lambda e, cc=cc: e.tensor_copy(SKVh[:], SKV[:, cc, :]),
                             r=(SKV,), w=(SKVh,))
                        E.op("vector", lambda e, cc=cc: e.tensor_copy(
                            SKch[:].rearrange("p (a b) -> p a b", a=8),
                            SKc[:, cc, :].unsqueeze(2).broadcast_to([128, 8, 2])), sv, 1,
                            r=(SKc,), w=(SKch,))
                        E.wait("tensor", sv)
                        for hg in range(4):
                            p = hg % 2
                            pslot = PTS[:, 4 * p:4 * p + 4, :]
                            if hg >= 2:
                                E.wait("tensor", sv)
                            for half in range(2):
                                for jj in range(2):
                                    h = 4 * hg + 2 * jj + half
                                    E.mm(PF[:, half, 256 * p + 128 * jj:256 * p + 128 * jj + 128],
                                         hs(kf, h), hs(PQf, h), True, True,
                                         st if jj == 1 else None, 1,
                                         tp=(64 * half, 0))
                            E.wait("vector", st)
                            E.op("vector", lambda e, p=p, ps=pslot: e.tensor_tensor(
                                ps.rearrange("p (a b) c -> p a b c", a=2),
                                PF[:, 0:2, 256 * p:256 * p + 256].rearrange(
                                    "p a (b c) -> p a b c", b=2),
                                MK[:, 0:1, :].unsqueeze(1).broadcast_to([128, 2, 2, 128]),
                                ALU.mult), sv, 1, r=(MK,), w=(PTS,))
                            E.wait("tensor", sv)
                            for idx in range(4):
                                half, jj = divmod(idx, 2)
                                h = 4 * hg + 2 * jj + half
                                slot = PTS[:, 4 * p + idx, :]
                                ob = PF[:, 2 + h // 8, 64 * (h % 8):64 * (h % 8) + 64]
                                E.mm(ob, slot, vtok[:, 64 * h:64 * h + 64], True, True)
                                E.mm(PF[:, den_bank, h:h + 1], slot, ONES[:],
                                     True, True)
                        E.wait("tensor", sv)
                        for h in range(0, 16, 2):
                            E.mm(PF[:, 0, 64 * (h // 2):64 * (h // 2) + 64],
                                 hs(PQf, h), SKVh[0:64, 64 * (h // 2):64 * (h // 2) + 64],
                                 True, True, None, 1, tp=(0, 0))
                        for h in range(0, 16, 2):
                            j = h // 2
                            E.mm(PF[:, den_bank, 16 + 2 * j:18 + 2 * j],
                                 hs(PQf, h), SKch[0:64, 2 * j:2 * j + 2],
                                 True, True, None, 1, tp=(0, 0))
                        for h in range(1, 16, 2):
                            E.mm(PF[:, 1, 64 * (h // 2):64 * (h // 2) + 64],
                                 hs(PQf, h), SKVh[64:128, 64 * (h // 2):64 * (h // 2) + 64],
                                 True, True, None, 1, tp=(64, 0))
                    if cc == 0 and _cut(3.3):
                        return
                    for j in range(8):
                        E.mm(PF[:, 5 + j // 4, (j % 4) * 128:(j % 4) * 128 + 128],
                             ktok[:, 128 * j:128 * (j + 1)],
                             vtok[:, 128 * j:128 * (j + 1)],
                             True, True, None, 1)
                        E.mm(PF[:, den_bank, 32 + j:33 + j],
                             ktok[:, 128 * j:128 * (j + 1)], sk_rhs,
                             True, True,
                             (st if j == 7 and not own else None), 1)
                    if cc == 0 and _cut(3.4):
                        return
                    if own:
                        for h in range(1, 16, 2):
                            j = h // 2
                            E.mm(PF[:, den_bank, 40 + 2 * j:42 + 2 * j],
                                 hs(PQf, h), SKch[64:128, 2 * j:2 * j + 2],
                                 True, True, st if h == 15 else None, 1,
                                 tp=(64, 0))
                    oden_st = st.n
                    if cc == 0 and _cut(3.5):
                        return

                    E.wait("vector", st, oden_st)
                    if own:
                        E.op("vector", lambda e, b=den_bank: e.tensor_copy(
                            SM[:, 32:48].rearrange("p (a b) -> p a b", a=8)[:, :, 0:1],
                            PF[:, b, 16:32].rearrange("p (a b) -> p a b", a=8)[:, :, 0:1]),
                            w=(SM,))
                        E.op("vector", lambda e, b=den_bank: e.tensor_copy(
                            SM[:, 32:48].rearrange("p (a b) -> p a b", a=8)[:, :, 1:2],
                            PF[:, b, 40:56].rearrange("p (a b) -> p a b", a=8)[:, :, 0:1]),
                            w=(SM,))
                        E.op("vector", lambda e, b=den_bank: e.tensor_add(
                            SM[:, 16:32], PF[:, b, 0:16], SM[:, 32:48]),
                            r=(SM,), w=(SM,))
                        E.op("vector", lambda e: e.scalar_tensor_tensor(
                            SM[:, 32:48], SM[:, 16:32], -1.0,
                            SM[:, 52:53].broadcast_to([128, 16]),
                            ALU.mult, ALU.add), sv, 1, r=(SM,), w=(SM,))
                        E.wait("scalar", sv)
                        E.op("scalar", lambda e: e.activation(
                            SM[:, 32:48], SM[:, 32:48], AF.Relu), ss, 1,
                            r=(SM,), w=(SM,))
                        E.wait("vector", ss)
                        E.op("vector", lambda e: e.tensor_add(
                            SM[:, 16:32], SM[:, 16:32], SM[:, 32:48]),
                            r=(SM,), w=(SM,))
                        E.op("vector", lambda e: e.reciprocal(SM[:, 16:32], SM[:, 16:32]),
                             r=(SM,), w=(SM,))
                        if cc > 0:
                            E.op("vector", lambda e, cc=cc: e.tensor_scalar(
                                SM[:, 16:32], SM[:, 16:32], SWb[:, cc - 1:cc], None,
                                ALU.mult), r=(SM, SWb), w=(SM,))
                        for hb in range(2):
                            oa = PF[:, 2 + hb, :].rearrange(
                                "p (a b c) -> p a b c", a=4, b=2)
                            obp = PF[:, 0:2, 256 * hb:256 * hb + 256].rearrange(
                                "p a (b c) -> p b a c", b=4)
                            coef_b = SM[:, 16 + 8 * hb:24 + 8 * hb].rearrange(
                                "p (a b) -> p a b", a=4) \
                                .unsqueeze(3).broadcast_to([128, 4, 2, 64])
                            tm4 = TM0[:, 0:512].rearrange(
                                "p (a b c) -> p a b c", a=4, b=2)
                            E.op("vector", lambda e, ob=obp, t4=tm4: e.tensor_copy(
                                t4, ob), w=(TM0,))
                            E.op("vector", lambda e, oa=oa, t4=tm4: e.scalar_tensor_tensor(
                                t4, oa, 1.0, t4, ALU.mult, ALU.add), r=(TM0,), w=(TM0,))
                            E.op("vector", lambda e, cb=coef_b, t4=tm4: e.tensor_tensor(
                                t4, t4, cb, ALU.mult), r=(SM, TM0), w=(TM0,))
                            if cc == 0:
                                E.op("vector", lambda e, hb=hb: e.tensor_copy(
                                    GLB[:, 512 * hb:512 * (hb + 1)], TM0[:, 0:512]),
                                    r=(TM0,), w=(GLB,))
                            else:
                                E.op("vector", lambda e, hb=hb: e.tensor_add(
                                    GLB[:, 512 * hb:512 * (hb + 1)],
                                    GLB[:, 512 * hb:512 * (hb + 1)], TM0[:, 0:512]),
                                    r=(GLB, TM0), w=(GLB,))
                    if cc == 0 and _cut(3.7):
                        return
                    for half in range(2):
                        pr = slice(64 * half, 64 * (half + 1))
                        co = slice(64 * half, 64 * half + 64)
                        E.op("vector", lambda e, cc=cc, pr=pr, co=co: e.tensor_add(
                            SKV[pr, cc, :].rearrange("p (a b) -> p a b", a=8),
                            SKV[pr, cc, :].rearrange("p (a b) -> p a b", a=8),
                            PF[pr, 5:7, :].rearrange("p a (b c) -> p (a b) c", b=4)[:, :, co]),
                            r=(SKV,), w=(SKV,))
                    E.op("vector", lambda e, cc=cc, b=den_bank: e.tensor_add(
                        SKc[:, cc, :], SKc[:, cc, :], PF[:, b, 32:40]), sv, 1,
                        r=(SKc,), w=(SKc,))

                if _cut(9):
                    return
                if own:
                    E.wait("tensor", sv)
                    for hp in range(8):
                        p = hp % 2
                        plslot = PTS[:, 4 * p:4 * p + 4, :]
                        if hp >= 2:
                            E.wait("tensor", sg)
                        for half in range(2):
                            h = 2 * hp + half
                            for w, klw in ((0, kl_prv), (1, kl_cur)):
                                E.mm(PF[:, half, 256 * p + 128 * w:256 * p + 128 * w + 128],
                                     hs(klw, h), hs(Qf, h), True, True,
                                     st if w == 1 else None, 1,
                                     tp=(64 * half, 0))
                        E.wait("scalar", st)
                        E.op("scalar", lambda e, p=p, ps=plslot: e.activation(
                            ps.rearrange("p (a b) c -> p a b c", a=2),
                            PF[:, 0:2, 256 * p:256 * p + 256].rearrange(
                                "p a (b c) -> p a b c", b=2),
                            AF.Exp, bias=SM[:, 49:50]), ss, 1,
                            r=(SM,), w=(PTS,))
                        E.wait("vector", ss)
                        mk0 = 2 if ti == 0 else 4
                        E.op("vector", lambda e, ps=plslot, m0=mk0: e.tensor_tensor(
                            ps.rearrange("p (a b) c -> p a b c", a=2),
                            ps.rearrange("p (a b) c -> p a b c", a=2),
                            MK[:, m0:m0 + 2, :].unsqueeze(1).broadcast_to([128, 2, 2, 128]),
                            ALU.mult), sg, 1, r=(PTS, MK), w=(PTS,))
                        E.wait("tensor", sg)
                        for half in range(2):
                            h = 2 * hp + half
                            ob = PF[:, 2 + h // 8, 64 * (h % 8):64 * (h % 8) + 64]
                            dn = PF[:, 4, h:h + 1]
                            sprv = PTS[:, 4 * p + 2 * half, :]
                            scur = PTS[:, 4 * p + 2 * half + 1, :]
                            E.mm(ob, sprv, vl_prv[:, 64 * h:64 * h + 64],
                                 True, False)
                            E.mm(ob, scur, vl_cur[:, 64 * h:64 * h + 64],
                                 False, True)
                            E.mm(dn, sprv, ONES[:], True, False)
                            E.mm(dn, scur, ONES[:], False, True,
                                 st if h == H - 1 else None, 1)
                    loc_st = st.n
                    E.wait("vector", st, loc_st)
                    E.op("vector", lambda e: e.reciprocal(SM[:, 32:48], PF[:, 4, 0:16]), w=(SM,))
                    for hb in range(2):
                        coef_b = SM[:, 32 + 8 * hb:40 + 8 * hb] \
                            .unsqueeze(2).broadcast_to([128, 8, 64])
                        E.op("vector", lambda e, hb=hb, cb=coef_b: e.tensor_tensor(
                            TM0[:, 512 * hb:512 * (hb + 1)].rearrange(
                                "p (a b) -> p a b", a=8),
                            PF[:, 2 + hb, :].rearrange("p (a b) -> p a b", a=8),
                            cb, ALU.mult), r=(SM,), w=(TM0,))
                    E.op("vector", lambda e: e.scalar_tensor_tensor(
                        DIFW[:], GLB[:], -1.0, TM0[:], ALU.mult, ALU.add),
                        r=(TM0, GLB), w=(DIFW,))
                    E.op("vector", lambda e: e.tensor_copy(DIF[:], DIFW[:]), sv, 1,
                         r=(DIFW,), w=(DIF,))

                if _cut(10):
                    return
                if own:
                    pass
                if own:
                    trans8(DIF[:], DIFf[:], PH[:])
                    E.wait("tensor", sv)
                    for j in range(16):
                        lhsT = LNf[:, 128 * j:128 * (j + 1)] if j < 8 \
                            else DIFf[:, 128 * (j - 8):128 * (j - 7)]
                        for hcol in range(2):
                            E.mm(PF[:, hcol, :], lhsT,
                                 WG[:, j, 512 * hcol:512 * (hcol + 1)],
                                 j == 0, False)
                    for hcol in range(2):
                        E.mm(PF[:, hcol, :], MK[:, 7, :],
                             W5[:, 40, 512 * hcol:512 * (hcol + 1)], False, True,
                             st if hcol == 1 else None, 1)
                    E.wait("scalar", st)
                    E.op("scalar", lambda e: e.activation(GH[:, 0:512], PF[:, 0, :], AF.Silu), w=(GH,))
                    E.op("scalar", lambda e: e.activation(GH[:, 512:1024], PF[:, 1, :], AF.Silu), ss, 1, w=(GH,))
                    E.wait("vector", ss)
                    E.op("vector", lambda e: e.tensor_mul(KM[:], GH[:], WGOb[:]),
                         r=(GH, WGOb), w=(KM,))
                    E.op("vector", lambda e: e.reduce_sum(SM[:, 8:9], KM[:], AX.X), sv, 1,
                         r=(KM,), w=(SM,))
                    E.wait("scalar", sv)
                    E.op("scalar", lambda e: e.activation(
                        SM[:, 9:10], SM[:, 8:9], AF.Sigmoid, bias=BGOb[:]), ss, 1,
                        r=(SM, BGOb), w=(SM,))
                    E.wait("vector", ss)
                    E.op("vector", lambda e: e.scalar_tensor_tensor(
                        TM0[:], DIFW[:], SM[:, 9:10], GLB[:], ALU.mult, ALU.add),
                        r=(DIFW, SM, GLB), w=(TM0,))
                    E.op("vector", lambda e: e.tensor_copy(MX[:], TM0[:]), sv, 1,
                         r=(TM0,), w=(MX,))

                    trans8(MX[:], MXf[:], PH[:])
                    E.wait("tensor", sv)
                    for j in range(8):
                        for hcol in range(2):
                            E.mm(PF[:, 2 + hcol, :], MXf[:, 128 * j:128 * (j + 1)],
                                 WO[:, j, 512 * hcol:512 * (hcol + 1)],
                                 j == 0, False)
                    for hcol in range(2):
                        E.mm(PF[:, 2 + hcol, :], MK[:, 8, :],
                             W5[:, 40, 512 * hcol:512 * (hcol + 1)], False, True,
                             st if hcol == 1 else None, 1)
                    E.wait("vector", st)
                    for hb in range(2):
                        E.op("vector", lambda e, hb=hb, x=xi: e.tensor_add(
                            X1[:, 512 * hb:512 * (hb + 1)], PF[:, 2 + hb, :],
                            x[:, 512 * hb:512 * (hb + 1)]),
                            sxc if hb == 1 else None, 1, r=(XI,), w=(X1,))
                    E.wait("scalar", sxc)
                    emit_ln(X1[:], LN1[:])
                    ln1_ss = ss.n
                    E.wait("sync", sxc)
                    E.op("sync", lambda e, t=ti: e.dma_start(
                        d_x1[128 * t:128 * (t + 1), :], X1[:]), sd, 16)
                    if y_x1 is not None:
                        E.op("sync", lambda e, t=ti: e.dma_start(
                            y_x1[128 * t:128 * (t + 1), :], X1[:]), sd, 16)
                    E.wait("sync", ss, ln1_ss)
                    E.op("sync", lambda e, t=ti: e.dma_start(
                        d_ln1[128 * t:128 * (t + 1), :], LN1[:]), sd, 16)
                else:
                    E.op("scalar", lambda e: e.memzero(SM[:, 48:49]),
                         sxc, 1, w=(SM,))

            if max_phase >= 0.1:
                E.op("sync", lambda e: e.dma_start(XI[:, 0, :], x_pre[0:128, :]), sxp[0], 16)
                for ti in range(nt):
                    tile_body(ti, False, x_pre, PG[:], 0)

                n_prior_sx[0] += nt
                E.wait("sync", sxc, nt)
                E.op("sync", lambda e: e.dma_start(XI[:, 0, :], x_own[0:128, :]), sxp[0], 16)
                for ti in range(nt):
                    tile_body(ti, True, x_own, ONES[:], nt)

            with nc.named_scope("s1"):
                P.run_block()
            E.barrier_reset()

        if max_phase >= 11:
          with ExitStack() as c2:
            en = c2.enter_context
            SH = en(nc.sbuf_tensor("SH", [128, 24, 1024], BF16))
            EWb = [en(nc.sbuf_tensor(f"EWs{i}", [128, 12288], BF16)) for i in range(2)]
            RT = en(nc.sbuf_tensor("RT", [128, 8, 8], BF16))
            ACC = en(nc.sbuf_tensor("ACC", [128, 8, 1024], F32))
            LN1F = en(nc.sbuf_tensor("LN1F", [128, 8, 1024], BF16))
            LTK = en(nc.sbuf_tensor("LTK", [128, 2, 1024], BF16))
            XT1 = en(nc.sbuf_tensor("XT1", [128, 2, 1024], F32))
            HS2 = en(nc.sbuf_tensor("HS2", [128, 1024], BF16))
            HST = en(nc.sbuf_tensor("HST", [128, 1024], BF16))
            SGb = en(nc.sbuf_tensor("SGb", [128, 1024], BF16))
            HE = en(nc.sbuf_tensor("HE2", [128, 512], BF16))
            HET = en(nc.sbuf_tensor("HET", [128, 512], BF16))
            CFA = en(nc.sbuf_tensor("CFA", [128, 8, 8], F32))
            SM2 = en(nc.sbuf_tensor("SM2", [128, 128], F32))
            SQ2 = en(nc.sbuf_tensor("SQ2", [128, 1024], F32))
            X2T = en(nc.sbuf_tensor("X2T", [128, 1024], F32))
            LN2T = en(nc.sbuf_tensor("LN2T", [128, 1024], BF16))
            ID2 = en(nc.sbuf_tensor("ID2", [128, 128], BF16))
            PB = en(nc.psum_tensor("PB", [128, 7, 512], F32))
            PH2 = en(nc.psum_tensor("PH2", [128, 1024], BF16))

            for c0 in range(0, 24, 4):
                E.op("gpsimd", lambda e, c0=c0: e.dma_start(
                    SH[:, c0:c0 + 4, :], shw[:, c0:c0 + 4, :]), sw, 16)
            E.op("gpsimd", lambda e: e.dma_start(RT[:], rt8[:]), sw, 16)
            E.op("gpsimd", lambda e: e.dma_start(ID2[:], masks[:, 1, :]), sw, 16)
            E.op("gpsimd", lambda e: e.memset(SM2[:, 96:97], 1.0))
            E.op("gpsimd", lambda e: e.memset(SM2[:, 97:98], -5.0), sg, 1)

            def eload(eidx, pi):
                for c0 in range(0, 12288, 3072):
                    E.op("gpsimd", lambda e, eidx=eidx, pi=pi, c0=c0: e.dma_start(
                        EWb[pi][:, c0:c0 + 3072], ew[eidx, :, c0:c0 + 3072]),
                        sep[pi], 16)
            eload(0, 0)

            def emit_ln2(xsrc, out_bf, eps=1e-5):
                E.op("vector", lambda e, x=xsrc: e.reduce_sum(SM2[:, 0:1], x, AX.X), sv, 1,
                     r=(xsrc,), w=(SM2,))
                E.op("scalar", lambda e, x=xsrc: e.activation(
                    SQ2[:], x, AF.Square, accum_out=SM2[:, 1:2]), ss, 1,
                    r=(xsrc,), w=(SQ2, SM2))
                E.wait("vector", ss)
                E.op("vector", lambda e: e.tensor_scalar(
                    SM2[:, 2:3], SM2[:, 0:1], 1.0 / D, None, ALU.mult), r=(SM2,), w=(SM2,))
                E.op("vector", lambda e: e.tensor_mul(SM2[:, 3:4], SM2[:, 2:3], SM2[:, 2:3]), r=(SM2,), w=(SM2,))
                E.op("vector", lambda e: e.scalar_tensor_tensor(
                    SM2[:, 4:5], SM2[:, 1:2], 1.0 / D, SM2[:, 3:4], ALU.mult, ALU.subtract), r=(SM2,), w=(SM2,))
                E.op("vector", lambda e, p=eps: e.tensor_scalar(
                    SM2[:, 4:5], SM2[:, 4:5], p, None, ALU.add), sv, 1, r=(SM2,), w=(SM2,))
                E.wait("scalar", sv)
                E.op("scalar", lambda e: e.sqrt(SM2[:, 5:6], SM2[:, 4:5]), ss, 1, r=(SM2,), w=(SM2,))
                E.wait("vector", ss)
                E.op("vector", lambda e: e.reciprocal(SM2[:, 6:7], SM2[:, 5:6]), r=(SM2,), w=(SM2,))
                E.op("vector", lambda e: e.tensor_mul(SM2[:, 7:8], SM2[:, 2:3], SM2[:, 6:7]), r=(SM2,), w=(SM2,))
                E.op("vector", lambda e: e.tensor_scalar(
                    SM2[:, 7:8], SM2[:, 7:8], -1.0, None, ALU.mult), sv, 1, r=(SM2,), w=(SM2,))
                E.wait("scalar", sv)
                E.op("scalar", lambda e, o=out_bf, x=xsrc: e.activation(
                    o, x, AF.Identity, bias=SM2[:, 7:8], scale=SM2[:, 6:7]), ss, 1,
                    r=(xsrc, SM2), w=(out_bf,))

            def trans2(src_bf, dst_ap, nchunk, dst_name):
                E.wait("tensor", sv)
                E.wait("tensor", ss)
                for j in range(nchunk):
                    E.trans(PH2[:, 128 * j:128 * (j + 1)],
                            src_bf[:, 128 * j:128 * (j + 1)], ID2[:],
                            st if j == nchunk - 1 else None, 1)
                E.wait("vector", st)
                E.op("vector", lambda e, d=dst_ap, n=nchunk: e.tensor_copy(
                    d, PH2[:, 0:128 * n]), sv, 1, w=(dst_name,))

            E.wait("sync", sd)
            E.wait("tensor", sw)
            E.wait("tensor", sg)

            nsc = (nt + 7) // 8
            for sc in range(nsc):
                tl = list(range(sc * 8, min(nt, sc * 8 + 8)))
                for t in tl:
                    ts_ = t - sc * 8
                    par = t % 2
                    E.wait("sync", st)
                    E.op("sync", lambda e, t=t, par=par: e.dma_start(
                        LTK[:, par, :], d_ln1[128 * t:128 * (t + 1), :]), sxp[par], 16)
                    E.wait("tensor", sxp[par])
                    trans2(LTK[:, par, :], LN1F[:, ts_, :], 8, LN1F)
                    E.wait("tensor", sv)
                    E.wait("tensor", ss)
                    for j in range(8):
                        lhsT = LN1F[:, ts_, 128 * j:128 * (j + 1)]
                        E.mm(PB[:, 6, 0:8], lhsT, RT[:, j, :], j == 0, j == 7)
                        for hb in range(2):
                            E.mm(PB[:, hb, :], lhsT, SH[:, j, 512 * hb:512 * (hb + 1)],
                                 j == 0, j == 7)
                        for hb in range(2):
                            E.mm(PB[:, 2 + hb, :], lhsT, SH[:, 8 + j, 512 * hb:512 * (hb + 1)],
                                 j == 0, j == 7, st if j == 7 and hb == 1 else None, 1)
                    E.wait("scalar", st)
                    E.op("scalar", lambda e: e.activation(SGb[:, 0:512], PB[:, 0, :], AF.Silu), w=(SGb,))
                    E.op("scalar", lambda e: e.activation(SGb[:, 512:1024], PB[:, 1, :], AF.Silu), w=(SGb,))
                    E.op("scalar", lambda e: e.activation(
                        SM2[:, 16:24], PB[:, 6, 0:8], AF.Exp,
                        accum_out=SM2[:, 24:25]), ss, 1, w=(SM2,))
                    E.wait("vector", ss)
                    E.op("vector", lambda e: e.tensor_tensor(
                        HS2[:, 0:512], PB[:, 2, :], SGb[:, 0:512], ALU.mult),
                        r=(SGb,), w=(HS2,))
                    E.op("vector", lambda e: e.tensor_tensor(
                        HS2[:, 512:1024], PB[:, 3, :], SGb[:, 512:1024], ALU.mult), sv, 1,
                        r=(SGb,), w=(HS2,))
                    E.op("vector", lambda e: e.reciprocal(SM2[:, 25:26], SM2[:, 24:25]), r=(SM2,), w=(SM2,))
                    E.op("vector", lambda e: e.tensor_scalar(
                        SM2[:, 16:24], SM2[:, 16:24], SM2[:, 25:26], None, ALU.mult), r=(SM2,), w=(SM2,))
                    T88 = SM2[:, 32:96].rearrange("p (a b) -> p a b", a=8)
                    p8 = SM2[:, 16:24]
                    E.op("vector", lambda e, T=T88: e.scalar_tensor_tensor(
                        T, p8.unsqueeze(2).broadcast_to([128, 8, 8]), -1.0,
                        p8.unsqueeze(1).broadcast_to([128, 8, 8]),
                        ALU.mult, ALU.add), r=(SM2,), w=(SM2,))
                    E.op("vector", lambda e, T=T88: e.scalar_tensor_tensor(
                        T, T, 1e8, SM2[:, 96:97].unsqueeze(1).broadcast_to([128, 8, 8]),
                        ALU.mult, ALU.add), sv, 1, r=(SM2,), w=(SM2,))
                    E.wait("scalar", sv)
                    E.op("scalar", lambda e: e.activation(SM2[:, 32:96], SM2[:, 32:96], AF.Relu),
                         ss, 1, r=(SM2,), w=(SM2,))
                    E.wait("vector", ss)
                    E.op("vector", lambda e, T=T88: e.scalar_tensor_tensor(
                        T, T, -1.0, SM2[:, 96:97].unsqueeze(1).broadcast_to([128, 8, 8]),
                        ALU.mult, ALU.add), sv, 1, r=(SM2,), w=(SM2,))
                    E.wait("scalar", sv)
                    E.op("scalar", lambda e: e.activation(SM2[:, 32:96], SM2[:, 32:96], AF.Relu),
                         ss, 1, r=(SM2,), w=(SM2,))
                    E.wait("vector", ss)
                    E.op("vector", lambda e, T=T88: e.reduce_sum(SM2[:, 104:112], T, AX.X),
                         r=(SM2,), w=(SM2,))
                    E.op("vector", lambda e: e.tensor_scalar(
                        SM2[:, 104:112], SM2[:, 104:112], SM2[:, 97:98], None, ALU.add), sv, 1,
                        r=(SM2,), w=(SM2,))
                    E.wait("scalar", sv)
                    E.op("scalar", lambda e: e.activation(SM2[:, 104:112], SM2[:, 104:112], AF.Relu),
                         ss, 1, r=(SM2,), w=(SM2,))
                    E.wait("vector", ss)
                    E.op("vector", lambda e: e.scalar_tensor_tensor(
                        SM2[:, 104:112], SM2[:, 104:112], -1.0,
                        SM2[:, 96:97].broadcast_to([128, 8]), ALU.mult, ALU.add), sv, 1,
                        r=(SM2,), w=(SM2,))
                    E.wait("scalar", sv)
                    E.op("scalar", lambda e: e.activation(SM2[:, 104:112], SM2[:, 104:112], AF.Relu),
                         ss, 1, r=(SM2,), w=(SM2,))
                    E.wait("vector", ss)
                    E.op("vector", lambda e: e.scalar_tensor_tensor(
                        SM2[:, 104:112], SM2[:, 104:112], -1.0,
                        SM2[:, 96:97].broadcast_to([128, 8]), ALU.mult, ALU.add),
                        r=(SM2,), w=(SM2,))
                    E.op("vector", lambda e, t=ts_: e.tensor_tensor(
                        CFA[:, t, :], SM2[:, 16:24], SM2[:, 104:112], ALU.mult), sv, 1,
                        r=(SM2,), w=(CFA,))
                    trans2(HS2[:], HST[:], 8, HST)
                    E.wait("tensor", sv)
                    for j in range(8):
                        for hb in range(2):
                            E.mm(PB[:, 4 + hb, :], HST[:, 128 * j:128 * (j + 1)],
                                 SH[:, 16 + j, 512 * hb:512 * (hb + 1)],
                                 j == 0, j == 7, st if j == 7 and hb == 1 else None, 1)
                    E.wait("vector", st)
                    E.op("vector", lambda e, t=ts_: e.tensor_copy(ACC[:, t, 0:512], PB[:, 4, :]), w=(ACC,))
                    E.op("vector", lambda e, t=ts_: e.tensor_copy(ACC[:, t, 512:1024], PB[:, 5, :]), sv, 1, w=(ACC,))

                sv_hcons = None
                for ei in range(8):
                    par = ei % 2
                    if ei < 7 or sc + 1 < nsc:
                        E.wait("gpsimd", sec)
                        eload((ei + 1) % 8, 1 - par)
                    E.wait("tensor", sep[par])
                    for t in tl:
                        ts_ = t - sc * 8
                        E.wait("tensor", ss)
                        E.wait("tensor", sv, sv_hcons)
                        for j in range(8):
                            lhsT = LN1F[:, ts_, 128 * j:128 * (j + 1)]
                            E.mm(PB[:, 0, :], lhsT, EWb[par][:, 512 * j:512 * (j + 1)],
                                 j == 0, j == 7)
                            E.mm(PB[:, 1, :], lhsT, EWb[par][:, 4096 + 512 * j:4096 + 512 * (j + 1)],
                                 j == 0, j == 7, st if j == 7 else None, 1)
                        E.wait("scalar", st)
                        E.op("scalar", lambda e: e.activation(SGb[:, 0:512], PB[:, 0, :], AF.Silu),
                             ss, 1, w=(SGb,))
                        E.wait("vector", ss)
                        E.op("vector", lambda e: e.tensor_tensor(
                            HE[:], PB[:, 1, :], SGb[:, 0:512], ALU.mult), r=(SGb,), w=(HE,))
                        E.op("vector", lambda e, t=ts_, ei=ei: e.tensor_scalar(
                            HE[:], HE[:], CFA[:, t, ei:ei + 1], None, ALU.mult), sv, 1,
                            r=(HE, CFA), w=(HE,))
                        sv_hcons = sv.n
                        trans2(HE[:], HET[:], 4, HET)
                        E.wait("tensor", sv)
                        for jd in range(4):
                            for hb in range(2):
                                E.mm(PB[:, 2 + hb, :], HET[:, 128 * jd:128 * (jd + 1)],
                                     EWb[par][:, 8192 + 1024 * jd + 512 * hb:8192 + 1024 * jd + 512 * (hb + 1)],
                                     jd == 0, jd == 3, st if jd == 3 and hb == 1 else None, 1)
                        E.wait("vector", st)
                        E.op("vector", lambda e, t=ts_: e.scalar_tensor_tensor(
                            ACC[:, t, 0:512], PB[:, 2, :], 1.0, ACC[:, t, 0:512],
                            ALU.mult, ALU.add), r=(ACC,), w=(ACC,))
                        E.op("vector", lambda e, t=ts_: e.scalar_tensor_tensor(
                            ACC[:, t, 512:1024], PB[:, 3, :], 1.0, ACC[:, t, 512:1024],
                            ALU.mult, ALU.add), sv, 1, r=(ACC,), w=(ACC,))
                        if t == tl[-1]:
                            E.op("vector", lambda e: e.memzero(SM2[:, 98:99]),
                                 sec, 1, w=(SM2,))

                for t in tl:
                    ts_ = t - sc * 8
                    par = t % 2
                    E.wait("sync", sv)
                    E.op("sync", lambda e, t=t, par=par: e.dma_start(
                        XT1[:, par, :], d_x1[128 * t:128 * (t + 1), :]), sxq[par], 16)
                    E.wait("vector", sxq[par])
                    E.wait("vector", sd)
                    E.op("vector", lambda e, t=ts_, par=par: e.tensor_add(
                        X2T[:], ACC[:, t, :], XT1[:, par, :]), sv, 1,
                        r=(ACC,), w=(X2T,))
                    E.wait("scalar", sv)
                    emit_ln2(X2T[:], LN2T[:])
                    ln2_ss = ss.n
                    E.wait("sync", sv)
                    E.op("sync", lambda e, t=t: e.dma_start(
                        d_x2[128 * t:128 * (t + 1), :], X2T[:]), sd, 16)
                    if y_x2 is not None:
                        E.op("sync", lambda e, t=t: e.dma_start(
                            y_x2[128 * t:128 * (t + 1), :], X2T[:]), sd, 16)
                    E.wait("sync", ss, ln2_ss)
                    E.op("sync", lambda e, t=t: e.dma_start(
                        d_ln2[128 * t:128 * (t + 1), :], LN2T[:]), sd, 16)

            with nc.named_scope("s2"):
                P.run_block()
            E.barrier_reset()

        if max_phase >= 12:
          with ExitStack() as c3:
            en = c3.enter_context
            W3 = en(nc.sbuf_tensor("W3", [128, 26, 1024], BF16))
            MK3 = en(nc.sbuf_tensor("MK3", [128, 4, 128], BF16))
            GMB = en(nc.sbuf_tensor("GMB", [128, 1024], F32))
            BTB = en(nc.sbuf_tensor("BTB", [128, 1024], F32))
            RPT = en(nc.sbuf_tensor("RPT", [128, 2, 1024], BF16))
            LTK3 = en(nc.sbuf_tensor("LTK3", [128, 2, 1024], BF16))
            XTK3 = en(nc.sbuf_tensor("XTK3", [128, 2, 1024], F32))
            LN2F = en(nc.sbuf_tensor("LN2F", [128, 1024], BF16))
            QT3 = en(nc.sbuf_tensor("QT3", [128, 1024], BF16))
            KT3 = en(nc.sbuf_tensor("KT3", [128, 1024], BF16))
            VT3 = en(nc.sbuf_tensor("VT3", [128, 1024], BF16))
            QR3 = en(nc.sbuf_tensor("QR3", [128, 1024], BF16))
            KR3 = en(nc.sbuf_tensor("KR3", [128, 1024], BF16))
            RS3 = en(nc.sbuf_tensor("RS3", [128, 1024], F32))
            QF3b = en(nc.sbuf_tensor("QF3b", [128, 1024], BF16))
            QRF3 = en(nc.sbuf_tensor("QRF3", [128, 1024], BF16))
            KVP = en(nc.sbuf_tensor("KVP", [128, 520], F32))
            KVQ = en(nc.sbuf_tensor("KVQ", [128, 520], F32))
            KVSh3 = en(nc.sbuf_tensor("KVSh3", [128, 512], BF16))
            KMH = en(nc.sbuf_tensor("KMH", [128, 8, 16], BF16))
            ONES3 = en(nc.sbuf_tensor("ONES3", [128, 1], BF16))
            SL3 = en(nc.sbuf_tensor("SL3", [1, 1], U32))
            FL3 = en(nc.sbuf_tensor("FL3", [1, 16], U32))
            FLo = en(nc.sbuf_tensor("FLo", [1, 16], U32))
            RES3 = en(nc.sbuf_tensor("RES3", [128, 1024], F32))
            LNO3 = en(nc.sbuf_tensor("LNO3", [128, 1024], F32))
            X3T = en(nc.sbuf_tensor("X3T", [128, 1024], F32))
            LN3T = en(nc.sbuf_tensor("LN3T", [128, 1024], BF16))
            SM3 = en(nc.sbuf_tensor("SM3", [128, 128], F32))
            SQ3 = en(nc.sbuf_tensor("SQ3", [128, 1024], F32))
            PC = en(nc.psum_tensor("PC", [128, 7, 512], F32))
            PH3 = en(nc.psum_tensor("PH3", [128, 1024], BF16))

            for c0 in range(0, 26, 4):
                c1_ = min(26, c0 + 4)
                E.op("gpsimd", lambda e, c0=c0, c1_=c1_: e.dma_start(
                    W3[:, c0:c1_, :], w3T[:, c0:c1_, :]), sw, 16)
            for i, sl_ in enumerate((1, 7, 8, 9)):
                E.op("gpsimd", lambda e, i=i, sl_=sl_: e.dma_start(
                    MK3[:, i, :], masks[:, sl_, :]), sw, 16)
            E.op("gpsimd", lambda e: e.dma_start(SL3[:], slot[:]), sw, 16)
            E.op("gpsimd", lambda e: e.memset(FLo[:], 7))
            E.op("gpsimd", lambda e: e.memset(ONES3[:], 1.0))
            E.op("gpsimd", lambda e: e.memset(KVP[:], 0.0))
            E.op("gpsimd", lambda e: e.memset(KVQ[:], 0.0))
            E.op("gpsimd", lambda e: e.memset(KMH[:], 0.0), sg, 1)
            E.wait("tensor", sw)
            E.wait("tensor", sg)
            for hb in range(2):
                E.mm(PC[:, hb, :], MK3[:, 1, :], W3[:, 25, 512 * hb:512 * (hb + 1)],
                     True, True)
                E.mm(PC[:, 2 + hb, :], MK3[:, 2, :], W3[:, 25, 512 * hb:512 * (hb + 1)],
                     True, True, st if hb == 1 else None, 1)
            E.wait("vector", st)
            for hb in range(2):
                E.op("vector", lambda e, hb=hb: e.tensor_copy(
                    GMB[:, 512 * hb:512 * (hb + 1)], PC[:, hb, :]), w=(GMB,))
                E.op("vector", lambda e, hb=hb: e.tensor_copy(
                    BTB[:, 512 * hb:512 * (hb + 1)], PC[:, 2 + hb, :]), sv, 1, w=(BTB,))
            E.wait("sync", sd)

            def trans3(src_bf, dst_ap, nchunk, dst_name, wait_sv=False):
                E.wait("tensor", stf)
                if wait_sv:
                    E.wait("tensor", sv)
                E.wait("tensor", ss)
                for j in range(nchunk):
                    E.trans(PH3[:, 128 * j:128 * (j + 1)],
                            src_bf[:, 128 * j:128 * (j + 1)], MK3[:, 0, :],
                            st if j == nchunk - 1 else None, 1)
                E.wait("vector", st)
                E.op("vector", lambda e, d=dst_ap, n=nchunk: e.tensor_copy(
                    d, PH3[:, 0:128 * n]), stf, 1, w=(dst_name,))

            def emit_ln3(xsrc, out_t, eps):
                E.op("vector", lambda e, x=xsrc: e.reduce_sum(SM3[:, 0:1], x, AX.X), sv, 1,
                     r=(xsrc,), w=(SM3,))
                E.op("scalar", lambda e, x=xsrc: e.activation(
                    SQ3[:], x, AF.Square, accum_out=SM3[:, 1:2]), ss, 1,
                    r=(xsrc,), w=(SQ3, SM3))
                E.wait("vector", ss)
                E.op("vector", lambda e: e.tensor_scalar(
                    SM3[:, 2:3], SM3[:, 0:1], 1.0 / D, None, ALU.mult), r=(SM3,), w=(SM3,))
                E.op("vector", lambda e: e.tensor_mul(SM3[:, 3:4], SM3[:, 2:3], SM3[:, 2:3]), r=(SM3,), w=(SM3,))
                E.op("vector", lambda e: e.scalar_tensor_tensor(
                    SM3[:, 4:5], SM3[:, 1:2], 1.0 / D, SM3[:, 3:4], ALU.mult, ALU.subtract), r=(SM3,), w=(SM3,))
                E.op("vector", lambda e, p=eps: e.tensor_scalar(
                    SM3[:, 4:5], SM3[:, 4:5], p, None, ALU.add), sv, 1, r=(SM3,), w=(SM3,))
                E.wait("scalar", sv)
                E.op("scalar", lambda e: e.sqrt(SM3[:, 5:6], SM3[:, 4:5]), ss, 1, r=(SM3,), w=(SM3,))
                E.wait("vector", ss)
                E.op("vector", lambda e: e.reciprocal(SM3[:, 6:7], SM3[:, 5:6]), r=(SM3,), w=(SM3,))
                E.op("vector", lambda e: e.tensor_mul(SM3[:, 7:8], SM3[:, 2:3], SM3[:, 6:7]), r=(SM3,), w=(SM3,))
                E.op("vector", lambda e: e.tensor_scalar(
                    SM3[:, 7:8], SM3[:, 7:8], -1.0, None, ALU.mult), sv, 1, r=(SM3,), w=(SM3,))
                E.wait("scalar", sv)
                E.op("scalar", lambda e, o=out_t, x=xsrc: e.activation(
                    o, x, AF.Identity, bias=SM3[:, 7:8], scale=SM3[:, 6:7]), ss, 1,
                    r=(xsrc, SM3), w=(out_t,))

            def emit_rope(src_bf, dst_bf, par):
                xr = src_bf.rearrange("p (a b) -> p a b", a=512)
                dr = dst_bf.rearrange("p (a b) -> p a b", a=512)
                cos3 = RPT[:, par, 0:512].unsqueeze(2)
                sin3 = RPT[:, par, 512:1024].unsqueeze(2)
                t1 = RS3[:, 0:512].rearrange("p (a b) -> p a b", a=512)
                t2 = RS3[:, 512:1024].rearrange("p (a b) -> p a b", a=512)
                E.op("vector", lambda e: e.tensor_tensor(
                    t1, xr[:, :, 0:1], cos3, ALU.mult), r=(src_bf, RPT), w=(RS3,))
                E.op("vector", lambda e: e.tensor_tensor(
                    t2, xr[:, :, 1:2], sin3, ALU.mult), r=(src_bf, RPT), w=(RS3,))
                E.op("vector", lambda e: e.scalar_tensor_tensor(
                    dr[:, :, 0:1], t2, -1.0, t1, ALU.mult, ALU.add),
                    r=(RS3,), w=(dst_bf,))
                E.op("vector", lambda e: e.tensor_tensor(
                    t1, xr[:, :, 1:2], cos3, ALU.mult), r=(src_bf, RPT), w=(RS3,))
                E.op("vector", lambda e: e.tensor_tensor(
                    t2, xr[:, :, 0:1], sin3, ALU.mult), r=(src_bf, RPT), w=(RS3,))
                E.op("vector", lambda e: e.tensor_add(
                    dr[:, :, 1:2], t1, t2), sv, 1, r=(RS3,), w=(dst_bf,))

            sv_kcons = None
            for t in range(nt):
                par = t % 2
                E.wait("sync", st)
                E.wait("sync", sv)
                E.op("sync", lambda e, t=t, par=par: e.dma_start(
                    LTK3[:, par, :], d_ln2[128 * t:128 * (t + 1), :]), sxp[par], 16)
                E.op("sync", lambda e, t=t, par=par: e.dma_start(
                    RPT[:, par, :], d_rope[128 * t:128 * (t + 1), :]), sxp[par], 16)
                E.wait("tensor", sxp[par])
                trans3(LTK3[:, par, :], LN2F[:], 8, LN2F)
                E.wait("tensor", stf)
                E.wait("tensor", sv, sv_kcons)
                E.wait("tensor", ss)
                for j in range(8):
                    lhsT = LN2F[:, 128 * j:128 * (j + 1)]
                    for hb in range(2):
                        E.mm(PC[:, hb, :], lhsT, W3[:, 8 + j, 512 * hb:512 * (hb + 1)],
                             j == 0, False)
                    for hb in range(2):
                        E.mm(PC[:, 2 + hb, :], lhsT, W3[:, 16 + j, 512 * hb:512 * (hb + 1)],
                             j == 0, False)
                for hb in range(2):
                    E.mm(PC[:, hb, :], MK3[:, 2, :], W3[:, 24, 512 * hb:512 * (hb + 1)],
                         False, True)
                    E.mm(PC[:, 2 + hb, :], MK3[:, 3, :], W3[:, 24, 512 * hb:512 * (hb + 1)],
                         False, True, st if hb == 1 else None, 1)
                E.wait("scalar", st)
                E.op("scalar", lambda e: e.activation(KT3[:, 0:512], PC[:, 0, :], AF.Gelu), w=(KT3,))
                E.op("scalar", lambda e: e.activation(KT3[:, 512:1024], PC[:, 1, :], AF.Gelu), ss, 1, w=(KT3,))
                E.wait("vector", ss)
                E.op("vector", lambda e: e.tensor_scalar(
                    KT3[:], KT3[:], 0.21, None, ALU.add), r=(KT3,), w=(KT3,))
                E.op("vector", lambda e: e.tensor_copy(VT3[:, 0:512], PC[:, 2, :]), w=(VT3,))
                E.op("vector", lambda e: e.tensor_copy(VT3[:, 512:1024], PC[:, 3, :]), sv, 1, w=(VT3,))
                sv_kcons = sv.n
                emit_rope(KT3[:], KR3[:], par)
                E.wait("tensor", sv)
                for j in range(8):
                    E.mm(PC[:, 4 + j // 4, (j % 4) * 128:(j % 4) * 128 + 128],
                         KR3[:, 128 * j:128 * (j + 1)], VT3[:, 128 * j:128 * (j + 1)],
                         j % 4 == 0, j % 4 == 3, None, 1)
                    E.mm(PC[:, 6, j:j + 1], KT3[:, 128 * j:128 * (j + 1)], ONES3[:],
                         True, True, st if j == 7 else None, 1)
                E.wait("vector", st)
                for half in range(2):
                    pr = slice(64 * half, 64 * (half + 1))
                    co = slice(64 * half, 64 * half + 64)
                    E.op("vector", lambda e, pr=pr, co=co: e.tensor_add(
                        KVP[pr, 0:512].rearrange("p (a b) -> p a b", a=8),
                        KVP[pr, 0:512].rearrange("p (a b) -> p a b", a=8),
                        PC[pr, 4:6, :].rearrange("p a (b c) -> p (a b) c", b=4)[:, :, co]),
                        r=(KVP,), w=(KVP,))
                E.op("vector", lambda e: e.tensor_add(
                    KVP[:, 512:520], KVP[:, 512:520], PC[:, 6, 0:8]), sv, 1,
                    r=(KVP,), w=(KVP,))

            if not solo:
                E.wait("gpsimd", sv)

                def exch(g, base=sg.n, spl=spl):
                    with g.register("sl3") as rsl:
                        g.reg_load(rsl, SL3[0:1, 0:1].bitcast(I32))
                        for me in (0, 1):
                            sib = 1 - me
                            with g.If_eq(rsl, me):
                                g.dma_start(d_ex[me, :, :], KVP[:]).then_inc(sg.h, 16)
                                g.wait_ge(sg.h, base + 16)
                                g.dma_start(d_exf[me:me + 1, :], FLo[:]).then_inc(sg.h, 16)
                                g.wait_ge(sg.h, base + 32)
                                with g.register(f"fl3{me}") as rfl:
                                    g.reg_mov(rfl, 0)
                                    with g.Fori(0, 4000) as _i:
                                        with g.If_ne(rfl, 7):
                                            g.dma_start(FL3[:], d_exf[sib:sib + 1, :]).then_inc(spl.h, 16)
                                            g.drain()
                                            g.reg_load(rfl, FL3[0:1, 0:1].bitcast(I32))
                                g.dma_start(KVQ[:], d_ex[sib, :, :]).then_inc(sg.h, 16)
                                g.wait_ge(sg.h, base + 48)
                            g.end_ifs()
                E.op("gpsimd", exch, None, 0, r=(KVP,), w=(KVQ,))
                sg.n += 48
            E.wait("vector", sg)
            E.op("vector", lambda e: e.tensor_add(RS3[:, 0:512], KVP[:, 0:512], KVQ[:, 0:512]),
                 r=(KVP, KVQ), w=(RS3,))
            E.op("vector", lambda e: e.tensor_scalar(
                RS3[:, 0:512], RS3[:, 0:512], 1.0 / 8192.0, None, ALU.mult), r=(RS3,), w=(RS3,))
            E.op("vector", lambda e: e.tensor_copy(KVSh3[:], RS3[:, 0:512]), r=(RS3,), w=(KVSh3,))
            E.op("vector", lambda e: e.tensor_add(SM3[:, 8:16], KVP[:, 512:520], KVQ[:, 512:520]),
                 r=(KVP, KVQ), w=(SM3,))
            E.op("vector", lambda e: e.tensor_scalar(
                SM3[:, 8:16], SM3[:, 8:16], 1.0 / 8192.0, None, ALU.mult), r=(SM3,), w=(SM3,))
            for j in range(8):
                E.op("vector", lambda e, j=j: e.tensor_copy(
                    KMH[0:64, j, 2 * j:2 * j + 1], SM3[0:64, 8 + j:9 + j]), r=(SM3,), w=(KMH,))
                E.op("vector", lambda e, j=j: e.tensor_copy(
                    KMH[64:128, j, 2 * j + 1:2 * j + 2], SM3[64:128, 8 + j:9 + j]),
                    sv if j == 7 else None, 1, r=(SM3,), w=(KMH,))

            for t in range(nt):
                par = t % 2
                E.wait("sync", st)
                E.wait("sync", sv)
                E.op("sync", lambda e, t=t, par=par: e.dma_start(
                    LTK3[:, par, :], d_ln2[128 * t:128 * (t + 1), :]), sxp[par], 16)
                E.op("sync", lambda e, t=t, par=par: e.dma_start(
                    RPT[:, par, :], d_rope[128 * t:128 * (t + 1), :]), sxp[par], 16)
                E.wait("sync", sv)
                E.op("sync", lambda e, t=t, par=par: e.dma_start(
                    XTK3[:, par, :], d_x2[128 * t:128 * (t + 1), :]), sxq[par], 16)
                E.wait("tensor", sxp[par])
                trans3(LTK3[:, par, :], LN2F[:], 8, LN2F)
                E.wait("tensor", stf)
                E.wait("tensor", ss)
                for j in range(8):
                    lhsT = LN2F[:, 128 * j:128 * (j + 1)]
                    for hb in range(2):
                        E.mm(PC[:, hb, :], lhsT, W3[:, j, 512 * hb:512 * (hb + 1)],
                             j == 0, False)
                for hb in range(2):
                    E.mm(PC[:, hb, :], MK3[:, 1, :], W3[:, 24, 512 * hb:512 * (hb + 1)],
                         False, True, st if hb == 1 else None, 1)
                E.wait("scalar", st)
                E.op("scalar", lambda e: e.activation(QT3[:, 0:512], PC[:, 0, :], AF.Gelu), w=(QT3,))
                E.op("scalar", lambda e: e.activation(QT3[:, 512:1024], PC[:, 1, :], AF.Gelu), ss, 1, w=(QT3,))
                E.wait("vector", ss)
                E.op("vector", lambda e: e.tensor_scalar(
                    QT3[:], QT3[:], 0.21, None, ALU.add), sv, 1, r=(QT3,), w=(QT3,))
                emit_rope(QT3[:], QR3[:], par)
                trans3(QT3[:], QF3b[:], 8, QF3b, wait_sv=True)
                trans3(QR3[:], QRF3[:], 8, QRF3, wait_sv=True)
                E.wait("tensor", stf)
                E.wait("tensor", sv)
                for j in range(8):
                    E.mm(PC[:, 6, 0:16], QF3b[:, 128 * j:128 * (j + 1)], KMH[:, j, :],
                         j == 0, j == 7)
                for h in range(0, 16, 2):
                    E.mm(PC[:, 2, 64 * (h // 2):64 * (h // 2) + 64],
                         hs(QRF3, h), KVSh3[0:64, 64 * (h // 2):64 * (h // 2) + 64],
                         True, True, st if h == 14 else None, 1, tp=(0, 0))
                for h in range(1, 16, 2):
                    E.mm(PC[:, 3, 64 * (h // 2):64 * (h // 2) + 64],
                         hs(QRF3, h), KVSh3[64:128, 64 * (h // 2):64 * (h // 2) + 64],
                         True, True, st if h == 15 else None, 1, tp=(64, 0))
                E.wait("vector", st)
                E.op("vector", lambda e: e.tensor_scalar(
                    SM3[:, 16:32], PC[:, 6, 0:16], 1e-6, None, ALU.add), w=(SM3,))
                E.op("vector", lambda e: e.reciprocal(SM3[:, 16:32], SM3[:, 16:32]),
                     r=(SM3,), w=(SM3,))
                zb = SM3[:, 16:32].rearrange("p (a b) -> p a b", a=8) \
                    .unsqueeze(3).broadcast_to([128, 8, 2, 64])
                E.op("vector", lambda e, zb=zb: e.tensor_tensor(
                    RES3[:].rearrange("p (a b c) -> p a b c", a=8, b=2),
                    PC[:, 2:4, :].rearrange("p a (b c) -> p b a c", b=8),
                    zb, ALU.mult), r=(SM3,), w=(RES3,))
                E.op("vector", lambda e, par=par: e.tensor_add(
                    RES3[:], RES3[:], LTK3[:, par, :]), sv, 1, r=(RES3,), w=(RES3,))
                E.wait("scalar", sv)
                emit_ln3(RES3[:], LNO3[:], 1e-12)
                E.wait("vector", ss)
                E.op("vector", lambda e: e.tensor_tensor(RS3[:], LNO3[:], GMB[:], ALU.mult),
                     r=(LNO3, GMB), w=(RS3,))
                E.op("vector", lambda e: e.tensor_add(RS3[:], RS3[:], BTB[:]),
                     r=(RS3, BTB), w=(RS3,))
                E.wait("vector", sxq[par])
                E.wait("vector", sd)
                E.op("vector", lambda e, par=par: e.tensor_add(
                    X3T[:], RS3[:], XTK3[:, par, :]), sv, 1, r=(RS3,), w=(X3T,))
                E.wait("scalar", sv)
                emit_ln3(X3T[:], LN3T[:], 1e-5)
                ln3_ss = ss.n
                E.wait("sync", sv)
                E.op("sync", lambda e, t=t: e.dma_start(
                    d_x3[128 * t:128 * (t + 1), :], X3T[:]), sd, 16)
                if y_x3 is not None:
                    E.op("sync", lambda e, t=t: e.dma_start(
                        y_x3[128 * t:128 * (t + 1), :], X3T[:]), sd, 16)
                E.wait("sync", ss, ln3_ss)
                E.op("sync", lambda e, t=t: e.dma_start(
                    d_ln3[128 * t:128 * (t + 1), :], LN3T[:]), sd, 16)

            with nc.named_scope("s3"):
                P.run_block()
            E.barrier_reset()

        if max_phase >= 13:
          with ExitStack() as c4:
            en = c4.enter_context
            EA = en(nc.sbuf_tensor("EA4", [128, 17, 1024], BF16))
            EB = en(nc.sbuf_tensor("EB4", [128, 17, 1024], BF16))
            WR4s = en(nc.sbuf_tensor("WR4s", [128, 9, 8], BF16))
            MK4 = en(nc.sbuf_tensor("MK4", [128, 3, 128], BF16))
            LTK4 = en(nc.sbuf_tensor("LTK4", [128, 2, 1024], BF16))
            XTK4 = en(nc.sbuf_tensor("XTK4", [128, 2, 1024], F32))
            LN3F = en(nc.sbuf_tensor("LN3F", [128, 1024], BF16))
            HA4 = en(nc.sbuf_tensor("HA4", [128, 1024], BF16))
            HB4 = en(nc.sbuf_tensor("HB4", [128, 1024], BF16))
            HAF = en(nc.sbuf_tensor("HAF", [128, 1024], BF16))
            HBF = en(nc.sbuf_tensor("HBF", [128, 1024], BF16))
            X4T = en(nc.sbuf_tensor("X4T", [128, 2, 1024], F32))
            SM4 = en(nc.sbuf_tensor("SM4", [128, 128], F32))
            SL4 = en(nc.sbuf_tensor("SL4", [1, 1], U32))
            FL4 = en(nc.sbuf_tensor("FL4", [1, 1], U32))
            FLo4 = en(nc.sbuf_tensor("FLo4", [1, 1], U32))
            PD = en(nc.psum_tensor("PD", [128, 7, 512], F32))
            PH4 = en(nc.psum_tensor("PH4", [128, 1024], BF16))

            E.op("gpsimd", lambda e: e.dma_start(WR4s[:], wr4[:]), sw, 16)
            E.op("gpsimd", lambda e: e.dma_start(MK4[:, 0, :], masks[:, 1, :]), sw, 16)
            E.op("gpsimd", lambda e: e.dma_start(MK4[:, 1, :], masks[:, 7, :]), sw, 16)
            E.op("gpsimd", lambda e: e.dma_start(MK4[:, 2, :], masks[:, 8, :]), sw, 16)
            E.op("gpsimd", lambda e: e.dma_start(SL4[:], slot[:]), sw, 16)
            E.op("gpsimd", lambda e: e.memset(FLo4[:], 9))
            E.op("gpsimd", lambda e: e.memset(SM4[:, 32:33], 1.0), sg, 1)
            E.wait("sync", sd)
            E.wait("tensor", sw)
            E.wait("tensor", sg)

            def trans4(src_bf, dst_ap, nchunk, dst_name):
                E.wait("tensor", stf)
                E.wait("tensor", ss)
                for j in range(nchunk):
                    E.trans(PH4[:, 128 * j:128 * (j + 1)],
                            src_bf[:, 128 * j:128 * (j + 1)], MK4[:, 0, :],
                            st if j == nchunk - 1 else None, 1)
                E.wait("vector", st)
                E.op("vector", lambda e, d=dst_ap, n=nchunk: e.tensor_copy(
                    d, PH4[:, 0:128 * n]), stf, 1, w=(dst_name,))

            E.op("sync", lambda e: e.dma_start(
                LTK4[:, 0, :], d_ln3[0:128, :]), sxp[0], 16)
            E.wait("tensor", sxp[0])
            trans4(LTK4[:, 0, :], LN3F[:], 8, LN3F)
            E.wait("tensor", stf)
            for j in range(8):
                E.mm(PD[:, 6, 0:8], LN3F[:, 128 * j:128 * (j + 1)], WR4s[:, j, :],
                     j == 0, False)
            E.mm(PD[:, 6, 0:8], MK4[:, 1, :], WR4s[:, 8, :], False, True, st, 1)
            E.wait("vector", st)
            E.op("vector", lambda e: e.tensor_copy(SM4[:, 0:8], PD[:, 6, 0:8]), sv, 1,
                 w=(SM4,))
            if not solo:
                E.wait("gpsimd", sv)

                def exch4(g, base=sg.n, spl=spl):
                    with g.register("sl4") as rsl:
                        g.reg_load(rsl, SL4[0:1, 0:1].bitcast(I32))
                        for me in (0, 1):
                            with g.If_eq(rsl, me):
                                g.dma_start(d_ex2[me, :, :], SM4[0:1, 0:8]).then_inc(sg.h, 16)
                                g.wait_ge(sg.h, base + 16)
                                g.dma_start(d_exf[me:me + 1, 4:5], FLo4[:]).then_inc(sg.h, 16)
                                g.wait_ge(sg.h, base + 32)
                            g.end_ifs()
                    with g.register("fl4") as rfl:
                        g.reg_mov(rfl, 0)
                        with g.Fori(0, 4000) as _i:
                            with g.If_ne(rfl, 9):
                                g.dma_start(FL4[:], d_exf[0:1, 4:5]).then_inc(spl.h, 16)
                                g.drain()
                                g.reg_load(rfl, FL4[0:1, 0:1].bitcast(I32))
                    g.dma_start(SM4[0:1, 0:8], d_ex2[0, :, :]).then_inc(sg.h, 16)
                    g.wait_ge(sg.h, base + 48)
                E.op("gpsimd", exch4, None, 0, r=(SM4,), w=(SM4,))
                sg.n += 48
                E.wait("vector", sg)
            p8b = SM4[:, 0:8]
            T884 = SM4[:, 40:104].rearrange("p (a b) -> p a b", a=8)
            E.op("vector", lambda e, T=T884: e.scalar_tensor_tensor(
                T, p8b.unsqueeze(2).broadcast_to([128, 8, 8]), -1.0,
                p8b.unsqueeze(1).broadcast_to([128, 8, 8]),
                ALU.mult, ALU.add), r=(SM4,), w=(SM4,))
            E.op("vector", lambda e, T=T884: e.scalar_tensor_tensor(
                T, T, 1e8, SM4[:, 32:33].unsqueeze(1).broadcast_to([128, 8, 8]),
                ALU.mult, ALU.add), sv, 1, r=(SM4,), w=(SM4,))
            E.wait("scalar", sv)
            E.op("scalar", lambda e: e.activation(SM4[:, 40:104], SM4[:, 40:104], AF.Relu),
                 ss, 1, r=(SM4,), w=(SM4,))
            E.wait("vector", ss)
            E.op("vector", lambda e, T=T884: e.scalar_tensor_tensor(
                T, T, -1.0, SM4[:, 32:33].unsqueeze(1).broadcast_to([128, 8, 8]),
                ALU.mult, ALU.add), sv, 1, r=(SM4,), w=(SM4,))
            E.wait("scalar", sv)
            E.op("scalar", lambda e: e.activation(SM4[:, 40:104], SM4[:, 40:104], AF.Relu),
                 ss, 1, r=(SM4,), w=(SM4,))
            E.wait("vector", ss)
            E.op("vector", lambda e, T=T884: e.reduce_sum(SM4[:, 8:16], T, AX.X),
                 r=(SM4,), w=(SM4,))
            E.op("vector", lambda e: e.tensor_scalar(
                SM4[:, 16:24], SM4[:, 8:16], -5.0, None, ALU.add), r=(SM4,), w=(SM4,))
            E.op("vector", lambda e: e.tensor_scalar(
                SM4[:, 24:32], SM4[:, 8:16], -6.0, None, ALU.add), sv, 1, r=(SM4,), w=(SM4,))
            E.wait("scalar", sv)
            E.op("scalar", lambda e: e.activation(SM4[:, 16:32], SM4[:, 16:32], AF.Relu),
                 ss, 1, r=(SM4,), w=(SM4,))
            E.wait("vector", ss)
            E.op("vector", lambda e: e.scalar_tensor_tensor(
                SM4[:, 16:24], SM4[:, 16:24], -1.0,
                SM4[:, 32:33].broadcast_to([128, 8]), ALU.mult, ALU.add), sv, 1,
                r=(SM4,), w=(SM4,))
            E.wait("scalar", sv)
            E.op("scalar", lambda e: e.activation(SM4[:, 16:24], SM4[:, 16:24], AF.Relu),
                 ss, 1, r=(SM4,), w=(SM4,))
            E.wait("vector", ss)
            E.op("vector", lambda e: e.scalar_tensor_tensor(
                SM4[:, 16:24], SM4[:, 16:24], -1.0,
                SM4[:, 32:33].broadcast_to([128, 8]), ALU.mult, ALU.add),
                r=(SM4,), w=(SM4,))
            E.op("vector", lambda e: e.scalar_tensor_tensor(
                SM4[:, 24:32], SM4[:, 24:32], -1.0, SM4[:, 16:24],
                ALU.mult, ALU.add), sv, 1, r=(SM4,), w=(SM4,))
            E.op("vector", lambda e: e.scalar_tensor_tensor(
                SM4[:, 8:16], SM4[:, 24:32], -1.0, SM4[:, 16:24],
                ALU.mult, ALU.add), sv, 1, r=(SM4,), w=(SM4,))
            E.wait("gpsimd", sv)

            def gather_emit(g, base=sgw.n):
                for cols, buf in ((8, EA), (24, EB)):
                    with g.register(f"rk{cols}") as rk:
                        for e in range(8):
                            g.reg_load(rk, SM4[0:1, cols + e:cols + e + 1].bitcast(I32))
                            with g.If_eq(rk, 0x3F800000):
                                for c0 in range(0, 17, 3):
                                    c1_ = min(17, c0 + 3)
                                    g.dma_start(buf[:, c0:c1_, :],
                                                ew4[e, :, c0:c1_, :]).then_inc(sgw.h, 16)
                            g.end_ifs()
                g.wait_ge(sgw.h, base + 2 * 6 * 16)
            E.op("gpsimd", gather_emit, None, 0, r=(SM4,), w=(EA, EB))
            sgw.n += 2 * 6 * 16
            E.wait("tensor", sgw)

            def s4_load(t):
                par = t % 2
                E.op("sync", lambda e, t=t, par=par: e.dma_start(
                    LTK4[:, par, :], d_ln3[128 * t:128 * (t + 1), :]), sxp[par], 16)
                E.op("sync", lambda e, t=t, par=par: e.dma_start(
                    XTK4[:, par, :], d_x3[128 * t:128 * (t + 1), :]), sxq[par], 16)
            E.wait("sync", st)
            E.wait("sync", sv)
            s4_load(0)
            sd_mark = [None, None]
            for t in range(nt):
                par = t % 2
                if t + 1 < nt:
                    E.wait("sync", st)
                    E.wait("sync", sv)
                    s4_load(t + 1)
                E.wait("tensor", sxp[par])
                trans4(LTK4[:, par, :], LN3F[:], 8, LN3F)
                E.wait("tensor", stf)
                E.wait("tensor", ss)
                for j in range(8):
                    lhsT = LN3F[:, 128 * j:128 * (j + 1)]
                    for hb in range(2):
                        E.mm(PD[:, hb, :], lhsT, EA[:, j, 512 * hb:512 * (hb + 1)],
                             j == 0, False)
                        E.mm(PD[:, 2 + hb, :], lhsT, EB[:, j, 512 * hb:512 * (hb + 1)],
                             j == 0, False)
                for hb in range(2):
                    E.mm(PD[:, hb, :], MK4[:, 1, :], EA[:, 16, 512 * hb:512 * (hb + 1)],
                         False, True)
                    E.mm(PD[:, 2 + hb, :], MK4[:, 1, :], EB[:, 16, 512 * hb:512 * (hb + 1)],
                         False, True, st if hb == 1 else None, 1)
                E.wait("scalar", st)
                E.op("scalar", lambda e: e.activation(HA4[:, 0:512], PD[:, 0, :], AF.Gelu), w=(HA4,))
                E.op("scalar", lambda e: e.activation(HA4[:, 512:1024], PD[:, 1, :], AF.Gelu), w=(HA4,))
                E.op("scalar", lambda e: e.activation(HB4[:, 0:512], PD[:, 2, :], AF.Gelu), w=(HB4,))
                E.op("scalar", lambda e: e.activation(HB4[:, 512:1024], PD[:, 3, :], AF.Gelu), ss, 1, w=(HB4,))
                trans4(HA4[:], HAF[:], 8, HAF)
                trans4(HB4[:], HBF[:], 8, HBF)
                E.wait("tensor", stf)
                E.wait("tensor", sv)
                for j in range(8):
                    for hb in range(2):
                        E.mm(PD[:, 4 + hb, :], HAF[:, 128 * j:128 * (j + 1)],
                             EA[:, 8 + j, 512 * hb:512 * (hb + 1)], j == 0, False)
                for j in range(8):
                    for hb in range(2):
                        E.mm(PD[:, 4 + hb, :], HBF[:, 128 * j:128 * (j + 1)],
                             EB[:, 8 + j, 512 * hb:512 * (hb + 1)], False, False)
                for hb in range(2):
                    E.mm(PD[:, 4 + hb, :], MK4[:, 2, :],
                         EA[:, 16, 512 * hb:512 * (hb + 1)], False, False)
                for hb in range(2):
                    E.mm(PD[:, 4 + hb, :], MK4[:, 2, :],
                         EB[:, 16, 512 * hb:512 * (hb + 1)], False, True,
                         st if hb == 1 else None, 1)
                E.wait("vector", st)
                E.wait("vector", sxq[par])
                E.wait("vector", sd, sd_mark[par])
                for hb in range(2):
                    E.op("vector", lambda e, hb=hb, par=par: e.scalar_tensor_tensor(
                        X4T[:, par, 512 * hb:512 * (hb + 1)], PD[:, 4 + hb, :], 0.5,
                        XTK4[:, par, 512 * hb:512 * (hb + 1)], ALU.mult, ALU.add),
                        sv if hb == 1 else None, 1, w=(X4T,))
                E.wait("sync", sv)
                E.op("sync", lambda e, t=t, par=par: e.dma_start(
                    y[128 * t:128 * (t + 1), :], X4T[:, par, :]), sd, 16)
                sd_mark[par] = sd.n

            with nc.named_scope("s4"):
                P.run_block()
            E.barrier_reset()

        E.wait("sync", sd)
        if max_phase < 13:
            d_last = d_x3 if max_phase >= 12 else (d_x2 if max_phase >= 11 else d_x1)
            rows = min(HALF, nt * 128) if max_phase >= 10 else 0
            if rows:
                E.op("sync", lambda e, r=rows, s=d_last: e.dma_start(y[0:r, :], s[0:r, :]), sd, 16)
            if rows < HALF:
                E.op("sync", lambda e, r=rows: e.dma_start(
                    y[r:HALF, :], x_own[r:HALF, :]), sd, 16)
        E.wait("sync", sd)
        E.wait("vector", sw)
        E.wait("vector", sep[0])
        E.wait("vector", sep[1])
        E.wait("sync", sxp[0])
        E.wait("sync", sxp[1])
        E.wait("sync", sxq[0])
        E.wait("sync", sxq[1])
        with nc.named_scope("fin"):
            P.run_block()

    return nc



def _prep_core_inputs(c, I):
    b, hf = c // 4, c % 2
    x = np.asarray(I["L0_attn_x"])
    x_own = np.ascontiguousarray(x[b, hf * HALF:(hf + 1) * HALF, :], np.float32)
    x_pre = (np.ascontiguousarray(x[b, 0:HALF, :], np.float32) if hf == 1
             else np.zeros((HALF, D), np.float32))
    pre_gate = np.full((128, 1), 1.0 if hf == 1 else 0.0, np.float32)

    bias_chunk = np.zeros((128, 1, D), np.float32)
    bias_chunk[0, 0] = np.asarray(I["L0_attn_bg"])
    bias_chunk[32, 0] = np.asarray(I["L0_attn_bo"])
    bias_chunk[64, 0] = np.asarray(I["L0_attn_Wgo"]).reshape(-1)
    w5 = np.concatenate([_chunks_T(np.asarray(I[f"L0_attn_{n}"]))
                         for n in ("Wq", "Wk", "Wv", "Wkl", "Wvl")]
                        + [bias_chunk], axis=1)
    wg = _chunks_T(np.asarray(I["L0_attn_Wg"]))
    wo = _chunks_T(np.asarray(I["L0_attn_Wo"]))
    mk = _host_masks(hf == 1)
    misc = np.zeros((128, 1, 128), np.float32)
    misc[0, 0, 0:4] = np.asarray(I["L0_attn_haar_scale"])
    misc[0, 0, 4] = float(np.asarray(I["L0_attn_bgo"]).reshape(-1)[0])
    sel = np.zeros((128, 3, 128), np.float32)
    sel[0, 0, :] = 1.0
    sel[32, 1, :] = 1.0
    sel[64, 2, :] = 1.0
    mk = np.concatenate([mk, misc, sel], axis=1)
    bd = _host_bd(np.asarray(I["L0_attn_haar_Wk"]), np.asarray(I["L0_attn_haar_Wv"]))

    def _pack_expert(g, u, d):
        return np.concatenate([_chunks_N(g).reshape(128, -1),
                               _chunks_N(u).reshape(128, -1),
                               _chunks_N(d).reshape(128, -1)], 1)
    eg = np.asarray(I["L0_ffn_experts_gate"])
    eu = np.asarray(I["L0_ffn_experts_up"])
    ed = np.asarray(I["L0_ffn_experts_down"])
    ew = np.ascontiguousarray(np.stack(
        [_pack_expert(eg[e], eu[e], ed[e]) for e in range(8)]))
    shw = np.ascontiguousarray(np.concatenate(
        [_chunks_N(np.asarray(I["L0_ffn_shared_gate"])),
         _chunks_N(np.asarray(I["L0_ffn_shared_up"])),
         _chunks_N(np.asarray(I["L0_ffn_shared_down"]))], 1))
    rt8 = _chunks_N(np.asarray(I["L0_ffn_router"]))

    b3 = np.zeros((128, 1, D), np.float32)
    b3[0, 0] = np.asarray(I["L1_attn_bq"])
    b3[32, 0] = np.asarray(I["L1_attn_bk"])
    b3[64, 0] = np.asarray(I["L1_attn_bv"])
    gb = np.zeros((128, 1, D), np.float32)
    gb[0, 0] = np.asarray(I["L1_attn_gamma"])
    gb[32, 0] = np.asarray(I["L1_attn_beta"])
    w3 = np.concatenate(
        [_chunks_T(np.asarray(I[f"L1_attn_{n}"])) for n in ("Wq", "Wk", "Wv")]
        + [b3, gb], axis=1)
    g = hf * HALF + np.arange(HALF, dtype=np.float64)
    inv = 1.0 / (10000.0 ** (np.arange(512, dtype=np.float64) / 512.0))
    ang = g[:, None] * inv[None, :]
    rope = np.concatenate([np.cos(ang), np.sin(ang)], 1).astype(
        ml_dtypes.bfloat16)
    wr4 = np.concatenate([_chunks_N(np.asarray(I["L1_ffn_Wr"])),
                          np.zeros((128, 1, 8), np.float32)], 1)
    wr4[0, 8, :] = np.asarray(I["L1_ffn_br"])
    w1 = np.asarray(I["L1_ffn_W1"])
    w2 = np.asarray(I["L1_ffn_W2"])
    b1 = np.asarray(I["L1_ffn_b1"])
    b2 = np.asarray(I["L1_ffn_b2"])

    def _pack4(e):
        bc = np.zeros((128, 1, D), np.float32)
        bc[0, 0] = b1[e]
        bc[32, 0] = b2[e]
        return np.concatenate([_chunks_N(w1[e]), _chunks_N(w2[e]), bc], 1)
    ew4 = np.ascontiguousarray(np.stack([_pack4(e) for e in range(8)]))
    return {
        "x_own": x_own, "x_pre": x_pre, "pre_gate": pre_gate,
        "w5T": w5, "wgT": wg, "woT": wo,
        "bd_mats": bd, "at_mats": _host_at(), "masks": np.ascontiguousarray(mk),
        "rt8": rt8, "ew": ew, "shw": shw,
        "w3T": w3, "d_rope": np.ascontiguousarray(rope),
        "slot": np.array([[hf]], dtype=np.uint32),
        "wr4": np.ascontiguousarray(wr4), "ew4": ew4,
    }


def kernel(**inputs):
    nc = build_program()
    in_maps = [_prep_core_inputs(c, inputs) for c in range(NCORE)]
    res = None
    err = None
    for _att in range(3):
        try:
            res = run_bass_kernel_spmd(nc, in_maps, core_ids=list(range(NCORE)))
            break
        except Exception as e:
            err = e
    if res is None:
        raise err
    out = np.zeros((B, N, D), np.float32)
    for b in range(B):
        out[b, 0:HALF, :] = np.asarray(res.results[4 * b]["y"])
        out[b, HALF:N, :] = np.asarray(res.results[4 * b + 1]["y"])
    return out


# version 05
# speedup vs baseline: 1.0367x; 1.0097x over previous
import numpy as np
import ml_dtypes
from contextlib import ExitStack

import concourse.bass as bass
import concourse.mybir as mybir
from concourse.bass_utils import run_bass_kernel_spmd

F32 = mybir.dt.float32
BF16 = mybir.dt.bfloat16
U32 = mybir.dt.uint32
I32 = mybir.dt.int32
AF = mybir.ActivationFunctionType
ALU = mybir.AluOpType
AX = mybir.AxisListType

NCORE = 8
B, N, D = 2, 8192, 1024
H = 16
HALF = 4096
NT = HALF // 128
OUTT = 2048
NOT = OUTT // 128
LOCW = 64
EXPB = 20.0



def _chunks_T(w):
    dout, din = w.shape
    return np.ascontiguousarray(
        w.T.reshape(din // 128, 128, dout).transpose(1, 0, 2), dtype=np.float32)


def _chunks_N(w):
    din, dout = w.shape
    return np.ascontiguousarray(
        w.reshape(din // 128, 128, dout).transpose(1, 0, 2), dtype=np.float32)


def _host_masks(first_prev_real):
    j = np.arange(128)
    tri = (j[:, None] <= j[None, :]).astype(np.float32)
    i = np.arange(128)[None, :]
    jj = np.arange(128)[:, None]
    prev = ((i - jj + 128 >= 0) & (i - jj + 128 <= LOCW - 1)).astype(np.float32)
    cur = ((i - jj >= 0) & (i - jj <= LOCW - 1)).astype(np.float32)
    ident = np.eye(128, dtype=np.float32)
    lpf = prev if first_prev_real else np.zeros_like(prev)
    return np.ascontiguousarray(np.stack([tri, ident, lpf, cur, prev, cur], 1))


def _host_at():
    out = np.zeros((128, 4, 128), np.float32)
    t = np.arange(128)
    for lv in range(4):
        blk = 2 << lv
        A = ((t[:, None] // blk == t[None, :] // blk)
             & (t[None, :] <= t[:, None])).astype(np.float32)
        A /= (t % blk + 1)[:, None]
        out[:, lv, :] = A.T
    return out


def _host_bd(haar_wk, haar_wv):
    out = np.zeros((128, 8, 128), np.float32)
    for lv in range(4):
        out[:64, lv, :64] = haar_wk[lv].T
        out[64:, lv, 64:] = haar_wk[lv].T
        out[:64, 4 + lv, :64] = haar_wv[lv].T
        out[64:, 4 + lv, 64:] = haar_wv[lv].T
    return out



ENGS = ("sync", "tensor", "vector", "scalar", "gpsimd")


class Prog:
    def __init__(self, nc):
        self.nc = nc
        self.ops = {k: [] for k in ENGS}

    def run_block(self):
        ops, self.ops = self.ops, {k: [] for k in ENGS}
        with self.nc.Block() as blk:
            for name in ENGS:
                if ops[name]:
                    getattr(blk, name)(
                        (lambda _o: (lambda e: [f(e) for f in _o]))(ops[name]))


class Sem:
    def __init__(self, ctx, nc, name):
        self.h = ctx.enter_context(nc.semaphore(name))
        self.n = 0


def _apname(a):
    try:
        return a.tensor.name
    except AttributeError:
        pass
    try:
        return a.name
    except AttributeError:
        return str(a)


class Emit:

    TRACKED = ("vector", "scalar", "gpsimd")

    def __init__(self, P):
        self.P = P
        self.written = {k: set() for k in self.TRACKED}
        self.readed = {k: set() for k in self.TRACKED}

    def op(self, eng, fn, sem=None, k=1, r=(), w=()):
        if eng in self.written:
            rn = {_apname(a) for a in r}
            wn = {_apname(a) for a in w}
            if (rn & self.written[eng]) or (wn & (self.written[eng] | self.readed[eng])):
                self.P.ops[eng].append(lambda e: e.drain())
                self.written[eng].clear()
                self.readed[eng].clear()
            self.written[eng] |= wn
            self.readed[eng] |= rn
        semh = None
        if sem is not None:
            sem.n += k
            semh = sem.h
        def f(e, fn=fn, semh=semh, k=k):
            ins = fn(e)
            if semh is not None:
                ins.then_inc(semh, k)
        self.P.ops[eng].append(f)

    def barrier_reset(self):
        for s in self.written.values():
            s.clear()
        for s in self.readed.values():
            s.clear()

    def wait(self, eng, sem, val=None):
        v = sem.n if val is None else val
        if v <= 0:
            return
        self.P.ops[eng].append(lambda e, s=sem.h, v=v: e.wait_ge(s, v))

    def mm(self, out, lhsT, rhs, start, stop, sem=None, k=1, tp=None):
        self.op("tensor",
                lambda e, o=out, l=lhsT, r=rhs, a=start, b=stop, tp=tp:
                e.matmul(o, l, r, start=a, stop=b, tile_position=tp),
                sem, k)

    def trans(self, out, in_, ident, sem=None, k=1):
        self.op("tensor",
                lambda e, o=out, i=in_, d=ident: e.transpose(o, i, d), sem, k)


def hs(t, h):
    j, half = divmod(h, 2)
    return t[64 * half:64 * (half + 1), 128 * j:128 * (j + 1)]



def build_program(debug_taps=(), nt=NT, max_phase=99, solo=False):
    nc = bass.Bass("TRN2", target_bir_lowering=False)

    def ext(name, shape, dtype=F32):
        return nc.dram_tensor(name, list(shape), dtype, kind="ExternalInput")

    x_own = ext("x_own", [HALF, D])
    x_pre = ext("x_pre", [HALF, D])
    pre_gate = ext("pre_gate", [128, 1])
    w5T = ext("w5T", [128, 41, D])
    wgT = ext("wgT", [128, 16, D])
    woT = ext("woT", [128, 8, D])
    bd_mats = ext("bd_mats", [128, 8, 128])
    at_mats = ext("at_mats", [128, 4, 128])
    masks = ext("masks", [128, 10, 128])
    rt8 = ext("rt8", [128, 8, 8])
    ew = ext("ew", [8, 128, 12288])
    shw = ext("shw", [128, 24, 1024])
    w3T = ext("w3T", [128, 26, 1024])
    d_rope = ext("d_rope", [HALF, 1024], BF16)
    slot = ext("slot", [1, 1], U32)
    wr4 = ext("wr4", [128, 9, 8])
    ew4 = ext("ew4", [8, 128, 17, 1024])

    y = nc.dram_tensor("y", [HALF, D], F32, kind="ExternalOutput")
    y_x1 = None
    if "x1" in debug_taps:
        y_x1 = nc.dram_tensor("y_x1", [HALF, D], F32, kind="ExternalOutput")
    y_x2 = None
    if "x2" in debug_taps:
        y_x2 = nc.dram_tensor("y_x2", [HALF, D], F32, kind="ExternalOutput")
    y_x3 = None
    if "x3" in debug_taps:
        y_x3 = nc.dram_tensor("y_x3", [HALF, D], F32, kind="ExternalOutput")

    d_x1 = nc.dram_tensor("d_x1", [HALF, D], F32, kind="Internal")
    d_ln1 = nc.dram_tensor("d_ln1", [HALF, D], BF16, kind="Internal")
    d_x2 = nc.dram_tensor("d_x2", [HALF, D], F32, kind="Internal")
    d_ln2 = nc.dram_tensor("d_ln2", [HALF, D], BF16, kind="Internal")
    d_x3 = nc.dram_tensor("d_x3", [HALF, D], F32, kind="Internal")
    d_ln3 = nc.dram_tensor("d_ln3", [HALF, D], BF16, kind="Internal")
    d_ex = nc.dram_tensor("d_ex", [2, 128, 520], F32, kind="Internal",
                          addr_space="Shared")
    d_exf = nc.dram_tensor("d_exf", [2, 16], U32, kind="Internal",
                           addr_space="Shared")
    d_ex2 = nc.dram_tensor("d_ex2", [2, 1, 8], F32, kind="Internal",
                           addr_space="Shared")

    P = Prog(nc)
    E = Emit(P)

    with ExitStack() as pc:
        sx = Sem(pc, nc, "sx")
        sxp = [Sem(pc, nc, "sxp0"), Sem(pc, nc, "sxp1")]
        sw = Sem(pc, nc, "sw")
        st = Sem(pc, nc, "st")
        sv = Sem(pc, nc, "sv")
        ss = Sem(pc, nc, "ss")
        sg = Sem(pc, nc, "sg")
        sd = Sem(pc, nc, "sd")
        sdg = Sem(pc, nc, "sdg")
        sxc = Sem(pc, nc, "sxc")
        sep = [Sem(pc, nc, "sep0"), Sem(pc, nc, "sep1")]
        sec = Sem(pc, nc, "sec")
        sxq = [Sem(pc, nc, "sxq0"), Sem(pc, nc, "sxq1")]
        spl = Sem(pc, nc, "spl")
        sgw = Sem(pc, nc, "sgw")
        stf = Sem(pc, nc, "stf")

        with ExitStack() as c1:
            en = c1.enter_context
            W5 = en(nc.sbuf_tensor("W5", [128, 41, D], BF16))
            WG = en(nc.sbuf_tensor("WG", [128, 16, D], BF16))
            WO = en(nc.sbuf_tensor("WO", [128, 8, D], BF16))
            BD = en(nc.sbuf_tensor("BD", [128, 8, 128], BF16))
            AT = en(nc.sbuf_tensor("AT", [128, 4, 128], BF16))
            MK = en(nc.sbuf_tensor("MK", [128, 10, 128], BF16))
            BRh = en(nc.sbuf_tensor("BRh", [1, 64], BF16))
            PG = en(nc.sbuf_tensor("PG", [128, 1], BF16))
            ONES = en(nc.sbuf_tensor("ONES", [128, 1], BF16))
            ONR = en(nc.sbuf_tensor("ONR", [128, 128], BF16))
            WGOb = en(nc.sbuf_tensor("WGOb", [128, D], BF16))
            SWb = en(nc.sbuf_tensor("SWb", [128, 4], F32))
            BGOb = en(nc.sbuf_tensor("BGOb", [128, 1], F32))
            SKV = en(nc.sbuf_tensor("SKV", [128, 5, 512], F32))
            SKc = en(nc.sbuf_tensor("SKc", [128, 5, 8], F32))
            SKVh = en(nc.sbuf_tensor("SKVh", [128, 512], BF16))
            SKch = en(nc.sbuf_tensor("SKch", [128, 16], BF16))
            XI = en(nc.sbuf_tensor("XI", [128, 2, D], F32))
            SM = en(nc.sbuf_tensor("SM", [128, 64], F32))
            LNt = en(nc.sbuf_tensor("LNt", [128, D], BF16))
            LNf = en(nc.sbuf_tensor("LNf", [128, D], BF16))
            Qt = en(nc.sbuf_tensor("Qt", [128, D], BF16))
            Qf = en(nc.sbuf_tensor("Qf", [128, D], BF16))
            PQf = en(nc.sbuf_tensor("PQf", [128, D], BF16))
            Kt = en(nc.sbuf_tensor("Kt", [128, D], BF16))
            PKt = en(nc.sbuf_tensor("PKt", [128, D], BF16))
            PKf = en(nc.sbuf_tensor("PKf", [128, D], BF16))
            Vt = en(nc.sbuf_tensor("Vt", [128, D], BF16))
            KLt = en(nc.sbuf_tensor("KLt", [128, D], BF16))
            KLfb = [en(nc.sbuf_tensor(f"KLf{i}", [128, D], BF16)) for i in range(2)]
            VLtb = [en(nc.sbuf_tensor(f"VLt{i}", [128, D], BF16)) for i in range(2)]
            KM = en(nc.sbuf_tensor("KM", [128, D], BF16))
            PTS = en(nc.sbuf_tensor("PTS", [128, 8, 128], BF16))
            GLB = en(nc.sbuf_tensor("GLB", [128, D], F32))
            DIFW = en(nc.sbuf_tensor("DIFW", [128, D], F32))
            DIFf = en(nc.sbuf_tensor("DIFf", [128, D], BF16))
            MX = en(nc.sbuf_tensor("MXD", [128, D], BF16))
            TM0 = en(nc.sbuf_tensor("TM0", [128, D], F32))
            PF = en(nc.psum_tensor("PF", [128, 7, 512], F32))
            PH = en(nc.psum_tensor("PH", [128, 1024], BF16))

            ident = MK[:, 1, :]
            DIF = Qt
            LN1 = LNt
            PS1 = LNt
            GH = LNt
            MXf = DIFf
            PS3 = MX
            SQS = DIFW
            VLVt = KM
            X1 = GLB
            KLVf = PKf
            KLVt = PKt

            def wload(dst, srcp, nch):
                step = 4
                for c0 in range(0, nch, step):
                    c1 = min(nch, c0 + step)
                    E.op("gpsimd", lambda e, d=dst[:, c0:c1, :], s=srcp[:, c0:c1, :]:
                         e.dma_start(d, s), sw, 16)
            wload(W5, w5T, 41)
            wload(WG, wgT, 16)
            wload(WO, woT, 8)
            for dst, srcp in ((BD[:], bd_mats[:]), (AT[:], at_mats[:]),
                              (MK[:], masks[:]), (PG[:], pre_gate[:])):
                E.op("gpsimd", lambda e, d=dst, s=srcp: e.dma_start(d, s), sw, 16)
            E.op("gpsimd", lambda e: e.memset(ONES[:], 1.0))
            E.op("gpsimd", lambda e: e.memset(ONR[:], 1.0))
            E.op("gpsimd", lambda e: e.memset(SKV[:], 0.0))
            E.op("gpsimd", lambda e: e.memset(SKc[:], 0.0))
            E.op("gpsimd", lambda e: e.memset(KLfb[1][:], 0.0))
            E.op("gpsimd", lambda e: e.memset(VLtb[1][:], 0.0))
            E.op("gpsimd", lambda e: e.memset(SM[:, 49:50], -EXPB))
            E.op("gpsimd", lambda e: e.memset(SM[:, 52:53], 1e-6))
            E.op("gpsimd", lambda e: e.memset(SM[:, 50:51], 0.125), sg, 1)

            E.wait("tensor", sw)
            E.wait("tensor", sg)
            E.mm(PF[:, 4, :], ONR[64:65, :], W5[64:65, 40, 0:512], True, True,
                 None, 1, tp=(64, 0))
            E.mm(PF[:, 5, :], ONR[64:65, :], W5[64:65, 40, 512:1024], True, True,
                 st, 1, tp=(64, 0))
            E.wait("vector", st)
            E.wait("vector", sw)
            E.op("vector", lambda e: e.tensor_copy(WGOb[:, 0:512], PF[:, 4, :]), w=(WGOb,))
            E.op("vector", lambda e: e.tensor_copy(WGOb[:, 512:1024], PF[:, 5, :]), w=(WGOb,))
            E.op("vector", lambda e: e.tensor_scalar(
                W5[:, 24:32, :], W5[:, 24:32, :], 0.125, None, ALU.mult), w=(W5,))
            E.op("vector", lambda e: e.tensor_copy(SM[0:1, 16:20], MK[0:1, 6, 0:4]), r=(MK,), w=(SM,))
            E.op("vector", lambda e: e.scalar_tensor_tensor(
                SM[0:1, 20:21], SM[0:1, 16:17], -1.0, SM[0:1, 17:18],
                ALU.mult, ALU.add), r=(SM,), w=(SM,))
            E.op("vector", lambda e: e.scalar_tensor_tensor(
                SM[0:1, 21:22], SM[0:1, 18:19], -1.0, SM[0:1, 19:20],
                ALU.mult, ALU.add), sv, 1, r=(SM,), w=(SM,))
            E.wait("scalar", sv)
            E.op("scalar", lambda e: e.activation(SM[0:1, 20:22], SM[0:1, 20:22], AF.Relu),
                 ss, 1, r=(SM,), w=(SM,))
            E.wait("vector", ss)
            E.op("vector", lambda e: e.tensor_add(SM[0:1, 22:23], SM[0:1, 16:17], SM[0:1, 20:21]), r=(SM,), w=(SM,))
            E.op("vector", lambda e: e.tensor_add(SM[0:1, 23:24], SM[0:1, 18:19], SM[0:1, 21:22]), r=(SM,), w=(SM,))
            E.op("vector", lambda e: e.scalar_tensor_tensor(
                SM[0:1, 24:25], SM[0:1, 22:23], -1.0, SM[0:1, 23:24],
                ALU.mult, ALU.add), sv, 1, r=(SM,), w=(SM,))
            E.wait("scalar", sv)
            E.op("scalar", lambda e: e.activation(SM[0:1, 24:25], SM[0:1, 24:25], AF.Relu),
                 ss, 1, r=(SM,), w=(SM,))
            E.wait("vector", ss)
            E.op("vector", lambda e: e.tensor_add(SM[0:1, 0:1], SM[0:1, 22:23], SM[0:1, 24:25]), r=(SM,), w=(SM,))
            E.op("vector", lambda e: e.tensor_scalar(
                SM[0:1, 25:26], SM[0:1, 0:1], -1.0, None, ALU.mult), r=(SM,), w=(SM,))
            E.op("vector", lambda e: e.tensor_scalar(
                SM[0:1, 1:5], SM[0:1, 16:20], SM[0:1, 25:26], None, ALU.add), sv, 1, r=(SM,), w=(SM,))
            E.wait("scalar", sv)
            E.op("scalar", lambda e: e.activation(
                SM[0:1, 5:9], SM[0:1, 1:5], AF.Exp, accum_out=SM[0:1, 9:10]), ss, 1, r=(SM,), w=(SM,))
            E.wait("vector", ss)
            E.op("vector", lambda e: e.reciprocal(SM[0:1, 10:11], SM[0:1, 9:10]), r=(SM,), w=(SM,))
            E.op("vector", lambda e: e.tensor_scalar(
                SM[0:1, 11:15], SM[0:1, 5:9], SM[0:1, 10:11], None, ALU.mult), r=(SM,), w=(SM,))
            E.op("vector", lambda e: e.tensor_copy(BRh[0:1, 0:4], SM[0:1, 11:15]), r=(SM,), w=(BRh,))
            E.op("vector", lambda e: e.tensor_copy(BRh[0:1, 4:5], MK[0:1, 6, 4:5]), sv, 1, r=(MK,), w=(BRh,))
            E.wait("tensor", sv)
            E.mm(PF[:, 0, 0:5], ONR[0:1, :], BRh[0:1, 0:5], True, True, st, 1)
            E.wait("vector", st)
            E.op("vector", lambda e: e.tensor_copy(SWb[:], PF[:, 0, 0:4]), w=(SWb,))
            E.op("vector", lambda e: e.tensor_copy(BGOb[:], PF[:, 0, 4:5]), sv, 1, w=(BGOb,))

            with nc.named_scope("s1pre"):
                P.run_block()
            E.barrier_reset()

            n_prior_sx = [sx.n // 16]

            def emit_ln(xsrc, out_bf, eps=1e-5):
                E.op("vector", lambda e, x=xsrc: e.reduce_sum(SM[:, 0:1], x, AX.X), sv, 1,
                     r=(xsrc,), w=(SM,))
                E.op("scalar", lambda e, x=xsrc: e.activation(
                    SQS[:], x, AF.Square, accum_out=SM[:, 1:2]), ss, 1,
                    r=(xsrc,), w=(SQS, SM))
                E.wait("vector", ss)
                E.op("vector", lambda e: e.tensor_scalar(
                    SM[:, 2:3], SM[:, 0:1], 1.0 / D, None, ALU.mult), r=(SM,), w=(SM,))
                E.op("vector", lambda e: e.tensor_mul(SM[:, 3:4], SM[:, 2:3], SM[:, 2:3]), r=(SM,), w=(SM,))
                E.op("vector", lambda e: e.scalar_tensor_tensor(
                    SM[:, 4:5], SM[:, 1:2], 1.0 / D, SM[:, 3:4], ALU.mult, ALU.subtract), r=(SM,), w=(SM,))
                E.op("vector", lambda e, p=eps: e.tensor_scalar(
                    SM[:, 4:5], SM[:, 4:5], p, None, ALU.add), sv, 1, r=(SM,), w=(SM,))
                E.wait("scalar", sv)
                E.op("scalar", lambda e: e.sqrt(SM[:, 5:6], SM[:, 4:5]), ss, 1, r=(SM,), w=(SM,))
                E.wait("vector", ss)
                E.op("vector", lambda e: e.reciprocal(SM[:, 6:7], SM[:, 5:6]), r=(SM,), w=(SM,))
                E.op("vector", lambda e: e.tensor_mul(SM[:, 7:8], SM[:, 2:3], SM[:, 6:7]), r=(SM,), w=(SM,))
                E.op("vector", lambda e: e.tensor_scalar(
                    SM[:, 7:8], SM[:, 7:8], -1.0, None, ALU.mult), sv, 1, r=(SM,), w=(SM,))
                E.wait("scalar", sv)
                E.op("scalar", lambda e, o=out_bf, x=xsrc: e.activation(
                    o, x, AF.Identity, bias=SM[:, 7:8], scale=SM[:, 6:7]), ss, 1,
                    r=(xsrc, SM), w=(out_bf,))

            def emit_phi(xin_bf, out_bf):
                E.wait("scalar", sv)
                E.op("scalar", lambda e, x=xin_bf, o=out_bf: e.activation(o, x, AF.Relu),
                     ss, 1, r=(xin_bf,), w=(out_bf,))
                E.wait("vector", ss)
                E.op("vector", lambda e, x=xin_bf, o=out_bf: e.scalar_tensor_tensor(
                    PS1[:], o, -1.0, x, ALU.mult, ALU.add), sv, 1,
                    r=(xin_bf, out_bf), w=(PS1,))
                E.wait("scalar", sv)
                E.op("scalar", lambda e: e.activation(PS1[:], PS1[:], AF.Exp), ss, 1,
                     r=(PS1,), w=(PS1,))
                E.wait("vector", ss)
                E.op("vector", lambda e, o=out_bf: e.tensor_add(o, o, PS1[:]), sv, 1,
                     r=(out_bf, PS1), w=(out_bf,))

            def trans8(src_bf, dst_bf, bank):
                E.wait("tensor", sv)
                E.wait("tensor", ss)
                for j in range(8):
                    E.trans(bank[:, 128 * j:128 * (j + 1)],
                            src_bf[:, 128 * j:128 * (j + 1)], ident,
                            st if j == 7 else None, 1)
                E.wait("vector", st)
                E.op("vector", lambda e, d=dst_bf, b=bank: e.tensor_copy(d, b), sv, 1,
                     w=(dst_bf,))

            def hmm8(dst_bf, src_tok, rhs_at, bd_slice):
                E.wait("tensor", sv)
                for j in range(8):
                    bank, col = j // 4, (j % 4) * 128
                    if bd_slice is None:
                        E.mm(PF[:, bank, col:col + 128],
                             src_tok[:, 128 * j:128 * (j + 1)], rhs_at,
                             True, True, st if j == 7 else None, 1)
                    else:
                        E.mm(PF[:, bank, col:col + 128], bd_slice,
                             src_tok[:, 128 * j:128 * (j + 1)],
                             True, True, st if j == 7 else None, 1)
                E.wait("vector", st)
                E.op("vector", lambda e, d=dst_bf: e.tensor_copy(
                    d[:, 0:512], PF[:, 0, :]), w=(dst_bf,))
                E.op("vector", lambda e, d=dst_bf: e.tensor_copy(
                    d[:, 512:1024], PF[:, 1, :]), sv, 1, w=(dst_bf,))

            def haar_prep(lv):
                hmm8(KM[:], Kt[:], AT[:, lv, :], None)
                hmm8(PS3[:], KM[:], None, BD[:, lv, :])
                emit_phi(PS3[:], KLVf[:])
                trans8(KLVf[:], KLVt[:], PH[:])
                hmm8(KM[:], Vt[:], AT[:, lv, :], None)
                hmm8(PS3[:], KM[:], None, BD[:, 4 + lv, :])
                trans8(PS3[:], VLVt[:], PH[:])

            def tile_body(ti, own, xsrc_dram, sk_rhs, base_consumed):
                xi = XI[:, ti % 2, :]
                par = (ti % 2) if own else ((ti + nt) % 2)
                kl_cur, kl_prv = KLfb[par], KLfb[1 - par]
                vl_cur, vl_prv = VLtb[par], VLtb[1 - par]

                if ti + 1 < nt:
                    E.wait("sync", sxc, base_consumed + ti)
                    E.op("sync", lambda e, t=ti + 1, s=xsrc_dram: e.dma_start(
                        XI[:, t % 2, :], s[128 * t:128 * (t + 1), :]),
                        sxp[(ti + 1) % 2], 16)

                E.wait("vector", sxp[ti % 2])
                E.wait("scalar", sxp[ti % 2])
                E.wait("scalar", sd)
                E.wait("vector", sd)

                def _cut0(ph):
                    if max_phase < ph:
                        if own:
                            E.op("vector", lambda e: e.memzero(SM[:, 51:52]),
                                 sxc, 1, w=(SM,))
                        else:
                            E.op("scalar", lambda e: e.memzero(SM[:, 48:49]),
                                 sxc, 1, w=(SM,))
                        return True
                    return False

                emit_ln(xi, LNt[:])
                if _cut0(0.3):
                    return
                trans8(LNt[:], LNf[:], PH[:])
                if _cut0(0.6):
                    return

                E.wait("tensor", sv)
                grpA = [(0, 0), (1, 2), (2, 4)] if own else [(1, 0), (2, 2)]
                for j in range(8):
                    lhsT = LNf[:, 128 * j:128 * (j + 1)]
                    for gi, (pi, bank) in enumerate(grpA):
                        for hcol in range(2):
                            last = (j == 7 and gi == len(grpA) - 1 and hcol == 1)
                            E.mm(PF[:, bank + hcol, :], lhsT,
                                 W5[:, 8 * pi + j, 512 * hcol:512 * (hcol + 1)],
                                 j == 0, j == 7, st if last else None, 1)
                E.wait("scalar", st)
                E.wait("vector", st)
                kb = 2 if own else 0
                vb = 4 if own else 2
                if own:
                    E.op("scalar", lambda e: e.copy(Qt[:, 0:512], PF[:, 0, :]), w=(Qt,))
                    E.op("scalar", lambda e: e.copy(Qt[:, 512:1024], PF[:, 1, :]), ss, 1, w=(Qt,))
                E.op("vector", lambda e, b=kb: e.tensor_copy(Kt[:, 0:512], PF[:, b, :]), w=(Kt,))
                E.op("vector", lambda e, b=kb: e.tensor_copy(Kt[:, 512:1024], PF[:, b + 1, :]), w=(Kt,))
                E.op("vector", lambda e, b=vb: e.tensor_copy(Vt[:, 0:512], PF[:, b, :]), w=(Vt,))
                E.op("vector", lambda e, b=vb: e.tensor_copy(Vt[:, 512:1024], PF[:, b + 1, :]), sv, 1, w=(Vt,))

                _cut = _cut0

                if _cut(1):
                    return
                tail = (not own) and ti == nt - 1
                if own or tail:
                    E.wait("tensor", sv)
                    E.wait("tensor", ss)
                    for j in range(8):
                        lhsT = LNf[:, 128 * j:128 * (j + 1)]
                        for gi, (pi, bank) in enumerate([(3, 0), (4, 2)]):
                            for hcol in range(2):
                                last = (j == 7 and gi == 1 and hcol == 1)
                                E.mm(PF[:, bank + hcol, :], lhsT,
                                     W5[:, 8 * pi + j, 512 * hcol:512 * (hcol + 1)],
                                     j == 0, j == 7, st if last else None, 1)
                    E.wait("vector", st)
                    E.op("vector", lambda e: e.tensor_copy(KLt[:, 0:512], PF[:, 0, :]), w=(KLt,))
                    E.op("vector", lambda e: e.tensor_copy(KLt[:, 512:1024], PF[:, 1, :]), w=(KLt,))
                    E.op("vector", lambda e, v=vl_cur: e.tensor_copy(v[:, 0:512], PF[:, 2, :]), w=(vl_cur,))
                    E.op("vector", lambda e, v=vl_cur: e.tensor_copy(v[:, 512:1024], PF[:, 3, :]), sv, 1, w=(vl_cur,))

                if _cut(2):
                    return
                emit_phi(Kt[:], PKt[:])
                if _cut(2.2):
                    return
                if own:
                    trans8(PKt[:], PKf[:], PH[:])
                if _cut(2.4):
                    return
                if own:
                    trans8(Qt[:], Qf[:], PH[:])
                if _cut(2.6):
                    return
                if own or tail:
                    trans8(KLt[:], kl_cur[:], PH[:])
                if _cut(2.8):
                    return
                if own:
                    emit_phi(Qf[:], PQf[:])

                if _cut(3):
                    return
                ncalls = min(5, int(max_phase) - 3) if max_phase >= 4 \
                    else (1 if max_phase > 3.05 else 0)
                for cc in range(ncalls):
                    if cc > 0:
                        haar_prep(cc - 1)
                    kf = PKf[:] if cc == 0 else KLVf[:]
                    ktok = PKt[:] if cc == 0 else KLVt[:]
                    vtok = Vt[:] if cc == 0 else VLVt[:]
                    den_bank = 4

                    if cc == 0 and _cut(3.07):
                        return
                    E.wait("tensor", sv)
                    if cc == 0 and _cut(3.1):
                        return
                    if own:
                        E.op("vector", lambda e, cc=cc: e.tensor_copy(SKVh[:], SKV[:, cc, :]),
                             r=(SKV,), w=(SKVh,))
                        E.op("vector", lambda e, cc=cc: e.tensor_copy(
                            SKch[:].rearrange("p (a b) -> p a b", a=8),
                            SKc[:, cc, :].unsqueeze(2).broadcast_to([128, 8, 2])), sv, 1,
                            r=(SKc,), w=(SKch,))
                        E.wait("tensor", sv)
                        for hg in range(4):
                            p = hg % 2
                            pslot = PTS[:, 4 * p:4 * p + 4, :]
                            if hg >= 2:
                                E.wait("tensor", sv)
                            for half in range(2):
                                for jj in range(2):
                                    h = 4 * hg + 2 * jj + half
                                    E.mm(PF[:, half, 256 * p + 128 * jj:256 * p + 128 * jj + 128],
                                         hs(kf, h), hs(PQf, h), True, True,
                                         st if jj == 1 else None, 1,
                                         tp=(64 * half, 0))
                            E.wait("vector", st)
                            E.op("vector", lambda e, p=p, ps=pslot: e.tensor_tensor(
                                ps.rearrange("p (a b) c -> p a b c", a=2),
                                PF[:, 0:2, 256 * p:256 * p + 256].rearrange(
                                    "p a (b c) -> p a b c", b=2),
                                MK[:, 0:1, :].unsqueeze(1).broadcast_to([128, 2, 2, 128]),
                                ALU.mult), sv, 1, r=(MK,), w=(PTS,))
                            E.wait("tensor", sv)
                            for idx in range(4):
                                half, jj = divmod(idx, 2)
                                h = 4 * hg + 2 * jj + half
                                slot = PTS[:, 4 * p + idx, :]
                                ob = PF[:, 2 + h // 8, 64 * (h % 8):64 * (h % 8) + 64]
                                E.mm(ob, slot, vtok[:, 64 * h:64 * h + 64], True, True)
                                E.mm(PF[:, den_bank, h:h + 1], slot, ONES[:],
                                     True, True)
                        E.wait("tensor", sv)
                        for h in range(0, 16, 2):
                            E.mm(PF[:, 0, 64 * (h // 2):64 * (h // 2) + 64],
                                 hs(PQf, h), SKVh[0:64, 64 * (h // 2):64 * (h // 2) + 64],
                                 True, True, None, 1, tp=(0, 0))
                        for h in range(0, 16, 2):
                            j = h // 2
                            E.mm(PF[:, den_bank, 16 + 2 * j:18 + 2 * j],
                                 hs(PQf, h), SKch[0:64, 2 * j:2 * j + 2],
                                 True, True, None, 1, tp=(0, 0))
                        for h in range(1, 16, 2):
                            E.mm(PF[:, 1, 64 * (h // 2):64 * (h // 2) + 64],
                                 hs(PQf, h), SKVh[64:128, 64 * (h // 2):64 * (h // 2) + 64],
                                 True, True, None, 1, tp=(64, 0))
                    if cc == 0 and _cut(3.3):
                        return
                    for j in range(8):
                        E.mm(PF[:, 5 + j // 4, (j % 4) * 128:(j % 4) * 128 + 128],
                             ktok[:, 128 * j:128 * (j + 1)],
                             vtok[:, 128 * j:128 * (j + 1)],
                             True, True, None, 1)
                        E.mm(PF[:, den_bank, 32 + j:33 + j],
                             ktok[:, 128 * j:128 * (j + 1)], sk_rhs,
                             True, True,
                             (st if j == 7 and not own else None), 1)
                    if cc == 0 and _cut(3.4):
                        return
                    if own:
                        for h in range(1, 16, 2):
                            j = h // 2
                            E.mm(PF[:, den_bank, 40 + 2 * j:42 + 2 * j],
                                 hs(PQf, h), SKch[64:128, 2 * j:2 * j + 2],
                                 True, True, st if h == 15 else None, 1,
                                 tp=(64, 0))
                    oden_st = st.n
                    if cc == 0 and _cut(3.5):
                        return

                    E.wait("vector", st, oden_st)
                    if own:
                        E.op("vector", lambda e, b=den_bank: e.tensor_copy(
                            SM[:, 32:48].rearrange("p (a b) -> p a b", a=8)[:, :, 0:1],
                            PF[:, b, 16:32].rearrange("p (a b) -> p a b", a=8)[:, :, 0:1]),
                            w=(SM,))
                        E.op("vector", lambda e, b=den_bank: e.tensor_copy(
                            SM[:, 32:48].rearrange("p (a b) -> p a b", a=8)[:, :, 1:2],
                            PF[:, b, 40:56].rearrange("p (a b) -> p a b", a=8)[:, :, 0:1]),
                            w=(SM,))
                        E.op("vector", lambda e, b=den_bank: e.tensor_add(
                            SM[:, 16:32], PF[:, b, 0:16], SM[:, 32:48]),
                            r=(SM,), w=(SM,))
                        E.op("vector", lambda e: e.scalar_tensor_tensor(
                            SM[:, 32:48], SM[:, 16:32], -1.0,
                            SM[:, 52:53].broadcast_to([128, 16]),
                            ALU.mult, ALU.add), sv, 1, r=(SM,), w=(SM,))
                        E.wait("scalar", sv)
                        E.op("scalar", lambda e: e.activation(
                            SM[:, 32:48], SM[:, 32:48], AF.Relu), ss, 1,
                            r=(SM,), w=(SM,))
                        E.wait("vector", ss)
                        E.op("vector", lambda e: e.tensor_add(
                            SM[:, 16:32], SM[:, 16:32], SM[:, 32:48]),
                            r=(SM,), w=(SM,))
                        E.op("vector", lambda e: e.reciprocal(SM[:, 16:32], SM[:, 16:32]),
                             r=(SM,), w=(SM,))
                        if cc > 0:
                            E.op("vector", lambda e, cc=cc: e.tensor_scalar(
                                SM[:, 16:32], SM[:, 16:32], SWb[:, cc - 1:cc], None,
                                ALU.mult), r=(SM, SWb), w=(SM,))
                        for hb in range(2):
                            oa = PF[:, 2 + hb, :].rearrange(
                                "p (a b c) -> p a b c", a=4, b=2)
                            obp = PF[:, 0:2, 256 * hb:256 * hb + 256].rearrange(
                                "p a (b c) -> p b a c", b=4)
                            coef_b = SM[:, 16 + 8 * hb:24 + 8 * hb].rearrange(
                                "p (a b) -> p a b", a=4) \
                                .unsqueeze(3).broadcast_to([128, 4, 2, 64])
                            tm4 = TM0[:, 0:512].rearrange(
                                "p (a b c) -> p a b c", a=4, b=2)
                            E.op("vector", lambda e, ob=obp, t4=tm4: e.tensor_copy(
                                t4, ob), w=(TM0,))
                            E.op("vector", lambda e, oa=oa, t4=tm4: e.scalar_tensor_tensor(
                                t4, oa, 1.0, t4, ALU.mult, ALU.add), r=(TM0,), w=(TM0,))
                            E.op("vector", lambda e, cb=coef_b, t4=tm4: e.tensor_tensor(
                                t4, t4, cb, ALU.mult), r=(SM, TM0), w=(TM0,))
                            if cc == 0:
                                E.op("vector", lambda e, hb=hb: e.tensor_copy(
                                    GLB[:, 512 * hb:512 * (hb + 1)], TM0[:, 0:512]),
                                    r=(TM0,), w=(GLB,))
                            else:
                                E.op("vector", lambda e, hb=hb: e.tensor_add(
                                    GLB[:, 512 * hb:512 * (hb + 1)],
                                    GLB[:, 512 * hb:512 * (hb + 1)], TM0[:, 0:512]),
                                    r=(GLB, TM0), w=(GLB,))
                    if cc == 0 and _cut(3.7):
                        return
                    for half in range(2):
                        pr = slice(64 * half, 64 * (half + 1))
                        co = slice(64 * half, 64 * half + 64)
                        E.op("vector", lambda e, cc=cc, pr=pr, co=co: e.tensor_add(
                            SKV[pr, cc, :].rearrange("p (a b) -> p a b", a=8),
                            SKV[pr, cc, :].rearrange("p (a b) -> p a b", a=8),
                            PF[pr, 5:7, :].rearrange("p a (b c) -> p (a b) c", b=4)[:, :, co]),
                            r=(SKV,), w=(SKV,))
                    E.op("vector", lambda e, cc=cc, b=den_bank: e.tensor_add(
                        SKc[:, cc, :], SKc[:, cc, :], PF[:, b, 32:40]), sv, 1,
                        r=(SKc,), w=(SKc,))

                if _cut(9):
                    return
                if own:
                    E.wait("tensor", sv)
                    for hp in range(8):
                        p = hp % 2
                        plslot = PTS[:, 4 * p:4 * p + 4, :]
                        if hp >= 2:
                            E.wait("tensor", sg)
                        for half in range(2):
                            h = 2 * hp + half
                            for w, klw in ((0, kl_prv), (1, kl_cur)):
                                E.mm(PF[:, half, 256 * p + 128 * w:256 * p + 128 * w + 128],
                                     hs(klw, h), hs(Qf, h), True, True,
                                     st if w == 1 else None, 1,
                                     tp=(64 * half, 0))
                        E.wait("scalar", st)
                        E.op("scalar", lambda e, p=p, ps=plslot: e.activation(
                            ps.rearrange("p (a b) c -> p a b c", a=2),
                            PF[:, 0:2, 256 * p:256 * p + 256].rearrange(
                                "p a (b c) -> p a b c", b=2),
                            AF.Exp, bias=SM[:, 49:50]), ss, 1,
                            r=(SM,), w=(PTS,))
                        E.wait("vector", ss)
                        mk0 = 2 if ti == 0 else 4
                        E.op("vector", lambda e, ps=plslot, m0=mk0: e.tensor_tensor(
                            ps.rearrange("p (a b) c -> p a b c", a=2),
                            ps.rearrange("p (a b) c -> p a b c", a=2),
                            MK[:, m0:m0 + 2, :].unsqueeze(1).broadcast_to([128, 2, 2, 128]),
                            ALU.mult), sg, 1, r=(PTS, MK), w=(PTS,))
                        E.wait("tensor", sg)
                        for half in range(2):
                            h = 2 * hp + half
                            ob = PF[:, 2 + h // 8, 64 * (h % 8):64 * (h % 8) + 64]
                            dn = PF[:, 4, h:h + 1]
                            sprv = PTS[:, 4 * p + 2 * half, :]
                            scur = PTS[:, 4 * p + 2 * half + 1, :]
                            E.mm(ob, sprv, vl_prv[:, 64 * h:64 * h + 64],
                                 True, False)
                            E.mm(ob, scur, vl_cur[:, 64 * h:64 * h + 64],
                                 False, True)
                            E.mm(dn, sprv, ONES[:], True, False)
                            E.mm(dn, scur, ONES[:], False, True,
                                 st if h == H - 1 else None, 1)
                    loc_st = st.n
                    E.wait("vector", st, loc_st)
                    E.op("vector", lambda e: e.reciprocal(SM[:, 32:48], PF[:, 4, 0:16]), w=(SM,))
                    for hb in range(2):
                        coef_b = SM[:, 32 + 8 * hb:40 + 8 * hb] \
                            .unsqueeze(2).broadcast_to([128, 8, 64])
                        E.op("vector", lambda e, hb=hb, cb=coef_b: e.tensor_tensor(
                            TM0[:, 512 * hb:512 * (hb + 1)].rearrange(
                                "p (a b) -> p a b", a=8),
                            PF[:, 2 + hb, :].rearrange("p (a b) -> p a b", a=8),
                            cb, ALU.mult), r=(SM,), w=(TM0,))
                    E.op("vector", lambda e: e.scalar_tensor_tensor(
                        DIFW[:], GLB[:], -1.0, TM0[:], ALU.mult, ALU.add),
                        r=(TM0, GLB), w=(DIFW,))
                    E.op("vector", lambda e: e.tensor_copy(DIF[:], DIFW[:]), sv, 1,
                         r=(DIFW,), w=(DIF,))

                if _cut(10):
                    return
                if own:
                    pass
                if own:
                    trans8(DIF[:], DIFf[:], PH[:])
                    E.wait("tensor", sv)
                    for j in range(16):
                        lhsT = LNf[:, 128 * j:128 * (j + 1)] if j < 8 \
                            else DIFf[:, 128 * (j - 8):128 * (j - 7)]
                        for hcol in range(2):
                            E.mm(PF[:, hcol, :], lhsT,
                                 WG[:, j, 512 * hcol:512 * (hcol + 1)],
                                 j == 0, False)
                    for hcol in range(2):
                        E.mm(PF[:, hcol, :], MK[:, 7, :],
                             W5[:, 40, 512 * hcol:512 * (hcol + 1)], False, True,
                             st if hcol == 1 else None, 1)
                    E.wait("scalar", st)
                    E.op("scalar", lambda e: e.activation(GH[:, 0:512], PF[:, 0, :], AF.Silu), w=(GH,))
                    E.op("scalar", lambda e: e.activation(GH[:, 512:1024], PF[:, 1, :], AF.Silu), ss, 1, w=(GH,))
                    E.wait("vector", ss)
                    E.op("vector", lambda e: e.tensor_mul(KM[:], GH[:], WGOb[:]),
                         r=(GH, WGOb), w=(KM,))
                    E.op("vector", lambda e: e.reduce_sum(SM[:, 8:9], KM[:], AX.X), sv, 1,
                         r=(KM,), w=(SM,))
                    E.wait("scalar", sv)
                    E.op("scalar", lambda e: e.activation(
                        SM[:, 9:10], SM[:, 8:9], AF.Sigmoid, bias=BGOb[:]), ss, 1,
                        r=(SM, BGOb), w=(SM,))
                    E.wait("vector", ss)
                    E.op("vector", lambda e: e.scalar_tensor_tensor(
                        TM0[:], DIFW[:], SM[:, 9:10], GLB[:], ALU.mult, ALU.add),
                        r=(DIFW, SM, GLB), w=(TM0,))
                    E.op("vector", lambda e: e.tensor_copy(MX[:], TM0[:]), sv, 1,
                         r=(TM0,), w=(MX,))

                    trans8(MX[:], MXf[:], PH[:])
                    E.wait("tensor", sv)
                    for j in range(8):
                        for hcol in range(2):
                            E.mm(PF[:, 2 + hcol, :], MXf[:, 128 * j:128 * (j + 1)],
                                 WO[:, j, 512 * hcol:512 * (hcol + 1)],
                                 j == 0, False)
                    for hcol in range(2):
                        E.mm(PF[:, 2 + hcol, :], MK[:, 8, :],
                             W5[:, 40, 512 * hcol:512 * (hcol + 1)], False, True,
                             st if hcol == 1 else None, 1)
                    E.wait("vector", st)
                    for hb in range(2):
                        E.op("vector", lambda e, hb=hb, x=xi: e.tensor_add(
                            X1[:, 512 * hb:512 * (hb + 1)], PF[:, 2 + hb, :],
                            x[:, 512 * hb:512 * (hb + 1)]),
                            sxc if hb == 1 else None, 1, r=(XI,), w=(X1,))
                    E.wait("scalar", sxc)
                    emit_ln(X1[:], LN1[:])
                    ln1_ss = ss.n
                    E.wait("sync", sxc)
                    E.op("sync", lambda e, t=ti: e.dma_start(
                        d_x1[128 * t:128 * (t + 1), :], X1[:]), sd, 16)
                    if y_x1 is not None:
                        E.op("sync", lambda e, t=ti: e.dma_start(
                            y_x1[128 * t:128 * (t + 1), :], X1[:]), sd, 16)
                    E.wait("sync", ss, ln1_ss)
                    E.op("sync", lambda e, t=ti: e.dma_start(
                        d_ln1[128 * t:128 * (t + 1), :], LN1[:]), sd, 16)
                else:
                    E.op("scalar", lambda e: e.memzero(SM[:, 48:49]),
                         sxc, 1, w=(SM,))

            if max_phase >= 0.1:
                E.op("sync", lambda e: e.dma_start(XI[:, 0, :], x_pre[0:128, :]), sxp[0], 16)
                for ti in range(nt):
                    tile_body(ti, False, x_pre, PG[:], 0)

                n_prior_sx[0] += nt
                E.wait("sync", sxc, nt)
                E.op("sync", lambda e: e.dma_start(XI[:, 0, :], x_own[0:128, :]), sxp[0], 16)
                for ti in range(nt):
                    tile_body(ti, True, x_own, ONES[:], nt)

            with nc.named_scope("s1"):
                P.run_block()
            E.barrier_reset()

        if max_phase >= 11:
          with ExitStack() as c2:
            en = c2.enter_context
            SH = en(nc.sbuf_tensor("SH", [128, 24, 1024], BF16))
            EWb = [en(nc.sbuf_tensor(f"EWs{i}", [128, 12288], BF16)) for i in range(2)]
            RT = en(nc.sbuf_tensor("RT", [128, 8, 8], BF16))
            ACC = en(nc.sbuf_tensor("ACC", [128, 8, 1024], F32))
            LN1F = en(nc.sbuf_tensor("LN1F", [128, 8, 1024], BF16))
            LTK = en(nc.sbuf_tensor("LTK", [128, 2, 1024], BF16))
            XT1 = en(nc.sbuf_tensor("XT1", [128, 2, 1024], F32))
            HS2 = en(nc.sbuf_tensor("HS2", [128, 1024], BF16))
            HST = en(nc.sbuf_tensor("HST", [128, 1024], BF16))
            SGb = en(nc.sbuf_tensor("SGb", [128, 1024], BF16))
            HE = en(nc.sbuf_tensor("HE2", [128, 512], BF16))
            HET = en(nc.sbuf_tensor("HET", [128, 512], BF16))
            CFA = en(nc.sbuf_tensor("CFA", [128, 8, 8], F32))
            SM2 = en(nc.sbuf_tensor("SM2", [128, 128], F32))
            SQ2 = en(nc.sbuf_tensor("SQ2", [128, 1024], F32))
            X2T = en(nc.sbuf_tensor("X2T", [128, 2, 1024], F32))
            LN2T = en(nc.sbuf_tensor("LN2T", [128, 2, 1024], BF16))
            ID2 = en(nc.sbuf_tensor("ID2", [128, 128], BF16))
            PB = en(nc.psum_tensor("PB", [128, 7, 512], F32))
            PH2 = en(nc.psum_tensor("PH2", [128, 1024], BF16))

            for c0 in range(0, 24, 4):
                E.op("gpsimd", lambda e, c0=c0: e.dma_start(
                    SH[:, c0:c0 + 4, :], shw[:, c0:c0 + 4, :]), sw, 16)
            E.op("gpsimd", lambda e: e.dma_start(RT[:], rt8[:]), sw, 16)
            E.op("gpsimd", lambda e: e.dma_start(ID2[:], masks[:, 1, :]), sw, 16)
            E.op("gpsimd", lambda e: e.memset(SM2[:, 96:97], 1.0))
            E.op("gpsimd", lambda e: e.memset(SM2[:, 97:98], -5.0), sg, 1)

            def eload(eidx, pi):
                for c0 in range(0, 12288, 3072):
                    E.op("gpsimd", lambda e, eidx=eidx, pi=pi, c0=c0: e.dma_start(
                        EWb[pi][:, c0:c0 + 3072], ew[eidx, :, c0:c0 + 3072]),
                        sep[pi], 16)
            eload(0, 0)

            def emit_ln2(xsrc, out_bf, eps=1e-5):
                E.op("vector", lambda e, x=xsrc: e.reduce_sum(SM2[:, 0:1], x, AX.X), sv, 1,
                     r=(xsrc,), w=(SM2,))
                E.op("scalar", lambda e, x=xsrc: e.activation(
                    SQ2[:], x, AF.Square, accum_out=SM2[:, 1:2]), ss, 1,
                    r=(xsrc,), w=(SQ2, SM2))
                E.wait("vector", ss)
                E.op("vector", lambda e: e.tensor_scalar(
                    SM2[:, 2:3], SM2[:, 0:1], 1.0 / D, None, ALU.mult), r=(SM2,), w=(SM2,))
                E.op("vector", lambda e: e.tensor_mul(SM2[:, 3:4], SM2[:, 2:3], SM2[:, 2:3]), r=(SM2,), w=(SM2,))
                E.op("vector", lambda e: e.scalar_tensor_tensor(
                    SM2[:, 4:5], SM2[:, 1:2], 1.0 / D, SM2[:, 3:4], ALU.mult, ALU.subtract), r=(SM2,), w=(SM2,))
                E.op("vector", lambda e, p=eps: e.tensor_scalar(
                    SM2[:, 4:5], SM2[:, 4:5], p, None, ALU.add), sv, 1, r=(SM2,), w=(SM2,))
                E.wait("scalar", sv)
                E.op("scalar", lambda e: e.sqrt(SM2[:, 5:6], SM2[:, 4:5]), ss, 1, r=(SM2,), w=(SM2,))
                E.wait("vector", ss)
                E.op("vector", lambda e: e.reciprocal(SM2[:, 6:7], SM2[:, 5:6]), r=(SM2,), w=(SM2,))
                E.op("vector", lambda e: e.tensor_mul(SM2[:, 7:8], SM2[:, 2:3], SM2[:, 6:7]), r=(SM2,), w=(SM2,))
                E.op("vector", lambda e: e.tensor_scalar(
                    SM2[:, 7:8], SM2[:, 7:8], -1.0, None, ALU.mult), sv, 1, r=(SM2,), w=(SM2,))
                E.wait("scalar", sv)
                E.op("scalar", lambda e, o=out_bf, x=xsrc: e.activation(
                    o, x, AF.Identity, bias=SM2[:, 7:8], scale=SM2[:, 6:7]), ss, 1,
                    r=(xsrc, SM2), w=(out_bf,))

            def trans2(src_bf, dst_ap, nchunk, dst_name):
                E.wait("tensor", sv)
                E.wait("tensor", ss)
                for j in range(nchunk):
                    E.trans(PH2[:, 128 * j:128 * (j + 1)],
                            src_bf[:, 128 * j:128 * (j + 1)], ID2[:],
                            st if j == nchunk - 1 else None, 1)
                E.wait("vector", st)
                E.op("vector", lambda e, d=dst_ap, n=nchunk: e.tensor_copy(
                    d, PH2[:, 0:128 * n]), sv, 1, w=(dst_name,))

            E.wait("sync", sd)
            E.wait("tensor", sw)
            E.wait("tensor", sg)

            nsc = (nt + 7) // 8
            sd_mark2 = [None, None]
            for sc in range(nsc):
                tl = list(range(sc * 8, min(nt, sc * 8 + 8)))
                for t in tl:
                    ts_ = t - sc * 8
                    par = t % 2
                    E.wait("sync", st)
                    E.op("sync", lambda e, t=t, par=par: e.dma_start(
                        LTK[:, par, :], d_ln1[128 * t:128 * (t + 1), :]), sxp[par], 16)
                    E.wait("tensor", sxp[par])
                    trans2(LTK[:, par, :], LN1F[:, ts_, :], 8, LN1F)
                    E.wait("tensor", sv)
                    E.wait("tensor", ss)
                    for j in range(8):
                        lhsT = LN1F[:, ts_, 128 * j:128 * (j + 1)]
                        E.mm(PB[:, 6, 0:8], lhsT, RT[:, j, :], j == 0, j == 7)
                        for hb in range(2):
                            E.mm(PB[:, hb, :], lhsT, SH[:, j, 512 * hb:512 * (hb + 1)],
                                 j == 0, j == 7)
                        for hb in range(2):
                            E.mm(PB[:, 2 + hb, :], lhsT, SH[:, 8 + j, 512 * hb:512 * (hb + 1)],
                                 j == 0, j == 7, st if j == 7 and hb == 1 else None, 1)
                    E.wait("scalar", st)
                    E.op("scalar", lambda e: e.activation(SGb[:, 0:512], PB[:, 0, :], AF.Silu), w=(SGb,))
                    E.op("scalar", lambda e: e.activation(SGb[:, 512:1024], PB[:, 1, :], AF.Silu), w=(SGb,))
                    E.op("scalar", lambda e: e.activation(
                        SM2[:, 16:24], PB[:, 6, 0:8], AF.Exp,
                        accum_out=SM2[:, 24:25]), ss, 1, w=(SM2,))
                    E.wait("vector", ss)
                    E.op("vector", lambda e: e.tensor_tensor(
                        HS2[:, 0:512], PB[:, 2, :], SGb[:, 0:512], ALU.mult),
                        r=(SGb,), w=(HS2,))
                    E.op("vector", lambda e: e.tensor_tensor(
                        HS2[:, 512:1024], PB[:, 3, :], SGb[:, 512:1024], ALU.mult), sv, 1,
                        r=(SGb,), w=(HS2,))
                    E.op("vector", lambda e: e.reciprocal(SM2[:, 25:26], SM2[:, 24:25]), r=(SM2,), w=(SM2,))
                    E.op("vector", lambda e: e.tensor_scalar(
                        SM2[:, 16:24], SM2[:, 16:24], SM2[:, 25:26], None, ALU.mult), r=(SM2,), w=(SM2,))
                    T88 = SM2[:, 32:96].rearrange("p (a b) -> p a b", a=8)
                    p8 = SM2[:, 16:24]
                    E.op("vector", lambda e, T=T88: e.scalar_tensor_tensor(
                        T, p8.unsqueeze(2).broadcast_to([128, 8, 8]), -1.0,
                        p8.unsqueeze(1).broadcast_to([128, 8, 8]),
                        ALU.mult, ALU.add), r=(SM2,), w=(SM2,))
                    E.op("vector", lambda e, T=T88: e.scalar_tensor_tensor(
                        T, T, 1e8, SM2[:, 96:97].unsqueeze(1).broadcast_to([128, 8, 8]),
                        ALU.mult, ALU.add), sv, 1, r=(SM2,), w=(SM2,))
                    E.wait("scalar", sv)
                    E.op("scalar", lambda e: e.activation(SM2[:, 32:96], SM2[:, 32:96], AF.Relu),
                         ss, 1, r=(SM2,), w=(SM2,))
                    E.wait("vector", ss)
                    E.op("vector", lambda e, T=T88: e.scalar_tensor_tensor(
                        T, T, -1.0, SM2[:, 96:97].unsqueeze(1).broadcast_to([128, 8, 8]),
                        ALU.mult, ALU.add), sv, 1, r=(SM2,), w=(SM2,))
                    E.wait("scalar", sv)
                    E.op("scalar", lambda e: e.activation(SM2[:, 32:96], SM2[:, 32:96], AF.Relu),
                         ss, 1, r=(SM2,), w=(SM2,))
                    E.wait("vector", ss)
                    E.op("vector", lambda e, T=T88: e.reduce_sum(SM2[:, 104:112], T, AX.X),
                         r=(SM2,), w=(SM2,))
                    E.op("vector", lambda e: e.tensor_scalar(
                        SM2[:, 104:112], SM2[:, 104:112], SM2[:, 97:98], None, ALU.add), sv, 1,
                        r=(SM2,), w=(SM2,))
                    E.wait("scalar", sv)
                    E.op("scalar", lambda e: e.activation(SM2[:, 104:112], SM2[:, 104:112], AF.Relu),
                         ss, 1, r=(SM2,), w=(SM2,))
                    E.wait("vector", ss)
                    E.op("vector", lambda e: e.scalar_tensor_tensor(
                        SM2[:, 104:112], SM2[:, 104:112], -1.0,
                        SM2[:, 96:97].broadcast_to([128, 8]), ALU.mult, ALU.add), sv, 1,
                        r=(SM2,), w=(SM2,))
                    E.wait("scalar", sv)
                    E.op("scalar", lambda e: e.activation(SM2[:, 104:112], SM2[:, 104:112], AF.Relu),
                         ss, 1, r=(SM2,), w=(SM2,))
                    E.wait("vector", ss)
                    E.op("vector", lambda e: e.scalar_tensor_tensor(
                        SM2[:, 104:112], SM2[:, 104:112], -1.0,
                        SM2[:, 96:97].broadcast_to([128, 8]), ALU.mult, ALU.add),
                        r=(SM2,), w=(SM2,))
                    E.op("vector", lambda e, t=ts_: e.tensor_tensor(
                        CFA[:, t, :], SM2[:, 16:24], SM2[:, 104:112], ALU.mult), sv, 1,
                        r=(SM2,), w=(CFA,))
                    trans2(HS2[:], HST[:], 8, HST)
                    E.wait("tensor", sv)
                    for j in range(8):
                        for hb in range(2):
                            E.mm(PB[:, 4 + hb, :], HST[:, 128 * j:128 * (j + 1)],
                                 SH[:, 16 + j, 512 * hb:512 * (hb + 1)],
                                 j == 0, j == 7, st if j == 7 and hb == 1 else None, 1)
                    E.wait("vector", st)
                    E.op("vector", lambda e, t=ts_: e.tensor_copy(ACC[:, t, 0:512], PB[:, 4, :]), w=(ACC,))
                    E.op("vector", lambda e, t=ts_: e.tensor_copy(ACC[:, t, 512:1024], PB[:, 5, :]), sv, 1, w=(ACC,))

                sv_hcons = None
                for ei in range(8):
                    par = ei % 2
                    if ei < 7 or sc + 1 < nsc:
                        E.wait("gpsimd", sec)
                        eload((ei + 1) % 8, 1 - par)
                    E.wait("tensor", sep[par])
                    for t in tl:
                        ts_ = t - sc * 8
                        E.wait("tensor", ss)
                        E.wait("tensor", sv, sv_hcons)
                        for j in range(8):
                            lhsT = LN1F[:, ts_, 128 * j:128 * (j + 1)]
                            E.mm(PB[:, 0, :], lhsT, EWb[par][:, 512 * j:512 * (j + 1)],
                                 j == 0, j == 7)
                            E.mm(PB[:, 1, :], lhsT, EWb[par][:, 4096 + 512 * j:4096 + 512 * (j + 1)],
                                 j == 0, j == 7, st if j == 7 else None, 1)
                        E.wait("scalar", st)
                        E.op("scalar", lambda e: e.activation(SGb[:, 0:512], PB[:, 0, :], AF.Silu),
                             ss, 1, w=(SGb,))
                        E.wait("vector", ss)
                        E.op("vector", lambda e: e.tensor_tensor(
                            HE[:], PB[:, 1, :], SGb[:, 0:512], ALU.mult), r=(SGb,), w=(HE,))
                        E.op("vector", lambda e, t=ts_, ei=ei: e.tensor_scalar(
                            HE[:], HE[:], CFA[:, t, ei:ei + 1], None, ALU.mult), sv, 1,
                            r=(HE, CFA), w=(HE,))
                        sv_hcons = sv.n
                        trans2(HE[:], HET[:], 4, HET)
                        E.wait("tensor", sv)
                        for jd in range(4):
                            for hb in range(2):
                                E.mm(PB[:, 2 + hb, :], HET[:, 128 * jd:128 * (jd + 1)],
                                     EWb[par][:, 8192 + 1024 * jd + 512 * hb:8192 + 1024 * jd + 512 * (hb + 1)],
                                     jd == 0, jd == 3, st if jd == 3 and hb == 1 else None, 1)
                        E.wait("vector", st)
                        E.op("vector", lambda e, t=ts_: e.scalar_tensor_tensor(
                            ACC[:, t, 0:512], PB[:, 2, :], 1.0, ACC[:, t, 0:512],
                            ALU.mult, ALU.add), r=(ACC,), w=(ACC,))
                        E.op("vector", lambda e, t=ts_: e.scalar_tensor_tensor(
                            ACC[:, t, 512:1024], PB[:, 3, :], 1.0, ACC[:, t, 512:1024],
                            ALU.mult, ALU.add), sv, 1, r=(ACC,), w=(ACC,))
                        if t == tl[-1]:
                            E.op("vector", lambda e: e.memzero(SM2[:, 98:99]),
                                 sec, 1, w=(SM2,))

                def x1_load(t):
                    E.op("sync", lambda e, t=t, par=t % 2: e.dma_start(
                        XT1[:, par, :], d_x1[128 * t:128 * (t + 1), :]), sxq[t % 2], 16)
                E.wait("sync", sv)
                x1_load(tl[0])
                for i, t in enumerate(tl):
                    ts_ = t - sc * 8
                    par = t % 2
                    if i + 1 < len(tl):
                        E.wait("sync", sv)
                        x1_load(tl[i + 1])
                    E.wait("vector", sxq[par])
                    E.wait("vector", sd, sd_mark2[par])
                    E.op("vector", lambda e, t=ts_, par=par: e.tensor_add(
                        X2T[:, par, :], ACC[:, t, :], XT1[:, par, :]), sv, 1,
                        r=(ACC,), w=(X2T,))
                    E.wait("scalar", sv)
                    emit_ln2(X2T[:, par, :], LN2T[:, par, :])
                    ln2_ss = ss.n
                    E.wait("sync", sv)
                    E.op("sync", lambda e, t=t, par=par: e.dma_start(
                        d_x2[128 * t:128 * (t + 1), :], X2T[:, par, :]), sd, 16)
                    if y_x2 is not None:
                        E.op("sync", lambda e, t=t, par=par: e.dma_start(
                            y_x2[128 * t:128 * (t + 1), :], X2T[:, par, :]), sd, 16)
                    E.wait("sync", ss, ln2_ss)
                    E.op("sync", lambda e, t=t, par=par: e.dma_start(
                        d_ln2[128 * t:128 * (t + 1), :], LN2T[:, par, :]), sd, 16)
                    sd_mark2[par] = sd.n

            with nc.named_scope("s2"):
                P.run_block()
            E.barrier_reset()

        if max_phase >= 12:
          with ExitStack() as c3:
            en = c3.enter_context
            W3 = en(nc.sbuf_tensor("W3", [128, 26, 1024], BF16))
            MK3 = en(nc.sbuf_tensor("MK3", [128, 4, 128], BF16))
            GMB = en(nc.sbuf_tensor("GMB", [128, 1024], F32))
            BTB = en(nc.sbuf_tensor("BTB", [128, 1024], F32))
            RPT = en(nc.sbuf_tensor("RPT", [128, 2, 1024], BF16))
            LTK3 = en(nc.sbuf_tensor("LTK3", [128, 2, 1024], BF16))
            XTK3 = en(nc.sbuf_tensor("XTK3", [128, 2, 1024], F32))
            LN2F = en(nc.sbuf_tensor("LN2F", [128, 1024], BF16))
            QT3 = en(nc.sbuf_tensor("QT3", [128, 1024], BF16))
            KT3 = en(nc.sbuf_tensor("KT3", [128, 1024], BF16))
            VT3 = en(nc.sbuf_tensor("VT3", [128, 1024], BF16))
            QR3 = en(nc.sbuf_tensor("QR3", [128, 1024], BF16))
            KR3 = en(nc.sbuf_tensor("KR3", [128, 1024], BF16))
            RS3 = en(nc.sbuf_tensor("RS3", [128, 1024], F32))
            QF3b = en(nc.sbuf_tensor("QF3b", [128, 1024], BF16))
            QRF3 = en(nc.sbuf_tensor("QRF3", [128, 1024], BF16))
            KVP = en(nc.sbuf_tensor("KVP", [128, 520], F32))
            KVQ = en(nc.sbuf_tensor("KVQ", [128, 520], F32))
            KVSh3 = en(nc.sbuf_tensor("KVSh3", [128, 512], BF16))
            KMH = en(nc.sbuf_tensor("KMH", [128, 8, 16], BF16))
            ONES3 = en(nc.sbuf_tensor("ONES3", [128, 1], BF16))
            SL3 = en(nc.sbuf_tensor("SL3", [1, 1], U32))
            FL3 = en(nc.sbuf_tensor("FL3", [1, 16], U32))
            FLo = en(nc.sbuf_tensor("FLo", [1, 16], U32))
            RES3 = en(nc.sbuf_tensor("RES3", [128, 1024], F32))
            LNO3 = en(nc.sbuf_tensor("LNO3", [128, 1024], F32))
            X3T = en(nc.sbuf_tensor("X3T", [128, 2, 1024], F32))
            LN3T = en(nc.sbuf_tensor("LN3T", [128, 2, 1024], BF16))
            SM3 = en(nc.sbuf_tensor("SM3", [128, 128], F32))
            SQ3 = en(nc.sbuf_tensor("SQ3", [128, 1024], F32))
            PC = en(nc.psum_tensor("PC", [128, 7, 512], F32))
            PH3 = en(nc.psum_tensor("PH3", [128, 1024], BF16))

            for c0 in range(0, 26, 4):
                c1_ = min(26, c0 + 4)
                E.op("gpsimd", lambda e, c0=c0, c1_=c1_: e.dma_start(
                    W3[:, c0:c1_, :], w3T[:, c0:c1_, :]), sw, 16)
            for i, sl_ in enumerate((1, 7, 8, 9)):
                E.op("gpsimd", lambda e, i=i, sl_=sl_: e.dma_start(
                    MK3[:, i, :], masks[:, sl_, :]), sw, 16)
            E.op("gpsimd", lambda e: e.dma_start(SL3[:], slot[:]), sw, 16)
            E.op("gpsimd", lambda e: e.memset(FLo[:], 7))
            E.op("gpsimd", lambda e: e.memset(ONES3[:], 1.0))
            E.op("gpsimd", lambda e: e.memset(KVP[:], 0.0))
            E.op("gpsimd", lambda e: e.memset(KVQ[:], 0.0))
            E.op("gpsimd", lambda e: e.memset(KMH[:], 0.0), sg, 1)
            E.wait("tensor", sw)
            E.wait("tensor", sg)
            for hb in range(2):
                E.mm(PC[:, hb, :], MK3[:, 1, :], W3[:, 25, 512 * hb:512 * (hb + 1)],
                     True, True)
                E.mm(PC[:, 2 + hb, :], MK3[:, 2, :], W3[:, 25, 512 * hb:512 * (hb + 1)],
                     True, True, st if hb == 1 else None, 1)
            E.wait("vector", st)
            for hb in range(2):
                E.op("vector", lambda e, hb=hb: e.tensor_copy(
                    GMB[:, 512 * hb:512 * (hb + 1)], PC[:, hb, :]), w=(GMB,))
                E.op("vector", lambda e, hb=hb: e.tensor_copy(
                    BTB[:, 512 * hb:512 * (hb + 1)], PC[:, 2 + hb, :]), sv, 1, w=(BTB,))
            E.wait("sync", sd)

            def trans3(src_bf, dst_ap, nchunk, dst_name, wait_sv=False):
                E.wait("tensor", stf)
                if wait_sv:
                    E.wait("tensor", sv)
                E.wait("tensor", ss)
                for j in range(nchunk):
                    E.trans(PH3[:, 128 * j:128 * (j + 1)],
                            src_bf[:, 128 * j:128 * (j + 1)], MK3[:, 0, :],
                            st if j == nchunk - 1 else None, 1)
                E.wait("vector", st)
                E.op("vector", lambda e, d=dst_ap, n=nchunk: e.tensor_copy(
                    d, PH3[:, 0:128 * n]), stf, 1, w=(dst_name,))

            def emit_ln3(xsrc, out_t, eps):
                E.op("vector", lambda e, x=xsrc: e.reduce_sum(SM3[:, 0:1], x, AX.X), sv, 1,
                     r=(xsrc,), w=(SM3,))
                E.op("scalar", lambda e, x=xsrc: e.activation(
                    SQ3[:], x, AF.Square, accum_out=SM3[:, 1:2]), ss, 1,
                    r=(xsrc,), w=(SQ3, SM3))
                E.wait("vector", ss)
                E.op("vector", lambda e: e.tensor_scalar(
                    SM3[:, 2:3], SM3[:, 0:1], 1.0 / D, None, ALU.mult), r=(SM3,), w=(SM3,))
                E.op("vector", lambda e: e.tensor_mul(SM3[:, 3:4], SM3[:, 2:3], SM3[:, 2:3]), r=(SM3,), w=(SM3,))
                E.op("vector", lambda e: e.scalar_tensor_tensor(
                    SM3[:, 4:5], SM3[:, 1:2], 1.0 / D, SM3[:, 3:4], ALU.mult, ALU.subtract), r=(SM3,), w=(SM3,))
                E.op("vector", lambda e, p=eps: e.tensor_scalar(
                    SM3[:, 4:5], SM3[:, 4:5], p, None, ALU.add), sv, 1, r=(SM3,), w=(SM3,))
                E.wait("scalar", sv)
                E.op("scalar", lambda e: e.sqrt(SM3[:, 5:6], SM3[:, 4:5]), ss, 1, r=(SM3,), w=(SM3,))
                E.wait("vector", ss)
                E.op("vector", lambda e: e.reciprocal(SM3[:, 6:7], SM3[:, 5:6]), r=(SM3,), w=(SM3,))
                E.op("vector", lambda e: e.tensor_mul(SM3[:, 7:8], SM3[:, 2:3], SM3[:, 6:7]), r=(SM3,), w=(SM3,))
                E.op("vector", lambda e: e.tensor_scalar(
                    SM3[:, 7:8], SM3[:, 7:8], -1.0, None, ALU.mult), sv, 1, r=(SM3,), w=(SM3,))
                E.wait("scalar", sv)
                E.op("scalar", lambda e, o=out_t, x=xsrc: e.activation(
                    o, x, AF.Identity, bias=SM3[:, 7:8], scale=SM3[:, 6:7]), ss, 1,
                    r=(xsrc, SM3), w=(out_t,))

            def emit_rope(src_bf, dst_bf, par):
                xr = src_bf.rearrange("p (a b) -> p a b", a=512)
                dr = dst_bf.rearrange("p (a b) -> p a b", a=512)
                cos3 = RPT[:, par, 0:512].unsqueeze(2)
                sin3 = RPT[:, par, 512:1024].unsqueeze(2)
                t1 = RS3[:, 0:512].rearrange("p (a b) -> p a b", a=512)
                t2 = RS3[:, 512:1024].rearrange("p (a b) -> p a b", a=512)
                E.op("vector", lambda e: e.tensor_tensor(
                    t1, xr[:, :, 0:1], cos3, ALU.mult), r=(src_bf, RPT), w=(RS3,))
                E.op("vector", lambda e: e.tensor_tensor(
                    t2, xr[:, :, 1:2], sin3, ALU.mult), r=(src_bf, RPT), w=(RS3,))
                E.op("vector", lambda e: e.scalar_tensor_tensor(
                    dr[:, :, 0:1], t2, -1.0, t1, ALU.mult, ALU.add),
                    r=(RS3,), w=(dst_bf,))
                E.op("vector", lambda e: e.tensor_tensor(
                    t1, xr[:, :, 1:2], cos3, ALU.mult), r=(src_bf, RPT), w=(RS3,))
                E.op("vector", lambda e: e.tensor_tensor(
                    t2, xr[:, :, 0:1], sin3, ALU.mult), r=(src_bf, RPT), w=(RS3,))
                E.op("vector", lambda e: e.tensor_add(
                    dr[:, :, 1:2], t1, t2), sv, 1, r=(RS3,), w=(dst_bf,))

            def k_load(t):
                par = t % 2
                E.op("sync", lambda e, t=t, par=par: e.dma_start(
                    LTK3[:, par, :], d_ln2[128 * t:128 * (t + 1), :]), sxp[par], 16)
                E.op("sync", lambda e, t=t, par=par: e.dma_start(
                    RPT[:, par, :], d_rope[128 * t:128 * (t + 1), :]), sxp[par], 16)
            E.wait("sync", st)
            E.wait("sync", sv)
            k_load(0)
            sv_kcons = None
            for t in range(nt):
                par = t % 2
                if t + 1 < nt:
                    E.wait("sync", st)
                    E.wait("sync", sv)
                    k_load(t + 1)
                E.wait("tensor", sxp[par])
                trans3(LTK3[:, par, :], LN2F[:], 8, LN2F)
                E.wait("tensor", stf)
                E.wait("tensor", sv, sv_kcons)
                E.wait("tensor", ss)
                for j in range(8):
                    lhsT = LN2F[:, 128 * j:128 * (j + 1)]
                    for hb in range(2):
                        E.mm(PC[:, hb, :], lhsT, W3[:, 8 + j, 512 * hb:512 * (hb + 1)],
                             j == 0, False)
                    for hb in range(2):
                        E.mm(PC[:, 2 + hb, :], lhsT, W3[:, 16 + j, 512 * hb:512 * (hb + 1)],
                             j == 0, False)
                for hb in range(2):
                    E.mm(PC[:, hb, :], MK3[:, 2, :], W3[:, 24, 512 * hb:512 * (hb + 1)],
                         False, True)
                    E.mm(PC[:, 2 + hb, :], MK3[:, 3, :], W3[:, 24, 512 * hb:512 * (hb + 1)],
                         False, True, st if hb == 1 else None, 1)
                E.wait("scalar", st)
                E.op("scalar", lambda e: e.activation(KT3[:, 0:512], PC[:, 0, :], AF.Gelu), w=(KT3,))
                E.op("scalar", lambda e: e.activation(KT3[:, 512:1024], PC[:, 1, :], AF.Gelu), ss, 1, w=(KT3,))
                E.wait("vector", ss)
                E.op("vector", lambda e: e.tensor_scalar(
                    KT3[:], KT3[:], 0.21, None, ALU.add), r=(KT3,), w=(KT3,))
                E.op("vector", lambda e: e.tensor_copy(VT3[:, 0:512], PC[:, 2, :]), w=(VT3,))
                E.op("vector", lambda e: e.tensor_copy(VT3[:, 512:1024], PC[:, 3, :]), sv, 1, w=(VT3,))
                sv_kcons = sv.n
                emit_rope(KT3[:], KR3[:], par)
                E.wait("tensor", sv)
                for j in range(8):
                    E.mm(PC[:, 4 + j // 4, (j % 4) * 128:(j % 4) * 128 + 128],
                         KR3[:, 128 * j:128 * (j + 1)], VT3[:, 128 * j:128 * (j + 1)],
                         j % 4 == 0, j % 4 == 3, None, 1)
                    E.mm(PC[:, 6, j:j + 1], KT3[:, 128 * j:128 * (j + 1)], ONES3[:],
                         True, True, st if j == 7 else None, 1)
                E.wait("vector", st)
                for half in range(2):
                    pr = slice(64 * half, 64 * (half + 1))
                    co = slice(64 * half, 64 * half + 64)
                    E.op("vector", lambda e, pr=pr, co=co: e.tensor_add(
                        KVP[pr, 0:512].rearrange("p (a b) -> p a b", a=8),
                        KVP[pr, 0:512].rearrange("p (a b) -> p a b", a=8),
                        PC[pr, 4:6, :].rearrange("p a (b c) -> p (a b) c", b=4)[:, :, co]),
                        r=(KVP,), w=(KVP,))
                E.op("vector", lambda e: e.tensor_add(
                    KVP[:, 512:520], KVP[:, 512:520], PC[:, 6, 0:8]), sv, 1,
                    r=(KVP,), w=(KVP,))

            if not solo:
                E.wait("gpsimd", sv)

                def exch(g, base=sg.n, spl=spl):
                    with g.register("sl3") as rsl:
                        g.reg_load(rsl, SL3[0:1, 0:1].bitcast(I32))
                        for me in (0, 1):
                            sib = 1 - me
                            with g.If_eq(rsl, me):
                                g.dma_start(d_ex[me, :, :], KVP[:]).then_inc(sg.h, 16)
                                g.wait_ge(sg.h, base + 16)
                                g.dma_start(d_exf[me:me + 1, :], FLo[:]).then_inc(sg.h, 16)
                                g.wait_ge(sg.h, base + 32)
                                with g.register(f"fl3{me}") as rfl:
                                    g.reg_mov(rfl, 0)
                                    with g.Fori(0, 4000) as _i:
                                        with g.If_ne(rfl, 7):
                                            g.dma_start(FL3[:], d_exf[sib:sib + 1, :]).then_inc(spl.h, 16)
                                            g.drain()
                                            g.reg_load(rfl, FL3[0:1, 0:1].bitcast(I32))
                                g.dma_start(KVQ[:], d_ex[sib, :, :]).then_inc(sg.h, 16)
                                g.wait_ge(sg.h, base + 48)
                            g.end_ifs()
                E.op("gpsimd", exch, None, 0, r=(KVP,), w=(KVQ,))
                sg.n += 48
            E.wait("vector", sg)
            E.op("vector", lambda e: e.tensor_add(RS3[:, 0:512], KVP[:, 0:512], KVQ[:, 0:512]),
                 r=(KVP, KVQ), w=(RS3,))
            E.op("vector", lambda e: e.tensor_scalar(
                RS3[:, 0:512], RS3[:, 0:512], 1.0 / 8192.0, None, ALU.mult), r=(RS3,), w=(RS3,))
            E.op("vector", lambda e: e.tensor_copy(KVSh3[:], RS3[:, 0:512]), r=(RS3,), w=(KVSh3,))
            E.op("vector", lambda e: e.tensor_add(SM3[:, 8:16], KVP[:, 512:520], KVQ[:, 512:520]),
                 r=(KVP, KVQ), w=(SM3,))
            E.op("vector", lambda e: e.tensor_scalar(
                SM3[:, 8:16], SM3[:, 8:16], 1.0 / 8192.0, None, ALU.mult), r=(SM3,), w=(SM3,))
            for j in range(8):
                E.op("vector", lambda e, j=j: e.tensor_copy(
                    KMH[0:64, j, 2 * j:2 * j + 1], SM3[0:64, 8 + j:9 + j]), r=(SM3,), w=(KMH,))
                E.op("vector", lambda e, j=j: e.tensor_copy(
                    KMH[64:128, j, 2 * j + 1:2 * j + 2], SM3[64:128, 8 + j:9 + j]),
                    sv if j == 7 else None, 1, r=(SM3,), w=(KMH,))

            def q_load(t):
                par = t % 2
                E.op("sync", lambda e, t=t, par=par: e.dma_start(
                    LTK3[:, par, :], d_ln2[128 * t:128 * (t + 1), :]), sxp[par], 16)
                E.op("sync", lambda e, t=t, par=par: e.dma_start(
                    RPT[:, par, :], d_rope[128 * t:128 * (t + 1), :]), sxp[par], 16)
                E.op("sync", lambda e, t=t, par=par: e.dma_start(
                    XTK3[:, par, :], d_x2[128 * t:128 * (t + 1), :]), sxq[par], 16)
            E.wait("sync", st)
            E.wait("sync", sv)
            q_load(0)
            sd_mark3 = [None, None]
            for t in range(nt):
                par = t % 2
                if t + 1 < nt:
                    E.wait("sync", st)
                    E.wait("sync", sv)
                    q_load(t + 1)
                E.wait("tensor", sxp[par])
                trans3(LTK3[:, par, :], LN2F[:], 8, LN2F)
                E.wait("tensor", stf)
                E.wait("tensor", ss)
                for j in range(8):
                    lhsT = LN2F[:, 128 * j:128 * (j + 1)]
                    for hb in range(2):
                        E.mm(PC[:, hb, :], lhsT, W3[:, j, 512 * hb:512 * (hb + 1)],
                             j == 0, False)
                for hb in range(2):
                    E.mm(PC[:, hb, :], MK3[:, 1, :], W3[:, 24, 512 * hb:512 * (hb + 1)],
                         False, True, st if hb == 1 else None, 1)
                E.wait("scalar", st)
                E.op("scalar", lambda e: e.activation(QT3[:, 0:512], PC[:, 0, :], AF.Gelu), w=(QT3,))
                E.op("scalar", lambda e: e.activation(QT3[:, 512:1024], PC[:, 1, :], AF.Gelu), ss, 1, w=(QT3,))
                E.wait("vector", ss)
                E.op("vector", lambda e: e.tensor_scalar(
                    QT3[:], QT3[:], 0.21, None, ALU.add), sv, 1, r=(QT3,), w=(QT3,))
                emit_rope(QT3[:], QR3[:], par)
                trans3(QT3[:], QF3b[:], 8, QF3b, wait_sv=True)
                trans3(QR3[:], QRF3[:], 8, QRF3, wait_sv=True)
                E.wait("tensor", stf)
                E.wait("tensor", sv)
                for j in range(8):
                    E.mm(PC[:, 6, 0:16], QF3b[:, 128 * j:128 * (j + 1)], KMH[:, j, :],
                         j == 0, j == 7)
                for h in range(0, 16, 2):
                    E.mm(PC[:, 2, 64 * (h // 2):64 * (h // 2) + 64],
                         hs(QRF3, h), KVSh3[0:64, 64 * (h // 2):64 * (h // 2) + 64],
                         True, True, st if h == 14 else None, 1, tp=(0, 0))
                for h in range(1, 16, 2):
                    E.mm(PC[:, 3, 64 * (h // 2):64 * (h // 2) + 64],
                         hs(QRF3, h), KVSh3[64:128, 64 * (h // 2):64 * (h // 2) + 64],
                         True, True, st if h == 15 else None, 1, tp=(64, 0))
                E.wait("vector", st)
                E.op("vector", lambda e: e.tensor_scalar(
                    SM3[:, 16:32], PC[:, 6, 0:16], 1e-6, None, ALU.add), w=(SM3,))
                E.op("vector", lambda e: e.reciprocal(SM3[:, 16:32], SM3[:, 16:32]),
                     r=(SM3,), w=(SM3,))
                zb = SM3[:, 16:32].rearrange("p (a b) -> p a b", a=8) \
                    .unsqueeze(3).broadcast_to([128, 8, 2, 64])
                E.op("vector", lambda e, zb=zb: e.tensor_tensor(
                    RES3[:].rearrange("p (a b c) -> p a b c", a=8, b=2),
                    PC[:, 2:4, :].rearrange("p a (b c) -> p b a c", b=8),
                    zb, ALU.mult), r=(SM3,), w=(RES3,))
                E.op("vector", lambda e, par=par: e.tensor_add(
                    RES3[:], RES3[:], LTK3[:, par, :]), sv, 1, r=(RES3,), w=(RES3,))
                E.wait("scalar", sv)
                emit_ln3(RES3[:], LNO3[:], 1e-12)
                E.wait("vector", ss)
                E.op("vector", lambda e: e.tensor_tensor(RS3[:], LNO3[:], GMB[:], ALU.mult),
                     r=(LNO3, GMB), w=(RS3,))
                E.op("vector", lambda e: e.tensor_add(RS3[:], RS3[:], BTB[:]),
                     r=(RS3, BTB), w=(RS3,))
                E.wait("vector", sxq[par])
                E.wait("vector", sd, sd_mark3[par])
                E.op("vector", lambda e, par=par: e.tensor_add(
                    X3T[:, par, :], RS3[:], XTK3[:, par, :]), sv, 1, r=(RS3,), w=(X3T,))
                E.wait("scalar", sv)
                emit_ln3(X3T[:, par, :], LN3T[:, par, :], 1e-5)
                ln3_ss = ss.n
                E.wait("sync", sv)
                E.op("sync", lambda e, t=t, par=par: e.dma_start(
                    d_x3[128 * t:128 * (t + 1), :], X3T[:, par, :]), sd, 16)
                if y_x3 is not None:
                    E.op("sync", lambda e, t=t, par=par: e.dma_start(
                        y_x3[128 * t:128 * (t + 1), :], X3T[:, par, :]), sd, 16)
                E.wait("sync", ss, ln3_ss)
                E.op("sync", lambda e, t=t, par=par: e.dma_start(
                    d_ln3[128 * t:128 * (t + 1), :], LN3T[:, par, :]), sd, 16)
                sd_mark3[par] = sd.n

            with nc.named_scope("s3"):
                P.run_block()
            E.barrier_reset()

        if max_phase >= 13:
          with ExitStack() as c4:
            en = c4.enter_context
            EA = en(nc.sbuf_tensor("EA4", [128, 17, 1024], BF16))
            EB = en(nc.sbuf_tensor("EB4", [128, 17, 1024], BF16))
            WR4s = en(nc.sbuf_tensor("WR4s", [128, 9, 8], BF16))
            MK4 = en(nc.sbuf_tensor("MK4", [128, 3, 128], BF16))
            LTK4 = en(nc.sbuf_tensor("LTK4", [128, 2, 1024], BF16))
            XTK4 = en(nc.sbuf_tensor("XTK4", [128, 2, 1024], F32))
            LN3F = en(nc.sbuf_tensor("LN3F", [128, 1024], BF16))
            HA4 = en(nc.sbuf_tensor("HA4", [128, 1024], BF16))
            HB4 = en(nc.sbuf_tensor("HB4", [128, 1024], BF16))
            HAF = en(nc.sbuf_tensor("HAF", [128, 1024], BF16))
            HBF = en(nc.sbuf_tensor("HBF", [128, 1024], BF16))
            X4T = en(nc.sbuf_tensor("X4T", [128, 2, 1024], F32))
            SM4 = en(nc.sbuf_tensor("SM4", [128, 128], F32))
            SL4 = en(nc.sbuf_tensor("SL4", [1, 1], U32))
            FL4 = en(nc.sbuf_tensor("FL4", [1, 1], U32))
            FLo4 = en(nc.sbuf_tensor("FLo4", [1, 1], U32))
            PD = en(nc.psum_tensor("PD", [128, 7, 512], F32))
            PH4 = en(nc.psum_tensor("PH4", [128, 1024], BF16))

            E.op("gpsimd", lambda e: e.dma_start(WR4s[:], wr4[:]), sw, 16)
            E.op("gpsimd", lambda e: e.dma_start(MK4[:, 0, :], masks[:, 1, :]), sw, 16)
            E.op("gpsimd", lambda e: e.dma_start(MK4[:, 1, :], masks[:, 7, :]), sw, 16)
            E.op("gpsimd", lambda e: e.dma_start(MK4[:, 2, :], masks[:, 8, :]), sw, 16)
            E.op("gpsimd", lambda e: e.dma_start(SL4[:], slot[:]), sw, 16)
            E.op("gpsimd", lambda e: e.memset(FLo4[:], 9))
            E.op("gpsimd", lambda e: e.memset(SM4[:, 32:33], 1.0), sg, 1)
            E.wait("sync", sd)
            E.wait("tensor", sw)
            E.wait("tensor", sg)

            def trans4(src_bf, dst_ap, nchunk, dst_name):
                E.wait("tensor", stf)
                E.wait("tensor", ss)
                for j in range(nchunk):
                    E.trans(PH4[:, 128 * j:128 * (j + 1)],
                            src_bf[:, 128 * j:128 * (j + 1)], MK4[:, 0, :],
                            st if j == nchunk - 1 else None, 1)
                E.wait("vector", st)
                E.op("vector", lambda e, d=dst_ap, n=nchunk: e.tensor_copy(
                    d, PH4[:, 0:128 * n]), stf, 1, w=(dst_name,))

            E.op("sync", lambda e: e.dma_start(
                LTK4[:, 0, :], d_ln3[0:128, :]), sxp[0], 16)
            E.wait("tensor", sxp[0])
            trans4(LTK4[:, 0, :], LN3F[:], 8, LN3F)
            E.wait("tensor", stf)
            for j in range(8):
                E.mm(PD[:, 6, 0:8], LN3F[:, 128 * j:128 * (j + 1)], WR4s[:, j, :],
                     j == 0, False)
            E.mm(PD[:, 6, 0:8], MK4[:, 1, :], WR4s[:, 8, :], False, True, st, 1)
            E.wait("vector", st)
            E.op("vector", lambda e: e.tensor_copy(SM4[:, 0:8], PD[:, 6, 0:8]), sv, 1,
                 w=(SM4,))
            if not solo:
                E.wait("gpsimd", sv)

                def exch4(g, base=sg.n, spl=spl):
                    with g.register("sl4") as rsl:
                        g.reg_load(rsl, SL4[0:1, 0:1].bitcast(I32))
                        for me in (0, 1):
                            with g.If_eq(rsl, me):
                                g.dma_start(d_ex2[me, :, :], SM4[0:1, 0:8]).then_inc(sg.h, 16)
                                g.wait_ge(sg.h, base + 16)
                                g.dma_start(d_exf[me:me + 1, 4:5], FLo4[:]).then_inc(sg.h, 16)
                                g.wait_ge(sg.h, base + 32)
                            g.end_ifs()
                    with g.register("fl4") as rfl:
                        g.reg_mov(rfl, 0)
                        with g.Fori(0, 4000) as _i:
                            with g.If_ne(rfl, 9):
                                g.dma_start(FL4[:], d_exf[0:1, 4:5]).then_inc(spl.h, 16)
                                g.drain()
                                g.reg_load(rfl, FL4[0:1, 0:1].bitcast(I32))
                    g.dma_start(SM4[0:1, 0:8], d_ex2[0, :, :]).then_inc(sg.h, 16)
                    g.wait_ge(sg.h, base + 48)
                E.op("gpsimd", exch4, None, 0, r=(SM4,), w=(SM4,))
                sg.n += 48
                E.wait("vector", sg)
            p8b = SM4[:, 0:8]
            T884 = SM4[:, 40:104].rearrange("p (a b) -> p a b", a=8)
            E.op("vector", lambda e, T=T884: e.scalar_tensor_tensor(
                T, p8b.unsqueeze(2).broadcast_to([128, 8, 8]), -1.0,
                p8b.unsqueeze(1).broadcast_to([128, 8, 8]),
                ALU.mult, ALU.add), r=(SM4,), w=(SM4,))
            E.op("vector", lambda e, T=T884: e.scalar_tensor_tensor(
                T, T, 1e8, SM4[:, 32:33].unsqueeze(1).broadcast_to([128, 8, 8]),
                ALU.mult, ALU.add), sv, 1, r=(SM4,), w=(SM4,))
            E.wait("scalar", sv)
            E.op("scalar", lambda e: e.activation(SM4[:, 40:104], SM4[:, 40:104], AF.Relu),
                 ss, 1, r=(SM4,), w=(SM4,))
            E.wait("vector", ss)
            E.op("vector", lambda e, T=T884: e.scalar_tensor_tensor(
                T, T, -1.0, SM4[:, 32:33].unsqueeze(1).broadcast_to([128, 8, 8]),
                ALU.mult, ALU.add), sv, 1, r=(SM4,), w=(SM4,))
            E.wait("scalar", sv)
            E.op("scalar", lambda e: e.activation(SM4[:, 40:104], SM4[:, 40:104], AF.Relu),
                 ss, 1, r=(SM4,), w=(SM4,))
            E.wait("vector", ss)
            E.op("vector", lambda e, T=T884: e.reduce_sum(SM4[:, 8:16], T, AX.X),
                 r=(SM4,), w=(SM4,))
            E.op("vector", lambda e: e.tensor_scalar(
                SM4[:, 16:24], SM4[:, 8:16], -5.0, None, ALU.add), r=(SM4,), w=(SM4,))
            E.op("vector", lambda e: e.tensor_scalar(
                SM4[:, 24:32], SM4[:, 8:16], -6.0, None, ALU.add), sv, 1, r=(SM4,), w=(SM4,))
            E.wait("scalar", sv)
            E.op("scalar", lambda e: e.activation(SM4[:, 16:32], SM4[:, 16:32], AF.Relu),
                 ss, 1, r=(SM4,), w=(SM4,))
            E.wait("vector", ss)
            E.op("vector", lambda e: e.scalar_tensor_tensor(
                SM4[:, 16:24], SM4[:, 16:24], -1.0,
                SM4[:, 32:33].broadcast_to([128, 8]), ALU.mult, ALU.add), sv, 1,
                r=(SM4,), w=(SM4,))
            E.wait("scalar", sv)
            E.op("scalar", lambda e: e.activation(SM4[:, 16:24], SM4[:, 16:24], AF.Relu),
                 ss, 1, r=(SM4,), w=(SM4,))
            E.wait("vector", ss)
            E.op("vector", lambda e: e.scalar_tensor_tensor(
                SM4[:, 16:24], SM4[:, 16:24], -1.0,
                SM4[:, 32:33].broadcast_to([128, 8]), ALU.mult, ALU.add),
                r=(SM4,), w=(SM4,))
            E.op("vector", lambda e: e.scalar_tensor_tensor(
                SM4[:, 24:32], SM4[:, 24:32], -1.0, SM4[:, 16:24],
                ALU.mult, ALU.add), sv, 1, r=(SM4,), w=(SM4,))
            E.op("vector", lambda e: e.scalar_tensor_tensor(
                SM4[:, 8:16], SM4[:, 24:32], -1.0, SM4[:, 16:24],
                ALU.mult, ALU.add), sv, 1, r=(SM4,), w=(SM4,))
            E.wait("gpsimd", sv)

            def gather_emit(g, base=sgw.n):
                for cols, buf in ((8, EA), (24, EB)):
                    with g.register(f"rk{cols}") as rk:
                        for e in range(8):
                            g.reg_load(rk, SM4[0:1, cols + e:cols + e + 1].bitcast(I32))
                            with g.If_eq(rk, 0x3F800000):
                                for c0 in range(0, 17, 3):
                                    c1_ = min(17, c0 + 3)
                                    g.dma_start(buf[:, c0:c1_, :],
                                                ew4[e, :, c0:c1_, :]).then_inc(sgw.h, 16)
                            g.end_ifs()
                g.wait_ge(sgw.h, base + 2 * 6 * 16)
            E.op("gpsimd", gather_emit, None, 0, r=(SM4,), w=(EA, EB))
            sgw.n += 2 * 6 * 16
            E.wait("tensor", sgw)

            def s4_load(t):
                par = t % 2
                E.op("sync", lambda e, t=t, par=par: e.dma_start(
                    LTK4[:, par, :], d_ln3[128 * t:128 * (t + 1), :]), sxp[par], 16)
                E.op("sync", lambda e, t=t, par=par: e.dma_start(
                    XTK4[:, par, :], d_x3[128 * t:128 * (t + 1), :]), sxq[par], 16)
            E.wait("sync", st)
            E.wait("sync", sv)
            s4_load(0)
            sd_mark = [None, None]
            for t in range(nt):
                par = t % 2
                if t + 1 < nt:
                    E.wait("sync", st)
                    E.wait("sync", sv)
                    s4_load(t + 1)
                E.wait("tensor", sxp[par])
                trans4(LTK4[:, par, :], LN3F[:], 8, LN3F)
                E.wait("tensor", stf)
                E.wait("tensor", ss)
                for j in range(8):
                    lhsT = LN3F[:, 128 * j:128 * (j + 1)]
                    for hb in range(2):
                        E.mm(PD[:, hb, :], lhsT, EA[:, j, 512 * hb:512 * (hb + 1)],
                             j == 0, False)
                        E.mm(PD[:, 2 + hb, :], lhsT, EB[:, j, 512 * hb:512 * (hb + 1)],
                             j == 0, False)
                for hb in range(2):
                    E.mm(PD[:, hb, :], MK4[:, 1, :], EA[:, 16, 512 * hb:512 * (hb + 1)],
                         False, True)
                    E.mm(PD[:, 2 + hb, :], MK4[:, 1, :], EB[:, 16, 512 * hb:512 * (hb + 1)],
                         False, True, st if hb == 1 else None, 1)
                E.wait("scalar", st)
                E.op("scalar", lambda e: e.activation(HA4[:, 0:512], PD[:, 0, :], AF.Gelu), w=(HA4,))
                E.op("scalar", lambda e: e.activation(HA4[:, 512:1024], PD[:, 1, :], AF.Gelu), w=(HA4,))
                E.op("scalar", lambda e: e.activation(HB4[:, 0:512], PD[:, 2, :], AF.Gelu), w=(HB4,))
                E.op("scalar", lambda e: e.activation(HB4[:, 512:1024], PD[:, 3, :], AF.Gelu), ss, 1, w=(HB4,))
                trans4(HA4[:], HAF[:], 8, HAF)
                trans4(HB4[:], HBF[:], 8, HBF)
                E.wait("tensor", stf)
                E.wait("tensor", sv)
                for j in range(8):
                    for hb in range(2):
                        E.mm(PD[:, 4 + hb, :], HAF[:, 128 * j:128 * (j + 1)],
                             EA[:, 8 + j, 512 * hb:512 * (hb + 1)], j == 0, False)
                for j in range(8):
                    for hb in range(2):
                        E.mm(PD[:, 4 + hb, :], HBF[:, 128 * j:128 * (j + 1)],
                             EB[:, 8 + j, 512 * hb:512 * (hb + 1)], False, False)
                for hb in range(2):
                    E.mm(PD[:, 4 + hb, :], MK4[:, 2, :],
                         EA[:, 16, 512 * hb:512 * (hb + 1)], False, False)
                for hb in range(2):
                    E.mm(PD[:, 4 + hb, :], MK4[:, 2, :],
                         EB[:, 16, 512 * hb:512 * (hb + 1)], False, True,
                         st if hb == 1 else None, 1)
                E.wait("vector", st)
                E.wait("vector", sxq[par])
                E.wait("vector", sd, sd_mark[par])
                for hb in range(2):
                    E.op("vector", lambda e, hb=hb, par=par: e.scalar_tensor_tensor(
                        X4T[:, par, 512 * hb:512 * (hb + 1)], PD[:, 4 + hb, :], 0.5,
                        XTK4[:, par, 512 * hb:512 * (hb + 1)], ALU.mult, ALU.add),
                        sv if hb == 1 else None, 1, w=(X4T,))
                E.wait("sync", sv)
                E.op("sync", lambda e, t=t, par=par: e.dma_start(
                    y[128 * t:128 * (t + 1), :], X4T[:, par, :]), sd, 16)
                sd_mark[par] = sd.n

            with nc.named_scope("s4"):
                P.run_block()
            E.barrier_reset()

        E.wait("sync", sd)
        if max_phase < 13:
            d_last = d_x3 if max_phase >= 12 else (d_x2 if max_phase >= 11 else d_x1)
            rows = min(HALF, nt * 128) if max_phase >= 10 else 0
            if rows:
                E.op("sync", lambda e, r=rows, s=d_last: e.dma_start(y[0:r, :], s[0:r, :]), sd, 16)
            if rows < HALF:
                E.op("sync", lambda e, r=rows: e.dma_start(
                    y[r:HALF, :], x_own[r:HALF, :]), sd, 16)
        E.wait("sync", sd)
        E.wait("vector", sw)
        E.wait("vector", sep[0])
        E.wait("vector", sep[1])
        E.wait("sync", sxp[0])
        E.wait("sync", sxp[1])
        E.wait("sync", sxq[0])
        E.wait("sync", sxq[1])
        with nc.named_scope("fin"):
            P.run_block()

    return nc



def _prep_core_inputs(c, I):
    b, hf = c // 4, c % 2
    x = np.asarray(I["L0_attn_x"])
    x_own = np.ascontiguousarray(x[b, hf * HALF:(hf + 1) * HALF, :], np.float32)
    x_pre = (np.ascontiguousarray(x[b, 0:HALF, :], np.float32) if hf == 1
             else np.zeros((HALF, D), np.float32))
    pre_gate = np.full((128, 1), 1.0 if hf == 1 else 0.0, np.float32)

    bias_chunk = np.zeros((128, 1, D), np.float32)
    bias_chunk[0, 0] = np.asarray(I["L0_attn_bg"])
    bias_chunk[32, 0] = np.asarray(I["L0_attn_bo"])
    bias_chunk[64, 0] = np.asarray(I["L0_attn_Wgo"]).reshape(-1)
    w5 = np.concatenate([_chunks_T(np.asarray(I[f"L0_attn_{n}"]))
                         for n in ("Wq", "Wk", "Wv", "Wkl", "Wvl")]
                        + [bias_chunk], axis=1)
    wg = _chunks_T(np.asarray(I["L0_attn_Wg"]))
    wo = _chunks_T(np.asarray(I["L0_attn_Wo"]))
    mk = _host_masks(hf == 1)
    misc = np.zeros((128, 1, 128), np.float32)
    misc[0, 0, 0:4] = np.asarray(I["L0_attn_haar_scale"])
    misc[0, 0, 4] = float(np.asarray(I["L0_attn_bgo"]).reshape(-1)[0])
    sel = np.zeros((128, 3, 128), np.float32)
    sel[0, 0, :] = 1.0
    sel[32, 1, :] = 1.0
    sel[64, 2, :] = 1.0
    mk = np.concatenate([mk, misc, sel], axis=1)
    bd = _host_bd(np.asarray(I["L0_attn_haar_Wk"]), np.asarray(I["L0_attn_haar_Wv"]))

    def _pack_expert(g, u, d):
        return np.concatenate([_chunks_N(g).reshape(128, -1),
                               _chunks_N(u).reshape(128, -1),
                               _chunks_N(d).reshape(128, -1)], 1)
    eg = np.asarray(I["L0_ffn_experts_gate"])
    eu = np.asarray(I["L0_ffn_experts_up"])
    ed = np.asarray(I["L0_ffn_experts_down"])
    ew = np.ascontiguousarray(np.stack(
        [_pack_expert(eg[e], eu[e], ed[e]) for e in range(8)]))
    shw = np.ascontiguousarray(np.concatenate(
        [_chunks_N(np.asarray(I["L0_ffn_shared_gate"])),
         _chunks_N(np.asarray(I["L0_ffn_shared_up"])),
         _chunks_N(np.asarray(I["L0_ffn_shared_down"]))], 1))
    rt8 = _chunks_N(np.asarray(I["L0_ffn_router"]))

    b3 = np.zeros((128, 1, D), np.float32)
    b3[0, 0] = np.asarray(I["L1_attn_bq"])
    b3[32, 0] = np.asarray(I["L1_attn_bk"])
    b3[64, 0] = np.asarray(I["L1_attn_bv"])
    gb = np.zeros((128, 1, D), np.float32)
    gb[0, 0] = np.asarray(I["L1_attn_gamma"])
    gb[32, 0] = np.asarray(I["L1_attn_beta"])
    w3 = np.concatenate(
        [_chunks_T(np.asarray(I[f"L1_attn_{n}"])) for n in ("Wq", "Wk", "Wv")]
        + [b3, gb], axis=1)
    g = hf * HALF + np.arange(HALF, dtype=np.float64)
    inv = 1.0 / (10000.0 ** (np.arange(512, dtype=np.float64) / 512.0))
    ang = g[:, None] * inv[None, :]
    rope = np.concatenate([np.cos(ang), np.sin(ang)], 1).astype(
        ml_dtypes.bfloat16)
    wr4 = np.concatenate([_chunks_N(np.asarray(I["L1_ffn_Wr"])),
                          np.zeros((128, 1, 8), np.float32)], 1)
    wr4[0, 8, :] = np.asarray(I["L1_ffn_br"])
    w1 = np.asarray(I["L1_ffn_W1"])
    w2 = np.asarray(I["L1_ffn_W2"])
    b1 = np.asarray(I["L1_ffn_b1"])
    b2 = np.asarray(I["L1_ffn_b2"])

    def _pack4(e):
        bc = np.zeros((128, 1, D), np.float32)
        bc[0, 0] = b1[e]
        bc[32, 0] = b2[e]
        return np.concatenate([_chunks_N(w1[e]), _chunks_N(w2[e]), bc], 1)
    ew4 = np.ascontiguousarray(np.stack([_pack4(e) for e in range(8)]))
    return {
        "x_own": x_own, "x_pre": x_pre, "pre_gate": pre_gate,
        "w5T": w5, "wgT": wg, "woT": wo,
        "bd_mats": bd, "at_mats": _host_at(), "masks": np.ascontiguousarray(mk),
        "rt8": rt8, "ew": ew, "shw": shw,
        "w3T": w3, "d_rope": np.ascontiguousarray(rope),
        "slot": np.array([[hf]], dtype=np.uint32),
        "wr4": np.ascontiguousarray(wr4), "ew4": ew4,
    }


def kernel(**inputs):
    nc = build_program()
    in_maps = [_prep_core_inputs(c, inputs) for c in range(NCORE)]
    res = None
    err = None
    for _att in range(3):
        try:
            res = run_bass_kernel_spmd(nc, in_maps, core_ids=list(range(NCORE)))
            break
        except Exception as e:
            err = e
    if res is None:
        raise err
    out = np.zeros((B, N, D), np.float32)
    for b in range(B):
        out[b, 0:HALF, :] = np.asarray(res.results[4 * b]["y"])
        out[b, HALF:N, :] = np.asarray(res.results[4 * b + 1]["y"])
    return out
